# version 08
# speedup vs baseline: 1.6497x; 1.0093x over previous
import contextlib
import numpy as np
import ml_dtypes
import concourse.bass as bass
import concourse.mybir as mybir
from concourse.bass_utils import run_bass_kernel_spmd

F32 = mybir.dt.float32
BF16 = mybir.dt.bfloat16
I32 = mybir.dt.int32
AF = mybir.ActivationFunctionType
ALU = mybir.AluOpType
AX = mybir.AxisListType
f32 = np.float32

NCORE = 8
D = 4096
T = 8192
TL = T // NCORE
NCH = D // 128
EXP_SHIFT = 16.0
INV_SQRT_HD = 1.0 / float(np.sqrt(128.0))
LN_EPS = 1e-5
CAPACITY = 3072.0
NEG = -1e30

ENGINES = ("sync", "gpsimd", "scalar", "vector", "tensor")
ENG_TYPE = {"sync": mybir.EngineType.SP, "gpsimd": mybir.EngineType.Pool,
            "scalar": mybir.EngineType.Activation, "vector": mybir.EngineType.DVE,
            "tensor": mybir.EngineType.PE}


class Prog:

    def __init__(self, nc, es):
        self.nc = nc
        self.sems = {e: es.enter_context(nc.semaphore("sem_" + e)) for e in ENGINES}
        self.cnt = {e: 0 for e in ENGINES}
        self.seen = {e: {x: 0 for x in ENGINES} for e in ENGINES}
        self.items = {e: [] for e in ENGINES}
        self.force_drain = False

    def mark(self):
        return dict(self.cnt)

    def _waits(self, eng, wait_at):
        if wait_at is None:
            wait_at = self.cnt
        elif wait_at == 0:
            wait_at = {}
        seen = self.seen[eng]
        ws = []
        for x, v in wait_at.items():
            if x == eng or v <= 0 or v <= seen[x]:
                continue
            seen[x] = v
            ws.append((self.sems[x], v))
        return ws

    def op(self, eng, fn, inc=1, wait_at=None, drain=False):
        if eng == "vector" and self.force_drain:
            drain = True
        ws = self._waits(eng, wait_at)
        self.cnt[eng] += inc

        def emit(e, fn=fn, ws=ws, inc=inc, sem=self.sems[eng], drain=drain):
            for s, v in ws:
                e.wait_ge(s, v)
            if drain:
                e.drain()
            fn(e).then_inc(sem, inc)
        self.items[eng].append(emit)

    def switch(self, eng, body_fn, inc, wait_at=None):
        ws = self._waits(eng, wait_at)
        self.cnt[eng] += inc
        nc = self.nc

        def emit(e, body_fn=body_fn, ws=ws, sem=self.sems[eng], nc=nc, eng=eng):
            for s, v in ws:
                e.wait_ge(s, v)
            pid = nc.partition_id(engines=[ENG_TYPE[eng]])
            for c in nc.Switch(engines=[e], index=[pid], n=8):
                body_fn(e, c, sem)
        self.items[eng].append(emit)

    def chain(self, eng, fns, inc=1, wait_at=None):
        ws = self._waits(eng, wait_at)
        self.cnt[eng] += inc

        def emit(e, fns=fns, ws=ws, inc=inc, sem=self.sems[eng]):
            for s, v in ws:
                e.wait_ge(s, v)
            ins = None
            for fn in fns:
                ins = fn(e)
            ins.then_inc(sem, inc)
        self.items[eng].append(emit)

    def finish(self, block):
        finals = dict(self.cnt)

        for name in ENGINES:
            items = self.items[name]

            def body(e, items=items, name=name, finals=finals, sems=self.sems):
                for emit in items:
                    emit(e)
                for x in ENGINES:
                    if x != name and finals[x] > 0:
                        e.wait_ge(sems[x], finals[x])
            getattr(block, name)(body)


def mmax(*marks):
    out = {}
    for m in marks:
        if not m:
            continue
        for k, v in m.items():
            if v > out.get(k, 0):
                out[k] = v
    return out


def dma(p, eng, out, in_):
    p.op(eng, lambda e, out=out, in_=in_: e.dma_start(out=out, in_=in_), inc=16)


def _ident():
    return np.eye(128, dtype=np.float32)


def _to_col(v):
    return np.ascontiguousarray(v.reshape(-1, 128).T)


def _tile_w(w_in_out):
    din, dout = w_in_out.shape
    a = w_in_out.reshape(din // 128, 128, dout // 128, 128)
    return np.ascontiguousarray(a.transpose(2, 1, 0, 3))



def build_A():
    nc = bass.Bass("TRN2", target_bir_lowering=False)
    x_in = nc.dram_tensor("x", [D, TL], F32, kind="ExternalInput").ap()
    gw1t_in = nc.dram_tensor("gw1t", [D, 32], F32, kind="ExternalInput").ap()
    gb1c_in = nc.dram_tensor("gb1c", [32, 1], F32, kind="ExternalInput").ap()
    gam_in = nc.dram_tensor("gam", [1, 32], F32, kind="ExternalInput").ap()
    bet_in = nc.dram_tensor("bet", [1, 32], F32, kind="ExternalInput").ap()
    gw2t_in = nc.dram_tensor("gw2t", [32, 8], F32, kind="ExternalInput").ap()
    gb2_in = nc.dram_tensor("gb2", [1, 8], F32, kind="ExternalInput").ap()
    id_in = nc.dram_tensor("ident", [128, 128], F32, kind="ExternalInput").ap()
    xn_out = nc.dram_tensor("xn", [D, TL], BF16, kind="ExternalOutput").ap()
    cnt_out = nc.dram_tensor("cnt", [1, 16], F32, kind="ExternalOutput").ap()
    wg_out = nc.dram_tensor("wg", [2, TL, 8], F32, kind="ExternalOutput").ap()

    es = contextlib.ExitStack()
    with es:
        E = es.enter_context
        xb = E(nc.sbuf_tensor("xb", [128, NCH, TL], BF16))
        xnb = E(nc.sbuf_tensor("xnb", [128, NCH, TL], BF16))
        sq = E(nc.sbuf_tensor("sq", [128, TL], BF16))
        xf = E(nc.sbuf_tensor("xf", [128, TL], F32))
        gw1b = E(nc.sbuf_tensor("gw1b", [128, NCH, 32], BF16))
        gb1c = E(nc.sbuf_tensor("gb1c_s", [32, 1], F32))
        gamr = E(nc.sbuf_tensor("gamr", [1, 32], F32))
        betr = E(nc.sbuf_tensor("betr", [1, 32], F32))
        gw2t = E(nc.sbuf_tensor("gw2ts", [32, 8], F32))
        gb2r = E(nc.sbuf_tensor("gb2r", [1, 8], F32))
        ident = E(nc.sbuf_tensor("identsb", [128, 128], F32))
        onesf = E(nc.sbuf_tensor("onesf", [1, 128], F32))
        onesb = E(nc.sbuf_tensor("onesb", [128, 1], BF16))
        mrow = E(nc.sbuf_tensor("mrow", [1, TL], F32))
        qrow = E(nc.sbuf_tensor("qrow", [1, TL], F32))
        vrow = E(nc.sbuf_tensor("vrow", [1, TL], F32))
        rrow = E(nc.sbuf_tensor("rrow", [1, TL], F32))
        mbc = E(nc.sbuf_tensor("mbc", [128, TL], F32))
        rbc = E(nc.sbuf_tensor("rbc", [128, TL], F32))
        hfm = E(nc.sbuf_tensor("hfm", [32, TL], F32))
        gamb = E(nc.sbuf_tensor("gamb", [128, 32], F32))
        betb = E(nc.sbuf_tensor("betb", [128, 32], F32))
        gb2b = E(nc.sbuf_tensor("gb2b", [128, 8], F32))
        htm = E(nc.sbuf_tensor("htm", [128, 32], F32))
        hm = E(nc.sbuf_tensor("hm", [128, 1], F32))
        hv = E(nc.sbuf_tensor("hv", [128, 1], F32))
        hrs = E(nc.sbuf_tensor("hrs", [128, 1], F32))
        hsq = E(nc.sbuf_tensor("hsq", [128, 32], F32))
        hft = E(nc.sbuf_tensor("hft", [32, 128], F32))
        gts = E(nc.sbuf_tensor("gts", [128, 8], F32))
        iotf = E(nc.sbuf_tensor("iotf", [128, 8], F32))
        ioti = E(nc.sbuf_tensor("ioti", [128, 8], I32))
        v1 = E(nc.sbuf_tensor("v1", [128, 1], F32))
        nv1 = E(nc.sbuf_tensor("nv1", [128, 1], F32))
        v2 = E(nc.sbuf_tensor("v2", [128, 1], F32))
        e2 = E(nc.sbuf_tensor("e2", [128, 1], F32))
        g1 = E(nc.sbuf_tensor("g1", [128, 1], F32))
        g2c = E(nc.sbuf_tensor("g2c", [128, 1], F32))
        sel8 = E(nc.sbuf_tensor("sel8", [128, 8], F32))
        m1c = E(nc.sbuf_tensor("m1c", [128, 1], F32))
        oh1 = E(nc.sbuf_tensor("oh1", [128, 8], F32))
        oh2 = E(nc.sbuf_tensor("oh2", [128, 8], F32))
        g2t = E(nc.sbuf_tensor("g2t", [128, 8], F32))
        wgall1 = E(nc.sbuf_tensor("wgall1", [128, 8, 8], F32))
        wgall2 = E(nc.sbuf_tensor("wgall2", [128, 8, 8], F32))
        cac1 = E(nc.sbuf_tensor("cac1", [128, 8], F32))
        cac2 = E(nc.sbuf_tensor("cac2", [128, 8], F32))
        cacb = E(nc.sbuf_tensor("cacb", [128, 8], BF16))
        cnt = E(nc.sbuf_tensor("cntsb", [1, 16], F32))
        psA = E(nc.psum_tensor("psA", [128, 4, 512], F32))
        psB = E(nc.psum_tensor("psB", [128, 4, 512], F32))
        block = E(nc.Block())
        p = Prog(nc, es)

        xr = x_in.rearrange("(c p) t -> p c t", p=128)
        gw1r = gw1t_in.rearrange("(c p) f -> p c f", p=128)
        xnr = xn_out.rearrange("(c p) t -> p c t", p=128)

        for i in range(4):
            dma(p, "gpsimd", xb[:, i * 8:(i + 1) * 8, :], xr[:, i * 8:(i + 1) * 8, :])
        dma(p, "gpsimd", gw1b[:], gw1r)
        p.op("gpsimd", lambda e: e.iota(ioti[:], pattern=[[1, 8]], base=0, channel_multiplier=0))
        dma(p, "sync", gb1c[:], gb1c_in)
        dma(p, "sync", gamr[:], gam_in)
        dma(p, "sync", betr[:], bet_in)
        dma(p, "sync", gw2t[:], gw2t_in)
        dma(p, "sync", gb2r[:], gb2_in)
        dma(p, "sync", ident[:], id_in)
        p.op("vector", lambda e: e.memset(onesf[:], 1.0))
        p.op("vector", lambda e: e.memset(onesb[:], 1.0))
        p.op("vector", lambda e: e.tensor_copy(iotf[:], ioti[:]))
        p.op("vector", lambda e: e.memset(cac1[:], 0.0))
        p.op("vector", lambda e: e.memset(cac2[:], 0.0))

        for c in range(NCH):
            p.op("scalar", lambda e, c=c: e.activation(sq[:], xb[:, c, :], AF.Square))
            st, sp = (c == 0), (c == NCH - 1)
            p.chain("tensor", [
                lambda e, c=c, st=st, sp=sp: e.matmul(psA[0:1, 0, :], onesb[:], xb[:, c, 0:512], start=st, stop=sp),
                lambda e, c=c, st=st, sp=sp: e.matmul(psA[0:1, 1, :], onesb[:], xb[:, c, 512:1024], start=st, stop=sp),
                lambda e, st=st, sp=sp: e.matmul(psA[0:1, 2, :], onesb[:], sq[:, 0:512], start=st, stop=sp),
                lambda e, st=st, sp=sp: e.matmul(psA[0:1, 3, :], onesb[:], sq[:, 512:1024], start=st, stop=sp),
            ])
        p.op("scalar", lambda e: e.activation(mrow[0:1, 0:512], psA[0:1, 0, :], AF.Copy, scale=f32(1.0 / D)))
        p.op("scalar", lambda e: e.activation(mrow[0:1, 512:1024], psA[0:1, 1, :], AF.Copy, scale=f32(1.0 / D)))
        p.op("scalar", lambda e: e.activation(qrow[0:1, 0:512], psA[0:1, 2, :], AF.Copy, scale=f32(1.0 / D)))
        p.op("scalar", lambda e: e.activation(qrow[0:1, 512:1024], psA[0:1, 3, :], AF.Copy, scale=f32(1.0 / D)))
        p.op("vector", lambda e: e.tensor_tensor(vrow[0:1, :], mrow[0:1, :], mrow[0:1, :], op=ALU.mult))
        p.op("vector", lambda e: e.tensor_tensor(vrow[0:1, :], qrow[0:1, :], vrow[0:1, :], op=ALU.subtract))
        p.op("vector", lambda e: e.tensor_scalar_add(vrow[0:1, :], vrow[0:1, :], LN_EPS))
        p.op("scalar", lambda e: e.activation(vrow[0:1, :], vrow[0:1, :], AF.Sqrt))
        p.op("vector", lambda e: e.reciprocal(rrow[0:1, :], vrow[0:1, :]))
        p.chain("tensor", [
            lambda e: e.matmul(psB[:, 0, :], onesf[:], mrow[0:1, 0:512], start=True, stop=True),
            lambda e: e.matmul(psB[:, 1, :], onesf[:], mrow[0:1, 512:1024], start=True, stop=True),
            lambda e: e.matmul(psB[:, 2, :], onesf[:], rrow[0:1, 0:512], start=True, stop=True),
            lambda e: e.matmul(psB[:, 3, :], onesf[:], rrow[0:1, 512:1024], start=True, stop=True),
        ])
        p.op("scalar", lambda e: e.copy(mbc[:, 0:512], psB[:, 0, :]))
        p.op("scalar", lambda e: e.copy(mbc[:, 512:1024], psB[:, 1, :]))
        p.op("scalar", lambda e: e.copy(rbc[:, 0:512], psB[:, 2, :]))
        p.op("scalar", lambda e: e.copy(rbc[:, 512:1024], psB[:, 3, :]))
        for c in range(NCH):
            p.op("vector", lambda e, c=c: e.tensor_copy(xf[:], xb[:, c, :]))
            p.op("vector", lambda e: e.tensor_tensor(xf[:], xf[:], mbc[:], op=ALU.subtract))
            p.op("vector", lambda e, c=c: e.tensor_tensor(xnb[:, c, :], xf[:], rbc[:], op=ALU.mult))
        for i in range(4):
            dma(p, "sync", xnr[:, i * 8:(i + 1) * 8, :], xnb[:, i * 8:(i + 1) * 8, :])

        for c in range(NCH):
            st, sp = (c == 0), (c == NCH - 1)
            p.chain("tensor", [
                lambda e, c=c, st=st, sp=sp: e.matmul(psA[0:32, 0, :], gw1b[:, c, :], xnb[:, c, 0:512], start=st, stop=sp),
                lambda e, c=c, st=st, sp=sp: e.matmul(psA[0:32, 1, :], gw1b[:, c, :], xnb[:, c, 512:1024], start=st, stop=sp),
            ])
        p.op("vector", lambda e: e.tensor_scalar(hfm[:, 0:512], psA[0:32, 0, :], gb1c[:], None, op0=ALU.add))
        p.op("vector", lambda e: e.tensor_scalar(hfm[:, 512:1024], psA[0:32, 1, :], gb1c[:], None, op0=ALU.add))
        p.chain("tensor", [
            lambda e: e.matmul(psB[:, 0, 0:32], onesf[:], gamr[:], start=True, stop=True),
            lambda e: e.matmul(psB[:, 1, 0:32], onesf[:], betr[:], start=True, stop=True),
            lambda e: e.matmul(psB[:, 2, 0:8], onesf[:], gb2r[:], start=True, stop=True),
        ])
        p.op("scalar", lambda e: e.copy(gamb[:], psB[:, 0, 0:32]))
        p.op("scalar", lambda e: e.copy(betb[:], psB[:, 1, 0:32]))
        p.op("scalar", lambda e: e.copy(gb2b[:], psB[:, 2, 0:8]))

        for t in range(8):
            p.op("tensor", lambda e, t=t: e.transpose(psB[:, 3, 0:32], hfm[:, t * 128:(t + 1) * 128], ident[0:32, 0:32]))
            p.op("scalar", lambda e: e.copy(htm[:], psB[:, 3, 0:32]))
            p.op("vector", lambda e: e.reduce_sum(hm[:], htm[:], axis=AX.X))
            p.op("vector", lambda e: e.tensor_scalar_mul(hm[:], hm[:], 1.0 / 32.0), drain=True)
            p.op("vector", lambda e: e.tensor_scalar(htm[:], htm[:], hm[:], None, op0=ALU.subtract), drain=True)
            if debug and t == 0:
                dma(p, "sync", dbg_ln[:, 0:32], htm[:])
                p.op("sync", lambda e: e.dma_start(out=dbg_ln[:, 32:33], in_=hm[:], allow_slow_non_contiguous=True), inc=16)
            p.op("vector", lambda e: e.tensor_tensor(hsq[:], htm[:], htm[:], op=ALU.mult))
            p.op("vector", lambda e: e.reduce_sum(hv[:], hsq[:], axis=AX.X))
            p.op("vector", lambda e: e.tensor_scalar_mul(hv[:], hv[:], 1.0 / 32.0), drain=True)
            p.op("vector", lambda e: e.tensor_scalar_add(hv[:], hv[:], LN_EPS))
            p.op("scalar", lambda e: e.activation(hv[:], hv[:], AF.Sqrt))
            p.op("vector", lambda e: e.reciprocal(hrs[:], hv[:]))
            p.op("vector", lambda e: e.tensor_scalar(htm[:], htm[:], hrs[:], None, op0=ALU.mult), drain=True)
            if debug and t == 0:
                p.op("sync", lambda e: e.dma_start(out=dbg_ln[:, 33:34], in_=hv[:], allow_slow_non_contiguous=True), inc=16)
                p.op("sync", lambda e: e.dma_start(out=dbg_ln[:, 34:35], in_=hrs[:], allow_slow_non_contiguous=True), inc=16)
            p.op("vector", lambda e: e.tensor_tensor(htm[:], htm[:], gamb[:], op=ALU.mult))
            p.op("vector", lambda e: e.tensor_tensor(htm[:], htm[:], betb[:], op=ALU.add))
            p.op("vector", lambda e: e.tensor_relu(htm[:], htm[:]))
            p.op("tensor", lambda e: e.transpose(psB[0:32, 3, 0:128], htm[:], ident[:]))
            p.op("scalar", lambda e: e.copy(hft[:], psB[0:32, 3, 0:128]))
            p.op("tensor", lambda e: e.matmul(psB[:, 2, 64:72], hft[:], gw2t[:], start=True, stop=True))
            p.op("vector", lambda e: e.tensor_tensor(gts[:], psB[:, 2, 64:72], gb2b[:], op=ALU.add))
            p.op("vector", lambda e: e.reduce_max(v1[:], gts[:], axis=AX.X))
            p.op("vector", lambda e: e.tensor_scalar(oh1[:], gts[:], v1[:], None, op0=ALU.is_equal), drain=True)
            p.op("vector", lambda e: e.scalar_tensor_tensor(sel8[:], oh1[:], -256.0, iotf[:], op0=ALU.mult, op1=ALU.add))
            p.op("vector", lambda e: e.tensor_reduce(m1c[:], sel8[:], axis=AX.X, op=ALU.min))
            p.op("vector", lambda e: e.tensor_scalar(oh1[:], sel8[:], m1c[:], None, op0=ALU.is_equal), drain=True)
            p.op("vector", lambda e: e.scalar_tensor_tensor(g2t[:], oh1[:], NEG, gts[:], op0=ALU.mult, op1=ALU.add))
            p.op("vector", lambda e: e.reduce_max(v2[:], g2t[:], axis=AX.X))
            p.op("vector", lambda e: e.tensor_scalar(oh2[:], g2t[:], v2[:], None, op0=ALU.is_equal))
            p.op("vector", lambda e: e.scalar_tensor_tensor(sel8[:], oh2[:], -256.0, iotf[:], op0=ALU.mult, op1=ALU.add))
            p.op("vector", lambda e: e.tensor_reduce(m1c[:], sel8[:], axis=AX.X, op=ALU.min))
            p.op("vector", lambda e: e.tensor_scalar(oh2[:], sel8[:], m1c[:], None, op0=ALU.is_equal), drain=True)
            p.op("vector", lambda e: e.tensor_scalar_mul(nv1[:], v1[:], -1.0))
            p.op("scalar", lambda e: e.activation(e2[:], v2[:], AF.Exp, bias=nv1[:]))
            p.op("vector", lambda e: e.tensor_scalar_add(g1[:], e2[:], 1.0))
            p.op("vector", lambda e: e.reciprocal(g1[:], g1[:]))
            p.op("vector", lambda e: e.tensor_tensor(g2c[:], e2[:], g1[:], op=ALU.mult), drain=True)
            p.op("vector", lambda e: e.tensor_scalar(sel8[:], oh1[:], g1[:], None, op0=ALU.mult), drain=True)
            dma(p, "sync", wgscr.ap()[0, t * 128:(t + 1) * 128, :], sel8[:])
            p.op("vector", lambda e: e.tensor_scalar(sel8[:], oh2[:], g2c[:], None, op0=ALU.mult), drain=True)
            dma(p, "sync", wgscr.ap()[1, t * 128:(t + 1) * 128, :], sel8[:])
            p.op("vector", lambda e: e.tensor_tensor(cac1[:], cac1[:], oh1[:], op=ALU.add))
            p.op("vector", lambda e: e.tensor_tensor(cac2[:], cac2[:], oh2[:], op=ALU.add))
            if debug and t == 0:
                dma(p, "sync", dbg_hb, htb[:])
                dma(p, "sync", dbg_pc[:, 0:32], htm[:])
                dma(p, "sync", dbg_pc[:, 32:40], gts[:])
                dma(p, "sync", dbg_pc[:, 40:48], oh1[:])
                dma(p, "sync", dbg_pc[:, 48:56], oh2[:])
                dma(p, "sync", dbg_pc[:, 56:64], iotf[:])

        p.op("vector", lambda e: e.tensor_copy(cacb[:], cac1[:]))
        p.op("tensor", lambda e: e.matmul(psB[0:1, 0, 128:136], onesb[:], cacb[:], start=True, stop=True))
        p.op("scalar", lambda e: e.copy(cnt[0:1, 0:8], psB[0:1, 0, 128:136]))
        p.op("vector", lambda e: e.tensor_copy(cacb[:], cac2[:]))
        p.op("tensor", lambda e: e.matmul(psB[0:1, 0, 136:144], onesb[:], cacb[:], start=True, stop=True))
        p.op("scalar", lambda e: e.copy(cnt[0:1, 8:16], psB[0:1, 0, 136:144]))
        dma(p, "sync", cnt_out, cnt[:])
        dma(p, "sync", wg_out[0].rearrange("(t p) e -> p t e", p=128), wgall1[:])
        dma(p, "sync", wg_out[1].rearrange("(t p) e -> p t e", p=128), wgall2[:])
        p.finish(block)
    return nc



def build_attn(l1: bool):
    nc = bass.Bass("TRN2", target_bir_lowering=False)
    xn4_in = nc.dram_tensor("xn4", [32, 128, NCH, 256], BF16, kind="ExternalInput").ap()
    wq_in = nc.dram_tensor("wq4", [128, NCH, 512], F32, kind="ExternalInput").ap()
    if l1:
        wk_in = nc.dram_tensor("wk4", [128, NCH, 512], F32, kind="ExternalInput").ap()
        wv_in = nc.dram_tensor("wv4", [128, NCH, 512], F32, kind="ExternalInput").ap()
        bq_in = nc.dram_tensor("bqc", [128, 4], F32, kind="ExternalInput").ap()
        bk_in = nc.dram_tensor("bkc", [128, 4], F32, kind="ExternalInput").ap()
        bv_in = nc.dram_tensor("bvr", [1, 512], F32, kind="ExternalInput").ap()
    else:
        wkv_in = nc.dram_tensor("wkv4", [128, NCH, 256], F32, kind="ExternalInput").ap()
    tri_in = nc.dram_tensor("tri", [128, 128], F32, kind="ExternalInput").ap()
    ctx_out = nc.dram_tensor("ctx", [512, T], BF16, kind="ExternalOutput").ap()

    q_scr = nc.dram_tensor("q_scr", [512, T], BF16)

    es = contextlib.ExitStack()
    with es:
        E = es.enter_context
        xt = E(nc.sbuf_tensor("xt", [128, NCH, 256], BF16))
        wsh = E(nc.sbuf_tensor("wsh", [128, NCH, 512], BF16))
        if l1:
            bqc = E(nc.sbuf_tensor("bqc_s", [128, 4], F32))
            bkc = E(nc.sbuf_tensor("bkc_s", [128, 4], F32))
            bvr = E(nc.sbuf_tensor("bvr_s", [1, 512], F32))
            bvb = E(nc.sbuf_tensor("bvb", [128, 512], F32))
            kres = E(nc.sbuf_tensor("kres", [128, 4, T], BF16))
            vres = E(nc.sbuf_tensor("vres", [128, 64, 512], BF16))
        else:
            wkvb = E(nc.sbuf_tensor("wkvb", [128, NCH, 256], BF16))
            kres = E(nc.sbuf_tensor("kres", [128, 1, T], BF16))
            vres = E(nc.sbuf_tensor("vres", [128, 64, 128], BF16))
        tri = E(nc.sbuf_tensor("tri_s", [128, 128], F32))
        qt = E(nc.sbuf_tensor("qt", [128, 512], BF16))
        qsb = E(nc.sbuf_tensor("qsb", [128, 256], BF16))
        ptb = E(nc.sbuf_tensor("ptb", [128, 512], BF16))
        rs = E(nc.sbuf_tensor("rs", [1, 512], F32))
        rcb = E(nc.sbuf_tensor("rcb", [128, 512], F32))
        cts = E(nc.sbuf_tensor("cts", [128, 512], BF16))
        onesf = E(nc.sbuf_tensor("onesf", [1, 128], F32))
        onesb = E(nc.sbuf_tensor("onesb", [128, 1], BF16))
        psA = E(nc.psum_tensor("psA", [128, 4, 512], F32))
        psB = E(nc.psum_tensor("psB", [128, 4, 512], F32))
        block = E(nc.Block())
        p = Prog(nc, es)

        p.op("vector", lambda e: e.memset(onesf[:], 1.0))
        p.op("vector", lambda e: e.memset(onesb[:], 1.0))
        dma(p, "sync", tri[:], tri_in)
        if l1:
            dma(p, "sync", bqc[:], bq_in)
            dma(p, "sync", bkc[:], bk_in)
            dma(p, "sync", bvr[:], bv_in)
            p.op("tensor", lambda e: e.matmul(psB[:, 0, :], onesf[:], bvr[:], start=True, stop=True))
            p.op("scalar", lambda e: e.copy(bvb[:], psB[:, 0, :]))

        def load_w(src_ap):
            for i in range(4):
                sl = slice(i * 8, (i + 1) * 8)
                dma(p, "gpsimd", wsh[:, sl, :], src_ap[:, sl, :])

        if not l1:
            load_w(wq_in)
            for i in range(4):
                sl = slice(i * 8, (i + 1) * 8)
                dma(p, "gpsimd", wkvb[:, sl, :], wkv_in[:, sl, :])
            for tb in range(32):
                dma(p, "sync", xt[:], xn4_in[tb])
                for oc in range(4):
                    p.chain("tensor", [
                        lambda e, c=c, oc=oc, st=(c == 0), sp=(c == NCH - 1): e.matmul(
                            psA[:, 0, 0:256], wsh[:, c, oc * 128:(oc + 1) * 128], xt[:, c, :], start=st, stop=sp)
                        for c in range(NCH)])
                    p.op("scalar", lambda e: e.copy(qsb[:], psA[:, 0, 0:256]))
                    dma(p, "sync", q_scr.ap()[oc * 128:(oc + 1) * 128, tb * 256:(tb + 1) * 256], qsb[:])
                p.chain("tensor", [
                    lambda e, c=c, st=(c == 0), sp=(c == NCH - 1): e.matmul(
                        psA[:, 1, 0:256], wkvb[:, c, 0:128], xt[:, c, :], start=st, stop=sp)
                    for c in range(NCH)])
                p.op("scalar", lambda e, tb=tb: e.copy(kres[:, 0, tb * 256:(tb + 1) * 256], psA[:, 1, 0:256]))
                for vt in range(2):
                    ci = tb * 2 + vt
                    p.chain("tensor", [
                        lambda e, c=c, vt=vt, st=(c == 0), sp=(c == NCH - 1): e.matmul(
                            psA[:, 2, 0:128], xt[:, c, vt * 128:(vt + 1) * 128], wkvb[:, c, 128:256], start=st, stop=sp)
                        for c in range(NCH)])
                    p.op("scalar", lambda e, ci=ci: e.copy(vres[:, ci, :], psA[:, 2, 0:128]))
        else:
            load_w(wk_in)
            for tb in range(32):
                dma(p, "sync", xt[:], xn4_in[tb])
                for kc in range(4):
                    p.chain("tensor", [
                        lambda e, c=c, kc=kc, st=(c == 0), sp=(c == NCH - 1): e.matmul(
                            psA[:, 0, 0:256], wsh[:, c, kc * 128:(kc + 1) * 128], xt[:, c, :], start=st, stop=sp)
                        for c in range(NCH)])
                    p.op("vector", lambda e, kc=kc, tb=tb: e.tensor_scalar(
                        kres[:, kc, tb * 256:(tb + 1) * 256], psA[:, 0, 0:256], bkc[:, kc:kc + 1], None, op0=ALU.add))
            load_w(wv_in)
            for tb in range(32):
                dma(p, "sync", xt[:], xn4_in[tb])
                for vt in range(2):
                    ci = tb * 2 + vt
                    p.chain("tensor", [
                        lambda e, c=c, vt=vt, st=(c == 0), sp=(c == NCH - 1): e.matmul(
                            psA[:, 2, :], xt[:, c, vt * 128:(vt + 1) * 128], wsh[:, c, :], start=st, stop=sp)
                        for c in range(NCH)])
                    p.op("vector", lambda e, ci=ci: e.tensor_tensor(vres[:, ci, :], psA[:, 2, :], bvb[:], op=ALU.add))
            load_w(wq_in)
            for tb in range(32):
                dma(p, "sync", xt[:], xn4_in[tb])
                for oc in range(4):
                    p.chain("tensor", [
                        lambda e, c=c, oc=oc, st=(c == 0), sp=(c == NCH - 1): e.matmul(
                            psA[:, 0, 0:256], wsh[:, c, oc * 128:(oc + 1) * 128], xt[:, c, :], start=st, stop=sp)
                        for c in range(NCH)])
                    p.op("vector", lambda e, oc=oc: e.tensor_scalar(
                        qsb[:], psA[:, 0, 0:256], bqc[:, oc:oc + 1], None, op0=ALU.add))
                    dma(p, "sync", q_scr.ap()[oc * 128:(oc + 1) * 128, tb * 256:(tb + 1) * 256], qsb[:])

        for lh in range(4):
            for qg in range(16):
                dma(p, "sync", qt[:], q_scr.ap()[lh * 128:(lh + 1) * 128, qg * 512:(qg + 1) * 512])
                nch_k = 64 if l1 else 4 * (qg + 1)
                for c in range(nch_k):
                    d = c - 4 * qg
                    w0 = 128 * d if (not l1 and d >= 0) else 0
                    diag = (not l1) and d >= 0
                    klh = lh if l1 else 0
                    p.op("tensor", lambda e, c=c, klh=klh, w0=w0: e.matmul(
                        psA[:, 0, w0:512], kres[:, klh, c * 128:(c + 1) * 128], qt[:, w0:512],
                        start=True, stop=True))
                    if diag:
                        p.op("vector", lambda e, w0=w0: e.tensor_tensor(
                            psA[:, 0, w0:w0 + 128], psA[:, 0, w0:w0 + 128], tri[:], op=ALU.add))
                    p.op("scalar", lambda e, w0=w0: e.activation(
                        ptb[:, w0:512], psA[:, 0, w0:512], AF.Exp,
                        bias=f32(-EXP_SHIFT), scale=f32(INV_SQRT_HD)))
                    st = (c == 0)
                    sp = (c == nch_k - 1)
                    voff = lh * 128 if l1 else 0
                    p.chain("tensor", [
                        lambda e, c=c, w0=w0, st=st, sp=sp, voff=voff: e.matmul(
                            psB[:, 0, w0:512], vres[:, c, voff:voff + 128], ptb[:, w0:512], start=st, stop=sp),
                        lambda e, c=c, w0=w0, st=st, sp=sp: e.matmul(
                            psB[0:1, 2, w0:512], onesb[:], ptb[:, w0:512], start=st, stop=sp),
                    ])
                p.op("vector", lambda e: e.reciprocal(rs[:], psB[0:1, 2, :]))
                p.op("tensor", lambda e: e.matmul(psB[:, 1, :], onesf[:], rs[:], start=True, stop=True))
                p.op("scalar", lambda e: e.copy(rcb[:], psB[:, 1, :]))
                p.op("vector", lambda e: e.tensor_tensor(cts[:], psB[:, 0, :], rcb[:], op=ALU.mult))
                dma(p, "sync", ctx_out[lh * 128:(lh + 1) * 128, qg * 512:(qg + 1) * 512], cts[:])
        p.finish(block)
    return nc



def build_D():
    nc = bass.Bass("TRN2", target_bir_lowering=False)
    xn1_in = nc.dram_tensor("xn1", [D, TL], BF16, kind="ExternalInput").ap()
    wq4_in = nc.dram_tensor("wq4", [32, 128, NCH, 128], F32, kind="ExternalInput").ap()
    wk4_in = nc.dram_tensor("wk4", [32, 128, NCH, 128], F32, kind="ExternalInput").ap()
    wv4_in = nc.dram_tensor("wv4", [32, 128, NCH, 128], F32, kind="ExternalInput").ap()
    bq_in = nc.dram_tensor("bqc", [128, 32], F32, kind="ExternalInput").ap()
    bk_in = nc.dram_tensor("bkc", [128, 32], F32, kind="ExternalInput").ap()
    bv_in = nc.dram_tensor("bvc", [128, 32], F32, kind="ExternalInput").ap()
    idb_in = nc.dram_tensor("identb", [128, 128], BF16, kind="ExternalInput").ap()
    ctx_out = nc.dram_tensor("ctx", [D, TL], BF16, kind="ExternalOutput").ap()

    es = contextlib.ExitStack()
    with es:
        E = es.enter_context
        xnb = E(nc.sbuf_tensor("xnb", [128, NCH, TL], BF16))
        qstr = E(nc.sbuf_tensor("qstr", [128, 32, 512], BF16))
        kstr = E(nc.sbuf_tensor("kstr", [128, 32, 512], BF16))
        vstr = E(nc.sbuf_tensor("vstr", [128, 32, 512], BF16))
        ctb2 = E(nc.sbuf_tensor("ctb2", [128, 32, 256], BF16))
        wtl = E(nc.sbuf_tensor("wtl", [128, NCH, 128], BF16))
        bqc = E(nc.sbuf_tensor("bqc_s", [128, 32], F32))
        bkc = E(nc.sbuf_tensor("bkc_s", [128, 32], F32))
        bvc = E(nc.sbuf_tensor("bvc_s", [128, 32], F32))
        identb = E(nc.sbuf_tensor("identb_s", [128, 128], BF16))
        pt = E(nc.sbuf_tensor("pt", [32, 512], BF16))
        rs = E(nc.sbuf_tensor("rs", [1, 512], F32))
        rb = E(nc.sbuf_tensor("rb", [128, 512], F32))
        vts = E(nc.sbuf_tensor("vts", [32, 16, 128], BF16))
        onesf = E(nc.sbuf_tensor("onesf", [1, 128], F32))
        ones32 = E(nc.sbuf_tensor("ones32", [32, 1], BF16))
        psA = E(nc.psum_tensor("psA", [128, 4, 512], F32))
        psVt = E(nc.psum_tensor("psVt", [32, 2, 1024], BF16))
        block = E(nc.Block())
        p = Prog(nc, es)

        xn1r = xn1_in.rearrange("(c p) t -> p c t", p=128)
        ctxr = ctx_out.rearrange("(h p) n -> p h n", p=128)

        p.op("vector", lambda e: e.memset(onesf[:], 1.0))
        p.op("vector", lambda e: e.memset(ones32[:], 1.0))
        dma(p, "sync", bqc[:], bq_in)
        dma(p, "sync", bkc[:], bk_in)
        dma(p, "sync", bvc[:], bv_in)
        dma(p, "sync", identb[:], idb_in)
        for i in range(4):
            sl = slice(i * 8, (i + 1) * 8)
            dma(p, "sync", xnb[:, sl, :], xn1r[:, sl, :])

        for half in range(2):
            hs = slice(half * 512, (half + 1) * 512)
            for name, w4, bc, dst in (("q", wq4_in, bqc, qstr), ("k", wk4_in, bkc, kstr), ("v", wv4_in, bvc, vstr)):
                for oc in range(32):
                    dma(p, "gpsimd", wtl[:], w4[oc])
                    p.chain("tensor", [
                        lambda e, c=c, hs=hs, st=(c == 0), sp=(c == NCH - 1): e.matmul(
                            psA[:, 0, :], wtl[:, c, :], xnb[:, c, hs], start=st, stop=sp)
                        for c in range(NCH)])
                    p.op("vector", lambda e, oc=oc, bc=bc, dst=dst: e.tensor_scalar(
                        dst[:, oc, :], psA[:, 0, :], bc[:, oc:oc + 1], None, op0=ALU.add))
            for sub in range(2):
                for g in range(16):
                    hh = sub * 256 + g * 16
                    ng = g * 16
                    p.chain("tensor", [
                        lambda e, j=j, hh=hh: e.matmul(
                            psA[0:32, 1, j * 32:(j + 1) * 32], kstr[:, :, hh + j],
                            qstr[:, :, hh + j], start=True, stop=True)
                        for j in range(16)])
                    p.op("scalar", lambda e: e.activation(
                        pt[:], psA[0:32, 1, :], AF.Exp, bias=f32(-EXP_SHIFT), scale=f32(INV_SQRT_HD)))
                    p.op("tensor", lambda e: e.matmul(psA[0:1, 2, :], ones32[:], pt[:], start=True, stop=True))
                    p.op("vector", lambda e: e.reciprocal(rs[:], psA[0:1, 2, :]))
                    p.op("tensor", lambda e: e.matmul(psA[:, 2, :], onesf[:], rs[:], start=True, stop=True))
                    p.op("scalar", lambda e: e.copy(rb[:], psA[:, 2, :]))
                    p.chain("tensor", [
                        lambda e, j=j, hh=hh: e.transpose(
                            psVt[0:32, j // 8, (j % 8) * 128:(j % 8 + 1) * 128],
                            vstr[:, :, hh + j], identb[:])
                        for j in range(16)])
                    p.op("scalar", lambda e: e.copy(vts[:, 0:8, :], psVt[0:32, 0, :].rearrange("p (j d) -> p j d", d=128)))
                    p.op("scalar", lambda e: e.copy(vts[:, 8:16, :], psVt[0:32, 1, :].rearrange("p (j d) -> p j d", d=128)))
                    p.chain("tensor", [
                        lambda e, j=j: e.matmul(
                            psA[:, 3, j * 32:(j + 1) * 32], vts[:, j, :], pt[:, j * 32:(j + 1) * 32],
                            start=True, stop=True)
                        for j in range(16)])
                    p.op("vector", lambda e, ng=ng: e.tensor_tensor(
                        ctb2[:, :, ng:ng + 16].rearrange("p h n -> p n h"),
                        psA[:, 3, :].rearrange("p (n h) -> p n h", h=32), rb[:].rearrange("p (n h) -> p n h", h=32),
                        op=ALU.mult))
                for i in range(4):
                    sl = slice(i * 8, (i + 1) * 8)
                    dma(p, "sync", ctxr[:, sl, half * 512 + sub * 256:half * 512 + (sub + 1) * 256], ctb2[:, sl, :])
        p.finish(block)
    return nc



def build_C():
    nc = bass.Bass("TRN2", target_bir_lowering=False)
    x_in = nc.dram_tensor("x", [D, TL], F32, kind="ExternalInput").ap()
    xn_in = nc.dram_tensor("xn", [D, TL], BF16, kind="ExternalInput").ap()
    ctx_in = nc.dram_tensor("ctxt", [D, TL], BF16, kind="ExternalInput").ap()
    wo4_in = nc.dram_tensor("wo4", [32, 128, NCH, 128], F32, kind="ExternalInput").ap()
    fw4_in = nc.dram_tensor("fw4", [32, 128, NCH, 128], F32, kind="ExternalInput").ap()
    boc_in = nc.dram_tensor("boc", [128, 32], F32, kind="ExternalInput").ap()
    la4_in = nc.dram_tensor("la4", [128, NCH, 128], F32, kind="ExternalInput").ap()
    lb_in = nc.dram_tensor("lb", [128, D], F32, kind="ExternalInput").ap()
    cnts_in = nc.dram_tensor("cnts", [16, 8], F32, kind="ExternalInput").ap()
    sel2_in = nc.dram_tensor("sel2", [16, 2], F32, kind="ExternalInput").ap()
    wg_in = nc.dram_tensor("wg", [2, TL, 8], F32, kind="ExternalInput").ap()
    ssel_in = nc.dram_tensor("ssel", [8, 128], F32, kind="ExternalInput").ap()
    id_in = nc.dram_tensor("ident", [128, 128], F32, kind="ExternalInput").ap()
    x1_out = nc.dram_tensor("x1", [D, TL], F32, kind="ExternalOutput").ap()
    xn1_out = nc.dram_tensor("xn1", [D, TL], BF16, kind="ExternalOutput").ap()

    es = contextlib.ExitStack()
    with es:
        E = es.enter_context
        xnb = E(nc.sbuf_tensor("xnb", [128, NCH, TL], BF16))
        ctb = E(nc.sbuf_tensor("ctb", [128, NCH, TL], BF16))
        lab = E(nc.sbuf_tensor("lab", [128, NCH, 128], BF16))
        lbb = E(nc.sbuf_tensor("lbb", [128, D], BF16))
        wob = E(nc.sbuf_tensor("wob", [128, NCH, 128], BF16))
        fwb = E(nc.sbuf_tensor("fwb", [128, NCH, 128], BF16))
        boc = E(nc.sbuf_tensor("boc_s", [128, 32], F32))
        cnt16 = E(nc.sbuf_tensor("cnt16", [16, 8], F32))
        sel2 = E(nc.sbuf_tensor("sel2_s", [16, 2], F32))
        ssel = E(nc.sbuf_tensor("ssel_s", [8, 128], F32))
        ident = E(nc.sbuf_tensor("identsb", [128, 128], F32))
        cgrow = E(nc.sbuf_tensor("cgrow", [1, 16], F32))
        alw = E(nc.sbuf_tensor("alw", [1, 16], F32))
        alb1 = E(nc.sbuf_tensor("alb1", [128, 8], F32))
        alb2 = E(nc.sbuf_tensor("alb2", [128, 8], F32))
        wg1t = E(nc.sbuf_tensor("wg1t", [128, 8], F32))
        wg2t = E(nc.sbuf_tensor("wg2t", [128, 8], F32))
        wt = E(nc.sbuf_tensor("wt", [128, 8], F32))
        wfm = E(nc.sbuf_tensor("wfm", [8, TL], F32))
        wbc = E(nc.sbuf_tensor("wbc", [128, TL], F32))
        us = E(nc.sbuf_tensor("us", [128, TL], BF16))
        ssb = E(nc.sbuf_tensor("ssb", [128, TL], F32))
        gsb = E(nc.sbuf_tensor("gsb", [128, TL], F32))
        x1bb = E(nc.sbuf_tensor("x1bb", [128, TL], BF16))
        sqx = E(nc.sbuf_tensor("sqx", [128, TL], BF16))
        onesf = E(nc.sbuf_tensor("onesf", [1, 128], F32))
        onesb = E(nc.sbuf_tensor("onesb", [128, 1], BF16))
        mrow = E(nc.sbuf_tensor("mrow", [1, TL], F32))
        qrow = E(nc.sbuf_tensor("qrow", [1, TL], F32))
        vrow = E(nc.sbuf_tensor("vrow", [1, TL], F32))
        rrow = E(nc.sbuf_tensor("rrow", [1, TL], F32))
        psA = E(nc.psum_tensor("psA", [128, 4, 512], F32))
        psB = E(nc.psum_tensor("psB", [128, 4, 512], F32))
        block = E(nc.Block())
        p = Prog(nc, es)
        mbc = wbc
        rbc = gsb

        xr = x_in.rearrange("(c p) t -> p c t", p=128)
        xnr = xn_in.rearrange("(c p) t -> p c t", p=128)
        ctr = ctx_in.rearrange("(c p) t -> p c t", p=128)
        lar = la4_in
        x1r = x1_out.rearrange("(c p) t -> p c t", p=128)
        xn1r = xn1_out.rearrange("(c p) t -> p c t", p=128)

        p.op("vector", lambda e: e.memset(onesf[:], 1.0))
        p.op("vector", lambda e: e.memset(onesb[:], 1.0))
        for i in range(4):
            sl = slice(i * 8, (i + 1) * 8)
            dma(p, "sync", xnb[:, sl, :], xnr[:, sl, :])
            dma(p, "sync", ctb[:, sl, :], ctr[:, sl, :])
        dma(p, "gpsimd", lab[:], lar)
        dma(p, "gpsimd", lbb[:], lb_in)
        dma(p, "sync", boc[:], boc_in)
        dma(p, "sync", cnt16[:], cnts_in)
        dma(p, "sync", sel2[:], sel2_in)
        dma(p, "sync", ssel[:], ssel_in)
        dma(p, "sync", ident[:], id_in)

        p.op("tensor", lambda e: e.matmul(psA[0:1, 0, 0:8], sel2[:, 0:1], cnt16[:], start=True, stop=True))
        p.op("tensor", lambda e: e.matmul(psA[0:1, 1, 0:8], sel2[:, 1:2], cnt16[:], start=True, stop=True))
        p.op("scalar", lambda e: e.copy(cgrow[0:1, 0:8], psA[0:1, 0, 0:8]))
        p.op("scalar", lambda e: e.copy(cgrow[0:1, 8:16], psA[0:1, 1, 0:8]))
        p.op("vector", lambda e: e.tensor_scalar(alw[:], cgrow[:], CAPACITY, None, op0=ALU.is_le))
        p.op("tensor", lambda e: e.matmul(psA[:, 2, 0:8], onesf[:], alw[0:1, 0:8], start=True, stop=True))
        p.op("tensor", lambda e: e.matmul(psA[:, 3, 0:8], onesf[:], alw[0:1, 8:16], start=True, stop=True))
        p.op("scalar", lambda e: e.copy(alb1[:], psA[:, 2, 0:8]))
        p.op("scalar", lambda e: e.copy(alb2[:], psA[:, 3, 0:8]))

        for t in range(8):
            dma(p, "sync", wg1t[:], wg_in[0, t * 128:(t + 1) * 128, :])
            dma(p, "sync", wg2t[:], wg_in[1, t * 128:(t + 1) * 128, :])
            p.op("vector", lambda e: e.tensor_tensor(wt[:], wg1t[:], alb1[:], op=ALU.mult))
            p.op("vector", lambda e: e.tensor_tensor(wg2t[:], wg2t[:], alb2[:], op=ALU.mult))
            p.op("vector", lambda e: e.tensor_tensor(wt[:], wt[:], wg2t[:], op=ALU.add))
            p.op("tensor", lambda e: e.transpose(psB[0:8, 0, 0:128], wt[:], ident[:]))
            p.op("scalar", lambda e, t=t: e.copy(wfm[:, t * 128:(t + 1) * 128], psB[0:8, 0, 0:128]))
        p.op("tensor", lambda e: e.matmul(psB[:, 1, :], ssel[:], wfm[:, 0:512], start=True, stop=True))
        p.op("tensor", lambda e: e.matmul(psB[:, 2, :], ssel[:], wfm[:, 512:1024], start=True, stop=True))
        p.op("scalar", lambda e: e.activation(wbc[:, 0:512], psB[:, 1, :], AF.Copy, scale=f32(1.0 / 16.0)))
        p.op("scalar", lambda e: e.activation(wbc[:, 512:1024], psB[:, 2, :], AF.Copy, scale=f32(1.0 / 16.0)))

        for half in range(2):
            mms = [lambda e, c=c, half=half, st=(c == 0), sp=(c == NCH - 1):
                   e.matmul(psA[:, half, :], lab[:, c, :], xnb[:, c, half * 512:(half + 1) * 512], start=st, stop=sp)
                   for c in range(NCH)]
            p.chain("tensor", mms)
        p.op("vector", lambda e: e.tensor_tensor(us[:, 0:512], psA[:, 0, :], wbc[:, 0:512], op=ALU.mult))
        p.op("vector", lambda e: e.tensor_tensor(us[:, 512:1024], psA[:, 1, :], wbc[:, 512:1024], op=ALU.mult))

        for oc in range(32):
            dma(p, "gpsimd", wob[:], wo4_in[oc])
            dma(p, "gpsimd", fwb[:], fw4_in[oc])
            for half in range(2):
                hs = slice(half * 512, (half + 1) * 512)
                mms = [lambda e, c=c, half=half, hs=hs, st=(c == 0), sp=(c == NCH - 1):
                       e.matmul(psA[:, half, :], wob[:, c, :], ctb[:, c, hs], start=st, stop=sp)
                       for c in range(NCH)]
                p.chain("tensor", mms)
                mms2 = [lambda e, c=c, half=half, hs=hs:
                        e.matmul(psA[:, 2 + half, :], fwb[:, c, :], xnb[:, c, hs], start=(c == 0), stop=False)
                        for c in range(NCH)]
                mms2.append(lambda e, half=half, hs=hs, oc=oc: e.matmul(
                    psA[:, 2 + half, :], lbb[:, oc * 128:(oc + 1) * 128], us[:, hs], start=False, stop=True))
                p.chain("tensor", mms2)
            p.op("vector", lambda e, oc=oc: e.tensor_scalar(ssb[:, 0:512], psA[:, 0, :], boc[:, oc:oc + 1], None, op0=ALU.add))
            p.op("vector", lambda e, oc=oc: e.tensor_scalar(ssb[:, 512:1024], psA[:, 1, :], boc[:, oc:oc + 1], None, op0=ALU.add))
            p.op("scalar", lambda e: e.activation(gsb[:, 0:512], psA[:, 2, :], AF.Silu))
            p.op("scalar", lambda e: e.activation(gsb[:, 512:1024], psA[:, 3, :], AF.Silu))
            p.op("vector", lambda e: e.tensor_tensor(ssb[:], ssb[:], gsb[:], op=ALU.mult))
            dma(p, "sync", gsb[:], xr[:, oc, :])
            p.op("vector", lambda e: e.tensor_tensor(ssb[:], gsb[:], ssb[:], op=ALU.add))
            dma(p, "sync", x1r[:, oc, :], ssb[:])
            p.op("vector", lambda e: e.tensor_copy(x1bb[:], ssb[:]))
            p.op("scalar", lambda e: e.activation(sqx[:], x1bb[:], AF.Square))
            st, sp = (oc == 0), (oc == 31)
            p.chain("tensor", [
                lambda e, st=st, sp=sp: e.matmul(psB[0:1, 0, :], onesb[:], x1bb[:, 0:512], start=st, stop=sp),
                lambda e, st=st, sp=sp: e.matmul(psB[0:1, 1, :], onesb[:], x1bb[:, 512:1024], start=st, stop=sp),
                lambda e, st=st, sp=sp: e.matmul(psB[0:1, 2, :], onesb[:], sqx[:, 0:512], start=st, stop=sp),
                lambda e, st=st, sp=sp: e.matmul(psB[0:1, 3, :], onesb[:], sqx[:, 512:1024], start=st, stop=sp),
            ])

        p.op("scalar", lambda e: e.activation(mrow[0:1, 0:512], psB[0:1, 0, :], AF.Copy, scale=f32(1.0 / D)))
        p.op("scalar", lambda e: e.activation(mrow[0:1, 512:1024], psB[0:1, 1, :], AF.Copy, scale=f32(1.0 / D)))
        p.op("scalar", lambda e: e.activation(qrow[0:1, 0:512], psB[0:1, 2, :], AF.Copy, scale=f32(1.0 / D)))
        p.op("scalar", lambda e: e.activation(qrow[0:1, 512:1024], psB[0:1, 3, :], AF.Copy, scale=f32(1.0 / D)))
        p.op("vector", lambda e: e.tensor_tensor(vrow[0:1, :], mrow[0:1, :], mrow[0:1, :], op=ALU.mult))
        p.op("vector", lambda e: e.tensor_tensor(vrow[0:1, :], qrow[0:1, :], vrow[0:1, :], op=ALU.subtract))
        p.op("vector", lambda e: e.tensor_scalar_add(vrow[0:1, :], vrow[0:1, :], LN_EPS))
        p.op("scalar", lambda e: e.activation(vrow[0:1, :], vrow[0:1, :], AF.Sqrt))
        p.op("vector", lambda e: e.reciprocal(rrow[0:1, :], vrow[0:1, :]))
        p.chain("tensor", [
            lambda e: e.matmul(psB[:, 0, :], onesf[:], mrow[0:1, 0:512], start=True, stop=True),
            lambda e: e.matmul(psB[:, 1, :], onesf[:], mrow[0:1, 512:1024], start=True, stop=True),
            lambda e: e.matmul(psB[:, 2, :], onesf[:], rrow[0:1, 0:512], start=True, stop=True),
            lambda e: e.matmul(psB[:, 3, :], onesf[:], rrow[0:1, 512:1024], start=True, stop=True),
        ])
        p.op("scalar", lambda e: e.copy(mbc[:, 0:512], psB[:, 0, :]))
        p.op("scalar", lambda e: e.copy(mbc[:, 512:1024], psB[:, 1, :]))
        p.op("scalar", lambda e: e.copy(rbc[:, 0:512], psB[:, 2, :]))
        p.op("scalar", lambda e: e.copy(rbc[:, 512:1024], psB[:, 3, :]))
        for oc in range(32):
            dma(p, "gpsimd", x1bb[:], x1r[:, oc, :])
            p.op("vector", lambda e: e.tensor_copy(ssb[:], x1bb[:]))
            p.op("vector", lambda e: e.tensor_tensor(ssb[:], ssb[:], mbc[:], op=ALU.subtract))
            p.op("vector", lambda e: e.tensor_tensor(x1bb[:], ssb[:], rbc[:], op=ALU.mult))
            dma(p, "sync", xn1r[:, oc, :], x1bb[:])
        p.finish(block)
    return nc



def build_E():
    nc = bass.Bass("TRN2", target_bir_lowering=False)
    x1_in = nc.dram_tensor("x1", [D, TL], F32, kind="ExternalInput").ap()
    xn1_in = nc.dram_tensor("xn1", [D, TL], BF16, kind="ExternalInput").ap()
    ctx_in = nc.dram_tensor("ctxt", [D, TL], BF16, kind="ExternalInput").ap()
    w1t4_in = nc.dram_tensor("w1t4", [32, 128, NCH, 128], F32, kind="ExternalInput").ap()
    w1n4_in = nc.dram_tensor("w1n4", [32, 128, NCH, 128], F32, kind="ExternalInput").ap()
    a1t4_in = nc.dram_tensor("a1t4", [128, NCH, 32], F32, kind="ExternalInput").ap()
    a1_in = nc.dram_tensor("a1", [32, D], F32, kind="ExternalInput").ap()
    b1t_in = nc.dram_tensor("b1t", [32, D], F32, kind="ExternalInput").ap()
    mag_in = nc.dram_tensor("magc", [128, 32], F32, kind="ExternalInput").ap()
    x2_out = nc.dram_tensor("x2", [D, TL], F32, kind="ExternalOutput").ap()
    es = contextlib.ExitStack()
    with es:
        E = es.enter_context
        xn1b = E(nc.sbuf_tensor("xn1b", [128, NCH, TL], BF16))
        ct1 = E(nc.sbuf_tensor("ct1", [128, NCH, TL], BF16))
        a1tb = E(nc.sbuf_tensor("a1tb", [128, NCH, 32], BF16))
        a1b = E(nc.sbuf_tensor("a1b", [32, D], BF16))
        b1tb = E(nc.sbuf_tensor("b1tb", [32, D], BF16))
        w1nf = E(nc.sbuf_tensor("w1nf", [128, D], F32))
        fw1b = E(nc.sbuf_tensor("fw1b", [128, NCH, 128], BF16))
        magc = E(nc.sbuf_tensor("magc_s", [128, 32], F32))
        u1f = E(nc.sbuf_tensor("u1f", [32, TL], F32))
        u1b = E(nc.sbuf_tensor("u1b", [32, TL], BF16))
        effb = E(nc.sbuf_tensor("effb", [128, 512], BF16))
        effs = E(nc.sbuf_tensor("effs", [128, 512], BF16))
        rs1 = E(nc.sbuf_tensor("rs1", [128, 1], F32))
        cnacc = E(nc.sbuf_tensor("cnacc", [128, 32], F32))
        rsc = E(nc.sbuf_tensor("rsc", [128, 32], F32))
        resf = E(nc.sbuf_tensor("resf", [128, TL], F32))
        gsb = E(nc.sbuf_tensor("gsb", [128, TL], F32))
        cf = E(nc.sbuf_tensor("cf", [128, TL], F32))
        psA = E(nc.psum_tensor("psA", [128, 4, 512], F32))
        block = E(nc.Block())
        p = Prog(nc, es)

        xn1r = xn1_in.rearrange("(c p) t -> p c t", p=128)
        ctr = ctx_in.rearrange("(c p) t -> p c t", p=128)
        x1r = x1_in.rearrange("(c p) t -> p c t", p=128)
        x2r = x2_out.rearrange("(c p) t -> p c t", p=128)

        for i in range(4):
            sl = slice(i * 8, (i + 1) * 8)
            dma(p, "sync", xn1b[:, sl, :], xn1r[:, sl, :])
            dma(p, "sync", ct1[:, sl, :], ctr[:, sl, :])
        dma(p, "gpsimd", a1tb[:], a1t4_in)
        dma(p, "gpsimd", a1b[:], a1_in)
        dma(p, "gpsimd", b1tb[:], b1t_in)
        dma(p, "sync", magc[:], mag_in)
        p.op("vector", lambda e: e.memset(cnacc[:], 0.0))

        for half in range(2):
            mms = [lambda e, c=c, half=half, st=(c == 0), sp=(c == NCH - 1):
                   e.matmul(psA[0:32, half, :], a1tb[:, c, :], xn1b[:, c, half * 512:(half + 1) * 512], start=st, stop=sp)
                   for c in range(NCH)]
            p.chain("tensor", mms)
        p.op("scalar", lambda e: e.copy(u1f[:, 0:512], psA[0:32, 0, :]))
        p.op("scalar", lambda e: e.copy(u1f[:, 512:1024], psA[0:32, 1, :]))
        p.op("vector", lambda e: e.tensor_copy(u1b[:], u1f[:]))

        for oc in range(32):
            dma(p, "sync", w1nf[:], w1n4_in[oc].rearrange("p c i -> p (c i)"))
            for ib in range(8):
                isl = slice(ib * 512, (ib + 1) * 512)
                p.op("tensor", lambda e, oc=oc, isl=isl: e.matmul(
                    psA[:, 2, :], b1tb[:, oc * 128:(oc + 1) * 128], a1b[:, isl], start=True, stop=True))
                p.op("vector", lambda e, isl=isl: e.tensor_tensor(effb[:], psA[:, 2, :], w1nf[:, isl], op=ALU.add))
                p.op("scalar", lambda e: e.activation(effs[:], effb[:], AF.Square))
                p.op("vector", lambda e: e.reduce_sum(rs1[:], effs[:], axis=AX.X))
                p.op("vector", lambda e, oc=oc: e.tensor_tensor(cnacc[:, oc:oc + 1], cnacc[:, oc:oc + 1], rs1[:], op=ALU.add))
        p.op("scalar", lambda e: e.activation(cnacc[:], cnacc[:], AF.Sqrt))
        p.op("vector", lambda e: e.reciprocal(rsc[:], cnacc[:]))
        p.op("vector", lambda e: e.tensor_tensor(rsc[:], rsc[:], magc[:], op=ALU.mult), drain=True)

        for oc in range(32):
            dma(p, "gpsimd", fw1b[:], w1t4_in[oc])
            for half in range(2):
                hs = slice(half * 512, (half + 1) * 512)
                mms = [lambda e, c=c, half=half, hs=hs:
                       e.matmul(psA[:, half, :], fw1b[:, c, :], xn1b[:, c, hs], start=(c == 0), stop=False)
                       for c in range(NCH)]
                mms.append(lambda e, half=half, hs=hs, oc=oc: e.matmul(
                    psA[:, half, :], b1tb[:, oc * 128:(oc + 1) * 128], u1b[:, hs], start=False, stop=True))
                p.chain("tensor", mms)
            p.op("vector", lambda e, oc=oc: e.tensor_scalar(resf[:, 0:512], psA[:, 0, :], rsc[:, oc:oc + 1], None, op0=ALU.mult))
            p.op("vector", lambda e, oc=oc: e.tensor_scalar(resf[:, 512:1024], psA[:, 1, :], rsc[:, oc:oc + 1], None, op0=ALU.mult))
            p.op("scalar", lambda e: e.activation(gsb[:], resf[:], AF.Silu))
            p.op("vector", lambda e, oc=oc: e.tensor_copy(cf[:], ct1[:, oc, :]))
            p.op("vector", lambda e: e.tensor_tensor(cf[:], cf[:], gsb[:], op=ALU.mult))
            dma(p, "sync", gsb[:], x1r[:, oc, :])
            p.op("vector", lambda e: e.tensor_tensor(cf[:], gsb[:], cf[:], op=ALU.add))
            dma(p, "sync", x2r[:, oc, :], cf[:])
        p.finish(block)
    return nc



def _run(nc, in_maps):
    res = run_bass_kernel_spmd(nc, in_maps, core_ids=list(range(NCORE)))
    return res.results


USE_FUSED = True


def kernel(L0_attn_x, L0_attn_Wq, L0_attn_Wkv, L0_attn_Wo, L0_attn_bo,
           L0_ffn_weight, L0_ffn_lora_A, L0_ffn_lora_B, L0_ffn_gw1, L0_ffn_gb1,
           L0_ffn_ln_gamma, L0_ffn_ln_beta, L0_ffn_gw2, L0_ffn_gb2,
           L1_attn_wq, L1_attn_bq, L1_attn_wk, L1_attn_bk, L1_attn_wv,
           L1_attn_bv, L1_attn_wl, L1_attn_bl, L1_ffn_W, L1_ffn_lora_A,
           L1_ffn_lora_B, L1_ffn_magnitude):
    if USE_FUSED:
        return kernel_fused(dict(
            L0_attn_x=L0_attn_x, L0_attn_Wq=L0_attn_Wq, L0_attn_Wkv=L0_attn_Wkv,
            L0_attn_Wo=L0_attn_Wo, L0_attn_bo=L0_attn_bo, L0_ffn_weight=L0_ffn_weight,
            L0_ffn_lora_A=L0_ffn_lora_A, L0_ffn_lora_B=L0_ffn_lora_B,
            L0_ffn_gw1=L0_ffn_gw1, L0_ffn_gb1=L0_ffn_gb1,
            L0_ffn_ln_gamma=L0_ffn_ln_gamma, L0_ffn_ln_beta=L0_ffn_ln_beta,
            L0_ffn_gw2=L0_ffn_gw2, L0_ffn_gb2=L0_ffn_gb2,
            L1_attn_wq=L1_attn_wq, L1_attn_bq=L1_attn_bq, L1_attn_wk=L1_attn_wk,
            L1_attn_bk=L1_attn_bk, L1_attn_wv=L1_attn_wv, L1_attn_bv=L1_attn_bv,
            L1_ffn_W=L1_ffn_W, L1_ffn_lora_A=L1_ffn_lora_A,
            L1_ffn_lora_B=L1_ffn_lora_B, L1_ffn_magnitude=L1_ffn_magnitude))
    c32 = np.ascontiguousarray
    xT = c32(np.asarray(L0_attn_x, dtype=np.float32).reshape(T, D).T)
    ident = _ident()

    gw1t = c32(np.asarray(L0_ffn_gw1, np.float32).T)
    nc_a = build_A()
    in_a = []
    for c in range(NCORE):
        in_a.append({
            "x": c32(xT[:, c * TL:(c + 1) * TL]),
            "gw1t": gw1t,
            "gb1c": c32(np.asarray(L0_ffn_gb1, np.float32).reshape(32, 1)),
            "gam": c32(np.asarray(L0_ffn_ln_gamma, np.float32).reshape(1, 32)),
            "bet": c32(np.asarray(L0_ffn_ln_beta, np.float32).reshape(1, 32)),
            "gw2t": c32(np.asarray(L0_ffn_gw2, np.float32).T),
            "gb2": c32(np.asarray(L0_ffn_gb2, np.float32).reshape(1, 8)),
            "ident": ident,
        })
    res_a = _run(nc_a, in_a)
    xn_full = np.concatenate([res_a[c]["xn"] for c in range(NCORE)], axis=1)
    cnts = np.stack([res_a[c]["cnt"].reshape(2, 8) for c in range(NCORE)])
    wgs = [res_a[c]["wg"] for c in range(NCORE)]

    xn4 = c32(xn_full.reshape(NCH, 128, 32, 256).transpose(2, 1, 0, 3))

    def tile_pcf(w, width):
        return c32(w.reshape(NCH, 128, width).transpose(1, 0, 2))

    nc_b = build_attn(l1=False)
    tri = np.where(np.arange(128)[:, None] > np.arange(128)[None, :],
                   np.float32(NEG), np.float32(0.0)).astype(np.float32)
    Wq = np.asarray(L0_attn_Wq, np.float32)
    Wkv = np.asarray(L0_attn_Wkv, np.float32)
    in_b = []
    for c in range(NCORE):
        wq_sl = c32(Wq[:, 512 * c:512 * (c + 1)])
        wkv_sl = c32(Wkv[:, 256 * c:256 * (c + 1)])
        in_b.append({
            "xn4": xn4,
            "wq4": tile_pcf(wq_sl, 512),
            "wkv4": tile_pcf(wkv_sl, 256),
            "tri": tri,
        })
    res_b = _run(nc_b, in_b)
    ctx_full = np.concatenate([res_b[c]["ctx"] for c in range(NCORE)], axis=0)

    nc_c = build_C()
    wo4 = _tile_w(np.asarray(L0_attn_Wo, np.float32))
    fw4 = _tile_w(c32(np.asarray(L0_ffn_weight, np.float32).T))
    la_cat = c32(np.asarray(L0_ffn_lora_A, np.float32).transpose(1, 0, 2).reshape(D, 128))
    la4 = c32(la_cat.reshape(NCH, 128, 128).transpose(1, 0, 2))
    lb_cat = c32(np.asarray(L0_ffn_lora_B, np.float32).reshape(128, D))
    boc = _to_col(np.asarray(L0_attn_bo, np.float32))
    sel2 = np.zeros((16, 2), np.float32)
    sel2[0::2, 0] = 1.0
    sel2[1::2, 1] = 1.0
    ssel = np.zeros((8, 128), np.float32)
    for e_ in range(8):
        ssel[e_, e_ * 16:(e_ + 1) * 16] = 1.0
    cnt16 = c32(cnts.reshape(16, 8))
    in_c = []
    for c in range(NCORE):
        in_c.append({
            "x": c32(xT[:, c * TL:(c + 1) * TL]),
            "xn": c32(xn_full[:, c * TL:(c + 1) * TL]),
            "ctxt": c32(ctx_full[:, c * TL:(c + 1) * TL]),
            "wo4": wo4,
            "fw4": fw4,
            "boc": boc,
            "la4": la4,
            "lb": lb_cat,
            "cnts": cnt16,
            "sel2": sel2,
            "wg": wgs[c],
            "ssel": ssel,
            "ident": ident,
        })
    res_c = _run(nc_c, in_c)
    x1_sh = [res_c[c]["x1"] for c in range(NCORE)]
    xn1_full = np.concatenate([res_c[c]["xn1"] for c in range(NCORE)], axis=1)
    xn14 = c32(xn1_full.reshape(NCH, 128, 32, 256).transpose(2, 1, 0, 3))

    nc_d = build_D()
    wq1 = np.asarray(L1_attn_wq, np.float32)
    wk1 = np.asarray(L1_attn_wk, np.float32)
    wv1 = np.asarray(L1_attn_wv, np.float32)
    wq4 = _tile_w(c32(wq1.T))
    wk4 = _tile_w(c32(wk1.T))
    wv4 = _tile_w(c32(wv1.T))
    bqc = _to_col(np.asarray(L1_attn_bq, np.float32))
    bkc = _to_col(np.asarray(L1_attn_bk, np.float32))
    bvc = _to_col(np.asarray(L1_attn_bv, np.float32))
    identb = np.eye(128).astype(ml_dtypes.bfloat16)
    in_d = []
    for c in range(NCORE):
        in_d.append({
            "xn1": c32(xn1_full[:, c * TL:(c + 1) * TL]),
            "wq4": wq4, "wk4": wk4, "wv4": wv4,
            "bqc": bqc, "bkc": bkc, "bvc": bvc,
            "identb": identb,
        })
    res_d = _run(nc_d, in_d)
    ctx1_sh = [np.asarray(res_d[c]["ctx"]) for c in range(NCORE)]

    nc_e = build_E()
    W1 = np.asarray(L1_ffn_W, np.float32)
    w1t4 = _tile_w(c32(W1.T))
    w1n4 = c32(W1.reshape(32, 128, NCH, 128))
    A1 = np.asarray(L1_ffn_lora_A, np.float32)
    B1 = np.asarray(L1_ffn_lora_B, np.float32)
    a1t4 = c32(A1.T.reshape(NCH, 128, 32).transpose(1, 0, 2))
    b1t = c32(B1.T)
    magc = _to_col(np.asarray(L1_ffn_magnitude, np.float32))
    in_e = []
    for c in range(NCORE):
        in_e.append({
            "x1": x1_sh[c],
            "xn1": c32(xn1_full[:, c * TL:(c + 1) * TL]),
            "ctxt": ctx1_sh[c],
            "w1t4": w1t4,
            "w1n4": w1n4,
            "a1t4": a1t4,
            "a1": A1,
            "b1t": b1t,
            "magc": magc,
        })
    res_e = _run(nc_e, in_e)
    x2 = np.concatenate([res_e[c]["x2"] for c in range(NCORE)], axis=1)
    return np.ascontiguousarray(x2.T).reshape(1, T, D).astype(np.float32)




def build_fused(debug=False):
    nc = bass.Bass("TRN2", target_bir_lowering=False)
    AP_ = lambda name, shape, dt: nc.dram_tensor(name, shape, dt, kind="ExternalInput").ap()
    xfull_in = AP_("xfull", [D, T], F32)
    xown_in = AP_("xown", [D, TL], F32)
    wkk4_in = AP_("wkk4", [128, NCH, 1024], F32)
    wvv4_in = AP_("wvv4", [128, NCH, 1024], F32)
    gw1t4_in = AP_("gw1t4", [128, NCH, 32], F32)
    gb1c_in = AP_("gb1c", [32, 1], F32)
    gam_in = AP_("gam", [1, 32], F32)
    bet_in = AP_("bet", [1, 32], F32)
    gw2t_in = AP_("gw2t", [32, 8], F32)
    gb2_in = AP_("gb2", [1, 8], F32)
    wq4_in = AP_("wq4", [32, 128, NCH, 128], F32)
    wo4_in = AP_("wo4", [32, 128, NCH, 128], F32)
    fw4_in = AP_("fw4", [32, 128, NCH, 128], F32)
    boc_in = AP_("boc", [128, 32], F32)
    la4_in = AP_("la4", [128, NCH, 128], F32)
    lb_in = AP_("lb", [128, D], F32)
    ssel_in = AP_("ssel", [8, 128], F32)
    l1q4_in = AP_("l1q4", [32, 128, NCH, 128], F32)
    l1k4_in = AP_("l1k4", [32, 128, NCH, 128], F32)
    l1v4_in = AP_("l1v4", [32, 128, NCH, 128], F32)
    bqc_in = AP_("bqc", [128, 32], F32)
    bkc_in = AP_("bkc", [128, 32], F32)
    bvc_in = AP_("bvc", [128, 32], F32)
    w1t4_in = AP_("w1t4", [32, 128, NCH, 128], F32)
    w1n4_in = AP_("w1n4", [32, 128, NCH, 128], F32)
    a1t4_in = AP_("a1t4", [128, NCH, 32], F32)
    a1_in = AP_("a1", [32, D], F32)
    b1t_in = AP_("b1t", [32, D], F32)
    magc_in = AP_("magc", [128, 32], F32)
    id_in = AP_("identf", [128, 128], F32)
    idb_in = AP_("identb", [128, 128], BF16)
    tri_in = AP_("tri", [128, 128], F32)
    biasq_in = AP_("biasq", [128, 32], F32)
    x2_out = nc.dram_tensor("x2", [D, TL], F32, kind="ExternalOutput").ap()
    if debug:
        dbg_mr = nc.dram_tensor("dbg_mr", [2, 512], F32, kind="ExternalOutput").ap()
        dbg_k = nc.dram_tensor("dbg_k", [128, 512], BF16, kind="ExternalOutput").ap()
        dbg_v = nc.dram_tensor("dbg_v", [256, 128], BF16, kind="ExternalOutput").ap()
        dbg_h = nc.dram_tensor("dbg_h", [32, 512], BF16, kind="ExternalOutput").ap()
        dbg_q = nc.dram_tensor("dbg_q", [128, TL], BF16, kind="ExternalOutput").ap()
        dbg_x1 = nc.dram_tensor("dbg_x1", [128, TL], F32, kind="ExternalOutput").ap()
        dbg_c1 = nc.dram_tensor("dbg_c1", [128, TL], BF16, kind="ExternalOutput").ap()
        dbg_ctx = nc.dram_tensor("dbg_ctx", [128, 512], BF16, kind="ExternalOutput").ap()
        dbg_xn = nc.dram_tensor("dbg_xn", [128, TL], BF16, kind="ExternalOutput").ap()
        dbg_wb = nc.dram_tensor("dbg_wb", [128, TL], F32, kind="ExternalOutput").ap()
        dbg_xn1 = nc.dram_tensor("dbg_xn1", [128, TL], BF16, kind="ExternalOutput").ap()
        dbg_q1 = nc.dram_tensor("dbg_q1", [D, TL], BF16, kind="ExternalOutput").ap()
        dbg_k1 = nc.dram_tensor("dbg_k1", [D, TL], BF16, kind="ExternalOutput").ap()
        dbg_v1 = nc.dram_tensor("dbg_v1", [D, TL], BF16, kind="ExternalOutput").ap()
        dbg_c1f = nc.dram_tensor("dbg_c1f", [D, TL], BF16, kind="ExternalOutput").ap()
        dbg_st = nc.dram_tensor("dbg_st", [6, TL], F32, kind="ExternalOutput").ap()
        dbg_x1b = nc.dram_tensor("dbg_x1b", [128, 3, TL], F32, kind="ExternalOutput").ap()
        dbg_sg = nc.dram_tensor("dbg_sg", [128, 2, TL], F32, kind="ExternalOutput").ap()
        dbg_cnt = nc.dram_tensor("dbg_cnt", [1, 16], F32, kind="ExternalOutput").ap()
        dbg_al = nc.dram_tensor("dbg_al", [128, 16], F32, kind="ExternalOutput").ap()
        dbg_wt = nc.dram_tensor("dbg_wt", [128, 2, 64], F32, kind="ExternalOutput").ap()
        dbg_hf = nc.dram_tensor("dbg_hf", [32, T], BF16, kind="ExternalOutput").ap()
        dbg_us = nc.dram_tensor("dbg_us", [128, TL], BF16, kind="ExternalOutput").ap()
        dbg_pc = nc.dram_tensor("dbg_pc", [128, 64], F32, kind="ExternalOutput").ap()
        dbg_hb = nc.dram_tensor("dbg_hb", [32, 128], BF16, kind="ExternalOutput").ap()
        dbg_hr = nc.dram_tensor("dbg_hr", [128, 32], F32, kind="ExternalOutput").ap()
        dbg_ln = nc.dram_tensor("dbg_ln", [128, 40], F32, kind="ExternalOutput").ap()

    kscr = nc.dram_tensor("kscr", [1024, T], BF16)
    vscr = nc.dram_tensor("vscr", [T, 1024], BF16)
    qscr = nc.dram_tensor("qscr", [D, TL], BF16)
    x1scr = nc.dram_tensor("x1scr", [D, TL], F32)
    hscr = nc.dram_tensor("hscr", [32, T], BF16)
    mrscr = nc.dram_tensor("mrscr", [2, T], F32)
    wgscr = nc.dram_tensor("wgscr", [2, T, 8], F32)
    q1scr = nc.dram_tensor("q1scr", [D, TL], BF16)
    k1scr = nc.dram_tensor("k1scr", [D, TL], BF16)
    v1scr = nc.dram_tensor("v1scr", [D, TL], BF16)
    ctxscr = nc.dram_tensor("ctxscr", [D, TL], BF16)
    kdscr = nc.dram_tensor("kdscr", [16, 128, 512], BF16)
    vdscr = nc.dram_tensor("vdscr", [16, 512, 128], BF16)

    es = contextlib.ExitStack()
    with es:
        E = es.enter_context
        M = E(nc.sbuf_tensor("M", [128, 86016], BF16))

        def mv(off, *shape):
            n = 1
            for s in shape:
                n *= s
            ap = M[:, off:off + n]
            if len(shape) == 2:
                ap = ap.rearrange("p (a b) -> p a b", b=shape[1])
            elif len(shape) == 3:
                ap = ap.rearrange("p (a b c) -> p a b c", c=shape[2], b=shape[1])
            return ap
        W2 = 65536
        wkb = mv(0, 32, 1024)
        wvb = mv(0, 32, 1024)
        xnb = mv(0, 32, 1024)
        xtile = mv(32768, 32, 256)
        ctxb = mv(32768, 32, 1024)
        wtl = mv(W2, 32, 128)
        wtl2 = mv(W2 + 4096, 32, 128)
        wtlB = mv(W2 + 12288, 32, 128)
        wtl2B = mv(W2 + 16384, 32, 128)
        wtlE = (wtl, wtl2)
        wtlH = (wtl, wtlB)
        wtl2H = (wtl2, wtl2B)
        xtileB = mv(32768 + 8192, 32, 256)
        qtile = M[:, W2 + 8192:W2 + 9216]
        usb = M[:, W2 + 9216:W2 + 10240]
        x1bb = M[:, W2 + 10240:W2 + 11264]
        sqx = M[:, W2 + 11264:W2 + 12288]
        sqt = M[:, W2 + 12288:W2 + 12544]
        ktile = M[:, W2 + 12544:W2 + 12800]
        vtile = M[:, W2 + 12800:W2 + 13312]
        sqtB = M[:, W2:W2 + 256]
        ktileB = M[:, W2 + 256:W2 + 512]
        gtile = M[:, W2 + 512:W2 + 768]
        vtileB = M[:, W2 + 768:W2 + 1280]
        gw1b = mv(W2 + 13312, 32, 32)
        u1b_full = M[:, W2 + 14336:W2 + 15360]
        u1b = u1b_full[0:32, :]
        effb = M[:, W2 + 15360:W2 + 15872]
        effs = M[:, W2 + 15872:W2 + 16384]
        kgb = mv(W2, 64, 128)
        vgb = mv(W2 + 8192, 64, 128)
        ptb4 = M[:, W2 + 16384:W2 + 18432]
        kdiag = M[:, W2 + 18432:W2 + 18944]
        vdiag = mv(W2 + 18944, 4, 128)
        qb = mv(W2 + 4096, 32, 128)
        kb = mv(W2 + 8192, 32, 128)
        vb = mv(W2 + 12288, 32, 128)
        qbB = mv(32768, 32, 128)
        kbB = mv(32768 + 4096, 32, 128)
        vbB = mv(32768 + 8192, 32, 128)
        ctb2 = mv(W2 + 16384, 32, 128)
        vts1 = mv(W2 + 2048, 16, 128)
        pt1f = M[:, W2 + 0:W2 + 512]
        pt1 = pt1f[0:32, :]
        w1nb_f = M[:, W2:W2 + 4096]
        w1nbB = M[:, W2 + 16384:W2 + 20480]
        effbB = M[:, W2 + 12288:W2 + 12800]
        effsB = M[:, W2 + 12800:W2 + 13312]

        Ff = E(nc.sbuf_tensor("Ff", [128, 5120], F32))
        slotA = Ff[:, 0:1024]
        slotB = Ff[:, 1024:2048]
        slotC = Ff[:, 2048:3072]
        slotD = Ff[:, 3072:4096]
        colvb = slotD
        mbc = Ff[:, 4096:4352]
        rbc = Ff[:, 4352:4608]
        tmpc = Ff[:, 4608:4864]

        onesf = E(nc.sbuf_tensor("onesf", [1, 128], F32))
        onesb = E(nc.sbuf_tensor("onesb", [128, 1], BF16))
        ones11 = E(nc.sbuf_tensor("ones11", [1, 1], F32))
        colkc = E(nc.sbuf_tensor("colkc", [128, 8], F32))
        colg = E(nc.sbuf_tensor("colg", [32, 1], F32))
        mrow = E(nc.sbuf_tensor("mrow", [1, 256], F32))
        qrow = E(nc.sbuf_tensor("qrow", [1, 256], F32))
        vrow = E(nc.sbuf_tensor("vrow", [1, 256], F32))
        rrow = E(nc.sbuf_tensor("rrow", [1, 256], F32))
        mcol = E(nc.sbuf_tensor("mcol", [128, 2], F32))
        rcol = E(nc.sbuf_tensor("rcol", [128, 2], F32))
        morow = E(nc.sbuf_tensor("morow", [1, TL], F32))
        gb1c = E(nc.sbuf_tensor("gb1c_s", [32, 1], F32))
        gamr = E(nc.sbuf_tensor("gamr", [1, 32], F32))
        betr = E(nc.sbuf_tensor("betr", [1, 32], F32))
        gw2t = E(nc.sbuf_tensor("gw2ts", [32, 8], F32))
        gb2r = E(nc.sbuf_tensor("gb2r", [1, 8], F32))
        gamb = E(nc.sbuf_tensor("gamb", [128, 32], F32))
        betb = E(nc.sbuf_tensor("betb", [128, 32], F32))
        gb2b = E(nc.sbuf_tensor("gb2b", [128, 8], F32))
        htm = E(nc.sbuf_tensor("htm", [128, 32], F32))
        htm2 = E(nc.sbuf_tensor("htm2", [128, 32], F32))
        htb = E(nc.sbuf_tensor("htb", [32, 128], BF16))
        hm = E(nc.sbuf_tensor("hm", [128, 1], F32))
        hv = E(nc.sbuf_tensor("hv", [128, 1], F32))
        hrs = E(nc.sbuf_tensor("hrs", [128, 1], F32))
        hsq = E(nc.sbuf_tensor("hsq", [128, 32], F32))
        hft = E(nc.sbuf_tensor("hft", [32, 128], F32))
        gts = E(nc.sbuf_tensor("gts", [128, 8], F32))
        iotf = E(nc.sbuf_tensor("iotf", [128, 8], F32))
        ioti = E(nc.sbuf_tensor("ioti", [128, 8], I32))
        v1 = E(nc.sbuf_tensor("v1", [128, 1], F32))
        nv1 = E(nc.sbuf_tensor("nv1", [128, 1], F32))
        v2 = E(nc.sbuf_tensor("v2", [128, 1], F32))
        e2 = E(nc.sbuf_tensor("e2", [128, 1], F32))
        g1 = E(nc.sbuf_tensor("g1", [128, 1], F32))
        g2c = E(nc.sbuf_tensor("g2c", [128, 1], F32))
        sel8 = E(nc.sbuf_tensor("sel8", [128, 8], F32))
        m1c = E(nc.sbuf_tensor("m1c", [128, 1], F32))
        oh1 = E(nc.sbuf_tensor("oh1", [128, 8], F32))
        oh2 = E(nc.sbuf_tensor("oh2", [128, 8], F32))
        wtown1 = E(nc.sbuf_tensor("wtown1", [128, 8, 8], F32))
        wtown2 = E(nc.sbuf_tensor("wtown2", [128, 8, 8], F32))
        cac1 = E(nc.sbuf_tensor("cac1", [128, 8], F32))
        cac2 = E(nc.sbuf_tensor("cac2", [128, 8], F32))
        cacb = E(nc.sbuf_tensor("cacb", [128, 8], BF16))
        cgrow = E(nc.sbuf_tensor("cgrow", [1, 16], F32))
        alw = E(nc.sbuf_tensor("alw", [1, 16], F32))
        alb1 = E(nc.sbuf_tensor("alb1", [128, 8], F32))
        alb2 = E(nc.sbuf_tensor("alb2", [128, 8], F32))
        wfm = slotD[0:8, :]
        wbcb = slotC
        ssel_sb = E(nc.sbuf_tensor("ssel_sb", [8, 128], F32))
        boc = E(nc.sbuf_tensor("boc_s", [128, 32], F32))
        bqc = E(nc.sbuf_tensor("bqc_s", [128, 32], F32))
        bkc = E(nc.sbuf_tensor("bkc_s", [128, 32], F32))
        bvc = E(nc.sbuf_tensor("bvc_s", [128, 32], F32))
        magc = E(nc.sbuf_tensor("magc_s", [128, 32], F32))
        biasq = E(nc.sbuf_tensor("biasq_s", [128, 32], F32))
        tri = E(nc.sbuf_tensor("tri_s", [128, 128], F32))
        ident = E(nc.sbuf_tensor("ident_s", [128, 128], F32))
        identb = E(nc.sbuf_tensor("identb_s", [128, 128], BF16))
        qts = M[:, 84992:85504]
        qtsB = M[:, 85504:86016]
        lbt = E(nc.sbuf_tensor("lbt", [128, 128], BF16))
        lbtB = E(nc.sbuf_tensor("lbtB", [128, 128], BF16))
        lbtH = (lbt, lbtB)
        rcb = Ff[:, 4096:4608]
        rs = Ff[0:1, 4608:5120]
        lab = mv(W2, 32, 128)
        a1tb = mv(W2, 32, 32)
        a1b = M[:, W2 + 4096:W2 + 8192][0:32, :]
        b1tb = M[:, W2 + 8192:W2 + 12288][0:32, :]
        ctxrd = M[:, W2 + 12288:W2 + 13312]
        rs1 = E(nc.sbuf_tensor("rs1", [128, 1], F32))
        cnacc = E(nc.sbuf_tensor("cnacc", [128, 32], F32))
        rsc = E(nc.sbuf_tensor("rsc", [128, 32], F32))
        psA = E(nc.psum_tensor("psA", [128, 7, 512], F32))
        psVt = E(nc.psum_tensor("psVt", [128, 1, 1024], BF16))
        block = E(nc.Block())
        p = Prog(nc, es)

        xfr = xfull_in.rearrange("(c p) t -> p c t", p=128)
        xor_ = xown_in.rearrange("(c p) t -> p c t", p=128)
        x1r = x1scr.ap().rearrange("(c p) t -> p c t", p=128)
        x2r = x2_out.rearrange("(c p) t -> p c t", p=128)
        AXX = AX.X

        p.op("vector", lambda e: e.memset(onesf[:], 1.0))
        p.op("vector", lambda e: e.memset(onesb[:], 1.0))
        p.op("vector", lambda e: e.memset(ones11[:], 1.0))
        p.op("vector", lambda e: e.memset(cac1[:], 0.0))
        p.op("vector", lambda e: e.memset(cac2[:], 0.0))
        p.op("gpsimd", lambda e: e.iota(ioti[:], pattern=[[1, 8]], base=0, channel_multiplier=0))
        p.op("vector", lambda e: e.tensor_copy(iotf[:], ioti[:]))
        for dst, src_ in ((gb1c, gb1c_in), (gamr, gam_in), (betr, bet_in), (gw2t, gw2t_in),
                          (gb2r, gb2_in), (boc, boc_in), (bqc, bqc_in), (bkc, bkc_in),
                          (bvc, bvc_in), (magc, magc_in), (biasq, biasq_in), (tri, tri_in),
                          (ident, id_in), (identb, idb_in), (ssel_sb, ssel_in)):
            dma(p, "sync", dst[:], src_)
        dma(p, "gpsimd", gw1b, gw1t4_in)
        p.chain("tensor", [
            lambda e: e.matmul(psA[:, 0, 0:32], onesf[:], gamr[:], start=True, stop=True),
            lambda e: e.matmul(psA[:, 1, 0:32], onesf[:], betr[:], start=True, stop=True),
            lambda e: e.matmul(psA[:, 2, 0:8], onesf[:], gb2r[:], start=True, stop=True),
        ])
        p.op("scalar", lambda e: e.copy(gamb[:], psA[:, 0, 0:32]))
        p.op("scalar", lambda e: e.copy(betb[:], psA[:, 1, 0:32]))
        p.op("scalar", lambda e: e.copy(gb2b[:], psA[:, 2, 0:8]))

        for i in range(4):
            sl = slice(i * 8, (i + 1) * 8)
            dma(p, "gpsimd", wkb[:, sl, :], wkk4_in[:, sl, :])
        for oc in range(8):
            p.chain("tensor", [
                lambda e, c=c, oc=oc, st=(c == 0), sp=(c == NCH - 1): e.matmul(
                    psA[:, 3, 0:1], wkb[:, c, oc * 128:(oc + 1) * 128], onesb[:], start=st, stop=sp)
                for c in range(NCH)])
            p.op("scalar", lambda e, oc=oc: e.copy(colkc[:, oc:oc + 1], psA[:, 3, 0:1]))
        p.chain("tensor", [
            lambda e, c=c, st=(c == 0), sp=(c == NCH - 1): e.matmul(
                psA[0:32, 3, 0:1], gw1b[:, c, :], onesb[:], start=st, stop=sp)
            for c in range(NCH)])
        p.op("scalar", lambda e: e.copy(colg[:], psA[0:32, 3, 0:1]))

        endA = {}
        floorA = p.mark()
        sqtP = (sqt, sqtB)
        ktileP = (ktile, ktileB)
        mChA, mSubA, mKoutA, mGevA, mHoutA = {}, {}, {}, {}, {}
        for tb in range(32):
            ts_ = slice(tb * 256, (tb + 1) * 256)
            xt_ = (xtile, xtileB)[tb % 2]
            p.op("gpsimd", lambda e, xt_=xt_, ts_=ts_: e.dma_start(out=xt_[:], in_=xfr[:, :, ts_]),
                 inc=16, wait_at=endA.get(tb - 2, floorA))
            dmam = p.mark()
            for c in range(NCH):
                u = tb * NCH + c
                sq_ = sqtP[c % 2]
                p.op("scalar", lambda e, xt_=xt_, c=c, sq_=sq_: e.activation(sq_, xt_[:, c, :], AF.Square),
                     wait_at=mmax(dmam, mChA.get(u - 2, floorA)))
                msq = p.mark()
                st, sp = (c == 0), (c == NCH - 1)
                p.chain("tensor", [
                    lambda e, c=c, st=st, sp=sp, xt_=xt_: e.matmul(psA[0:1, 4, 0:256], onesb[:], xt_[:, c, :], start=st, stop=sp),
                    lambda e, st=st, sp=sp, sq_=sq_: e.matmul(psA[0:1, 5, 0:256], onesb[:], sq_, start=st, stop=sp),
                ], wait_at=msq)
                mChA[u] = p.mark()
            p.op("scalar", lambda e: e.activation(mrow[:], psA[0:1, 4, 0:256], AF.Copy, scale=f32(1.0 / D)))
            p.op("scalar", lambda e: e.activation(qrow[:], psA[0:1, 5, 0:256], AF.Copy, scale=f32(1.0 / D)))
            p.op("vector", lambda e: e.tensor_tensor(vrow[:], mrow[:], mrow[:], op=ALU.mult))
            p.op("vector", lambda e: e.tensor_tensor(vrow[:], qrow[:], vrow[:], op=ALU.subtract))
            p.op("vector", lambda e: e.tensor_scalar_add(vrow[:], vrow[:], LN_EPS))
            p.op("scalar", lambda e: e.activation(vrow[:], vrow[:], AF.Sqrt))
            p.op("vector", lambda e: e.reciprocal(rrow[:], vrow[:]))
            dma(p, "sync", mrscr.ap()[0:1, ts_], mrow[:])
            dma(p, "sync", mrscr.ap()[1:2, ts_], rrow[:])
            p.op("tensor", lambda e: e.matmul(psA[:, 6, 0:256], onesf[:], mrow[:], start=True, stop=True))
            p.op("scalar", lambda e: e.copy(mbc, psA[:, 6, 0:256]))
            p.op("tensor", lambda e: e.matmul(psA[:, 6, 0:256], onesf[:], rrow[:], start=True, stop=True))
            p.op("scalar", lambda e: e.copy(rbc, psA[:, 6, 0:256]))
            mRB = p.mark()
            for oc in range(8):
                g_ = tb * 8 + oc
                bq = oc % 2
                kt_ = ktileP[oc % 2]
                p.chain("tensor", [
                    lambda e, c=c, oc=oc, bq=bq, st=(c == 0), sp=(c == NCH - 1), xt_=xt_: e.matmul(
                        psA[:, bq, 0:256], wkb[:, c, oc * 128:(oc + 1) * 128], xt_[:, c, :], start=st, stop=sp)
                    for c in range(NCH)], wait_at=mSubA.get(g_ - 2, floorA))
                mchk = p.mark()
                p.op("vector", lambda e, oc=oc: e.tensor_scalar(tmpc, mbc, colkc[:, oc:oc + 1], None, op0=ALU.mult),
                     wait_at=(mRB if oc == 0 else 0))
                p.op("vector", lambda e, bq=bq: e.tensor_tensor(tmpc, psA[:, bq, 0:256], tmpc, op=ALU.subtract),
                     wait_at=mchk)
                mSubA[g_] = p.mark()
                p.op("vector", lambda e, kt_=kt_: e.tensor_tensor(kt_, tmpc, rbc, op=ALU.mult),
                     wait_at=mKoutA.get(g_ - 2, floorA))
                mmul = p.mark()
                p.op("sync", lambda e, oc=oc, ts_=ts_, kt_=kt_: e.dma_start(
                    out=kscr.ap()[oc * 128:(oc + 1) * 128, ts_], in_=kt_), inc=16, wait_at=mmul)
                mKoutA[g_] = p.mark()
            p.chain("tensor", [
                lambda e, c=c, st=(c == 0), sp=(c == NCH - 1), xt_=xt_: e.matmul(
                    psA[0:32, 2, 0:256], gw1b[:, c, :], xt_[:, c, :], start=st, stop=sp)
                for c in range(NCH)], wait_at=mGevA.get(tb - 1, floorA))
            mchg = p.mark()
            p.op("vector", lambda e: e.tensor_scalar(tmpc[0:32, :], mbc[0:32, :], colg[:], None, op0=ALU.mult),
                 wait_at=0)
            p.op("vector", lambda e: e.tensor_tensor(tmpc[0:32, :], psA[0:32, 2, 0:256], tmpc[0:32, :], op=ALU.subtract),
                 wait_at=mchg)
            mGevA[tb] = p.mark()
            p.op("vector", lambda e: e.tensor_tensor(tmpc[0:32, :], tmpc[0:32, :], rbc[0:32, :], op=ALU.mult),
                 wait_at=0)
            p.op("vector", lambda e: e.tensor_scalar(gtile[0:32, :], tmpc[0:32, :], gb1c[:], None, op0=ALU.add),
                 wait_at=mHoutA.get(tb - 1, floorA))
            mgt = p.mark()
            p.op("sync", lambda e, ts_=ts_: e.dma_start(out=hscr.ap()[:, ts_], in_=gtile[0:32, :]),
                 inc=16, wait_at=mgt)
            mHoutA[tb] = p.mark()
            endA[tb] = p.mark()

        for i in range(4):
            sl = slice(i * 8, (i + 1) * 8)
            dma(p, "gpsimd", wvb[:, sl, :], wvv4_in[:, sl, :])
        for half in range(2):
            p.chain("tensor", [
                lambda e, c=c, half=half, st=(c == 0), sp=(c == NCH - 1): e.matmul(
                    psA[0:1, 3, :], onesb[:], wvb[:, c, half * 512:(half + 1) * 512], start=st, stop=sp)
                for c in range(NCH)])
            p.op("scalar", lambda e: e.copy(rs[:], psA[0:1, 3, :]))
            p.op("tensor", lambda e: e.matmul(psA[:, 2, :], onesf[:], rs[:], start=True, stop=True))
            p.op("scalar", lambda e, half=half: e.copy(colvb[:, half * 512:(half + 1) * 512], psA[:, 2, :]))
        endB = {}
        floorB = p.mark()
        vtileP = (vtile, vtileB)
        slAB = (slotA[:, 0:512], slotA[:, 512:1024])
        mSubB, mVoutB, mColB = {}, {}, {}

        htmP = (htm, htm2)
        def emit_c_front(t):
            ht_ = htmP[t % 2]
            p.op("sync", lambda e, t=t: e.dma_start(out=htb[:], in_=hscr.ap()[:, t * 128:(t + 1) * 128]), inc=16)
            p.op("tensor", lambda e: e.transpose(psVt[:, 0, 0:32], htb[:], identb[0:32, 0:32]))
            p.op("scalar", lambda e, ht_=ht_: e.copy(ht_[:], psVt[:, 0, 0:32]))
            p.op("vector", lambda e, ht_=ht_: e.reduce_sum(hm[:], ht_[:], axis=AXX), drain=True)
            p.op("vector", lambda e: e.tensor_scalar_mul(hm[:], hm[:], 1.0 / 32.0), drain=True)
            p.op("vector", lambda e, ht_=ht_: e.tensor_scalar(ht_[:], ht_[:], hm[:], None, op0=ALU.subtract), drain=True)
            p.op("vector", lambda e, ht_=ht_: e.tensor_tensor(hsq[:], ht_[:], ht_[:], op=ALU.mult), drain=True)
            p.op("vector", lambda e: e.reduce_sum(hv[:], hsq[:], axis=AXX), drain=True)
            p.op("vector", lambda e: e.tensor_scalar_mul(hv[:], hv[:], 1.0 / 32.0), drain=True)
            p.op("vector", lambda e: e.tensor_scalar_add(hv[:], hv[:], LN_EPS), drain=True)
            p.op("scalar", lambda e: e.activation(hv[:], hv[:], AF.Ln))
            p.op("scalar", lambda e: e.activation(hrs[:], hv[:], AF.Exp, scale=f32(-0.5)))
            p.op("vector", lambda e, ht_=ht_: e.tensor_scalar(ht_[:], ht_[:], hrs[:], None, op0=ALU.mult), drain=True)
            p.op("vector", lambda e, ht_=ht_: e.tensor_tensor(ht_[:], ht_[:], gamb[:], op=ALU.mult), drain=True)
            p.op("vector", lambda e, ht_=ht_: e.tensor_tensor(ht_[:], ht_[:], betb[:], op=ALU.add), drain=True)
            p.op("vector", lambda e, ht_=ht_: e.tensor_relu(ht_[:], ht_[:]), drain=True)

        def emit_c_back(t):
            ht_ = htmP[t % 2]
            p.op("tensor", lambda e, ht_=ht_: e.transpose(psA[0:32, 3, 0:128], ht_[:], ident[:]))
            p.op("scalar", lambda e: e.copy(hft[:], psA[0:32, 3, 0:128]))
            p.op("tensor", lambda e: e.matmul(psA[:, 2, 64:72], hft[:], gw2t[:], start=True, stop=True))
            p.op("vector", lambda e: e.tensor_tensor(gts[:], psA[:, 2, 64:72], gb2b[:], op=ALU.add), drain=True)
            p.op("vector", lambda e: e.reduce_max(v1[:], gts[:], axis=AXX), drain=True)
            p.op("vector", lambda e: e.tensor_scalar(oh1[:], gts[:], v1[:], None, op0=ALU.is_equal), drain=True)
            p.op("vector", lambda e: e.scalar_tensor_tensor(sel8[:], oh1[:], -256.0, iotf[:], op0=ALU.mult, op1=ALU.add), drain=True)
            p.op("vector", lambda e: e.tensor_reduce(m1c[:], sel8[:], axis=AXX, op=ALU.min), drain=True)
            p.op("vector", lambda e: e.tensor_scalar(oh1[:], sel8[:], m1c[:], None, op0=ALU.is_equal), drain=True)
            p.op("vector", lambda e: e.scalar_tensor_tensor(sel8[:], oh1[:], NEG, gts[:], op0=ALU.mult, op1=ALU.add), drain=True)
            p.op("vector", lambda e: e.reduce_max(v2[:], sel8[:], axis=AXX), drain=True)
            p.op("vector", lambda e: e.tensor_scalar(oh2[:], sel8[:], v2[:], None, op0=ALU.is_equal), drain=True)
            p.op("vector", lambda e: e.scalar_tensor_tensor(sel8[:], oh2[:], -256.0, iotf[:], op0=ALU.mult, op1=ALU.add), drain=True)
            p.op("vector", lambda e: e.tensor_reduce(m1c[:], sel8[:], axis=AXX, op=ALU.min), drain=True)
            p.op("vector", lambda e: e.tensor_scalar(oh2[:], sel8[:], m1c[:], None, op0=ALU.is_equal), drain=True)
            p.op("vector", lambda e: e.tensor_scalar_mul(nv1[:], v1[:], -1.0), drain=True)
            p.op("scalar", lambda e: e.activation(e2[:], v2[:], AF.Exp, bias=nv1[:]))
            p.op("vector", lambda e: e.tensor_scalar_add(g1[:], e2[:], 1.0), drain=True)
            p.op("vector", lambda e: e.reciprocal(g1[:], g1[:]), drain=True)
            p.op("vector", lambda e: e.tensor_tensor(g2c[:], e2[:], g1[:], op=ALU.mult), drain=True)
            p.op("vector", lambda e: e.tensor_scalar(sel8[:], oh1[:], g1[:], None, op0=ALU.mult), drain=True)
            dma(p, "sync", wgscr.ap()[0, t * 128:(t + 1) * 128, :], sel8[:])
            p.op("vector", lambda e: e.tensor_scalar(sel8[:], oh2[:], g2c[:], None, op0=ALU.mult), drain=True)
            dma(p, "sync", wgscr.ap()[1, t * 128:(t + 1) * 128, :], sel8[:])
            p.op("vector", lambda e: e.tensor_tensor(cac1[:], cac1[:], oh1[:], op=ALU.add), drain=True)
            p.op("vector", lambda e: e.tensor_tensor(cac2[:], cac2[:], oh2[:], op=ALU.add), drain=True)

        for tb in range(32):
            ts_ = slice(tb * 256, (tb + 1) * 256)
            xt_ = (xtile, xtileB)[tb % 2]
            p.op("gpsimd", lambda e, xt_=xt_, ts_=ts_: e.dma_start(out=xt_[:], in_=xfr[:, :, ts_]),
                 inc=16, wait_at=endB.get(tb - 2, floorB))
            dma(p, "sync", mrow[:], mrscr.ap()[0:1, ts_])
            dma(p, "sync", rrow[:], mrscr.ap()[1:2, ts_])
            dmam = p.mark()
            mcp = floorB
            for vt in range(2):
                p.chain("tensor", [
                    lambda e, vt=vt: e.matmul(psA[:, 4, 0:1], mrow[0:1, vt * 128:(vt + 1) * 128], ones11[:], start=True, stop=True),
                    lambda e, vt=vt: e.matmul(psA[:, 5, 0:1], rrow[0:1, vt * 128:(vt + 1) * 128], ones11[:], start=True, stop=True),
                ], wait_at=mmax(dmam, mColB.get(tb - 1, floorB)) if vt == 0 else mcp)
                mchc = p.mark()
                p.op("scalar", lambda e, vt=vt: e.copy(mcol[:, vt:vt + 1], psA[:, 4, 0:1]), wait_at=mchc)
                p.op("scalar", lambda e, vt=vt: e.copy(rcol[:, vt:vt + 1], psA[:, 5, 0:1]), wait_at=0)
                mcp = p.mark()
            mColB[tb] = p.mark()
            for vt in range(2):
                tk0 = tb * 256 + vt * 128
                for half in range(2):
                    u = tb * 4 + vt * 2 + half
                    bq = u % 2
                    vt_ = vtileP[u % 2]
                    sl_ = slAB[u % 2]
                    hsl = slice(half * 512, (half + 1) * 512)
                    p.chain("tensor", [
                        lambda e, c=c, vt=vt, hsl=hsl, bq=bq, st=(c == 0), sp=(c == NCH - 1), xt_=xt_: e.matmul(
                            psA[:, bq, :], xt_[:, c, vt * 128:(vt + 1) * 128], wvb[:, c, hsl], start=st, stop=sp)
                        for c in range(NCH)], wait_at=mSubB.get(u - 2, floorB))
                    mchv = p.mark()
                    p.op("vector", lambda e, hsl=hsl, vt=vt, sl_=sl_: e.tensor_scalar(
                        sl_, colvb[:, hsl], mcol[:, vt:vt + 1], None, op0=ALU.mult),
                        wait_at=(mColB[tb] if u % 4 == 0 else 0))
                    p.op("vector", lambda e, bq=bq, sl_=sl_: e.tensor_tensor(
                        sl_, psA[:, bq, :], sl_, op=ALU.subtract), wait_at=mchv)
                    mSubB[u] = p.mark()
                    p.op("vector", lambda e, vt=vt, vt_=vt_, sl_=sl_: e.tensor_scalar(
                        vt_, sl_, rcol[:, vt:vt + 1], None, op0=ALU.mult),
                        wait_at=mVoutB.get(u - 2, floorB))
                    mmul = p.mark()
                    p.op("sync", lambda e, tk0=tk0, hsl=hsl, vt_=vt_: e.dma_start(
                        out=vscr.ap()[tk0:tk0 + 128, hsl], in_=vt_), inc=16, wait_at=mmul)
                    mVoutB[u] = p.mark()
            endB[tb] = p.mark()
            if tb >= 1:
                emit_c_back(2 * (tb - 1))
                emit_c_back(2 * (tb - 1) + 1)
            emit_c_front(2 * tb)
            emit_c_front(2 * tb + 1)
        emit_c_back(62)
        emit_c_back(63)

        p.force_drain = True
        p.op("vector", lambda e: e.tensor_copy(cacb[:], cac1[:]))
        p.op("tensor", lambda e: e.matmul(psA[0:1, 3, 128:136], onesb[:], cacb[:], start=True, stop=True))
        p.op("scalar", lambda e: e.copy(cgrow[0:1, 0:8], psA[0:1, 3, 128:136]))
        p.op("vector", lambda e: e.tensor_copy(cacb[:], cac2[:]))
        p.op("tensor", lambda e: e.matmul(psA[0:1, 3, 136:144], onesb[:], cacb[:], start=True, stop=True))
        p.op("scalar", lambda e: e.copy(cgrow[0:1, 8:16], psA[0:1, 3, 136:144]))
        p.op("vector", lambda e: e.tensor_scalar(alw[:], cgrow[:], CAPACITY, None, op0=ALU.is_le))
        p.op("tensor", lambda e: e.matmul(psA[:, 2, 0:8], onesf[:], alw[0:1, 0:8], start=True, stop=True))
        p.op("tensor", lambda e: e.matmul(psA[:, 3, 0:8], onesf[:], alw[0:1, 8:16], start=True, stop=True))
        p.op("scalar", lambda e: e.copy(alb1[:], psA[:, 2, 0:8]))
        p.op("scalar", lambda e: e.copy(alb2[:], psA[:, 3, 0:8]))
        p.force_drain = False

        def percore_body(e, c, tok):
            for qg in range(2):
                for g in range(8):
                    idx = qg * 8 + g
                    tq0 = 1024 * c + 512 * qg
                    e.dma_start(out=kdscr.ap()[idx], in_=kscr.ap()[g * 128:(g + 1) * 128, tq0:tq0 + 512]).then_inc(tok, 16)
                    e.dma_start(out=vdscr.ap()[idx], in_=vscr.ap()[tq0:tq0 + 512, g * 128:(g + 1) * 128]).then_inc(tok, 16)
            e.dma_start(out=wtown1[:], in_=wgscr.ap()[0, 1024 * c:1024 * (c + 1), :].rearrange(
                "(t pp) ee -> pp t ee", pp=128)).then_inc(tok, 16)
            e.dma_start(out=wtown2[:], in_=wgscr.ap()[1, 1024 * c:1024 * (c + 1), :].rearrange(
                "(t pp) ee -> pp t ee", pp=128)).then_inc(tok, 16)
        p.switch("gpsimd", percore_body, 544)

        for i in range(4):
            sl = slice(i * 8, (i + 1) * 8)
            dma(p, "gpsimd", xnb[:, sl, :], xor_[:, sl, :])
        for c in range(NCH):
            p.op("scalar", lambda e, c=c: e.activation(sqx, xnb[:, c, :], AF.Square))
            st, sp = (c == 0), (c == NCH - 1)
            p.chain("tensor", [
                lambda e, c=c, st=st, sp=sp: e.matmul(psA[0:1, 2, :], onesb[:], xnb[:, c, 0:512], start=st, stop=sp),
                lambda e, c=c, st=st, sp=sp: e.matmul(psA[0:1, 3, :], onesb[:], xnb[:, c, 512:1024], start=st, stop=sp),
                lambda e, st=st, sp=sp: e.matmul(psA[0:1, 4, :], onesb[:], sqx[:, 0:512], start=st, stop=sp),
                lambda e, st=st, sp=sp: e.matmul(psA[0:1, 5, :], onesb[:], sqx[:, 512:1024], start=st, stop=sp),
            ])
        p.op("scalar", lambda e: e.activation(morow[0:1, 0:512], psA[0:1, 2, :], AF.Copy, scale=f32(1.0 / D)))
        p.op("scalar", lambda e: e.activation(morow[0:1, 512:1024], psA[0:1, 3, :], AF.Copy, scale=f32(1.0 / D)))
        for half in range(2):
            hsl = slice(half * 512, (half + 1) * 512)
            p.op("tensor", lambda e, hsl=hsl: e.matmul(psA[:, 6, :], onesf[:], morow[0:1, hsl], start=True, stop=True))
            p.op("scalar", lambda e, hsl=hsl: e.copy(slotA[:, hsl], psA[:, 6, :]))
        p.op("scalar", lambda e: e.activation(morow[0:1, 0:512], psA[0:1, 4, :], AF.Copy, scale=f32(1.0 / D)))
        p.op("scalar", lambda e: e.activation(morow[0:1, 512:1024], psA[0:1, 5, :], AF.Copy, scale=f32(1.0 / D)))
        p.op("vector", lambda e: e.tensor_tensor(slotC[0:1, :], slotA[0:1, :], slotA[0:1, :], op=ALU.mult))
        p.op("vector", lambda e: e.tensor_tensor(morow[0:1, :], morow[0:1, :], slotC[0:1, :], op=ALU.subtract))
        p.op("vector", lambda e: e.tensor_scalar_add(morow[0:1, :], morow[0:1, :], LN_EPS))
        p.op("scalar", lambda e: e.activation(morow[0:1, :], morow[0:1, :], AF.Sqrt))
        p.op("vector", lambda e: e.reciprocal(slotC[0:1, :], morow[0:1, :]))
        for half in range(2):
            hsl = slice(half * 512, (half + 1) * 512)
            p.op("tensor", lambda e, hsl=hsl: e.matmul(psA[:, 6, :], onesf[:], slotC[0:1, hsl], start=True, stop=True))
            p.op("scalar", lambda e, hsl=hsl: e.copy(slotB[:, hsl], psA[:, 6, :]))
        for c in range(NCH):
            p.op("vector", lambda e, c=c: e.tensor_copy(slotC[:], xnb[:, c, :]))
            p.op("vector", lambda e: e.tensor_tensor(slotC[:], slotC[:], slotA[:], op=ALU.subtract))
            p.op("vector", lambda e, c=c: e.tensor_tensor(xnb[:, c, :], slotC[:], slotB[:], op=ALU.mult))

        if debug:
            dma(p, "sync", dbg_xn, xnb[:, 0, :])
        floorE = p.mark()
        qtileE = (qtile, usb)
        chmE, ev0E, ev1E, outE = {}, {}, {}, {}
        for oc in range(32):
            wb = wtlE[oc % 2]
            qt = qtileE[oc % 2]
            p.op("gpsimd", lambda e, wb=wb, oc=oc: e.dma_start(out=wb[:], in_=wq4_in[oc]),
                 inc=16, wait_at=chmE.get(oc - 2, floorE))
            dmam = p.mark()
            p.chain("tensor", [
                lambda e, c=c, wb=wb, st=(c == 0), sp=(c == NCH - 1): e.matmul(
                    psA[:, 0, :], wb[:, c, :], xnb[:, c, 0:512], start=st, stop=sp)
                for c in range(NCH)], wait_at=mmax(dmam, ev0E.get(oc - 1, floorE)))
            m0 = p.mark()
            p.chain("tensor", [
                lambda e, c=c, wb=wb, st=(c == 0), sp=(c == NCH - 1): e.matmul(
                    psA[:, 1, :], wb[:, c, :], xnb[:, c, 512:1024], start=st, stop=sp)
                for c in range(NCH)], wait_at=ev1E.get(oc - 1, floorE))
            chmE[oc] = p.mark()
            p.op("scalar", lambda e, qt=qt: e.copy(qt[:, 0:512], psA[:, 0, :]),
                 wait_at=mmax(m0, outE.get(oc - 2, floorE)))
            ev0E[oc] = p.mark()
            p.op("scalar", lambda e, qt=qt: e.copy(qt[:, 512:1024], psA[:, 1, :]),
                 wait_at=chmE[oc])
            ev1E[oc] = p.mark()
            p.op("sync", lambda e, qt=qt, oc=oc: e.dma_start(
                out=qscr.ap()[oc * 128:(oc + 1) * 128, :], in_=qt), inc=16,
                wait_at=ev1E[oc])
            outE[oc] = p.mark()

        floorF = p.mark()
        qtsP = (qts, qtsB)
        mKld, mVld = {}, {}
        mGkv, mDiagDone = {}, {}
        mSlast, mEvTT, mRecipF, mRcbF = {}, {}, {}, {}
        mSH, mExp, mPV = {}, {}, {}
        pendF = []

        def flushF(p=p):
            while pendF:
                pendF.pop(0)()

        for qg in range(2):
            for g in range(8):
                ga = qg * 8 + g
                for s in range(8):
                    p.op("sync", lambda e, g=g, s=s: e.dma_start(
                        out=kgb[:, s * 8:(s + 1) * 8, :].rearrange("p a b -> p (a b)"),
                        in_=kscr.ap()[g * 128:(g + 1) * 128, s * 1024:(s + 1) * 1024]),
                        inc=16, wait_at=(mGkv.get(ga - 1, floorF) if s == 0 else 0))
                    mKld[(ga, s)] = p.mark()
                    p.op("sync", lambda e, g=g, s=s: e.dma_start(
                        out=vgb[:, s * 8:(s + 1) * 8, :],
                        in_=vscr.ap()[s * 1024:(s + 1) * 1024, g * 128:(g + 1) * 128].rearrange(
                            "(a pp) h -> pp a h", pp=128)), inc=16, wait_at=0)
                    mVld[(ga, s)] = p.mark()
                p.op("sync", lambda e, ga=ga: e.dma_start(out=kdiag, in_=kdscr.ap()[ga]),
                     inc=16, wait_at=mDiagDone.get(ga - 1, floorF))
                p.op("sync", lambda e, ga=ga: e.dma_start(
                    out=vdiag, in_=vdscr.ap()[ga].rearrange("(a pp) h -> pp a h", pp=128)),
                    inc=16, wait_at=0)
                mKVd = p.mark()

                for ih in range(4):
                    h = 4 * g + ih
                    iq = ga * 4 + ih
                    qt_ = qtsP[iq % 2]
                    p.op("sync", lambda e, h=h, qg=qg, qt_=qt_: e.dma_start(
                        out=qt_[:], in_=qscr.ap()[h * 128:(h + 1) * 128, qg * 512:(qg + 1) * 512]),
                        inc=16, wait_at=mSlast.get(iq - 2, floorF))
                    mQ = p.mark()

                    def emit_S(sx, iq=iq, ga=ga, qt_=qt_, mQ=mQ, mKVd=mKVd):
                        b0 = (sx % 2) * 2
                        wW = (mExp.get((iq, sx - 2)) if sx >= 2
                              else mExp.get((iq - 1, 34 + sx), floorF))
                        if sx < 32:
                            w = mmax(wW, mKld[(ga, sx // 4)], mQ if sx == 0 else 0)
                            p.chain("tensor", [
                                lambda e, sx=sx, k=k, b0=b0, qt_=qt_: e.matmul(
                                    psA[:, b0 + k, :], kgb[:, 2 * sx + k, :], qt_[:],
                                    start=True, stop=True)
                                for k in range(2)], wait_at=w)
                        else:
                            dd = sx - 32
                            w0 = 128 * dd
                            p.op("tensor", lambda e, dd=dd, w0=w0, b0=b0, qt_=qt_: e.matmul(
                                psA[:, b0, w0:512], kdiag[:, dd * 128:(dd + 1) * 128], qt_[:, w0:512],
                                start=True, stop=True, skip_group_check=True),
                                wait_at=mmax(wW, mKVd))
                        mSH[(iq, sx)] = p.mark()

                    def emit_tail(sx, iq=iq, ga=ga, qg=qg, mKVd=mKVd):
                        b0 = (sx % 2) * 2
                        wPt = (mPV.get((iq, sx - 2)) if sx >= 2
                               else mPV.get((iq - 1, 35), floorF))
                        if sx < 32:
                            hb = sx % 2
                            quad = sx // 2
                            p.op("scalar", lambda e, qg=qg, quad=quad, hb=hb, b0=b0: e.activation(
                                ptb4[:, hb * 1024:(hb + 1) * 1024],
                                psA[:, b0:b0 + 2, :].rearrange("p a b -> p (a b)"), AF.Exp,
                                bias=biasq[:, qg * 16 + quad:qg * 16 + quad + 1], scale=f32(INV_SQRT_HD)),
                                wait_at=mmax(mSH[(iq, sx)], wPt))
                            mExp[(iq, sx)] = p.mark()
                            wPV = mmax(mExp[(iq, sx)], mVld[(ga, sx // 4)],
                                       mEvTT.get(iq - 1, floorF) if sx == 0 else 0,
                                       mRecipF.get(iq - 1, floorF) if sx == 0 else 0)
                            p.chain("tensor", [
                                lambda e, sx=sx, k=k, hb=hb, st=(sx == 0 and k == 0): e.matmul(
                                    psA[:, 4, :], vgb[:, 2 * sx + k, :],
                                    ptb4[:, hb * 1024 + k * 512:hb * 1024 + (k + 1) * 512],
                                    start=st, stop=False, skip_group_check=True)
                                for k in range(2)] + [
                                lambda e, sx=sx, k=k, hb=hb, st=(sx == 0 and k == 0): e.matmul(
                                    psA[0:1, 5, :], onesb[:],
                                    ptb4[:, hb * 1024 + k * 512:hb * 1024 + (k + 1) * 512],
                                    start=st, stop=False, skip_group_check=True)
                                for k in range(2)], wait_at=wPV)
                            mPV[(iq, sx)] = p.mark()
                        else:
                            dd = sx - 32
                            w0 = 128 * dd
                            p.op("vector", lambda e, w0=w0, b0=b0: e.tensor_tensor(
                                psA[:, b0, w0:w0 + 128], psA[:, b0, w0:w0 + 128], tri[:], op=ALU.add),
                                wait_at=mSH[(iq, sx)])
                            mTri = p.mark()
                            p.op("scalar", lambda e, w0=w0, dd=dd, b0=b0: e.activation(
                                ptb4[:, dd * 512 + w0:dd * 512 + 512], psA[:, b0, w0:512], AF.Exp,
                                bias=f32(-EXP_SHIFT), scale=f32(INV_SQRT_HD)),
                                wait_at=mmax(mTri, wPt))
                            mExp[(iq, sx)] = p.mark()
                            p.chain("tensor", [
                                lambda e, dd=dd, w0=w0: e.matmul(
                                    psA[:, 4, w0:512], vdiag[:, dd, :],
                                    ptb4[:, dd * 512 + w0:dd * 512 + 512],
                                    start=False, stop=(dd == 3), skip_group_check=True),
                                lambda e, dd=dd, w0=w0: e.matmul(
                                    psA[0:1, 5, w0:512], onesb[:],
                                    ptb4[:, dd * 512 + w0:dd * 512 + 512],
                                    start=False, stop=(dd == 3), skip_group_check=True),
                            ], wait_at=mmax(mExp[(iq, sx)], mKVd))
                            mPV[(iq, sx)] = p.mark()

                    emit_S(0)
                    flushF()
                    for s in range(1, 37):
                        if s < 36:
                            emit_S(s)
                        emit_tail(s - 1)
                    mSlast[iq] = mSH[(iq, 35)]
                    if ih == 3:
                        mGkv[ga] = mPV[(iq, 31)]
                        mDiagDone[ga] = mPV[(iq, 35)]

                    def epiF(iq=iq, h=h, qg=qg):
                        p.op("vector", lambda e: e.reciprocal(rs[:], psA[0:1, 5, :]),
                             wait_at=mPV[(iq, 35)])
                        mRecipF[iq] = p.mark()
                        p.op("tensor", lambda e: e.matmul(psA[:, 6, :], onesf[:], rs[:], start=True, stop=True),
                             wait_at=mmax(mRecipF[iq], mRcbF.get(iq - 1, floorF)))
                        mBc = p.mark()
                        p.op("scalar", lambda e: e.copy(rcb[:], psA[:, 6, :]),
                             wait_at=mmax(mBc, mEvTT.get(iq - 1, floorF)))
                        mRcbF[iq] = p.mark()
                        p.op("vector", lambda e, h=h, qg=qg: e.tensor_tensor(
                            ctxb[:, h, qg * 512:(qg + 1) * 512], psA[:, 4, :], rcb[:], op=ALU.mult),
                            wait_at=mmax(mRcbF[iq], mPV[(iq, 35)]))
                        mEvTT[iq] = p.mark()
                    pendF.append(epiF)
        flushF()

        if debug:
            dma(p, "sync", dbg_ctx, ctxb[:, 0, 0:512])
        dma(p, "gpsimd", lab, la4_in)
        for i in range(8):
            p.op("vector", lambda e, i=i: e.tensor_tensor(gts[:], wtown1[:, i, :], alb1[:], op=ALU.mult), drain=True)
            p.op("vector", lambda e, i=i: e.tensor_tensor(sel8[:], wtown2[:, i, :], alb2[:], op=ALU.mult))
            p.op("vector", lambda e: e.tensor_tensor(gts[:], gts[:], sel8[:], op=ALU.add), drain=True)
            p.op("tensor", lambda e: e.transpose(psA[0:8, 3, 0:128], gts[:], ident[:]))
            p.op("scalar", lambda e, i=i: e.copy(wfm[:, i * 128:(i + 1) * 128], psA[0:8, 3, 0:128]))
        p.op("tensor", lambda e: e.matmul(psA[:, 0, :], ssel_sb[:], wfm[:, 0:512], start=True, stop=True))
        p.op("tensor", lambda e: e.matmul(psA[:, 1, :], ssel_sb[:], wfm[:, 512:1024], start=True, stop=True))
        p.op("scalar", lambda e: e.activation(wbcb[:, 0:512], psA[:, 0, :], AF.Copy, scale=f32(1.0 / 16.0)))
        p.op("scalar", lambda e: e.activation(wbcb[:, 512:1024], psA[:, 1, :], AF.Copy, scale=f32(1.0 / 16.0)))
        for half in range(2):
            hsl = slice(half * 512, (half + 1) * 512)
            p.chain("tensor", [
                lambda e, c=c, half=half, hsl=hsl, st=(c == 0), sp=(c == NCH - 1): e.matmul(
                    psA[:, half, :], lab[:, c, :], xnb[:, c, hsl], start=st, stop=sp)
                for c in range(NCH)])
        p.op("vector", lambda e: e.tensor_tensor(usb[:, 0:512], psA[:, 0, :], wbcb[:, 0:512], op=ALU.mult))
        p.op("vector", lambda e: e.tensor_tensor(usb[:, 512:1024], psA[:, 1, :], wbcb[:, 512:1024], op=ALU.mult))

        if debug:
            dma(p, "sync", dbg_wb, wbcb[:])
            dma(p, "sync", dbg_cnt, cgrow[:])
            dma(p, "sync", dbg_al[:, 0:8], alb1[:])
            dma(p, "sync", dbg_al[:, 8:16], alb2[:])
            dma(p, "sync", dbg_wt[:, 0, :], wtown1[:].rearrange("p a b -> p (a b)"))
            dma(p, "sync", dbg_wt[:, 1, :], wtown2[:].rearrange("p a b -> p (a b)"))
            dma(p, "sync", dbg_hf, hscr.ap())
            dma(p, "sync", dbg_us, usb[:])
        p.op("vector", lambda e: e.memset(slotC[:], 0.0))
        p.op("vector", lambda e: e.memset(slotD[:], 0.0))
        floorH = p.mark()
        chWm, chFm = {}, {}
        vA0m, vA1m, sB0m, sB1m = {}, {}, {}, {}
        addm, sqm, st1m, st2m, cdm, outm, cpm = {}, {}, {}, {}, {}, {}, {}

        def stats_emit(j):
            p.chain("tensor", [
                lambda e: e.matmul(psA[0:1, 4, :], onesb[:], x1bb[:, 0:512], start=True, stop=True),
                lambda e: e.matmul(psA[0:1, 5, :], onesb[:], x1bb[:, 512:1024], start=True, stop=True),
            ], wait_at=mmax(cpm[j], cdm.get(j - 1, floorH)))
            st1m[j] = p.mark()
            p.op("vector", lambda e: e.tensor_tensor(slotC[0:1, 0:512], slotC[0:1, 0:512], psA[0:1, 4, :], op=ALU.add),
                 wait_at=st1m[j])
            p.op("vector", lambda e: e.tensor_tensor(slotC[0:1, 512:1024], slotC[0:1, 512:1024], psA[0:1, 5, :], op=ALU.add),
                 wait_at=0)
            m_c = p.mark()
            p.chain("tensor", [
                lambda e: e.matmul(psA[0:1, 4, :], onesb[:], sqx[:, 0:512], start=True, stop=True),
                lambda e: e.matmul(psA[0:1, 5, :], onesb[:], sqx[:, 512:1024], start=True, stop=True),
            ], wait_at=mmax(sqm[j], m_c))
            st2m[j] = p.mark()
            p.op("vector", lambda e: e.tensor_tensor(slotD[0:1, 0:512], slotD[0:1, 0:512], psA[0:1, 4, :], op=ALU.add),
                 wait_at=st2m[j])
            p.op("vector", lambda e: e.tensor_tensor(slotD[0:1, 512:1024], slotD[0:1, 512:1024], psA[0:1, 5, :], op=ALU.add),
                 wait_at=0)
            cdm[j] = p.mark()

        for oc in range(32):
            wb = wtlH[oc % 2]
            wb2 = wtl2H[oc % 2]
            lb_ = lbtH[oc % 2]
            p.op("gpsimd", lambda e, wb=wb, oc=oc: e.dma_start(out=wb[:], in_=wo4_in[oc]),
                 inc=16, wait_at=chWm.get(oc - 2, floorH))
            p.op("gpsimd", lambda e, wb2=wb2, oc=oc: e.dma_start(out=wb2[:], in_=fw4_in[oc]),
                 inc=16, wait_at=chFm.get(oc - 2, floorH))
            p.op("gpsimd", lambda e, lb_=lb_, oc=oc: e.dma_start(out=lb_[:], in_=lb_in[:, oc * 128:(oc + 1) * 128]),
                 inc=16, wait_at=0)
            dmam = p.mark()
            p.chain("tensor", [
                lambda e, c=c, wb=wb, st=(c == 0), sp=(c == NCH - 1): e.matmul(
                    psA[:, 0, :], wb[:, c, :], ctxb[:, c, 0:512], start=st, stop=sp)
                for c in range(NCH)], wait_at=mmax(dmam, vA0m.get(oc - 1, floorH)))
            m_s0 = p.mark()
            p.chain("tensor", [
                lambda e, c=c, wb=wb, st=(c == 0), sp=(c == NCH - 1): e.matmul(
                    psA[:, 1, :], wb[:, c, :], ctxb[:, c, 512:1024], start=st, stop=sp)
                for c in range(NCH)], wait_at=vA1m.get(oc - 1, floorH))
            m_s1 = p.mark()
            chWm[oc] = m_s1
            if oc > 0:
                stats_emit(oc - 1)
            for half in range(2):
                hsl = slice(half * 512, (half + 1) * 512)
                mm2 = [lambda e, c=c, half=half, hsl=hsl, wb2=wb2: e.matmul(
                    psA[:, 2 + half, :], wb2[:, c, :], xnb[:, c, hsl], start=(c == 0), stop=False)
                    for c in range(NCH)]
                mm2.append(lambda e, half=half, hsl=hsl, lb_=lb_: e.matmul(
                    psA[:, 2 + half, :], lb_[:], usb[:, hsl], start=False, stop=True))
                p.chain("tensor", mm2,
                        wait_at=(sB0m if half == 0 else sB1m).get(oc - 1, floorH))
                if half == 0:
                    m_f0 = p.mark()
            m_f1 = p.mark()
            chFm[oc] = m_f1
            p.op("vector", lambda e, oc=oc: e.tensor_scalar(slotA[:, 0:512], psA[:, 0, :], boc[:, oc:oc + 1], None, op0=ALU.add),
                 wait_at=mmax(m_s0, cpm.get(oc - 1, floorH), outm.get(oc - 1, floorH), st1m.get(oc - 1, floorH)))
            vA0m[oc] = p.mark()
            p.op("vector", lambda e, oc=oc: e.tensor_scalar(slotA[:, 512:1024], psA[:, 1, :], boc[:, oc:oc + 1], None, op0=ALU.add),
                 wait_at=m_s1)
            vA1m[oc] = p.mark()
            p.op("scalar", lambda e: e.activation(slotB[:, 0:512], psA[:, 2, :], AF.Silu),
                 wait_at=mmax(m_f0, addm.get(oc - 1, floorH)))
            sB0m[oc] = p.mark()
            p.op("scalar", lambda e: e.activation(slotB[:, 512:1024], psA[:, 3, :], AF.Silu),
                 wait_at=m_f1)
            sB1m[oc] = p.mark()
            if debug and oc == 1:
                dma(p, "sync", dbg_sg[:, 0, :], slotA[:])
                dma(p, "sync", dbg_sg[:, 1, :], slotB[:])
            p.op("vector", lambda e: e.tensor_tensor(slotA[:], slotA[:], slotB[:], op=ALU.mult),
                 wait_at=sB1m[oc])
            m_vM = p.mark()
            p.op("sync", lambda e, oc=oc: e.dma_start(out=slotB[:], in_=xor_[:, oc, :]),
                 inc=16, wait_at=m_vM)
            m_x = p.mark()
            p.op("vector", lambda e: e.tensor_tensor(slotA[:], slotB[:], slotA[:], op=ALU.add),
                 wait_at=m_x)
            addm[oc] = p.mark()
            p.op("sync", lambda e, oc=oc: e.dma_start(out=x1r[:, oc, :], in_=slotA[:]),
                 inc=16, wait_at=addm[oc])
            outm[oc] = p.mark()
            p.op("vector", lambda e: e.tensor_copy(x1bb, slotA[:]),
                 wait_at=st1m.get(oc - 1, floorH))
            cpm[oc] = p.mark()
            p.op("scalar", lambda e: e.activation(sqx, x1bb, AF.Square),
                 wait_at=mmax(cpm[oc], st2m.get(oc - 1, floorH)))
            sqm[oc] = p.mark()
        stats_emit(31)
        if debug:
            dma(p, "sync", dbg_st[0:1, :], slotC[0:1, :])
            dma(p, "sync", dbg_st[1:2, :], slotD[0:1, :])
        p.op("vector", lambda e: e.tensor_scalar_mul(morow[0:1, :], slotC[0:1, :], 1.0 / D))
        p.op("vector", lambda e: e.tensor_scalar_mul(slotD[0:1, :], slotD[0:1, :], 1.0 / D))
        p.op("vector", lambda e: e.tensor_tensor(slotC[0:1, :], morow[0:1, :], morow[0:1, :], op=ALU.mult))
        p.op("vector", lambda e: e.tensor_tensor(slotD[0:1, :], slotD[0:1, :], slotC[0:1, :], op=ALU.subtract))
        p.op("vector", lambda e: e.tensor_scalar_add(slotD[0:1, :], slotD[0:1, :], LN_EPS))
        p.op("scalar", lambda e: e.activation(slotD[0:1, :], slotD[0:1, :], AF.Sqrt))
        p.op("vector", lambda e: e.reciprocal(slotC[0:1, :], slotD[0:1, :]))
        if debug:
            dma(p, "sync", dbg_st[2:3, :], morow[0:1, :])
            dma(p, "sync", dbg_st[3:4, :], slotC[0:1, :])
        for half in range(2):
            hsl = slice(half * 512, (half + 1) * 512)
            p.op("tensor", lambda e, hsl=hsl: e.matmul(psA[:, 6, :], onesf[:], morow[0:1, hsl], start=True, stop=True))
            p.op("scalar", lambda e, hsl=hsl: e.copy(slotA[:, hsl], psA[:, 6, :]))
            p.op("tensor", lambda e, hsl=hsl: e.matmul(psA[:, 6, :], onesf[:], slotC[0:1, hsl], start=True, stop=True))
            p.op("scalar", lambda e, hsl=hsl: e.copy(slotB[:, hsl], psA[:, 6, :]))
        if debug:
            dma(p, "sync", dbg_st[4:5, :], slotA[0:1, :])
            dma(p, "sync", dbg_st[5:6, :], slotB[0:1, :])
        floorH2 = p.mark()
        mrdN, mcpN = {}, {}
        for c in range(NCH):
            xb_ = (x1bb, sqx)[c % 2]
            p.op("gpsimd", lambda e, xb_=xb_, c=c: e.dma_start(out=xb_, in_=x1r[:, c, :]),
                 inc=16, wait_at=mcpN.get(c - 2, floorH2))
            mrdN[c] = p.mark()
            p.op("vector", lambda e, xb_=xb_: e.tensor_copy(slotC[:], xb_),
                 wait_at=(mmax(mrdN[c], floorH2) if c == 0 else mrdN[c]))
            mcpN[c] = p.mark()
            p.op("vector", lambda e: e.tensor_tensor(slotC[:], slotC[:], slotA[:], op=ALU.subtract),
                 wait_at=0)
            p.op("vector", lambda e, c=c: e.tensor_tensor(xnb[:, c, :], slotC[:], slotB[:], op=ALU.mult),
                 wait_at=0)

        if debug:
            dma(p, "sync", dbg_xn1, xnb[:, 0, :])
        floorI = p.mark()
        chmI, ev0I, ev1I, outI = {}, {}, {}, {}
        for wi, (w4, bc, scr) in enumerate(
                ((l1q4_in, bqc, q1scr), (l1k4_in, bkc, k1scr), (l1v4_in, bvc, v1scr))):
            for oc in range(32):
                u = wi * 32 + oc
                wb = wtlE[u % 2]
                qt = qtileE[u % 2]
                p.op("gpsimd", lambda e, wb=wb, w4=w4, oc=oc: e.dma_start(out=wb[:], in_=w4[oc]),
                     inc=16, wait_at=chmI.get(u - 2, floorI))
                dmam = p.mark()
                p.chain("tensor", [
                    lambda e, c=c, wb=wb, st=(c == 0), sp=(c == NCH - 1): e.matmul(
                        psA[:, 0, :], wb[:, c, :], xnb[:, c, 0:512], start=st, stop=sp)
                    for c in range(NCH)], wait_at=mmax(dmam, ev0I.get(u - 1, floorI)))
                m0 = p.mark()
                p.chain("tensor", [
                    lambda e, c=c, wb=wb, st=(c == 0), sp=(c == NCH - 1): e.matmul(
                        psA[:, 1, :], wb[:, c, :], xnb[:, c, 512:1024], start=st, stop=sp)
                    for c in range(NCH)], wait_at=ev1I.get(u - 1, floorI))
                chmI[u] = p.mark()
                p.op("vector", lambda e, oc=oc, bc=bc, qt=qt: e.tensor_scalar(
                    qt[:, 0:512], psA[:, 0, :], bc[:, oc:oc + 1], None, op0=ALU.add),
                    wait_at=mmax(m0, outI.get(u - 2, floorI)))
                ev0I[u] = p.mark()
                p.op("vector", lambda e, oc=oc, bc=bc, qt=qt: e.tensor_scalar(
                    qt[:, 512:1024], psA[:, 1, :], bc[:, oc:oc + 1], None, op0=ALU.add),
                    wait_at=chmI[u])
                ev1I[u] = p.mark()
                p.op("sync", lambda e, qt=qt, oc=oc, scr=scr: e.dma_start(
                    out=scr.ap()[oc * 128:(oc + 1) * 128, :], in_=qt), inc=16,
                    wait_at=ev1I[u])
                outI[u] = p.mark()
        floorI2 = p.mark()
        mLastPE, mOutI2 = {}, {}
        for blk in range(8):
            bs = slice(blk * 128, (blk + 1) * 128)
            qb_, kb_, vb_ = ((qb, kb, vb), (qbB, kbB, vbB))[blk % 2]
            first = True
            for buf, scr in ((qb_, q1scr), (kb_, k1scr), (vb_, v1scr)):
                for i4 in range(4):
                    p.op("sync", lambda e, buf=buf, scr=scr, i4=i4, bs=bs: e.dma_start(
                        out=buf[:, i4 * 8:(i4 + 1) * 8, :],
                        in_=scr.ap().rearrange("(hh pp) n -> pp hh n", pp=128)[:, i4 * 8:(i4 + 1) * 8, bs]),
                        inc=16, wait_at=(mLastPE.get(blk - 2, floorI2) if first else 0))
                    first = False
            mLd = p.mark()
            mExpG, mPVG, mVcpG, mEvG, mRcb2G = {}, {}, {}, {}, {}
            pendI = []
            for gi in range(8):
                hh = gi * 16
                p.chain("tensor", [
                    lambda e, j=j, hh=hh, kb_=kb_, qb_=qb_: e.matmul(
                        psA[0:32, 1, j * 32:(j + 1) * 32],
                        kb_[:, :, hh + j], qb_[:, :, hh + j], start=True, stop=True)
                    for j in range(16)], wait_at=(mmax(mLd, mExpG.get(gi - 1, floorI2))
                                                  if gi == 0 else mExpG[gi - 1]))
                mS = p.mark()
                p.op("scalar", lambda e: e.activation(
                    pt1, psA[0:32, 1, :], AF.Exp, bias=f32(-EXP_SHIFT), scale=f32(INV_SQRT_HD)),
                    wait_at=mmax(mS, mPVG.get(gi - 1, floorI2)))
                mExpG[gi] = p.mark()
                for half8 in range(2):
                    p.chain("tensor", [
                        lambda e, j=j, hh=hh, half8=half8, vb_=vb_: e.transpose(
                            psVt[0:32, 0, (j % 8) * 128:(j % 8 + 1) * 128],
                            vb_[:, :, hh + half8 * 8 + j], identb[:])
                        for j in range(8)],
                        wait_at=(mmax(mVcpG.get(gi - 1, floorI2), mLd if gi == 0 else 0)
                                 if half8 == 0 else mVcpG[(gi, 0)]))
                    mT = p.mark()
                    p.op("scalar", lambda e, half8=half8: e.copy(
                        vts1[0:32, half8 * 8:(half8 + 1) * 8, :],
                        psVt[0:32, 0, :].rearrange("p (j dd) -> p j dd", dd=128)),
                        wait_at=(mmax(mT, mPVG.get(gi - 1, floorI2)) if half8 == 0 else mT))
                    mVcpG[(gi, half8)] = p.mark()
                mVcpG[gi] = mVcpG[(gi, 1)]
                while pendI:
                    pendI.pop(0)()
                p.op("tensor", lambda e: e.matmul(psA[0:1, 2, :], onesb[0:32, :], pt1, start=True, stop=True),
                     wait_at=mmax(mExpG[gi], mRcb2G.get(gi - 1, floorI2)))
                mSum = p.mark()
                p.op("vector", lambda e: e.reciprocal(rs[:], psA[0:1, 2, :]),
                     wait_at=mSum)
                mRc = p.mark()
                p.chain("tensor", [
                    lambda e, j=j: e.matmul(
                        psA[:, 3, j * 32:(j + 1) * 32], vts1[0:32, j, :], pt1[:, j * 32:(j + 1) * 32],
                        start=True, stop=True)
                    for j in range(16)], wait_at=mmax(mVcpG[gi], mExpG[gi], mEvG.get(gi - 1, floorI2)))
                mPVG[gi] = p.mark()

                def epiI(gi=gi, hh=hh, mRc=mRc, blk=blk):
                    p.op("tensor", lambda e: e.matmul(psA[:, 2, :], onesf[:], rs[:], start=True, stop=True),
                         wait_at=mRc)
                    mBc2 = p.mark()
                    p.op("scalar", lambda e: e.copy(rcb[:], psA[:, 2, :]),
                         wait_at=mmax(mBc2, mEvG.get(gi - 1, floorI2)))
                    mRcb2G[gi] = p.mark()
                    p.op("vector", lambda e, hh=hh: e.tensor_tensor(
                        ctb2[:, :, hh:hh + 16].rearrange("p h n -> p n h"),
                        psA[:, 3, :].rearrange("p (n h) -> p n h", h=32),
                        rcb[:].rearrange("p (n h) -> p n h", h=32),
                        op=ALU.mult),
                        wait_at=mmax(mPVG[gi], mRcb2G[gi],
                                     mOutI2.get(blk - 1, floorI2) if gi == 0 else 0))
                    mEvG[gi] = p.mark()
                pendI.append(epiI)
            while pendI:
                pendI.pop(0)()
            mLastPE[blk] = mPVG[7]
            for i4 in range(4):
                sl = slice(i4 * 8, (i4 + 1) * 8)
                p.op("sync", lambda e, sl=sl, bs=bs: e.dma_start(
                    out=ctxscr.ap().rearrange("(hh pp) n -> pp hh n", pp=128)[:, sl, bs],
                    in_=ctb2[:, sl, :]), inc=16, wait_at=(mEvG[7] if i4 == 0 else 0))
            mOutI2[blk] = p.mark()

        dma(p, "gpsimd", a1b, a1_in)
        dma(p, "gpsimd", b1tb, b1t_in)
        p.op("vector", lambda e: e.memset(cnacc[:], 0.0))
        floorJ = p.mark()
        w1nbP = (w1nb_f, w1nbB)
        effbP = (effb, effbB)
        effsP = (effs, effsB)
        mVaJ, mSqJ, mRdJ, mDmJ = {}, {}, {}, {}
        for oc in range(32):
            wn = w1nbP[oc % 2]
            p.op("gpsimd", lambda e, wn=wn, oc=oc: e.dma_start(
                out=wn, in_=w1n4_in[oc].rearrange("p c i -> p (c i)")),
                inc=16, wait_at=mVaJ.get((oc - 2) * 8 + 7, floorJ))
            mDmJ[oc] = p.mark()
            for ib in range(8):
                u = oc * 8 + ib
                bq = u % 2
                eb_ = effbP[u % 2]
                es_ = effsP[u % 2]
                p.op("tensor", lambda e, oc=oc, ib=ib, bq=bq: e.matmul(
                    psA[:, bq, :], b1tb[:, oc * 128:(oc + 1) * 128], a1b[:, ib * 512:(ib + 1) * 512],
                    start=True, stop=True), wait_at=mmax(mDmJ[oc], mVaJ.get(u - 2, floorJ)))
                mmj = p.mark()
                p.op("vector", lambda e, oc=oc, ib=ib, bq=bq, eb_=eb_, wn=wn: e.tensor_tensor(
                    eb_, psA[:, bq, :], wn[:, ib * 512:(ib + 1) * 512], op=ALU.add),
                    wait_at=mmax(mmj, mSqJ.get(u - 2, floorJ)))
                mVaJ[u] = p.mark()
                p.op("scalar", lambda e, eb_=eb_, es_=es_: e.activation(es_, eb_, AF.Square),
                     wait_at=mmax(mVaJ[u], mRdJ.get(u - 2, floorJ)))
                mSqJ[u] = p.mark()
                p.op("vector", lambda e, es_=es_: e.reduce_sum(rs1[:], es_, axis=AXX),
                     wait_at=mSqJ[u])
                mRdJ[u] = p.mark()
                p.op("vector", lambda e, oc=oc: e.tensor_tensor(cnacc[:, oc:oc + 1], cnacc[:, oc:oc + 1], rs1[:], op=ALU.add),
                     wait_at=0, drain=True)
        p.op("scalar", lambda e: e.activation(cnacc[:], cnacc[:], AF.Sqrt))
        p.op("vector", lambda e: e.reciprocal(rsc[:], cnacc[:]))
        p.op("vector", lambda e: e.tensor_tensor(rsc[:], rsc[:], magc[:], op=ALU.mult), drain=True)

        dma(p, "gpsimd", a1tb, a1t4_in)
        for half in range(2):
            hsl = slice(half * 512, (half + 1) * 512)
            p.chain("tensor", [
                lambda e, c=c, half=half, hsl=hsl, st=(c == 0), sp=(c == NCH - 1): e.matmul(
                    psA[0:32, 2 + half, :], a1tb[:, c, :], xnb[:, c, hsl], start=st, stop=sp)
                for c in range(NCH)])
        p.op("scalar", lambda e: e.copy(u1b[:, 0:512], psA[0:32, 2, :]))
        p.op("scalar", lambda e: e.copy(u1b[:, 512:1024], psA[0:32, 3, :]))
        floorK = p.mark()
        ctxrdP = (ctxrd, gw1b.rearrange("p a b -> p (a b)"))
        chmK, vA0K, vA1K, cpK, addK, out2K = {}, {}, {}, {}, {}, {}
        for oc in range(32):
            wb2 = wtl2H[oc % 2]
            cx_ = ctxrdP[oc % 2]
            p.op("gpsimd", lambda e, wb2=wb2, oc=oc: e.dma_start(out=wb2[:], in_=w1t4_in[oc]),
                 inc=16, wait_at=chmK.get(oc - 2, floorK))
            p.op("sync", lambda e, oc=oc, cx_=cx_: e.dma_start(
                out=cx_, in_=ctxscr.ap()[oc * 128:(oc + 1) * 128, :]), inc=16,
                wait_at=cpK.get(oc - 2, floorK))
            dmam = p.mark()
            for half in range(2):
                hsl = slice(half * 512, (half + 1) * 512)
                mm = [lambda e, c=c, half=half, hsl=hsl, wb2=wb2: e.matmul(
                    psA[:, half, :], wb2[:, c, :], xnb[:, c, hsl], start=(c == 0), stop=False)
                    for c in range(NCH)]
                mm.append(lambda e, half=half, hsl=hsl, oc=oc: e.matmul(
                    psA[:, half, :], b1tb[:, oc * 128:(oc + 1) * 128], u1b[:, hsl], start=False, stop=True))
                p.chain("tensor", mm, wait_at=(
                    mmax(dmam, vA0K.get(oc - 1, floorK)) if half == 0
                    else vA1K.get(oc - 1, floorK)))
                if half == 0:
                    m_k0 = p.mark()
            m_k1 = p.mark()
            chmK[oc] = m_k1
            p.op("vector", lambda e, oc=oc: e.tensor_scalar(slotA[:, 0:512], psA[:, 0, :], rsc[:, oc:oc + 1], None, op0=ALU.mult),
                 wait_at=mmax(m_k0, out2K.get(oc - 1, floorK)), drain=(oc == 0))
            vA0K[oc] = p.mark()
            p.op("vector", lambda e, oc=oc: e.tensor_scalar(slotA[:, 512:1024], psA[:, 1, :], rsc[:, oc:oc + 1], None, op0=ALU.mult),
                 wait_at=m_k1)
            vA1K[oc] = p.mark()
            p.op("scalar", lambda e: e.activation(slotB[:], slotA[:], AF.Silu),
                 wait_at=mmax(vA1K[oc], addK.get(oc - 1, floorK)))
            m_sil = p.mark()
            p.op("vector", lambda e, cx_=cx_: e.tensor_copy(slotA[:], cx_),
                 wait_at=m_sil)
            cpK[oc] = p.mark()
            p.op("vector", lambda e: e.tensor_tensor(slotA[:], slotA[:], slotB[:], op=ALU.mult),
                 wait_at=0)
            m_mu = p.mark()
            p.op("sync", lambda e, oc=oc: e.dma_start(out=slotB[:], in_=x1r[:, oc, :]),
                 inc=16, wait_at=m_mu)
            m_x1 = p.mark()
            p.op("vector", lambda e: e.tensor_tensor(slotA[:], slotB[:], slotA[:], op=ALU.add),
                 wait_at=m_x1)
            addK[oc] = p.mark()
            p.op("sync", lambda e, oc=oc: e.dma_start(out=x2r[:, oc, :], in_=slotA[:]),
                 inc=16, wait_at=addK[oc])
            out2K[oc] = p.mark()
        if debug:
            dma(p, "sync", dbg_mr, mrscr.ap()[:, 0:512])
            dma(p, "sync", dbg_k, kscr.ap()[0:128, 0:512])
            dma(p, "sync", dbg_v, vscr.ap()[0:256, 0:128])
            dma(p, "sync", dbg_h, hscr.ap()[:, 0:512])
            dma(p, "sync", dbg_q, qscr.ap()[0:128, :])
            dma(p, "sync", dbg_x1, x1scr.ap()[0:128, :])
            dma(p, "sync", dbg_c1, ctxscr.ap()[0:128, :])
            dma(p, "sync", dbg_q1, q1scr.ap())
            dma(p, "sync", dbg_x1b[:, 0, :], x1r[:, 10, :])
            dma(p, "sync", dbg_x1b[:, 1, :], x1r[:, 20, :])
            dma(p, "sync", dbg_x1b[:, 2, :], x1r[:, 31, :])
            dma(p, "sync", dbg_k1, k1scr.ap())
            dma(p, "sync", dbg_v1, v1scr.ap())
            dma(p, "sync", dbg_c1f, ctxscr.ap())
        p.finish(block)
    return nc


def kernel_fused(inputs, debug=False):
    c32 = np.ascontiguousarray
    xT = c32(np.asarray(inputs["L0_attn_x"], np.float32).reshape(T, D).T)
    Wq = np.asarray(inputs["L0_attn_Wq"], np.float32)
    Wkv = np.asarray(inputs["L0_attn_Wkv"], np.float32)
    WK = c32(Wkv.reshape(D, 8, 2, 128)[:, :, 0, :].reshape(D, 1024))
    WV = c32(Wkv.reshape(D, 8, 2, 128)[:, :, 1, :].reshape(D, 1024))

    def pcf(w):
        return c32(w.reshape(NCH, 128, w.shape[1]).transpose(1, 0, 2))
    wkk4 = pcf(WK)
    wvv4 = pcf(WV)
    gw1t4 = pcf(c32(np.asarray(inputs["L0_ffn_gw1"], np.float32).T))
    wq4 = _tile_w(Wq)
    wo4 = _tile_w(np.asarray(inputs["L0_attn_Wo"], np.float32))
    fw4 = _tile_w(c32(np.asarray(inputs["L0_ffn_weight"], np.float32).T))
    la4 = c32(np.asarray(inputs["L0_ffn_lora_A"], np.float32).transpose(1, 0, 2).reshape(D, 128)
              .reshape(NCH, 128, 128).transpose(1, 0, 2))
    lb = c32(np.asarray(inputs["L0_ffn_lora_B"], np.float32).reshape(128, D))
    ssel = np.zeros((8, 128), np.float32)
    for e_ in range(8):
        ssel[e_, e_ * 16:(e_ + 1) * 16] = 1.0
    l1q4 = _tile_w(c32(np.asarray(inputs["L1_attn_wq"], np.float32).T))
    l1k4 = _tile_w(c32(np.asarray(inputs["L1_attn_wk"], np.float32).T))
    l1v4 = _tile_w(c32(np.asarray(inputs["L1_attn_wv"], np.float32).T))
    W1 = np.asarray(inputs["L1_ffn_W"], np.float32)
    w1t4 = _tile_w(c32(W1.T))
    w1n4 = c32(W1.reshape(32, 128, NCH, 128))
    A1 = np.asarray(inputs["L1_ffn_lora_A"], np.float32)
    B1 = np.asarray(inputs["L1_ffn_lora_B"], np.float32)
    a1t4 = c32(A1.T.reshape(NCH, 128, 32).transpose(1, 0, 2))
    b1t = c32(B1.T)
    tri = np.where(np.arange(128)[:, None] > np.arange(128)[None, :],
                   np.float32(NEG), np.float32(0.0)).astype(np.float32)
    common = {
        "xfull": xT,
        "wkk4": wkk4, "wvv4": wvv4, "gw1t4": gw1t4,
        "gb1c": c32(np.asarray(inputs["L0_ffn_gb1"], np.float32).reshape(32, 1)),
        "gam": c32(np.asarray(inputs["L0_ffn_ln_gamma"], np.float32).reshape(1, 32)),
        "bet": c32(np.asarray(inputs["L0_ffn_ln_beta"], np.float32).reshape(1, 32)),
        "gw2t": c32(np.asarray(inputs["L0_ffn_gw2"], np.float32).T),
        "gb2": c32(np.asarray(inputs["L0_ffn_gb2"], np.float32).reshape(1, 8)),
        "wq4": wq4, "wo4": wo4, "fw4": fw4,
        "boc": _to_col(np.asarray(inputs["L0_attn_bo"], np.float32)),
        "la4": la4, "lb": lb, "ssel": ssel,
        "l1q4": l1q4, "l1k4": l1k4, "l1v4": l1v4,
        "bqc": _to_col(np.asarray(inputs["L1_attn_bq"], np.float32)),
        "bkc": _to_col(np.asarray(inputs["L1_attn_bk"], np.float32)),
        "bvc": _to_col(np.asarray(inputs["L1_attn_bv"], np.float32)),
        "w1t4": w1t4, "w1n4": w1n4, "a1t4": a1t4, "a1": A1, "b1t": b1t,
        "magc": _to_col(np.asarray(inputs["L1_ffn_magnitude"], np.float32)),
        "identf": _ident(), "identb": np.eye(128).astype(ml_dtypes.bfloat16),
        "tri": tri,
    }
    in_maps = []
    for c in range(NCORE):
        biasq = np.empty((128, 32), np.float32)
        for qg in range(2):
            band = 2 * c + qg
            for quad in range(16):
                biasq[:, qg * 16 + quad] = -EXP_SHIFT if quad < band else -1e9
        m = dict(common)
        m["xown"] = c32(xT[:, c * TL:(c + 1) * TL])
        m["biasq"] = biasq
        in_maps.append(m)
    nc = build_fused(debug=debug)
    res = run_bass_kernel_spmd(nc, in_maps, core_ids=list(range(NCORE)))
    if debug:
        return res
    x2 = np.concatenate([res.results[c]["x2"] for c in range(NCORE)], axis=1)
    return np.ascontiguousarray(x2.T).reshape(1, T, D).astype(np.float32)


# version 09
# speedup vs baseline: 1.6697x; 1.0121x over previous
import contextlib
import numpy as np
import ml_dtypes
import concourse.bass as bass
import concourse.mybir as mybir
from concourse.bass_utils import run_bass_kernel_spmd

F32 = mybir.dt.float32
BF16 = mybir.dt.bfloat16
I32 = mybir.dt.int32
AF = mybir.ActivationFunctionType
ALU = mybir.AluOpType
AX = mybir.AxisListType
f32 = np.float32

NCORE = 8
D = 4096
T = 8192
TL = T // NCORE
NCH = D // 128
EXP_SHIFT = 16.0
INV_SQRT_HD = 1.0 / float(np.sqrt(128.0))
LN_EPS = 1e-5
CAPACITY = 3072.0
NEG = -1e30

ENGINES = ("sync", "gpsimd", "scalar", "vector", "tensor")
ENG_TYPE = {"sync": mybir.EngineType.SP, "gpsimd": mybir.EngineType.Pool,
            "scalar": mybir.EngineType.Activation, "vector": mybir.EngineType.DVE,
            "tensor": mybir.EngineType.PE}


class Prog:

    def __init__(self, nc, es):
        self.nc = nc
        self.sems = {e: es.enter_context(nc.semaphore("sem_" + e)) for e in ENGINES}
        self.cnt = {e: 0 for e in ENGINES}
        self.seen = {e: {x: 0 for x in ENGINES} for e in ENGINES}
        self.items = {e: [] for e in ENGINES}
        self.force_drain = False

    def mark(self):
        return dict(self.cnt)

    def _waits(self, eng, wait_at):
        if wait_at is None:
            wait_at = self.cnt
        elif wait_at == 0:
            wait_at = {}
        seen = self.seen[eng]
        ws = []
        for x, v in wait_at.items():
            if x == eng or v <= 0 or v <= seen[x]:
                continue
            seen[x] = v
            ws.append((self.sems[x], v))
        return ws

    def op(self, eng, fn, inc=1, wait_at=None, drain=False):
        if eng == "vector" and self.force_drain:
            drain = True
        ws = self._waits(eng, wait_at)
        self.cnt[eng] += inc

        def emit(e, fn=fn, ws=ws, inc=inc, sem=self.sems[eng], drain=drain):
            for s, v in ws:
                e.wait_ge(s, v)
            if drain:
                e.drain()
            fn(e).then_inc(sem, inc)
        self.items[eng].append(emit)

    def switch(self, eng, body_fn, inc, wait_at=None):
        ws = self._waits(eng, wait_at)
        self.cnt[eng] += inc
        nc = self.nc

        def emit(e, body_fn=body_fn, ws=ws, sem=self.sems[eng], nc=nc, eng=eng):
            for s, v in ws:
                e.wait_ge(s, v)
            pid = nc.partition_id(engines=[ENG_TYPE[eng]])
            for c in nc.Switch(engines=[e], index=[pid], n=8):
                body_fn(e, c, sem)
        self.items[eng].append(emit)

    def chain(self, eng, fns, inc=1, wait_at=None):
        ws = self._waits(eng, wait_at)
        self.cnt[eng] += inc

        def emit(e, fns=fns, ws=ws, inc=inc, sem=self.sems[eng]):
            for s, v in ws:
                e.wait_ge(s, v)
            ins = None
            for fn in fns:
                ins = fn(e)
            ins.then_inc(sem, inc)
        self.items[eng].append(emit)

    def finish(self, block):
        finals = dict(self.cnt)

        for name in ENGINES:
            items = self.items[name]

            def body(e, items=items, name=name, finals=finals, sems=self.sems):
                for emit in items:
                    emit(e)
                for x in ENGINES:
                    if x != name and finals[x] > 0:
                        e.wait_ge(sems[x], finals[x])
            getattr(block, name)(body)


def mmax(*marks):
    out = {}
    for m in marks:
        if not m:
            continue
        for k, v in m.items():
            if v > out.get(k, 0):
                out[k] = v
    return out


def dma(p, eng, out, in_):
    p.op(eng, lambda e, out=out, in_=in_: e.dma_start(out=out, in_=in_), inc=16)


def _ident():
    return np.eye(128, dtype=np.float32)


def _to_col(v):
    return np.ascontiguousarray(v.reshape(-1, 128).T)


def _tile_w(w_in_out):
    din, dout = w_in_out.shape
    a = w_in_out.reshape(din // 128, 128, dout // 128, 128)
    return np.ascontiguousarray(a.transpose(2, 1, 0, 3))



def build_A():
    nc = bass.Bass("TRN2", target_bir_lowering=False)
    x_in = nc.dram_tensor("x", [D, TL], F32, kind="ExternalInput").ap()
    gw1t_in = nc.dram_tensor("gw1t", [D, 32], F32, kind="ExternalInput").ap()
    gb1c_in = nc.dram_tensor("gb1c", [32, 1], F32, kind="ExternalInput").ap()
    gam_in = nc.dram_tensor("gam", [1, 32], F32, kind="ExternalInput").ap()
    bet_in = nc.dram_tensor("bet", [1, 32], F32, kind="ExternalInput").ap()
    gw2t_in = nc.dram_tensor("gw2t", [32, 8], F32, kind="ExternalInput").ap()
    gb2_in = nc.dram_tensor("gb2", [1, 8], F32, kind="ExternalInput").ap()
    id_in = nc.dram_tensor("ident", [128, 128], F32, kind="ExternalInput").ap()
    xn_out = nc.dram_tensor("xn", [D, TL], BF16, kind="ExternalOutput").ap()
    cnt_out = nc.dram_tensor("cnt", [1, 16], F32, kind="ExternalOutput").ap()
    wg_out = nc.dram_tensor("wg", [2, TL, 8], F32, kind="ExternalOutput").ap()

    es = contextlib.ExitStack()
    with es:
        E = es.enter_context
        xb = E(nc.sbuf_tensor("xb", [128, NCH, TL], BF16))
        xnb = E(nc.sbuf_tensor("xnb", [128, NCH, TL], BF16))
        sq = E(nc.sbuf_tensor("sq", [128, TL], BF16))
        xf = E(nc.sbuf_tensor("xf", [128, TL], F32))
        gw1b = E(nc.sbuf_tensor("gw1b", [128, NCH, 32], BF16))
        gb1c = E(nc.sbuf_tensor("gb1c_s", [32, 1], F32))
        gamr = E(nc.sbuf_tensor("gamr", [1, 32], F32))
        betr = E(nc.sbuf_tensor("betr", [1, 32], F32))
        gw2t = E(nc.sbuf_tensor("gw2ts", [32, 8], F32))
        gb2r = E(nc.sbuf_tensor("gb2r", [1, 8], F32))
        ident = E(nc.sbuf_tensor("identsb", [128, 128], F32))
        onesf = E(nc.sbuf_tensor("onesf", [1, 128], F32))
        onesb = E(nc.sbuf_tensor("onesb", [128, 1], BF16))
        mrow = E(nc.sbuf_tensor("mrow", [1, TL], F32))
        qrow = E(nc.sbuf_tensor("qrow", [1, TL], F32))
        vrow = E(nc.sbuf_tensor("vrow", [1, TL], F32))
        rrow = E(nc.sbuf_tensor("rrow", [1, TL], F32))
        mbc = E(nc.sbuf_tensor("mbc", [128, TL], F32))
        rbc = E(nc.sbuf_tensor("rbc", [128, TL], F32))
        hfm = E(nc.sbuf_tensor("hfm", [32, TL], F32))
        gamb = E(nc.sbuf_tensor("gamb", [128, 32], F32))
        betb = E(nc.sbuf_tensor("betb", [128, 32], F32))
        gb2b = E(nc.sbuf_tensor("gb2b", [128, 8], F32))
        htm = E(nc.sbuf_tensor("htm", [128, 32], F32))
        hm = E(nc.sbuf_tensor("hm", [128, 1], F32))
        hv = E(nc.sbuf_tensor("hv", [128, 1], F32))
        hrs = E(nc.sbuf_tensor("hrs", [128, 1], F32))
        hsq = E(nc.sbuf_tensor("hsq", [128, 32], F32))
        hft = E(nc.sbuf_tensor("hft", [32, 128], F32))
        gts = E(nc.sbuf_tensor("gts", [128, 8], F32))
        iotf = E(nc.sbuf_tensor("iotf", [128, 8], F32))
        ioti = E(nc.sbuf_tensor("ioti", [128, 8], I32))
        v1 = E(nc.sbuf_tensor("v1", [128, 1], F32))
        nv1 = E(nc.sbuf_tensor("nv1", [128, 1], F32))
        v2 = E(nc.sbuf_tensor("v2", [128, 1], F32))
        e2 = E(nc.sbuf_tensor("e2", [128, 1], F32))
        g1 = E(nc.sbuf_tensor("g1", [128, 1], F32))
        g2c = E(nc.sbuf_tensor("g2c", [128, 1], F32))
        sel8 = E(nc.sbuf_tensor("sel8", [128, 8], F32))
        m1c = E(nc.sbuf_tensor("m1c", [128, 1], F32))
        oh1 = E(nc.sbuf_tensor("oh1", [128, 8], F32))
        oh2 = E(nc.sbuf_tensor("oh2", [128, 8], F32))
        g2t = E(nc.sbuf_tensor("g2t", [128, 8], F32))
        wgall1 = E(nc.sbuf_tensor("wgall1", [128, 8, 8], F32))
        wgall2 = E(nc.sbuf_tensor("wgall2", [128, 8, 8], F32))
        cac1 = E(nc.sbuf_tensor("cac1", [128, 8], F32))
        cac2 = E(nc.sbuf_tensor("cac2", [128, 8], F32))
        cacb = E(nc.sbuf_tensor("cacb", [128, 8], BF16))
        cnt = E(nc.sbuf_tensor("cntsb", [1, 16], F32))
        psA = E(nc.psum_tensor("psA", [128, 4, 512], F32))
        psB = E(nc.psum_tensor("psB", [128, 4, 512], F32))
        block = E(nc.Block())
        p = Prog(nc, es)

        xr = x_in.rearrange("(c p) t -> p c t", p=128)
        gw1r = gw1t_in.rearrange("(c p) f -> p c f", p=128)
        xnr = xn_out.rearrange("(c p) t -> p c t", p=128)

        for i in range(4):
            dma(p, "gpsimd", xb[:, i * 8:(i + 1) * 8, :], xr[:, i * 8:(i + 1) * 8, :])
        dma(p, "gpsimd", gw1b[:], gw1r)
        p.op("gpsimd", lambda e: e.iota(ioti[:], pattern=[[1, 8]], base=0, channel_multiplier=0))
        dma(p, "sync", gb1c[:], gb1c_in)
        dma(p, "sync", gamr[:], gam_in)
        dma(p, "sync", betr[:], bet_in)
        dma(p, "sync", gw2t[:], gw2t_in)
        dma(p, "sync", gb2r[:], gb2_in)
        dma(p, "sync", ident[:], id_in)
        p.op("vector", lambda e: e.memset(onesf[:], 1.0))
        p.op("vector", lambda e: e.memset(onesb[:], 1.0))
        p.op("vector", lambda e: e.tensor_copy(iotf[:], ioti[:]))
        p.op("vector", lambda e: e.memset(cac1[:], 0.0))
        p.op("vector", lambda e: e.memset(cac2[:], 0.0))

        for c in range(NCH):
            p.op("scalar", lambda e, c=c: e.activation(sq[:], xb[:, c, :], AF.Square))
            st, sp = (c == 0), (c == NCH - 1)
            p.chain("tensor", [
                lambda e, c=c, st=st, sp=sp: e.matmul(psA[0:1, 0, :], onesb[:], xb[:, c, 0:512], start=st, stop=sp),
                lambda e, c=c, st=st, sp=sp: e.matmul(psA[0:1, 1, :], onesb[:], xb[:, c, 512:1024], start=st, stop=sp),
                lambda e, st=st, sp=sp: e.matmul(psA[0:1, 2, :], onesb[:], sq[:, 0:512], start=st, stop=sp),
                lambda e, st=st, sp=sp: e.matmul(psA[0:1, 3, :], onesb[:], sq[:, 512:1024], start=st, stop=sp),
            ])
        p.op("scalar", lambda e: e.activation(mrow[0:1, 0:512], psA[0:1, 0, :], AF.Copy, scale=f32(1.0 / D)))
        p.op("scalar", lambda e: e.activation(mrow[0:1, 512:1024], psA[0:1, 1, :], AF.Copy, scale=f32(1.0 / D)))
        p.op("scalar", lambda e: e.activation(qrow[0:1, 0:512], psA[0:1, 2, :], AF.Copy, scale=f32(1.0 / D)))
        p.op("scalar", lambda e: e.activation(qrow[0:1, 512:1024], psA[0:1, 3, :], AF.Copy, scale=f32(1.0 / D)))
        p.op("vector", lambda e: e.tensor_tensor(vrow[0:1, :], mrow[0:1, :], mrow[0:1, :], op=ALU.mult))
        p.op("vector", lambda e: e.tensor_tensor(vrow[0:1, :], qrow[0:1, :], vrow[0:1, :], op=ALU.subtract))
        p.op("vector", lambda e: e.tensor_scalar_add(vrow[0:1, :], vrow[0:1, :], LN_EPS))
        p.op("scalar", lambda e: e.activation(vrow[0:1, :], vrow[0:1, :], AF.Sqrt))
        p.op("vector", lambda e: e.reciprocal(rrow[0:1, :], vrow[0:1, :]))
        p.chain("tensor", [
            lambda e: e.matmul(psB[:, 0, :], onesf[:], mrow[0:1, 0:512], start=True, stop=True),
            lambda e: e.matmul(psB[:, 1, :], onesf[:], mrow[0:1, 512:1024], start=True, stop=True),
            lambda e: e.matmul(psB[:, 2, :], onesf[:], rrow[0:1, 0:512], start=True, stop=True),
            lambda e: e.matmul(psB[:, 3, :], onesf[:], rrow[0:1, 512:1024], start=True, stop=True),
        ])
        p.op("scalar", lambda e: e.copy(mbc[:, 0:512], psB[:, 0, :]))
        p.op("scalar", lambda e: e.copy(mbc[:, 512:1024], psB[:, 1, :]))
        p.op("scalar", lambda e: e.copy(rbc[:, 0:512], psB[:, 2, :]))
        p.op("scalar", lambda e: e.copy(rbc[:, 512:1024], psB[:, 3, :]))
        for c in range(NCH):
            p.op("vector", lambda e, c=c: e.tensor_copy(xf[:], xb[:, c, :]))
            p.op("vector", lambda e: e.tensor_tensor(xf[:], xf[:], mbc[:], op=ALU.subtract))
            p.op("vector", lambda e, c=c: e.tensor_tensor(xnb[:, c, :], xf[:], rbc[:], op=ALU.mult))
        for i in range(4):
            dma(p, "sync", xnr[:, i * 8:(i + 1) * 8, :], xnb[:, i * 8:(i + 1) * 8, :])

        for c in range(NCH):
            st, sp = (c == 0), (c == NCH - 1)
            p.chain("tensor", [
                lambda e, c=c, st=st, sp=sp: e.matmul(psA[0:32, 0, :], gw1b[:, c, :], xnb[:, c, 0:512], start=st, stop=sp),
                lambda e, c=c, st=st, sp=sp: e.matmul(psA[0:32, 1, :], gw1b[:, c, :], xnb[:, c, 512:1024], start=st, stop=sp),
            ])
        p.op("vector", lambda e: e.tensor_scalar(hfm[:, 0:512], psA[0:32, 0, :], gb1c[:], None, op0=ALU.add))
        p.op("vector", lambda e: e.tensor_scalar(hfm[:, 512:1024], psA[0:32, 1, :], gb1c[:], None, op0=ALU.add))
        p.chain("tensor", [
            lambda e: e.matmul(psB[:, 0, 0:32], onesf[:], gamr[:], start=True, stop=True),
            lambda e: e.matmul(psB[:, 1, 0:32], onesf[:], betr[:], start=True, stop=True),
            lambda e: e.matmul(psB[:, 2, 0:8], onesf[:], gb2r[:], start=True, stop=True),
        ])
        p.op("scalar", lambda e: e.copy(gamb[:], psB[:, 0, 0:32]))
        p.op("scalar", lambda e: e.copy(betb[:], psB[:, 1, 0:32]))
        p.op("scalar", lambda e: e.copy(gb2b[:], psB[:, 2, 0:8]))

        for t in range(8):
            p.op("tensor", lambda e, t=t: e.transpose(psB[:, 3, 0:32], hfm[:, t * 128:(t + 1) * 128], ident[0:32, 0:32]))
            p.op("scalar", lambda e: e.copy(htm[:], psB[:, 3, 0:32]))
            p.op("vector", lambda e: e.reduce_sum(hm[:], htm[:], axis=AX.X))
            p.op("vector", lambda e: e.tensor_scalar_mul(hm[:], hm[:], 1.0 / 32.0), drain=True)
            p.op("vector", lambda e: e.tensor_scalar(htm[:], htm[:], hm[:], None, op0=ALU.subtract), drain=True)
            if debug and t == 0:
                dma(p, "sync", dbg_ln[:, 0:32], htm[:])
                p.op("sync", lambda e: e.dma_start(out=dbg_ln[:, 32:33], in_=hm[:], allow_slow_non_contiguous=True), inc=16)
            p.op("vector", lambda e: e.tensor_tensor(hsq[:], htm[:], htm[:], op=ALU.mult))
            p.op("vector", lambda e: e.reduce_sum(hv[:], hsq[:], axis=AX.X))
            p.op("vector", lambda e: e.tensor_scalar_mul(hv[:], hv[:], 1.0 / 32.0), drain=True)
            p.op("vector", lambda e: e.tensor_scalar_add(hv[:], hv[:], LN_EPS))
            p.op("scalar", lambda e: e.activation(hv[:], hv[:], AF.Sqrt))
            p.op("vector", lambda e: e.reciprocal(hrs[:], hv[:]))
            p.op("vector", lambda e: e.tensor_scalar(htm[:], htm[:], hrs[:], None, op0=ALU.mult), drain=True)
            if debug and t == 0:
                p.op("sync", lambda e: e.dma_start(out=dbg_ln[:, 33:34], in_=hv[:], allow_slow_non_contiguous=True), inc=16)
                p.op("sync", lambda e: e.dma_start(out=dbg_ln[:, 34:35], in_=hrs[:], allow_slow_non_contiguous=True), inc=16)
            p.op("vector", lambda e: e.tensor_tensor(htm[:], htm[:], gamb[:], op=ALU.mult))
            p.op("vector", lambda e: e.tensor_tensor(htm[:], htm[:], betb[:], op=ALU.add))
            p.op("vector", lambda e: e.tensor_relu(htm[:], htm[:]))
            p.op("tensor", lambda e: e.transpose(psB[0:32, 3, 0:128], htm[:], ident[:]))
            p.op("scalar", lambda e: e.copy(hft[:], psB[0:32, 3, 0:128]))
            p.op("tensor", lambda e: e.matmul(psB[:, 2, 64:72], hft[:], gw2t[:], start=True, stop=True))
            p.op("vector", lambda e: e.tensor_tensor(gts[:], psB[:, 2, 64:72], gb2b[:], op=ALU.add))
            p.op("vector", lambda e: e.reduce_max(v1[:], gts[:], axis=AX.X))
            p.op("vector", lambda e: e.tensor_scalar(oh1[:], gts[:], v1[:], None, op0=ALU.is_equal), drain=True)
            p.op("vector", lambda e: e.scalar_tensor_tensor(sel8[:], oh1[:], -256.0, iotf[:], op0=ALU.mult, op1=ALU.add))
            p.op("vector", lambda e: e.tensor_reduce(m1c[:], sel8[:], axis=AX.X, op=ALU.min))
            p.op("vector", lambda e: e.tensor_scalar(oh1[:], sel8[:], m1c[:], None, op0=ALU.is_equal), drain=True)
            p.op("vector", lambda e: e.scalar_tensor_tensor(g2t[:], oh1[:], NEG, gts[:], op0=ALU.mult, op1=ALU.add))
            p.op("vector", lambda e: e.reduce_max(v2[:], g2t[:], axis=AX.X))
            p.op("vector", lambda e: e.tensor_scalar(oh2[:], g2t[:], v2[:], None, op0=ALU.is_equal))
            p.op("vector", lambda e: e.scalar_tensor_tensor(sel8[:], oh2[:], -256.0, iotf[:], op0=ALU.mult, op1=ALU.add))
            p.op("vector", lambda e: e.tensor_reduce(m1c[:], sel8[:], axis=AX.X, op=ALU.min))
            p.op("vector", lambda e: e.tensor_scalar(oh2[:], sel8[:], m1c[:], None, op0=ALU.is_equal), drain=True)
            p.op("vector", lambda e: e.tensor_scalar_mul(nv1[:], v1[:], -1.0))
            p.op("scalar", lambda e: e.activation(e2[:], v2[:], AF.Exp, bias=nv1[:]))
            p.op("vector", lambda e: e.tensor_scalar_add(g1[:], e2[:], 1.0))
            p.op("vector", lambda e: e.reciprocal(g1[:], g1[:]))
            p.op("vector", lambda e: e.tensor_tensor(g2c[:], e2[:], g1[:], op=ALU.mult), drain=True)
            p.op("vector", lambda e: e.tensor_scalar(sel8[:], oh1[:], g1[:], None, op0=ALU.mult), drain=True)
            dma(p, "sync", wgscr.ap()[0, t * 128:(t + 1) * 128, :], sel8[:])
            p.op("vector", lambda e: e.tensor_scalar(sel8[:], oh2[:], g2c[:], None, op0=ALU.mult), drain=True)
            dma(p, "sync", wgscr.ap()[1, t * 128:(t + 1) * 128, :], sel8[:])
            p.op("vector", lambda e: e.tensor_tensor(cac1[:], cac1[:], oh1[:], op=ALU.add))
            p.op("vector", lambda e: e.tensor_tensor(cac2[:], cac2[:], oh2[:], op=ALU.add))
            if debug and t == 0:
                dma(p, "sync", dbg_hb, htb[:])
                dma(p, "sync", dbg_pc[:, 0:32], htm[:])
                dma(p, "sync", dbg_pc[:, 32:40], gts[:])
                dma(p, "sync", dbg_pc[:, 40:48], oh1[:])
                dma(p, "sync", dbg_pc[:, 48:56], oh2[:])
                dma(p, "sync", dbg_pc[:, 56:64], iotf[:])

        p.op("vector", lambda e: e.tensor_copy(cacb[:], cac1[:]))
        p.op("tensor", lambda e: e.matmul(psB[0:1, 0, 128:136], onesb[:], cacb[:], start=True, stop=True))
        p.op("scalar", lambda e: e.copy(cnt[0:1, 0:8], psB[0:1, 0, 128:136]))
        p.op("vector", lambda e: e.tensor_copy(cacb[:], cac2[:]))
        p.op("tensor", lambda e: e.matmul(psB[0:1, 0, 136:144], onesb[:], cacb[:], start=True, stop=True))
        p.op("scalar", lambda e: e.copy(cnt[0:1, 8:16], psB[0:1, 0, 136:144]))
        dma(p, "sync", cnt_out, cnt[:])
        dma(p, "sync", wg_out[0].rearrange("(t p) e -> p t e", p=128), wgall1[:])
        dma(p, "sync", wg_out[1].rearrange("(t p) e -> p t e", p=128), wgall2[:])
        p.finish(block)
    return nc



def build_attn(l1: bool):
    nc = bass.Bass("TRN2", target_bir_lowering=False)
    xn4_in = nc.dram_tensor("xn4", [32, 128, NCH, 256], BF16, kind="ExternalInput").ap()
    wq_in = nc.dram_tensor("wq4", [128, NCH, 512], F32, kind="ExternalInput").ap()
    if l1:
        wk_in = nc.dram_tensor("wk4", [128, NCH, 512], F32, kind="ExternalInput").ap()
        wv_in = nc.dram_tensor("wv4", [128, NCH, 512], F32, kind="ExternalInput").ap()
        bq_in = nc.dram_tensor("bqc", [128, 4], F32, kind="ExternalInput").ap()
        bk_in = nc.dram_tensor("bkc", [128, 4], F32, kind="ExternalInput").ap()
        bv_in = nc.dram_tensor("bvr", [1, 512], F32, kind="ExternalInput").ap()
    else:
        wkv_in = nc.dram_tensor("wkv4", [128, NCH, 256], F32, kind="ExternalInput").ap()
    tri_in = nc.dram_tensor("tri", [128, 128], F32, kind="ExternalInput").ap()
    ctx_out = nc.dram_tensor("ctx", [512, T], BF16, kind="ExternalOutput").ap()

    q_scr = nc.dram_tensor("q_scr", [512, T], BF16)

    es = contextlib.ExitStack()
    with es:
        E = es.enter_context
        xt = E(nc.sbuf_tensor("xt", [128, NCH, 256], BF16))
        wsh = E(nc.sbuf_tensor("wsh", [128, NCH, 512], BF16))
        if l1:
            bqc = E(nc.sbuf_tensor("bqc_s", [128, 4], F32))
            bkc = E(nc.sbuf_tensor("bkc_s", [128, 4], F32))
            bvr = E(nc.sbuf_tensor("bvr_s", [1, 512], F32))
            bvb = E(nc.sbuf_tensor("bvb", [128, 512], F32))
            kres = E(nc.sbuf_tensor("kres", [128, 4, T], BF16))
            vres = E(nc.sbuf_tensor("vres", [128, 64, 512], BF16))
        else:
            wkvb = E(nc.sbuf_tensor("wkvb", [128, NCH, 256], BF16))
            kres = E(nc.sbuf_tensor("kres", [128, 1, T], BF16))
            vres = E(nc.sbuf_tensor("vres", [128, 64, 128], BF16))
        tri = E(nc.sbuf_tensor("tri_s", [128, 128], F32))
        qt = E(nc.sbuf_tensor("qt", [128, 512], BF16))
        qsb = E(nc.sbuf_tensor("qsb", [128, 256], BF16))
        ptb = E(nc.sbuf_tensor("ptb", [128, 512], BF16))
        rs = E(nc.sbuf_tensor("rs", [1, 512], F32))
        rcb = E(nc.sbuf_tensor("rcb", [128, 512], F32))
        cts = E(nc.sbuf_tensor("cts", [128, 512], BF16))
        onesf = E(nc.sbuf_tensor("onesf", [1, 128], F32))
        onesb = E(nc.sbuf_tensor("onesb", [128, 1], BF16))
        psA = E(nc.psum_tensor("psA", [128, 4, 512], F32))
        psB = E(nc.psum_tensor("psB", [128, 4, 512], F32))
        block = E(nc.Block())
        p = Prog(nc, es)

        p.op("vector", lambda e: e.memset(onesf[:], 1.0))
        p.op("vector", lambda e: e.memset(onesb[:], 1.0))
        dma(p, "sync", tri[:], tri_in)
        if l1:
            dma(p, "sync", bqc[:], bq_in)
            dma(p, "sync", bkc[:], bk_in)
            dma(p, "sync", bvr[:], bv_in)
            p.op("tensor", lambda e: e.matmul(psB[:, 0, :], onesf[:], bvr[:], start=True, stop=True))
            p.op("scalar", lambda e: e.copy(bvb[:], psB[:, 0, :]))

        def load_w(src_ap):
            for i in range(4):
                sl = slice(i * 8, (i + 1) * 8)
                dma(p, "gpsimd", wsh[:, sl, :], src_ap[:, sl, :])

        if not l1:
            load_w(wq_in)
            for i in range(4):
                sl = slice(i * 8, (i + 1) * 8)
                dma(p, "gpsimd", wkvb[:, sl, :], wkv_in[:, sl, :])
            for tb in range(32):
                dma(p, "sync", xt[:], xn4_in[tb])
                for oc in range(4):
                    p.chain("tensor", [
                        lambda e, c=c, oc=oc, st=(c == 0), sp=(c == NCH - 1): e.matmul(
                            psA[:, 0, 0:256], wsh[:, c, oc * 128:(oc + 1) * 128], xt[:, c, :], start=st, stop=sp)
                        for c in range(NCH)])
                    p.op("scalar", lambda e: e.copy(qsb[:], psA[:, 0, 0:256]))
                    dma(p, "sync", q_scr.ap()[oc * 128:(oc + 1) * 128, tb * 256:(tb + 1) * 256], qsb[:])
                p.chain("tensor", [
                    lambda e, c=c, st=(c == 0), sp=(c == NCH - 1): e.matmul(
                        psA[:, 1, 0:256], wkvb[:, c, 0:128], xt[:, c, :], start=st, stop=sp)
                    for c in range(NCH)])
                p.op("scalar", lambda e, tb=tb: e.copy(kres[:, 0, tb * 256:(tb + 1) * 256], psA[:, 1, 0:256]))
                for vt in range(2):
                    ci = tb * 2 + vt
                    p.chain("tensor", [
                        lambda e, c=c, vt=vt, st=(c == 0), sp=(c == NCH - 1): e.matmul(
                            psA[:, 2, 0:128], xt[:, c, vt * 128:(vt + 1) * 128], wkvb[:, c, 128:256], start=st, stop=sp)
                        for c in range(NCH)])
                    p.op("scalar", lambda e, ci=ci: e.copy(vres[:, ci, :], psA[:, 2, 0:128]))
        else:
            load_w(wk_in)
            for tb in range(32):
                dma(p, "sync", xt[:], xn4_in[tb])
                for kc in range(4):
                    p.chain("tensor", [
                        lambda e, c=c, kc=kc, st=(c == 0), sp=(c == NCH - 1): e.matmul(
                            psA[:, 0, 0:256], wsh[:, c, kc * 128:(kc + 1) * 128], xt[:, c, :], start=st, stop=sp)
                        for c in range(NCH)])
                    p.op("vector", lambda e, kc=kc, tb=tb: e.tensor_scalar(
                        kres[:, kc, tb * 256:(tb + 1) * 256], psA[:, 0, 0:256], bkc[:, kc:kc + 1], None, op0=ALU.add))
            load_w(wv_in)
            for tb in range(32):
                dma(p, "sync", xt[:], xn4_in[tb])
                for vt in range(2):
                    ci = tb * 2 + vt
                    p.chain("tensor", [
                        lambda e, c=c, vt=vt, st=(c == 0), sp=(c == NCH - 1): e.matmul(
                            psA[:, 2, :], xt[:, c, vt * 128:(vt + 1) * 128], wsh[:, c, :], start=st, stop=sp)
                        for c in range(NCH)])
                    p.op("vector", lambda e, ci=ci: e.tensor_tensor(vres[:, ci, :], psA[:, 2, :], bvb[:], op=ALU.add))
            load_w(wq_in)
            for tb in range(32):
                dma(p, "sync", xt[:], xn4_in[tb])
                for oc in range(4):
                    p.chain("tensor", [
                        lambda e, c=c, oc=oc, st=(c == 0), sp=(c == NCH - 1): e.matmul(
                            psA[:, 0, 0:256], wsh[:, c, oc * 128:(oc + 1) * 128], xt[:, c, :], start=st, stop=sp)
                        for c in range(NCH)])
                    p.op("vector", lambda e, oc=oc: e.tensor_scalar(
                        qsb[:], psA[:, 0, 0:256], bqc[:, oc:oc + 1], None, op0=ALU.add))
                    dma(p, "sync", q_scr.ap()[oc * 128:(oc + 1) * 128, tb * 256:(tb + 1) * 256], qsb[:])

        for lh in range(4):
            for qg in range(16):
                dma(p, "sync", qt[:], q_scr.ap()[lh * 128:(lh + 1) * 128, qg * 512:(qg + 1) * 512])
                nch_k = 64 if l1 else 4 * (qg + 1)
                for c in range(nch_k):
                    d = c - 4 * qg
                    w0 = 128 * d if (not l1 and d >= 0) else 0
                    diag = (not l1) and d >= 0
                    klh = lh if l1 else 0
                    p.op("tensor", lambda e, c=c, klh=klh, w0=w0: e.matmul(
                        psA[:, 0, w0:512], kres[:, klh, c * 128:(c + 1) * 128], qt[:, w0:512],
                        start=True, stop=True))
                    if diag:
                        p.op("vector", lambda e, w0=w0: e.tensor_tensor(
                            psA[:, 0, w0:w0 + 128], psA[:, 0, w0:w0 + 128], tri[:], op=ALU.add))
                    p.op("scalar", lambda e, w0=w0: e.activation(
                        ptb[:, w0:512], psA[:, 0, w0:512], AF.Exp,
                        bias=f32(-EXP_SHIFT), scale=f32(INV_SQRT_HD)))
                    st = (c == 0)
                    sp = (c == nch_k - 1)
                    voff = lh * 128 if l1 else 0
                    p.chain("tensor", [
                        lambda e, c=c, w0=w0, st=st, sp=sp, voff=voff: e.matmul(
                            psB[:, 0, w0:512], vres[:, c, voff:voff + 128], ptb[:, w0:512], start=st, stop=sp),
                        lambda e, c=c, w0=w0, st=st, sp=sp: e.matmul(
                            psB[0:1, 2, w0:512], onesb[:], ptb[:, w0:512], start=st, stop=sp),
                    ])
                p.op("vector", lambda e: e.reciprocal(rs[:], psB[0:1, 2, :]))
                p.op("tensor", lambda e: e.matmul(psB[:, 1, :], onesf[:], rs[:], start=True, stop=True))
                p.op("scalar", lambda e: e.copy(rcb[:], psB[:, 1, :]))
                p.op("vector", lambda e: e.tensor_tensor(cts[:], psB[:, 0, :], rcb[:], op=ALU.mult))
                dma(p, "sync", ctx_out[lh * 128:(lh + 1) * 128, qg * 512:(qg + 1) * 512], cts[:])
        p.finish(block)
    return nc



def build_D():
    nc = bass.Bass("TRN2", target_bir_lowering=False)
    xn1_in = nc.dram_tensor("xn1", [D, TL], BF16, kind="ExternalInput").ap()
    wq4_in = nc.dram_tensor("wq4", [32, 128, NCH, 128], F32, kind="ExternalInput").ap()
    wk4_in = nc.dram_tensor("wk4", [32, 128, NCH, 128], F32, kind="ExternalInput").ap()
    wv4_in = nc.dram_tensor("wv4", [32, 128, NCH, 128], F32, kind="ExternalInput").ap()
    bq_in = nc.dram_tensor("bqc", [128, 32], F32, kind="ExternalInput").ap()
    bk_in = nc.dram_tensor("bkc", [128, 32], F32, kind="ExternalInput").ap()
    bv_in = nc.dram_tensor("bvc", [128, 32], F32, kind="ExternalInput").ap()
    idb_in = nc.dram_tensor("identb", [128, 128], BF16, kind="ExternalInput").ap()
    ctx_out = nc.dram_tensor("ctx", [D, TL], BF16, kind="ExternalOutput").ap()

    es = contextlib.ExitStack()
    with es:
        E = es.enter_context
        xnb = E(nc.sbuf_tensor("xnb", [128, NCH, TL], BF16))
        qstr = E(nc.sbuf_tensor("qstr", [128, 32, 512], BF16))
        kstr = E(nc.sbuf_tensor("kstr", [128, 32, 512], BF16))
        vstr = E(nc.sbuf_tensor("vstr", [128, 32, 512], BF16))
        ctb2 = E(nc.sbuf_tensor("ctb2", [128, 32, 256], BF16))
        wtl = E(nc.sbuf_tensor("wtl", [128, NCH, 128], BF16))
        bqc = E(nc.sbuf_tensor("bqc_s", [128, 32], F32))
        bkc = E(nc.sbuf_tensor("bkc_s", [128, 32], F32))
        bvc = E(nc.sbuf_tensor("bvc_s", [128, 32], F32))
        identb = E(nc.sbuf_tensor("identb_s", [128, 128], BF16))
        pt = E(nc.sbuf_tensor("pt", [32, 512], BF16))
        rs = E(nc.sbuf_tensor("rs", [1, 512], F32))
        rb = E(nc.sbuf_tensor("rb", [128, 512], F32))
        vts = E(nc.sbuf_tensor("vts", [32, 16, 128], BF16))
        onesf = E(nc.sbuf_tensor("onesf", [1, 128], F32))
        ones32 = E(nc.sbuf_tensor("ones32", [32, 1], BF16))
        psA = E(nc.psum_tensor("psA", [128, 4, 512], F32))
        psVt = E(nc.psum_tensor("psVt", [32, 2, 1024], BF16))
        block = E(nc.Block())
        p = Prog(nc, es)

        xn1r = xn1_in.rearrange("(c p) t -> p c t", p=128)
        ctxr = ctx_out.rearrange("(h p) n -> p h n", p=128)

        p.op("vector", lambda e: e.memset(onesf[:], 1.0))
        p.op("vector", lambda e: e.memset(ones32[:], 1.0))
        dma(p, "sync", bqc[:], bq_in)
        dma(p, "sync", bkc[:], bk_in)
        dma(p, "sync", bvc[:], bv_in)
        dma(p, "sync", identb[:], idb_in)
        for i in range(4):
            sl = slice(i * 8, (i + 1) * 8)
            dma(p, "sync", xnb[:, sl, :], xn1r[:, sl, :])

        for half in range(2):
            hs = slice(half * 512, (half + 1) * 512)
            for name, w4, bc, dst in (("q", wq4_in, bqc, qstr), ("k", wk4_in, bkc, kstr), ("v", wv4_in, bvc, vstr)):
                for oc in range(32):
                    dma(p, "gpsimd", wtl[:], w4[oc])
                    p.chain("tensor", [
                        lambda e, c=c, hs=hs, st=(c == 0), sp=(c == NCH - 1): e.matmul(
                            psA[:, 0, :], wtl[:, c, :], xnb[:, c, hs], start=st, stop=sp)
                        for c in range(NCH)])
                    p.op("vector", lambda e, oc=oc, bc=bc, dst=dst: e.tensor_scalar(
                        dst[:, oc, :], psA[:, 0, :], bc[:, oc:oc + 1], None, op0=ALU.add))
            for sub in range(2):
                for g in range(16):
                    hh = sub * 256 + g * 16
                    ng = g * 16
                    p.chain("tensor", [
                        lambda e, j=j, hh=hh: e.matmul(
                            psA[0:32, 1, j * 32:(j + 1) * 32], kstr[:, :, hh + j],
                            qstr[:, :, hh + j], start=True, stop=True)
                        for j in range(16)])
                    p.op("scalar", lambda e: e.activation(
                        pt[:], psA[0:32, 1, :], AF.Exp, bias=f32(-EXP_SHIFT), scale=f32(INV_SQRT_HD)))
                    p.op("tensor", lambda e: e.matmul(psA[0:1, 2, :], ones32[:], pt[:], start=True, stop=True))
                    p.op("vector", lambda e: e.reciprocal(rs[:], psA[0:1, 2, :]))
                    p.op("tensor", lambda e: e.matmul(psA[:, 2, :], onesf[:], rs[:], start=True, stop=True))
                    p.op("scalar", lambda e: e.copy(rb[:], psA[:, 2, :]))
                    p.chain("tensor", [
                        lambda e, j=j, hh=hh: e.transpose(
                            psVt[0:32, j // 8, (j % 8) * 128:(j % 8 + 1) * 128],
                            vstr[:, :, hh + j], identb[:])
                        for j in range(16)])
                    p.op("scalar", lambda e: e.copy(vts[:, 0:8, :], psVt[0:32, 0, :].rearrange("p (j d) -> p j d", d=128)))
                    p.op("scalar", lambda e: e.copy(vts[:, 8:16, :], psVt[0:32, 1, :].rearrange("p (j d) -> p j d", d=128)))
                    p.chain("tensor", [
                        lambda e, j=j: e.matmul(
                            psA[:, 3, j * 32:(j + 1) * 32], vts[:, j, :], pt[:, j * 32:(j + 1) * 32],
                            start=True, stop=True)
                        for j in range(16)])
                    p.op("vector", lambda e, ng=ng: e.tensor_tensor(
                        ctb2[:, :, ng:ng + 16].rearrange("p h n -> p n h"),
                        psA[:, 3, :].rearrange("p (n h) -> p n h", h=32), rb[:].rearrange("p (n h) -> p n h", h=32),
                        op=ALU.mult))
                for i in range(4):
                    sl = slice(i * 8, (i + 1) * 8)
                    dma(p, "sync", ctxr[:, sl, half * 512 + sub * 256:half * 512 + (sub + 1) * 256], ctb2[:, sl, :])
        p.finish(block)
    return nc



def build_C():
    nc = bass.Bass("TRN2", target_bir_lowering=False)
    x_in = nc.dram_tensor("x", [D, TL], F32, kind="ExternalInput").ap()
    xn_in = nc.dram_tensor("xn", [D, TL], BF16, kind="ExternalInput").ap()
    ctx_in = nc.dram_tensor("ctxt", [D, TL], BF16, kind="ExternalInput").ap()
    wo4_in = nc.dram_tensor("wo4", [32, 128, NCH, 128], F32, kind="ExternalInput").ap()
    fw4_in = nc.dram_tensor("fw4", [32, 128, NCH, 128], F32, kind="ExternalInput").ap()
    boc_in = nc.dram_tensor("boc", [128, 32], F32, kind="ExternalInput").ap()
    la4_in = nc.dram_tensor("la4", [128, NCH, 128], F32, kind="ExternalInput").ap()
    lb_in = nc.dram_tensor("lb", [128, D], F32, kind="ExternalInput").ap()
    cnts_in = nc.dram_tensor("cnts", [16, 8], F32, kind="ExternalInput").ap()
    sel2_in = nc.dram_tensor("sel2", [16, 2], F32, kind="ExternalInput").ap()
    wg_in = nc.dram_tensor("wg", [2, TL, 8], F32, kind="ExternalInput").ap()
    ssel_in = nc.dram_tensor("ssel", [8, 128], F32, kind="ExternalInput").ap()
    id_in = nc.dram_tensor("ident", [128, 128], F32, kind="ExternalInput").ap()
    x1_out = nc.dram_tensor("x1", [D, TL], F32, kind="ExternalOutput").ap()
    xn1_out = nc.dram_tensor("xn1", [D, TL], BF16, kind="ExternalOutput").ap()

    es = contextlib.ExitStack()
    with es:
        E = es.enter_context
        xnb = E(nc.sbuf_tensor("xnb", [128, NCH, TL], BF16))
        ctb = E(nc.sbuf_tensor("ctb", [128, NCH, TL], BF16))
        lab = E(nc.sbuf_tensor("lab", [128, NCH, 128], BF16))
        lbb = E(nc.sbuf_tensor("lbb", [128, D], BF16))
        wob = E(nc.sbuf_tensor("wob", [128, NCH, 128], BF16))
        fwb = E(nc.sbuf_tensor("fwb", [128, NCH, 128], BF16))
        boc = E(nc.sbuf_tensor("boc_s", [128, 32], F32))
        cnt16 = E(nc.sbuf_tensor("cnt16", [16, 8], F32))
        sel2 = E(nc.sbuf_tensor("sel2_s", [16, 2], F32))
        ssel = E(nc.sbuf_tensor("ssel_s", [8, 128], F32))
        ident = E(nc.sbuf_tensor("identsb", [128, 128], F32))
        cgrow = E(nc.sbuf_tensor("cgrow", [1, 16], F32))
        alw = E(nc.sbuf_tensor("alw", [1, 16], F32))
        alb1 = E(nc.sbuf_tensor("alb1", [128, 8], F32))
        alb2 = E(nc.sbuf_tensor("alb2", [128, 8], F32))
        wg1t = E(nc.sbuf_tensor("wg1t", [128, 8], F32))
        wg2t = E(nc.sbuf_tensor("wg2t", [128, 8], F32))
        wt = E(nc.sbuf_tensor("wt", [128, 8], F32))
        wfm = E(nc.sbuf_tensor("wfm", [8, TL], F32))
        wbc = E(nc.sbuf_tensor("wbc", [128, TL], F32))
        us = E(nc.sbuf_tensor("us", [128, TL], BF16))
        ssb = E(nc.sbuf_tensor("ssb", [128, TL], F32))
        gsb = E(nc.sbuf_tensor("gsb", [128, TL], F32))
        x1bb = E(nc.sbuf_tensor("x1bb", [128, TL], BF16))
        sqx = E(nc.sbuf_tensor("sqx", [128, TL], BF16))
        onesf = E(nc.sbuf_tensor("onesf", [1, 128], F32))
        onesb = E(nc.sbuf_tensor("onesb", [128, 1], BF16))
        mrow = E(nc.sbuf_tensor("mrow", [1, TL], F32))
        qrow = E(nc.sbuf_tensor("qrow", [1, TL], F32))
        vrow = E(nc.sbuf_tensor("vrow", [1, TL], F32))
        rrow = E(nc.sbuf_tensor("rrow", [1, TL], F32))
        psA = E(nc.psum_tensor("psA", [128, 4, 512], F32))
        psB = E(nc.psum_tensor("psB", [128, 4, 512], F32))
        block = E(nc.Block())
        p = Prog(nc, es)
        mbc = wbc
        rbc = gsb

        xr = x_in.rearrange("(c p) t -> p c t", p=128)
        xnr = xn_in.rearrange("(c p) t -> p c t", p=128)
        ctr = ctx_in.rearrange("(c p) t -> p c t", p=128)
        lar = la4_in
        x1r = x1_out.rearrange("(c p) t -> p c t", p=128)
        xn1r = xn1_out.rearrange("(c p) t -> p c t", p=128)

        p.op("vector", lambda e: e.memset(onesf[:], 1.0))
        p.op("vector", lambda e: e.memset(onesb[:], 1.0))
        for i in range(4):
            sl = slice(i * 8, (i + 1) * 8)
            dma(p, "sync", xnb[:, sl, :], xnr[:, sl, :])
            dma(p, "sync", ctb[:, sl, :], ctr[:, sl, :])
        dma(p, "gpsimd", lab[:], lar)
        dma(p, "gpsimd", lbb[:], lb_in)
        dma(p, "sync", boc[:], boc_in)
        dma(p, "sync", cnt16[:], cnts_in)
        dma(p, "sync", sel2[:], sel2_in)
        dma(p, "sync", ssel[:], ssel_in)
        dma(p, "sync", ident[:], id_in)

        p.op("tensor", lambda e: e.matmul(psA[0:1, 0, 0:8], sel2[:, 0:1], cnt16[:], start=True, stop=True))
        p.op("tensor", lambda e: e.matmul(psA[0:1, 1, 0:8], sel2[:, 1:2], cnt16[:], start=True, stop=True))
        p.op("scalar", lambda e: e.copy(cgrow[0:1, 0:8], psA[0:1, 0, 0:8]))
        p.op("scalar", lambda e: e.copy(cgrow[0:1, 8:16], psA[0:1, 1, 0:8]))
        p.op("vector", lambda e: e.tensor_scalar(alw[:], cgrow[:], CAPACITY, None, op0=ALU.is_le))
        p.op("tensor", lambda e: e.matmul(psA[:, 2, 0:8], onesf[:], alw[0:1, 0:8], start=True, stop=True))
        p.op("tensor", lambda e: e.matmul(psA[:, 3, 0:8], onesf[:], alw[0:1, 8:16], start=True, stop=True))
        p.op("scalar", lambda e: e.copy(alb1[:], psA[:, 2, 0:8]))
        p.op("scalar", lambda e: e.copy(alb2[:], psA[:, 3, 0:8]))

        for t in range(8):
            dma(p, "sync", wg1t[:], wg_in[0, t * 128:(t + 1) * 128, :])
            dma(p, "sync", wg2t[:], wg_in[1, t * 128:(t + 1) * 128, :])
            p.op("vector", lambda e: e.tensor_tensor(wt[:], wg1t[:], alb1[:], op=ALU.mult))
            p.op("vector", lambda e: e.tensor_tensor(wg2t[:], wg2t[:], alb2[:], op=ALU.mult))
            p.op("vector", lambda e: e.tensor_tensor(wt[:], wt[:], wg2t[:], op=ALU.add))
            p.op("tensor", lambda e: e.transpose(psB[0:8, 0, 0:128], wt[:], ident[:]))
            p.op("scalar", lambda e, t=t: e.copy(wfm[:, t * 128:(t + 1) * 128], psB[0:8, 0, 0:128]))
        p.op("tensor", lambda e: e.matmul(psB[:, 1, :], ssel[:], wfm[:, 0:512], start=True, stop=True))
        p.op("tensor", lambda e: e.matmul(psB[:, 2, :], ssel[:], wfm[:, 512:1024], start=True, stop=True))
        p.op("scalar", lambda e: e.activation(wbc[:, 0:512], psB[:, 1, :], AF.Copy, scale=f32(1.0 / 16.0)))
        p.op("scalar", lambda e: e.activation(wbc[:, 512:1024], psB[:, 2, :], AF.Copy, scale=f32(1.0 / 16.0)))

        for half in range(2):
            mms = [lambda e, c=c, half=half, st=(c == 0), sp=(c == NCH - 1):
                   e.matmul(psA[:, half, :], lab[:, c, :], xnb[:, c, half * 512:(half + 1) * 512], start=st, stop=sp)
                   for c in range(NCH)]
            p.chain("tensor", mms)
        p.op("vector", lambda e: e.tensor_tensor(us[:, 0:512], psA[:, 0, :], wbc[:, 0:512], op=ALU.mult))
        p.op("vector", lambda e: e.tensor_tensor(us[:, 512:1024], psA[:, 1, :], wbc[:, 512:1024], op=ALU.mult))

        for oc in range(32):
            dma(p, "gpsimd", wob[:], wo4_in[oc])
            dma(p, "gpsimd", fwb[:], fw4_in[oc])
            for half in range(2):
                hs = slice(half * 512, (half + 1) * 512)
                mms = [lambda e, c=c, half=half, hs=hs, st=(c == 0), sp=(c == NCH - 1):
                       e.matmul(psA[:, half, :], wob[:, c, :], ctb[:, c, hs], start=st, stop=sp)
                       for c in range(NCH)]
                p.chain("tensor", mms)
                mms2 = [lambda e, c=c, half=half, hs=hs:
                        e.matmul(psA[:, 2 + half, :], fwb[:, c, :], xnb[:, c, hs], start=(c == 0), stop=False)
                        for c in range(NCH)]
                mms2.append(lambda e, half=half, hs=hs, oc=oc: e.matmul(
                    psA[:, 2 + half, :], lbb[:, oc * 128:(oc + 1) * 128], us[:, hs], start=False, stop=True))
                p.chain("tensor", mms2)
            p.op("vector", lambda e, oc=oc: e.tensor_scalar(ssb[:, 0:512], psA[:, 0, :], boc[:, oc:oc + 1], None, op0=ALU.add))
            p.op("vector", lambda e, oc=oc: e.tensor_scalar(ssb[:, 512:1024], psA[:, 1, :], boc[:, oc:oc + 1], None, op0=ALU.add))
            p.op("scalar", lambda e: e.activation(gsb[:, 0:512], psA[:, 2, :], AF.Silu))
            p.op("scalar", lambda e: e.activation(gsb[:, 512:1024], psA[:, 3, :], AF.Silu))
            p.op("vector", lambda e: e.tensor_tensor(ssb[:], ssb[:], gsb[:], op=ALU.mult))
            dma(p, "sync", gsb[:], xr[:, oc, :])
            p.op("vector", lambda e: e.tensor_tensor(ssb[:], gsb[:], ssb[:], op=ALU.add))
            dma(p, "sync", x1r[:, oc, :], ssb[:])
            p.op("vector", lambda e: e.tensor_copy(x1bb[:], ssb[:]))
            p.op("scalar", lambda e: e.activation(sqx[:], x1bb[:], AF.Square))
            st, sp = (oc == 0), (oc == 31)
            p.chain("tensor", [
                lambda e, st=st, sp=sp: e.matmul(psB[0:1, 0, :], onesb[:], x1bb[:, 0:512], start=st, stop=sp),
                lambda e, st=st, sp=sp: e.matmul(psB[0:1, 1, :], onesb[:], x1bb[:, 512:1024], start=st, stop=sp),
                lambda e, st=st, sp=sp: e.matmul(psB[0:1, 2, :], onesb[:], sqx[:, 0:512], start=st, stop=sp),
                lambda e, st=st, sp=sp: e.matmul(psB[0:1, 3, :], onesb[:], sqx[:, 512:1024], start=st, stop=sp),
            ])

        p.op("scalar", lambda e: e.activation(mrow[0:1, 0:512], psB[0:1, 0, :], AF.Copy, scale=f32(1.0 / D)))
        p.op("scalar", lambda e: e.activation(mrow[0:1, 512:1024], psB[0:1, 1, :], AF.Copy, scale=f32(1.0 / D)))
        p.op("scalar", lambda e: e.activation(qrow[0:1, 0:512], psB[0:1, 2, :], AF.Copy, scale=f32(1.0 / D)))
        p.op("scalar", lambda e: e.activation(qrow[0:1, 512:1024], psB[0:1, 3, :], AF.Copy, scale=f32(1.0 / D)))
        p.op("vector", lambda e: e.tensor_tensor(vrow[0:1, :], mrow[0:1, :], mrow[0:1, :], op=ALU.mult))
        p.op("vector", lambda e: e.tensor_tensor(vrow[0:1, :], qrow[0:1, :], vrow[0:1, :], op=ALU.subtract))
        p.op("vector", lambda e: e.tensor_scalar_add(vrow[0:1, :], vrow[0:1, :], LN_EPS))
        p.op("scalar", lambda e: e.activation(vrow[0:1, :], vrow[0:1, :], AF.Sqrt))
        p.op("vector", lambda e: e.reciprocal(rrow[0:1, :], vrow[0:1, :]))
        p.chain("tensor", [
            lambda e: e.matmul(psB[:, 0, :], onesf[:], mrow[0:1, 0:512], start=True, stop=True),
            lambda e: e.matmul(psB[:, 1, :], onesf[:], mrow[0:1, 512:1024], start=True, stop=True),
            lambda e: e.matmul(psB[:, 2, :], onesf[:], rrow[0:1, 0:512], start=True, stop=True),
            lambda e: e.matmul(psB[:, 3, :], onesf[:], rrow[0:1, 512:1024], start=True, stop=True),
        ])
        p.op("scalar", lambda e: e.copy(mbc[:, 0:512], psB[:, 0, :]))
        p.op("scalar", lambda e: e.copy(mbc[:, 512:1024], psB[:, 1, :]))
        p.op("scalar", lambda e: e.copy(rbc[:, 0:512], psB[:, 2, :]))
        p.op("scalar", lambda e: e.copy(rbc[:, 512:1024], psB[:, 3, :]))
        for oc in range(32):
            dma(p, "gpsimd", x1bb[:], x1r[:, oc, :])
            p.op("vector", lambda e: e.tensor_copy(ssb[:], x1bb[:]))
            p.op("vector", lambda e: e.tensor_tensor(ssb[:], ssb[:], mbc[:], op=ALU.subtract))
            p.op("vector", lambda e: e.tensor_tensor(x1bb[:], ssb[:], rbc[:], op=ALU.mult))
            dma(p, "sync", xn1r[:, oc, :], x1bb[:])
        p.finish(block)
    return nc



def build_E():
    nc = bass.Bass("TRN2", target_bir_lowering=False)
    x1_in = nc.dram_tensor("x1", [D, TL], F32, kind="ExternalInput").ap()
    xn1_in = nc.dram_tensor("xn1", [D, TL], BF16, kind="ExternalInput").ap()
    ctx_in = nc.dram_tensor("ctxt", [D, TL], BF16, kind="ExternalInput").ap()
    w1t4_in = nc.dram_tensor("w1t4", [32, 128, NCH, 128], F32, kind="ExternalInput").ap()
    w1n4_in = nc.dram_tensor("w1n4", [32, 128, NCH, 128], F32, kind="ExternalInput").ap()
    a1t4_in = nc.dram_tensor("a1t4", [128, NCH, 32], F32, kind="ExternalInput").ap()
    a1_in = nc.dram_tensor("a1", [32, D], F32, kind="ExternalInput").ap()
    b1t_in = nc.dram_tensor("b1t", [32, D], F32, kind="ExternalInput").ap()
    mag_in = nc.dram_tensor("magc", [128, 32], F32, kind="ExternalInput").ap()
    x2_out = nc.dram_tensor("x2", [D, TL], F32, kind="ExternalOutput").ap()
    es = contextlib.ExitStack()
    with es:
        E = es.enter_context
        xn1b = E(nc.sbuf_tensor("xn1b", [128, NCH, TL], BF16))
        ct1 = E(nc.sbuf_tensor("ct1", [128, NCH, TL], BF16))
        a1tb = E(nc.sbuf_tensor("a1tb", [128, NCH, 32], BF16))
        a1b = E(nc.sbuf_tensor("a1b", [32, D], BF16))
        b1tb = E(nc.sbuf_tensor("b1tb", [32, D], BF16))
        w1nf = E(nc.sbuf_tensor("w1nf", [128, D], F32))
        fw1b = E(nc.sbuf_tensor("fw1b", [128, NCH, 128], BF16))
        magc = E(nc.sbuf_tensor("magc_s", [128, 32], F32))
        u1f = E(nc.sbuf_tensor("u1f", [32, TL], F32))
        u1b = E(nc.sbuf_tensor("u1b", [32, TL], BF16))
        effb = E(nc.sbuf_tensor("effb", [128, 512], BF16))
        effs = E(nc.sbuf_tensor("effs", [128, 512], BF16))
        rs1 = E(nc.sbuf_tensor("rs1", [128, 1], F32))
        cnacc = E(nc.sbuf_tensor("cnacc", [128, 32], F32))
        rsc = E(nc.sbuf_tensor("rsc", [128, 32], F32))
        resf = E(nc.sbuf_tensor("resf", [128, TL], F32))
        gsb = E(nc.sbuf_tensor("gsb", [128, TL], F32))
        cf = E(nc.sbuf_tensor("cf", [128, TL], F32))
        psA = E(nc.psum_tensor("psA", [128, 4, 512], F32))
        block = E(nc.Block())
        p = Prog(nc, es)

        xn1r = xn1_in.rearrange("(c p) t -> p c t", p=128)
        ctr = ctx_in.rearrange("(c p) t -> p c t", p=128)
        x1r = x1_in.rearrange("(c p) t -> p c t", p=128)
        x2r = x2_out.rearrange("(c p) t -> p c t", p=128)

        for i in range(4):
            sl = slice(i * 8, (i + 1) * 8)
            dma(p, "sync", xn1b[:, sl, :], xn1r[:, sl, :])
            dma(p, "sync", ct1[:, sl, :], ctr[:, sl, :])
        dma(p, "gpsimd", a1tb[:], a1t4_in)
        dma(p, "gpsimd", a1b[:], a1_in)
        dma(p, "gpsimd", b1tb[:], b1t_in)
        dma(p, "sync", magc[:], mag_in)
        p.op("vector", lambda e: e.memset(cnacc[:], 0.0))

        for half in range(2):
            mms = [lambda e, c=c, half=half, st=(c == 0), sp=(c == NCH - 1):
                   e.matmul(psA[0:32, half, :], a1tb[:, c, :], xn1b[:, c, half * 512:(half + 1) * 512], start=st, stop=sp)
                   for c in range(NCH)]
            p.chain("tensor", mms)
        p.op("scalar", lambda e: e.copy(u1f[:, 0:512], psA[0:32, 0, :]))
        p.op("scalar", lambda e: e.copy(u1f[:, 512:1024], psA[0:32, 1, :]))
        p.op("vector", lambda e: e.tensor_copy(u1b[:], u1f[:]))

        for oc in range(32):
            dma(p, "sync", w1nf[:], w1n4_in[oc].rearrange("p c i -> p (c i)"))
            for ib in range(8):
                isl = slice(ib * 512, (ib + 1) * 512)
                p.op("tensor", lambda e, oc=oc, isl=isl: e.matmul(
                    psA[:, 2, :], b1tb[:, oc * 128:(oc + 1) * 128], a1b[:, isl], start=True, stop=True))
                p.op("vector", lambda e, isl=isl: e.tensor_tensor(effb[:], psA[:, 2, :], w1nf[:, isl], op=ALU.add))
                p.op("scalar", lambda e: e.activation(effs[:], effb[:], AF.Square))
                p.op("vector", lambda e: e.reduce_sum(rs1[:], effs[:], axis=AX.X))
                p.op("vector", lambda e, oc=oc: e.tensor_tensor(cnacc[:, oc:oc + 1], cnacc[:, oc:oc + 1], rs1[:], op=ALU.add))
        p.op("scalar", lambda e: e.activation(cnacc[:], cnacc[:], AF.Sqrt))
        p.op("vector", lambda e: e.reciprocal(rsc[:], cnacc[:]))
        p.op("vector", lambda e: e.tensor_tensor(rsc[:], rsc[:], magc[:], op=ALU.mult), drain=True)

        for oc in range(32):
            dma(p, "gpsimd", fw1b[:], w1t4_in[oc])
            for half in range(2):
                hs = slice(half * 512, (half + 1) * 512)
                mms = [lambda e, c=c, half=half, hs=hs:
                       e.matmul(psA[:, half, :], fw1b[:, c, :], xn1b[:, c, hs], start=(c == 0), stop=False)
                       for c in range(NCH)]
                mms.append(lambda e, half=half, hs=hs, oc=oc: e.matmul(
                    psA[:, half, :], b1tb[:, oc * 128:(oc + 1) * 128], u1b[:, hs], start=False, stop=True))
                p.chain("tensor", mms)
            p.op("vector", lambda e, oc=oc: e.tensor_scalar(resf[:, 0:512], psA[:, 0, :], rsc[:, oc:oc + 1], None, op0=ALU.mult))
            p.op("vector", lambda e, oc=oc: e.tensor_scalar(resf[:, 512:1024], psA[:, 1, :], rsc[:, oc:oc + 1], None, op0=ALU.mult))
            p.op("scalar", lambda e: e.activation(gsb[:], resf[:], AF.Silu))
            p.op("vector", lambda e, oc=oc: e.tensor_copy(cf[:], ct1[:, oc, :]))
            p.op("vector", lambda e: e.tensor_tensor(cf[:], cf[:], gsb[:], op=ALU.mult))
            dma(p, "sync", gsb[:], x1r[:, oc, :])
            p.op("vector", lambda e: e.tensor_tensor(cf[:], gsb[:], cf[:], op=ALU.add))
            dma(p, "sync", x2r[:, oc, :], cf[:])
        p.finish(block)
    return nc



def _run(nc, in_maps):
    res = run_bass_kernel_spmd(nc, in_maps, core_ids=list(range(NCORE)))
    return res.results


USE_FUSED = True


def kernel(L0_attn_x, L0_attn_Wq, L0_attn_Wkv, L0_attn_Wo, L0_attn_bo,
           L0_ffn_weight, L0_ffn_lora_A, L0_ffn_lora_B, L0_ffn_gw1, L0_ffn_gb1,
           L0_ffn_ln_gamma, L0_ffn_ln_beta, L0_ffn_gw2, L0_ffn_gb2,
           L1_attn_wq, L1_attn_bq, L1_attn_wk, L1_attn_bk, L1_attn_wv,
           L1_attn_bv, L1_attn_wl, L1_attn_bl, L1_ffn_W, L1_ffn_lora_A,
           L1_ffn_lora_B, L1_ffn_magnitude):
    if USE_FUSED:
        return kernel_fused(dict(
            L0_attn_x=L0_attn_x, L0_attn_Wq=L0_attn_Wq, L0_attn_Wkv=L0_attn_Wkv,
            L0_attn_Wo=L0_attn_Wo, L0_attn_bo=L0_attn_bo, L0_ffn_weight=L0_ffn_weight,
            L0_ffn_lora_A=L0_ffn_lora_A, L0_ffn_lora_B=L0_ffn_lora_B,
            L0_ffn_gw1=L0_ffn_gw1, L0_ffn_gb1=L0_ffn_gb1,
            L0_ffn_ln_gamma=L0_ffn_ln_gamma, L0_ffn_ln_beta=L0_ffn_ln_beta,
            L0_ffn_gw2=L0_ffn_gw2, L0_ffn_gb2=L0_ffn_gb2,
            L1_attn_wq=L1_attn_wq, L1_attn_bq=L1_attn_bq, L1_attn_wk=L1_attn_wk,
            L1_attn_bk=L1_attn_bk, L1_attn_wv=L1_attn_wv, L1_attn_bv=L1_attn_bv,
            L1_ffn_W=L1_ffn_W, L1_ffn_lora_A=L1_ffn_lora_A,
            L1_ffn_lora_B=L1_ffn_lora_B, L1_ffn_magnitude=L1_ffn_magnitude))
    c32 = np.ascontiguousarray
    xT = c32(np.asarray(L0_attn_x, dtype=np.float32).reshape(T, D).T)
    ident = _ident()

    gw1t = c32(np.asarray(L0_ffn_gw1, np.float32).T)
    nc_a = build_A()
    in_a = []
    for c in range(NCORE):
        in_a.append({
            "x": c32(xT[:, c * TL:(c + 1) * TL]),
            "gw1t": gw1t,
            "gb1c": c32(np.asarray(L0_ffn_gb1, np.float32).reshape(32, 1)),
            "gam": c32(np.asarray(L0_ffn_ln_gamma, np.float32).reshape(1, 32)),
            "bet": c32(np.asarray(L0_ffn_ln_beta, np.float32).reshape(1, 32)),
            "gw2t": c32(np.asarray(L0_ffn_gw2, np.float32).T),
            "gb2": c32(np.asarray(L0_ffn_gb2, np.float32).reshape(1, 8)),
            "ident": ident,
        })
    res_a = _run(nc_a, in_a)
    xn_full = np.concatenate([res_a[c]["xn"] for c in range(NCORE)], axis=1)
    cnts = np.stack([res_a[c]["cnt"].reshape(2, 8) for c in range(NCORE)])
    wgs = [res_a[c]["wg"] for c in range(NCORE)]

    xn4 = c32(xn_full.reshape(NCH, 128, 32, 256).transpose(2, 1, 0, 3))

    def tile_pcf(w, width):
        return c32(w.reshape(NCH, 128, width).transpose(1, 0, 2))

    nc_b = build_attn(l1=False)
    tri = np.where(np.arange(128)[:, None] > np.arange(128)[None, :],
                   np.float32(NEG), np.float32(0.0)).astype(np.float32)
    Wq = np.asarray(L0_attn_Wq, np.float32)
    Wkv = np.asarray(L0_attn_Wkv, np.float32)
    in_b = []
    for c in range(NCORE):
        wq_sl = c32(Wq[:, 512 * c:512 * (c + 1)])
        wkv_sl = c32(Wkv[:, 256 * c:256 * (c + 1)])
        in_b.append({
            "xn4": xn4,
            "wq4": tile_pcf(wq_sl, 512),
            "wkv4": tile_pcf(wkv_sl, 256),
            "tri": tri,
        })
    res_b = _run(nc_b, in_b)
    ctx_full = np.concatenate([res_b[c]["ctx"] for c in range(NCORE)], axis=0)

    nc_c = build_C()
    wo4 = _tile_w(np.asarray(L0_attn_Wo, np.float32))
    fw4 = _tile_w(c32(np.asarray(L0_ffn_weight, np.float32).T))
    la_cat = c32(np.asarray(L0_ffn_lora_A, np.float32).transpose(1, 0, 2).reshape(D, 128))
    la4 = c32(la_cat.reshape(NCH, 128, 128).transpose(1, 0, 2))
    lb_cat = c32(np.asarray(L0_ffn_lora_B, np.float32).reshape(128, D))
    boc = _to_col(np.asarray(L0_attn_bo, np.float32))
    sel2 = np.zeros((16, 2), np.float32)
    sel2[0::2, 0] = 1.0
    sel2[1::2, 1] = 1.0
    ssel = np.zeros((8, 128), np.float32)
    for e_ in range(8):
        ssel[e_, e_ * 16:(e_ + 1) * 16] = 1.0
    cnt16 = c32(cnts.reshape(16, 8))
    in_c = []
    for c in range(NCORE):
        in_c.append({
            "x": c32(xT[:, c * TL:(c + 1) * TL]),
            "xn": c32(xn_full[:, c * TL:(c + 1) * TL]),
            "ctxt": c32(ctx_full[:, c * TL:(c + 1) * TL]),
            "wo4": wo4,
            "fw4": fw4,
            "boc": boc,
            "la4": la4,
            "lb": lb_cat,
            "cnts": cnt16,
            "sel2": sel2,
            "wg": wgs[c],
            "ssel": ssel,
            "ident": ident,
        })
    res_c = _run(nc_c, in_c)
    x1_sh = [res_c[c]["x1"] for c in range(NCORE)]
    xn1_full = np.concatenate([res_c[c]["xn1"] for c in range(NCORE)], axis=1)
    xn14 = c32(xn1_full.reshape(NCH, 128, 32, 256).transpose(2, 1, 0, 3))

    nc_d = build_D()
    wq1 = np.asarray(L1_attn_wq, np.float32)
    wk1 = np.asarray(L1_attn_wk, np.float32)
    wv1 = np.asarray(L1_attn_wv, np.float32)
    wq4 = _tile_w(c32(wq1.T))
    wk4 = _tile_w(c32(wk1.T))
    wv4 = _tile_w(c32(wv1.T))
    bqc = _to_col(np.asarray(L1_attn_bq, np.float32))
    bkc = _to_col(np.asarray(L1_attn_bk, np.float32))
    bvc = _to_col(np.asarray(L1_attn_bv, np.float32))
    identb = np.eye(128).astype(ml_dtypes.bfloat16)
    in_d = []
    for c in range(NCORE):
        in_d.append({
            "xn1": c32(xn1_full[:, c * TL:(c + 1) * TL]),
            "wq4": wq4, "wk4": wk4, "wv4": wv4,
            "bqc": bqc, "bkc": bkc, "bvc": bvc,
            "identb": identb,
        })
    res_d = _run(nc_d, in_d)
    ctx1_sh = [np.asarray(res_d[c]["ctx"]) for c in range(NCORE)]

    nc_e = build_E()
    W1 = np.asarray(L1_ffn_W, np.float32)
    w1t4 = _tile_w(c32(W1.T))
    w1n4 = c32(W1.reshape(32, 128, NCH, 128))
    A1 = np.asarray(L1_ffn_lora_A, np.float32)
    B1 = np.asarray(L1_ffn_lora_B, np.float32)
    a1t4 = c32(A1.T.reshape(NCH, 128, 32).transpose(1, 0, 2))
    b1t = c32(B1.T)
    magc = _to_col(np.asarray(L1_ffn_magnitude, np.float32))
    in_e = []
    for c in range(NCORE):
        in_e.append({
            "x1": x1_sh[c],
            "xn1": c32(xn1_full[:, c * TL:(c + 1) * TL]),
            "ctxt": ctx1_sh[c],
            "w1t4": w1t4,
            "w1n4": w1n4,
            "a1t4": a1t4,
            "a1": A1,
            "b1t": b1t,
            "magc": magc,
        })
    res_e = _run(nc_e, in_e)
    x2 = np.concatenate([res_e[c]["x2"] for c in range(NCORE)], axis=1)
    return np.ascontiguousarray(x2.T).reshape(1, T, D).astype(np.float32)




def build_fused(debug=False):
    nc = bass.Bass("TRN2", target_bir_lowering=False)
    AP_ = lambda name, shape, dt: nc.dram_tensor(name, shape, dt, kind="ExternalInput").ap()
    xfull_in = AP_("xfull", [D, T], F32)
    xown_in = AP_("xown", [D, TL], F32)
    wkk4_in = AP_("wkk4", [128, NCH, 1024], F32)
    wvv4_in = AP_("wvv4", [128, NCH, 1024], F32)
    gw1t4_in = AP_("gw1t4", [128, NCH, 32], F32)
    gb1c_in = AP_("gb1c", [32, 1], F32)
    gam_in = AP_("gam", [1, 32], F32)
    bet_in = AP_("bet", [1, 32], F32)
    gw2t_in = AP_("gw2t", [32, 8], F32)
    gb2_in = AP_("gb2", [1, 8], F32)
    wq4_in = AP_("wq4", [32, 128, NCH, 128], F32)
    wo4_in = AP_("wo4", [32, 128, NCH, 128], F32)
    fw4_in = AP_("fw4", [32, 128, NCH, 128], F32)
    boc_in = AP_("boc", [128, 32], F32)
    la4_in = AP_("la4", [128, NCH, 128], F32)
    lb_in = AP_("lb", [128, D], F32)
    ssel_in = AP_("ssel", [8, 128], F32)
    l1q4_in = AP_("l1q4", [32, 128, NCH, 128], F32)
    l1k4_in = AP_("l1k4", [32, 128, NCH, 128], F32)
    l1v4_in = AP_("l1v4", [32, 128, NCH, 128], F32)
    bqc_in = AP_("bqc", [128, 32], F32)
    bkc_in = AP_("bkc", [128, 32], F32)
    bvc_in = AP_("bvc", [128, 32], F32)
    w1t4_in = AP_("w1t4", [32, 128, NCH, 128], F32)
    w1n4_in = AP_("w1n4", [32, 128, NCH, 128], F32)
    a1t4_in = AP_("a1t4", [128, NCH, 32], F32)
    a1_in = AP_("a1", [32, D], F32)
    b1t_in = AP_("b1t", [32, D], F32)
    magc_in = AP_("magc", [128, 32], F32)
    id_in = AP_("identf", [128, 128], F32)
    idb_in = AP_("identb", [128, 128], BF16)
    tri_in = AP_("tri", [128, 128], F32)
    biasq_in = AP_("biasq", [128, 32], F32)
    x2_out = nc.dram_tensor("x2", [D, TL], F32, kind="ExternalOutput").ap()
    if debug:
        dbg_mr = nc.dram_tensor("dbg_mr", [2, 512], F32, kind="ExternalOutput").ap()
        dbg_k = nc.dram_tensor("dbg_k", [128, 512], BF16, kind="ExternalOutput").ap()
        dbg_v = nc.dram_tensor("dbg_v", [256, 128], BF16, kind="ExternalOutput").ap()
        dbg_h = nc.dram_tensor("dbg_h", [32, 512], BF16, kind="ExternalOutput").ap()
        dbg_q = nc.dram_tensor("dbg_q", [128, TL], BF16, kind="ExternalOutput").ap()
        dbg_x1 = nc.dram_tensor("dbg_x1", [128, TL], F32, kind="ExternalOutput").ap()
        dbg_c1 = nc.dram_tensor("dbg_c1", [128, TL], BF16, kind="ExternalOutput").ap()
        dbg_ctx = nc.dram_tensor("dbg_ctx", [128, 512], BF16, kind="ExternalOutput").ap()
        dbg_xn = nc.dram_tensor("dbg_xn", [128, TL], BF16, kind="ExternalOutput").ap()
        dbg_wb = nc.dram_tensor("dbg_wb", [128, TL], F32, kind="ExternalOutput").ap()
        dbg_xn1 = nc.dram_tensor("dbg_xn1", [128, TL], BF16, kind="ExternalOutput").ap()
        dbg_q1 = nc.dram_tensor("dbg_q1", [D, TL], BF16, kind="ExternalOutput").ap()
        dbg_k1 = nc.dram_tensor("dbg_k1", [D, TL], BF16, kind="ExternalOutput").ap()
        dbg_v1 = nc.dram_tensor("dbg_v1", [D, TL], BF16, kind="ExternalOutput").ap()
        dbg_c1f = nc.dram_tensor("dbg_c1f", [D, TL], BF16, kind="ExternalOutput").ap()
        dbg_st = nc.dram_tensor("dbg_st", [6, TL], F32, kind="ExternalOutput").ap()
        dbg_x1b = nc.dram_tensor("dbg_x1b", [128, 3, TL], F32, kind="ExternalOutput").ap()
        dbg_sg = nc.dram_tensor("dbg_sg", [128, 2, TL], F32, kind="ExternalOutput").ap()
        dbg_cnt = nc.dram_tensor("dbg_cnt", [1, 16], F32, kind="ExternalOutput").ap()
        dbg_al = nc.dram_tensor("dbg_al", [128, 16], F32, kind="ExternalOutput").ap()
        dbg_wt = nc.dram_tensor("dbg_wt", [128, 2, 64], F32, kind="ExternalOutput").ap()
        dbg_hf = nc.dram_tensor("dbg_hf", [32, T], BF16, kind="ExternalOutput").ap()
        dbg_us = nc.dram_tensor("dbg_us", [128, TL], BF16, kind="ExternalOutput").ap()
        dbg_pc = nc.dram_tensor("dbg_pc", [128, 64], F32, kind="ExternalOutput").ap()
        dbg_hb = nc.dram_tensor("dbg_hb", [32, 128], BF16, kind="ExternalOutput").ap()
        dbg_hr = nc.dram_tensor("dbg_hr", [128, 32], F32, kind="ExternalOutput").ap()
        dbg_ln = nc.dram_tensor("dbg_ln", [128, 40], F32, kind="ExternalOutput").ap()

    kscr = nc.dram_tensor("kscr", [1024, T], BF16)
    vscr = nc.dram_tensor("vscr", [T, 1024], BF16)
    qscr = nc.dram_tensor("qscr", [D, TL], BF16)
    x1scr = nc.dram_tensor("x1scr", [D, TL], F32)
    hscr = nc.dram_tensor("hscr", [32, T], BF16)
    mrscr = nc.dram_tensor("mrscr", [2, T], F32)
    wgscr = nc.dram_tensor("wgscr", [2, T, 8], F32)
    q1scr = nc.dram_tensor("q1scr", [D, TL], BF16)
    k1scr = nc.dram_tensor("k1scr", [D, TL], BF16)
    v1scr = nc.dram_tensor("v1scr", [D, TL], BF16)
    ctxscr = nc.dram_tensor("ctxscr", [D, TL], BF16)
    kdscr = nc.dram_tensor("kdscr", [16, 128, 512], BF16)
    vdscr = nc.dram_tensor("vdscr", [16, 512, 128], BF16)

    es = contextlib.ExitStack()
    with es:
        E = es.enter_context
        M = E(nc.sbuf_tensor("M", [128, 86016], BF16))

        def mv(off, *shape):
            n = 1
            for s in shape:
                n *= s
            ap = M[:, off:off + n]
            if len(shape) == 2:
                ap = ap.rearrange("p (a b) -> p a b", b=shape[1])
            elif len(shape) == 3:
                ap = ap.rearrange("p (a b c) -> p a b c", c=shape[2], b=shape[1])
            return ap
        W2 = 65536
        wkb = mv(0, 32, 1024)
        wvb = mv(0, 32, 1024)
        xnb = mv(0, 32, 1024)
        xtile = mv(32768, 32, 256)
        ctxb = mv(32768, 32, 1024)
        wtl = mv(W2, 32, 128)
        wtl2 = mv(W2 + 4096, 32, 128)
        wtlB = mv(W2 + 12288, 32, 128)
        wtl2B = mv(W2 + 16384, 32, 128)
        wtlE = (wtl, wtl2)
        wtlH = (wtl, wtlB)
        wtl2H = (wtl2, wtl2B)
        xtileB = mv(32768 + 8192, 32, 256)
        qtile = M[:, W2 + 8192:W2 + 9216]
        usb = M[:, W2 + 9216:W2 + 10240]
        x1bb = M[:, W2 + 10240:W2 + 11264]
        sqx = M[:, W2 + 11264:W2 + 12288]
        sqt = M[:, W2 + 12288:W2 + 12544]
        ktile = M[:, W2 + 12544:W2 + 12800]
        vtile = M[:, W2 + 12800:W2 + 13312]
        sqtB = M[:, W2:W2 + 256]
        ktileB = M[:, W2 + 256:W2 + 512]
        gtile = M[:, W2 + 512:W2 + 768]
        vtileB = M[:, W2 + 768:W2 + 1280]
        gw1b = mv(W2 + 13312, 32, 32)
        u1b_full = M[:, W2 + 14336:W2 + 15360]
        u1b = u1b_full[0:32, :]
        effb = M[:, W2 + 15360:W2 + 15872]
        effs = M[:, W2 + 15872:W2 + 16384]
        kgb = mv(W2, 64, 128)
        vgb = mv(W2 + 8192, 64, 128)
        ptb4 = M[:, W2 + 16384:W2 + 18432]
        kdiag = M[:, W2 + 18432:W2 + 18944]
        vdiag = mv(W2 + 18944, 4, 128)
        qb = mv(W2 + 4096, 32, 128)
        kb = mv(W2 + 8192, 32, 128)
        vb = mv(W2 + 12288, 32, 128)
        ctb2 = mv(W2 + 16384, 32, 128)
        vts1 = mv(W2 + 2048, 16, 128)
        pt1f = M[:, W2 + 0:W2 + 512]
        pt1 = pt1f[0:32, :]
        w1nb_f = M[:, W2:W2 + 4096]
        w1nbB = M[:, W2 + 16384:W2 + 20480]
        effbB = M[:, W2 + 12288:W2 + 12800]
        effsB = M[:, W2 + 12800:W2 + 13312]

        Ff = E(nc.sbuf_tensor("Ff", [128, 5120], F32))
        slotA = Ff[:, 0:1024]
        slotB = Ff[:, 1024:2048]
        slotC = Ff[:, 2048:3072]
        slotD = Ff[:, 3072:4096]
        colvb = slotD
        mbc = Ff[:, 4096:4352]
        rbc = Ff[:, 4352:4608]
        tmpc = Ff[:, 4608:4864]

        onesf = E(nc.sbuf_tensor("onesf", [1, 128], F32))
        onesb = E(nc.sbuf_tensor("onesb", [128, 1], BF16))
        onescf = E(nc.sbuf_tensor("onescf", [128, 1], F32))
        ones11 = E(nc.sbuf_tensor("ones11", [1, 1], F32))
        colkc = E(nc.sbuf_tensor("colkc", [128, 8], F32))
        colg = E(nc.sbuf_tensor("colg", [32, 1], F32))
        mrow = E(nc.sbuf_tensor("mrow", [1, 256], F32))
        qrow = E(nc.sbuf_tensor("qrow", [1, 256], F32))
        vrow = E(nc.sbuf_tensor("vrow", [1, 256], F32))
        rrow = E(nc.sbuf_tensor("rrow", [1, 256], F32))
        mcol = E(nc.sbuf_tensor("mcol", [128, 2], F32))
        rcol = E(nc.sbuf_tensor("rcol", [128, 2], F32))
        morow = E(nc.sbuf_tensor("morow", [1, TL], F32))
        gb1c = E(nc.sbuf_tensor("gb1c_s", [32, 1], F32))
        gamr = E(nc.sbuf_tensor("gamr", [1, 32], F32))
        betr = E(nc.sbuf_tensor("betr", [1, 32], F32))
        gw2t = E(nc.sbuf_tensor("gw2ts", [32, 8], F32))
        gb2r = E(nc.sbuf_tensor("gb2r", [1, 8], F32))
        gamb = E(nc.sbuf_tensor("gamb", [128, 32], F32))
        betb = E(nc.sbuf_tensor("betb", [128, 32], F32))
        gb2b = E(nc.sbuf_tensor("gb2b", [128, 8], F32))
        htm = E(nc.sbuf_tensor("htm", [128, 32], F32))
        htb = E(nc.sbuf_tensor("htb", [32, 128], BF16))
        hm = E(nc.sbuf_tensor("hm", [128, 1], F32))
        hv = E(nc.sbuf_tensor("hv", [128, 1], F32))
        hrs = E(nc.sbuf_tensor("hrs", [128, 1], F32))
        hsq = E(nc.sbuf_tensor("hsq", [128, 32], F32))
        hft = E(nc.sbuf_tensor("hft", [32, 128], F32))
        gts = E(nc.sbuf_tensor("gts", [128, 8], F32))
        iotf = E(nc.sbuf_tensor("iotf", [128, 8], F32))
        ioti = E(nc.sbuf_tensor("ioti", [128, 8], I32))
        v1 = E(nc.sbuf_tensor("v1", [128, 1], F32))
        nv1 = E(nc.sbuf_tensor("nv1", [128, 1], F32))
        v2 = E(nc.sbuf_tensor("v2", [128, 1], F32))
        e2 = E(nc.sbuf_tensor("e2", [128, 1], F32))
        g1 = E(nc.sbuf_tensor("g1", [128, 1], F32))
        g2c = E(nc.sbuf_tensor("g2c", [128, 1], F32))
        sel8 = E(nc.sbuf_tensor("sel8", [128, 8], F32))
        m1c = E(nc.sbuf_tensor("m1c", [128, 1], F32))
        oh1 = E(nc.sbuf_tensor("oh1", [128, 8], F32))
        oh2 = E(nc.sbuf_tensor("oh2", [128, 8], F32))
        wtown1 = E(nc.sbuf_tensor("wtown1", [128, 8, 8], F32))
        wtown2 = E(nc.sbuf_tensor("wtown2", [128, 8, 8], F32))
        cac1 = E(nc.sbuf_tensor("cac1", [128, 8], F32))
        cac2 = E(nc.sbuf_tensor("cac2", [128, 8], F32))
        cacb = E(nc.sbuf_tensor("cacb", [128, 8], BF16))
        cgrow = E(nc.sbuf_tensor("cgrow", [1, 16], F32))
        alw = E(nc.sbuf_tensor("alw", [1, 16], F32))
        alb1 = E(nc.sbuf_tensor("alb1", [128, 8], F32))
        alb2 = E(nc.sbuf_tensor("alb2", [128, 8], F32))
        wfm = slotD[0:8, :]
        wbcb = slotC
        ssel_sb = E(nc.sbuf_tensor("ssel_sb", [8, 128], F32))
        boc = E(nc.sbuf_tensor("boc_s", [128, 32], F32))
        bqc = E(nc.sbuf_tensor("bqc_s", [128, 32], F32))
        bkc = E(nc.sbuf_tensor("bkc_s", [128, 32], F32))
        bvc = E(nc.sbuf_tensor("bvc_s", [128, 32], F32))
        magc = E(nc.sbuf_tensor("magc_s", [128, 32], F32))
        biasq = E(nc.sbuf_tensor("biasq_s", [128, 32], F32))
        tri = E(nc.sbuf_tensor("tri_s", [128, 128], F32))
        ident = E(nc.sbuf_tensor("ident_s", [128, 128], F32))
        identb = E(nc.sbuf_tensor("identb_s", [128, 128], BF16))
        qts = M[:, 84992:85504]
        qtsB = M[:, 85504:86016]
        lbt = E(nc.sbuf_tensor("lbt", [128, 128], BF16))
        lbtB = E(nc.sbuf_tensor("lbtB", [128, 128], BF16))
        lbtH = (lbt, lbtB)
        rcb = Ff[:, 4096:4608]
        rs = Ff[0:1, 4608:5120]
        lab = mv(W2, 32, 128)
        a1tb = mv(W2, 32, 32)
        a1b = M[:, W2 + 4096:W2 + 8192][0:32, :]
        b1tb = M[:, W2 + 8192:W2 + 12288][0:32, :]
        ctxrd = M[:, W2 + 12288:W2 + 13312]
        rs1 = E(nc.sbuf_tensor("rs1", [128, 1], F32))
        cnacc = E(nc.sbuf_tensor("cnacc", [128, 32], F32))
        rsc = E(nc.sbuf_tensor("rsc", [128, 32], F32))
        psA = E(nc.psum_tensor("psA", [128, 7, 512], F32))
        psVt = E(nc.psum_tensor("psVt", [128, 1, 1024], BF16))
        block = E(nc.Block())
        p = Prog(nc, es)

        xfr = xfull_in.rearrange("(c p) t -> p c t", p=128)
        xor_ = xown_in.rearrange("(c p) t -> p c t", p=128)
        x1r = x1scr.ap().rearrange("(c p) t -> p c t", p=128)
        x2r = x2_out.rearrange("(c p) t -> p c t", p=128)
        AXX = AX.X

        p.op("vector", lambda e: e.memset(onesf[:], 1.0))
        p.op("vector", lambda e: e.memset(onesb[:], 1.0))
        p.op("vector", lambda e: e.memset(onescf[:], 1.0))
        p.op("vector", lambda e: e.memset(ones11[:], 1.0))
        p.op("vector", lambda e: e.memset(cac1[:], 0.0))
        p.op("vector", lambda e: e.memset(cac2[:], 0.0))
        p.op("gpsimd", lambda e: e.iota(ioti[:], pattern=[[1, 8]], base=0, channel_multiplier=0))
        p.op("vector", lambda e: e.tensor_copy(iotf[:], ioti[:]))
        for dst, src_ in ((gb1c, gb1c_in), (gamr, gam_in), (betr, bet_in), (gw2t, gw2t_in),
                          (gb2r, gb2_in), (boc, boc_in), (bqc, bqc_in), (bkc, bkc_in),
                          (bvc, bvc_in), (magc, magc_in), (biasq, biasq_in), (tri, tri_in),
                          (ident, id_in), (identb, idb_in), (ssel_sb, ssel_in)):
            dma(p, "sync", dst[:], src_)
        dma(p, "gpsimd", gw1b, gw1t4_in)
        p.chain("tensor", [
            lambda e: e.matmul(psA[:, 0, 0:32], onesf[:], gamr[:], start=True, stop=True),
            lambda e: e.matmul(psA[:, 1, 0:32], onesf[:], betr[:], start=True, stop=True),
            lambda e: e.matmul(psA[:, 2, 0:8], onesf[:], gb2r[:], start=True, stop=True),
        ])
        p.op("scalar", lambda e: e.copy(gamb[:], psA[:, 0, 0:32]))
        p.op("scalar", lambda e: e.copy(betb[:], psA[:, 1, 0:32]))
        p.op("scalar", lambda e: e.copy(gb2b[:], psA[:, 2, 0:8]))

        for i in range(4):
            sl = slice(i * 8, (i + 1) * 8)
            dma(p, "gpsimd", wkb[:, sl, :], wkk4_in[:, sl, :])
        for oc in range(8):
            p.chain("tensor", [
                lambda e, c=c, oc=oc, st=(c == 0), sp=(c == NCH - 1): e.matmul(
                    psA[:, 3, 0:1], wkb[:, c, oc * 128:(oc + 1) * 128], onesb[:], start=st, stop=sp)
                for c in range(NCH)])
            p.op("scalar", lambda e, oc=oc: e.copy(colkc[:, oc:oc + 1], psA[:, 3, 0:1]))
        p.chain("tensor", [
            lambda e, c=c, st=(c == 0), sp=(c == NCH - 1): e.matmul(
                psA[0:32, 3, 0:1], gw1b[:, c, :], onesb[:], start=st, stop=sp)
            for c in range(NCH)])
        p.op("scalar", lambda e: e.copy(colg[:], psA[0:32, 3, 0:1]))

        endA = {}
        floorA = p.mark()
        sqtP = (sqt, sqtB)
        ktileP = (ktile, ktileB)
        mChA, mSubA, mKoutA, mGevA, mHoutA = {}, {}, {}, {}, {}
        for tb in range(32):
            ts_ = slice(tb * 256, (tb + 1) * 256)
            xt_ = (xtile, xtileB)[tb % 2]
            p.op("gpsimd", lambda e, xt_=xt_, ts_=ts_: e.dma_start(out=xt_[:], in_=xfr[:, :, ts_]),
                 inc=16, wait_at=endA.get(tb - 2, floorA))
            dmam = p.mark()
            for c in range(NCH):
                u = tb * NCH + c
                sq_ = sqtP[c % 2]
                p.op("scalar", lambda e, xt_=xt_, c=c, sq_=sq_: e.activation(sq_, xt_[:, c, :], AF.Square),
                     wait_at=mmax(dmam, mChA.get(u - 2, floorA)))
                msq = p.mark()
                st, sp = (c == 0), (c == NCH - 1)
                p.chain("tensor", [
                    lambda e, c=c, st=st, sp=sp, xt_=xt_: e.matmul(psA[0:1, 4, 0:256], onesb[:], xt_[:, c, :], start=st, stop=sp),
                    lambda e, st=st, sp=sp, sq_=sq_: e.matmul(psA[0:1, 5, 0:256], onesb[:], sq_, start=st, stop=sp),
                ], wait_at=msq)
                mChA[u] = p.mark()
            p.op("scalar", lambda e: e.activation(mrow[:], psA[0:1, 4, 0:256], AF.Copy, scale=f32(1.0 / D)))
            p.op("scalar", lambda e: e.activation(qrow[:], psA[0:1, 5, 0:256], AF.Copy, scale=f32(1.0 / D)))
            p.op("vector", lambda e: e.tensor_tensor(vrow[:], mrow[:], mrow[:], op=ALU.mult))
            p.op("vector", lambda e: e.tensor_tensor(vrow[:], qrow[:], vrow[:], op=ALU.subtract))
            p.op("vector", lambda e: e.tensor_scalar_add(vrow[:], vrow[:], LN_EPS))
            p.op("scalar", lambda e: e.activation(vrow[:], vrow[:], AF.Sqrt))
            p.op("vector", lambda e: e.reciprocal(rrow[:], vrow[:]))
            dma(p, "sync", mrscr.ap()[0:1, ts_], mrow[:])
            dma(p, "sync", mrscr.ap()[1:2, ts_], rrow[:])
            p.op("tensor", lambda e: e.matmul(psA[:, 6, 0:256], onesf[:], mrow[:], start=True, stop=True))
            p.op("scalar", lambda e: e.copy(mbc, psA[:, 6, 0:256]))
            p.op("tensor", lambda e: e.matmul(psA[:, 6, 0:256], onesf[:], rrow[:], start=True, stop=True))
            p.op("scalar", lambda e: e.copy(rbc, psA[:, 6, 0:256]))
            mRB = p.mark()
            for oc in range(8):
                g_ = tb * 8 + oc
                bq = oc % 2
                kt_ = ktileP[oc % 2]
                p.chain("tensor", [
                    lambda e, c=c, oc=oc, bq=bq, st=(c == 0), sp=(c == NCH - 1), xt_=xt_: e.matmul(
                        psA[:, bq, 0:256], wkb[:, c, oc * 128:(oc + 1) * 128], xt_[:, c, :], start=st, stop=sp)
                    for c in range(NCH)], wait_at=mSubA.get(g_ - 2, floorA))
                mchk = p.mark()
                p.op("vector", lambda e, oc=oc: e.tensor_scalar(tmpc, mbc, colkc[:, oc:oc + 1], None, op0=ALU.mult),
                     wait_at=(mRB if oc == 0 else 0))
                p.op("vector", lambda e, bq=bq: e.tensor_tensor(tmpc, psA[:, bq, 0:256], tmpc, op=ALU.subtract),
                     wait_at=mchk)
                mSubA[g_] = p.mark()
                p.op("vector", lambda e, kt_=kt_: e.tensor_tensor(kt_, tmpc, rbc, op=ALU.mult),
                     wait_at=mKoutA.get(g_ - 2, floorA))
                mmul = p.mark()
                p.op("sync", lambda e, oc=oc, ts_=ts_, kt_=kt_: e.dma_start(
                    out=kscr.ap()[oc * 128:(oc + 1) * 128, ts_], in_=kt_), inc=16, wait_at=mmul)
                mKoutA[g_] = p.mark()
            p.chain("tensor", [
                lambda e, c=c, st=(c == 0), sp=(c == NCH - 1), xt_=xt_: e.matmul(
                    psA[0:32, 2, 0:256], gw1b[:, c, :], xt_[:, c, :], start=st, stop=sp)
                for c in range(NCH)], wait_at=mGevA.get(tb - 1, floorA))
            mchg = p.mark()
            p.op("vector", lambda e: e.tensor_scalar(tmpc[0:32, :], mbc[0:32, :], colg[:], None, op0=ALU.mult),
                 wait_at=0)
            p.op("vector", lambda e: e.tensor_tensor(tmpc[0:32, :], psA[0:32, 2, 0:256], tmpc[0:32, :], op=ALU.subtract),
                 wait_at=mchg)
            mGevA[tb] = p.mark()
            p.op("vector", lambda e: e.tensor_tensor(tmpc[0:32, :], tmpc[0:32, :], rbc[0:32, :], op=ALU.mult),
                 wait_at=0)
            p.op("vector", lambda e: e.tensor_scalar(gtile[0:32, :], tmpc[0:32, :], gb1c[:], None, op0=ALU.add),
                 wait_at=mHoutA.get(tb - 1, floorA))
            mgt = p.mark()
            p.op("sync", lambda e, ts_=ts_: e.dma_start(out=hscr.ap()[:, ts_], in_=gtile[0:32, :]),
                 inc=16, wait_at=mgt)
            mHoutA[tb] = p.mark()
            endA[tb] = p.mark()

        for i in range(4):
            sl = slice(i * 8, (i + 1) * 8)
            dma(p, "gpsimd", wvb[:, sl, :], wvv4_in[:, sl, :])
        for half in range(2):
            p.chain("tensor", [
                lambda e, c=c, half=half, st=(c == 0), sp=(c == NCH - 1): e.matmul(
                    psA[0:1, 3, :], onesb[:], wvb[:, c, half * 512:(half + 1) * 512], start=st, stop=sp)
                for c in range(NCH)])
            p.op("scalar", lambda e: e.copy(rs[:], psA[0:1, 3, :]))
            p.op("tensor", lambda e: e.matmul(psA[:, 2, :], onesf[:], rs[:], start=True, stop=True))
            p.op("scalar", lambda e, half=half: e.copy(colvb[:, half * 512:(half + 1) * 512], psA[:, 2, :]))
        endB = {}
        floorB = p.mark()
        vtileP = (vtile, vtileB)
        slAB = (slotA[:, 0:512], slotA[:, 512:1024])
        mSubB, mVoutB, mColB = {}, {}, {}
        for tb in range(32):
            ts_ = slice(tb * 256, (tb + 1) * 256)
            xt_ = (xtile, xtileB)[tb % 2]
            p.op("gpsimd", lambda e, xt_=xt_, ts_=ts_: e.dma_start(out=xt_[:], in_=xfr[:, :, ts_]),
                 inc=16, wait_at=endB.get(tb - 2, floorB))
            dma(p, "sync", mrow[:], mrscr.ap()[0:1, ts_])
            dma(p, "sync", rrow[:], mrscr.ap()[1:2, ts_])
            dmam = p.mark()
            mcp = floorB
            for vt in range(2):
                p.chain("tensor", [
                    lambda e, vt=vt: e.matmul(psA[:, 4, 0:1], mrow[0:1, vt * 128:(vt + 1) * 128], ones11[:], start=True, stop=True),
                    lambda e, vt=vt: e.matmul(psA[:, 5, 0:1], rrow[0:1, vt * 128:(vt + 1) * 128], ones11[:], start=True, stop=True),
                ], wait_at=mmax(dmam, mColB.get(tb - 1, floorB)) if vt == 0 else mcp)
                mchc = p.mark()
                p.op("scalar", lambda e, vt=vt: e.copy(mcol[:, vt:vt + 1], psA[:, 4, 0:1]), wait_at=mchc)
                p.op("scalar", lambda e, vt=vt: e.copy(rcol[:, vt:vt + 1], psA[:, 5, 0:1]), wait_at=0)
                mcp = p.mark()
            mColB[tb] = p.mark()
            for vt in range(2):
                tk0 = tb * 256 + vt * 128
                for half in range(2):
                    u = tb * 4 + vt * 2 + half
                    bq = u % 2
                    vt_ = vtileP[u % 2]
                    sl_ = slAB[u % 2]
                    hsl = slice(half * 512, (half + 1) * 512)
                    p.chain("tensor", [
                        lambda e, c=c, vt=vt, hsl=hsl, bq=bq, st=(c == 0), sp=(c == NCH - 1), xt_=xt_: e.matmul(
                            psA[:, bq, :], xt_[:, c, vt * 128:(vt + 1) * 128], wvb[:, c, hsl], start=st, stop=sp)
                        for c in range(NCH)], wait_at=mSubB.get(u - 2, floorB))
                    mchv = p.mark()
                    p.op("vector", lambda e, hsl=hsl, vt=vt, sl_=sl_: e.tensor_scalar(
                        sl_, colvb[:, hsl], mcol[:, vt:vt + 1], None, op0=ALU.mult),
                        wait_at=(mColB[tb] if u % 4 == 0 else 0))
                    p.op("vector", lambda e, bq=bq, sl_=sl_: e.tensor_tensor(
                        sl_, psA[:, bq, :], sl_, op=ALU.subtract), wait_at=mchv)
                    mSubB[u] = p.mark()
                    p.op("vector", lambda e, vt=vt, vt_=vt_, sl_=sl_: e.tensor_scalar(
                        vt_, sl_, rcol[:, vt:vt + 1], None, op0=ALU.mult),
                        wait_at=mVoutB.get(u - 2, floorB))
                    mmul = p.mark()
                    p.op("sync", lambda e, tk0=tk0, hsl=hsl, vt_=vt_: e.dma_start(
                        out=vscr.ap()[tk0:tk0 + 128, hsl], in_=vt_), inc=16, wait_at=mmul)
                    mVoutB[u] = p.mark()
            endB[tb] = p.mark()

        p.force_drain = True
        for t in range(64):
            p.op("sync", lambda e, t=t: e.dma_start(out=htb[:], in_=hscr.ap()[:, t * 128:(t + 1) * 128]), inc=16)
            p.op("tensor", lambda e: e.transpose(psVt[:, 0, 0:32], htb[:], identb[0:32, 0:32]))
            p.op("scalar", lambda e: e.copy(htm[:], psVt[:, 0, 0:32]))
            if debug and t == 0:
                dma(p, "sync", dbg_hr, htm[:])
            p.op("vector", lambda e: e.reduce_sum(hm[:], htm[:], axis=AXX))
            p.op("vector", lambda e: e.tensor_scalar_mul(hm[:], hm[:], 1.0 / 32.0), drain=True)
            p.op("vector", lambda e: e.tensor_scalar(htm[:], htm[:], hm[:], None, op0=ALU.subtract), drain=True)
            if debug and t == 0:
                dma(p, "sync", dbg_ln[:, 0:32], htm[:])
                p.op("sync", lambda e: e.dma_start(out=dbg_ln[:, 32:33], in_=hm[:], allow_slow_non_contiguous=True), inc=16)
            p.op("vector", lambda e: e.tensor_tensor(hsq[:], htm[:], htm[:], op=ALU.mult))
            p.op("vector", lambda e: e.reduce_sum(hv[:], hsq[:], axis=AXX))
            p.op("vector", lambda e: e.tensor_scalar_mul(hv[:], hv[:], 1.0 / 32.0), drain=True)
            p.op("vector", lambda e: e.tensor_scalar_add(hv[:], hv[:], LN_EPS))
            p.op("scalar", lambda e: e.activation(hv[:], hv[:], AF.Sqrt))
            p.op("vector", lambda e: e.reciprocal(hrs[:], hv[:]))
            p.op("vector", lambda e: e.tensor_scalar(htm[:], htm[:], hrs[:], None, op0=ALU.mult), drain=True)
            if debug and t == 0:
                p.op("sync", lambda e: e.dma_start(out=dbg_ln[:, 33:34], in_=hv[:], allow_slow_non_contiguous=True), inc=16)
                p.op("sync", lambda e: e.dma_start(out=dbg_ln[:, 34:35], in_=hrs[:], allow_slow_non_contiguous=True), inc=16)
            p.op("vector", lambda e: e.tensor_tensor(htm[:], htm[:], gamb[:], op=ALU.mult))
            p.op("vector", lambda e: e.tensor_tensor(htm[:], htm[:], betb[:], op=ALU.add))
            p.op("vector", lambda e: e.tensor_relu(htm[:], htm[:]))
            p.op("tensor", lambda e: e.transpose(psA[0:32, 3, 0:128], htm[:], ident[:]))
            p.op("scalar", lambda e: e.copy(hft[:], psA[0:32, 3, 0:128]))
            p.op("tensor", lambda e: e.matmul(psA[:, 2, 64:72], hft[:], gw2t[:], start=True, stop=True))
            p.op("vector", lambda e: e.tensor_tensor(gts[:], psA[:, 2, 64:72], gb2b[:], op=ALU.add))
            p.op("vector", lambda e: e.reduce_max(v1[:], gts[:], axis=AXX))
            p.op("vector", lambda e: e.tensor_scalar(oh1[:], gts[:], v1[:], None, op0=ALU.is_equal), drain=True)
            p.op("vector", lambda e: e.scalar_tensor_tensor(sel8[:], oh1[:], -256.0, iotf[:], op0=ALU.mult, op1=ALU.add))
            p.op("vector", lambda e: e.tensor_reduce(m1c[:], sel8[:], axis=AXX, op=ALU.min))
            p.op("vector", lambda e: e.tensor_scalar(oh1[:], sel8[:], m1c[:], None, op0=ALU.is_equal), drain=True)
            p.op("vector", lambda e: e.scalar_tensor_tensor(sel8[:], oh1[:], NEG, gts[:], op0=ALU.mult, op1=ALU.add))
            p.op("vector", lambda e: e.reduce_max(v2[:], sel8[:], axis=AXX))
            p.op("vector", lambda e: e.tensor_scalar(oh2[:], sel8[:], v2[:], None, op0=ALU.is_equal), drain=True)
            p.op("vector", lambda e: e.scalar_tensor_tensor(sel8[:], oh2[:], -256.0, iotf[:], op0=ALU.mult, op1=ALU.add))
            p.op("vector", lambda e: e.tensor_reduce(m1c[:], sel8[:], axis=AXX, op=ALU.min))
            p.op("vector", lambda e: e.tensor_scalar(oh2[:], sel8[:], m1c[:], None, op0=ALU.is_equal), drain=True)
            p.op("vector", lambda e: e.tensor_scalar_mul(nv1[:], v1[:], -1.0))
            p.op("scalar", lambda e: e.activation(e2[:], v2[:], AF.Exp, bias=nv1[:]))
            p.op("vector", lambda e: e.tensor_scalar_add(g1[:], e2[:], 1.0))
            p.op("vector", lambda e: e.reciprocal(g1[:], g1[:]))
            p.op("vector", lambda e: e.tensor_tensor(g2c[:], e2[:], g1[:], op=ALU.mult), drain=True)
            p.op("vector", lambda e: e.tensor_scalar(sel8[:], oh1[:], g1[:], None, op0=ALU.mult), drain=True)
            dma(p, "sync", wgscr.ap()[0, t * 128:(t + 1) * 128, :], sel8[:])
            p.op("vector", lambda e: e.tensor_scalar(sel8[:], oh2[:], g2c[:], None, op0=ALU.mult), drain=True)
            dma(p, "sync", wgscr.ap()[1, t * 128:(t + 1) * 128, :], sel8[:])
            p.op("vector", lambda e: e.tensor_tensor(cac1[:], cac1[:], oh1[:], op=ALU.add))
            p.op("vector", lambda e: e.tensor_tensor(cac2[:], cac2[:], oh2[:], op=ALU.add))
            if debug and t == 0:
                dma(p, "sync", dbg_hb, htb[:])
                dma(p, "sync", dbg_pc[:, 0:32], htm[:])
                dma(p, "sync", dbg_pc[:, 32:40], gts[:])
                dma(p, "sync", dbg_pc[:, 40:48], oh1[:])
                dma(p, "sync", dbg_pc[:, 48:56], oh2[:])
                dma(p, "sync", dbg_pc[:, 56:64], iotf[:])
        p.op("vector", lambda e: e.tensor_copy(cacb[:], cac1[:]))
        p.op("tensor", lambda e: e.matmul(psA[0:1, 3, 128:136], onesb[:], cacb[:], start=True, stop=True))
        p.op("scalar", lambda e: e.copy(cgrow[0:1, 0:8], psA[0:1, 3, 128:136]))
        p.op("vector", lambda e: e.tensor_copy(cacb[:], cac2[:]))
        p.op("tensor", lambda e: e.matmul(psA[0:1, 3, 136:144], onesb[:], cacb[:], start=True, stop=True))
        p.op("scalar", lambda e: e.copy(cgrow[0:1, 8:16], psA[0:1, 3, 136:144]))
        p.op("vector", lambda e: e.tensor_scalar(alw[:], cgrow[:], CAPACITY, None, op0=ALU.is_le))
        p.op("tensor", lambda e: e.matmul(psA[:, 2, 0:8], onesf[:], alw[0:1, 0:8], start=True, stop=True))
        p.op("tensor", lambda e: e.matmul(psA[:, 3, 0:8], onesf[:], alw[0:1, 8:16], start=True, stop=True))
        p.op("scalar", lambda e: e.copy(alb1[:], psA[:, 2, 0:8]))
        p.op("scalar", lambda e: e.copy(alb2[:], psA[:, 3, 0:8]))
        p.force_drain = False

        def percore_body(e, c, tok):
            for qg in range(2):
                for g in range(8):
                    idx = qg * 8 + g
                    tq0 = 1024 * c + 512 * qg
                    e.dma_start(out=kdscr.ap()[idx], in_=kscr.ap()[g * 128:(g + 1) * 128, tq0:tq0 + 512]).then_inc(tok, 16)
                    e.dma_start(out=vdscr.ap()[idx], in_=vscr.ap()[tq0:tq0 + 512, g * 128:(g + 1) * 128]).then_inc(tok, 16)
            e.dma_start(out=wtown1[:], in_=wgscr.ap()[0, 1024 * c:1024 * (c + 1), :].rearrange(
                "(t pp) ee -> pp t ee", pp=128)).then_inc(tok, 16)
            e.dma_start(out=wtown2[:], in_=wgscr.ap()[1, 1024 * c:1024 * (c + 1), :].rearrange(
                "(t pp) ee -> pp t ee", pp=128)).then_inc(tok, 16)
        p.switch("gpsimd", percore_body, 544)

        for i in range(4):
            sl = slice(i * 8, (i + 1) * 8)
            dma(p, "gpsimd", xnb[:, sl, :], xor_[:, sl, :])
        for c in range(NCH):
            p.op("scalar", lambda e, c=c: e.activation(sqx, xnb[:, c, :], AF.Square))
            st, sp = (c == 0), (c == NCH - 1)
            p.chain("tensor", [
                lambda e, c=c, st=st, sp=sp: e.matmul(psA[0:1, 2, :], onesb[:], xnb[:, c, 0:512], start=st, stop=sp),
                lambda e, c=c, st=st, sp=sp: e.matmul(psA[0:1, 3, :], onesb[:], xnb[:, c, 512:1024], start=st, stop=sp),
                lambda e, st=st, sp=sp: e.matmul(psA[0:1, 4, :], onesb[:], sqx[:, 0:512], start=st, stop=sp),
                lambda e, st=st, sp=sp: e.matmul(psA[0:1, 5, :], onesb[:], sqx[:, 512:1024], start=st, stop=sp),
            ])
        p.op("scalar", lambda e: e.activation(morow[0:1, 0:512], psA[0:1, 2, :], AF.Copy, scale=f32(1.0 / D)))
        p.op("scalar", lambda e: e.activation(morow[0:1, 512:1024], psA[0:1, 3, :], AF.Copy, scale=f32(1.0 / D)))
        for half in range(2):
            hsl = slice(half * 512, (half + 1) * 512)
            p.op("tensor", lambda e, hsl=hsl: e.matmul(psA[:, 6, :], onesf[:], morow[0:1, hsl], start=True, stop=True))
            p.op("scalar", lambda e, hsl=hsl: e.copy(slotA[:, hsl], psA[:, 6, :]))
        p.op("scalar", lambda e: e.activation(morow[0:1, 0:512], psA[0:1, 4, :], AF.Copy, scale=f32(1.0 / D)))
        p.op("scalar", lambda e: e.activation(morow[0:1, 512:1024], psA[0:1, 5, :], AF.Copy, scale=f32(1.0 / D)))
        p.op("vector", lambda e: e.tensor_tensor(slotC[0:1, :], slotA[0:1, :], slotA[0:1, :], op=ALU.mult))
        p.op("vector", lambda e: e.tensor_tensor(morow[0:1, :], morow[0:1, :], slotC[0:1, :], op=ALU.subtract))
        p.op("vector", lambda e: e.tensor_scalar_add(morow[0:1, :], morow[0:1, :], LN_EPS))
        p.op("scalar", lambda e: e.activation(morow[0:1, :], morow[0:1, :], AF.Sqrt))
        p.op("vector", lambda e: e.reciprocal(slotC[0:1, :], morow[0:1, :]))
        for half in range(2):
            hsl = slice(half * 512, (half + 1) * 512)
            p.op("tensor", lambda e, hsl=hsl: e.matmul(psA[:, 6, :], onesf[:], slotC[0:1, hsl], start=True, stop=True))
            p.op("scalar", lambda e, hsl=hsl: e.copy(slotB[:, hsl], psA[:, 6, :]))
        for c in range(NCH):
            p.op("vector", lambda e, c=c: e.tensor_copy(slotC[:], xnb[:, c, :]))
            p.op("vector", lambda e: e.tensor_tensor(slotC[:], slotC[:], slotA[:], op=ALU.subtract))
            p.op("vector", lambda e, c=c: e.tensor_tensor(xnb[:, c, :], slotC[:], slotB[:], op=ALU.mult))

        if debug:
            dma(p, "sync", dbg_xn, xnb[:, 0, :])
        floorE = p.mark()
        qtileE = (qtile, usb)
        wtlE3 = (wtl, wtl2, wtlB)
        chmE, ev0E, ev1E, outE = {}, {}, {}, {}
        for oc in range(32):
            wb = wtlE3[oc % 3]
            qt = qtileE[oc % 2]
            p.op("gpsimd", lambda e, wb=wb, oc=oc: e.dma_start(out=wb[:], in_=wq4_in[oc]),
                 inc=16, wait_at=chmE.get(oc - 3, floorE))
            dmam = p.mark()
            p.chain("tensor", [
                lambda e, c=c, wb=wb, st=(c == 0), sp=(c == NCH - 1): e.matmul(
                    psA[:, 0, :], wb[:, c, :], xnb[:, c, 0:512], start=st, stop=sp)
                for c in range(NCH)], wait_at=mmax(dmam, ev0E.get(oc - 1, floorE)))
            m0 = p.mark()
            p.chain("tensor", [
                lambda e, c=c, wb=wb, st=(c == 0), sp=(c == NCH - 1): e.matmul(
                    psA[:, 1, :], wb[:, c, :], xnb[:, c, 512:1024], start=st, stop=sp)
                for c in range(NCH)], wait_at=ev1E.get(oc - 1, floorE))
            chmE[oc] = p.mark()
            p.op("scalar", lambda e, qt=qt: e.copy(qt[:, 0:512], psA[:, 0, :]),
                 wait_at=mmax(m0, outE.get(oc - 2, floorE)))
            ev0E[oc] = p.mark()
            p.op("scalar", lambda e, qt=qt: e.copy(qt[:, 512:1024], psA[:, 1, :]),
                 wait_at=chmE[oc])
            ev1E[oc] = p.mark()
            p.op("sync", lambda e, qt=qt, oc=oc: e.dma_start(
                out=qscr.ap()[oc * 128:(oc + 1) * 128, :], in_=qt), inc=16,
                wait_at=ev1E[oc])
            outE[oc] = p.mark()

        floorF = p.mark()
        qtsP = (qts, qtsB)
        mKld, mVld = {}, {}
        mGkv, mDiagDone = {}, {}
        mSlast, mEvTT, mRecipF, mRcbF, mSumM = {}, {}, {}, {}, {}
        mSH, mExp, mPV, mAcc = {}, {}, {}, {}
        pendF = []

        def flushF(p=p):
            while pendF:
                pendF.pop(0)()

        for qg in range(2):
            for g in range(8):
                ga = qg * 8 + g
                for s in range(8):
                    p.op("sync", lambda e, g=g, s=s: e.dma_start(
                        out=kgb[:, s * 8:(s + 1) * 8, :].rearrange("p a b -> p (a b)"),
                        in_=kscr.ap()[g * 128:(g + 1) * 128, s * 1024:(s + 1) * 1024]),
                        inc=16, wait_at=(mGkv.get(ga - 1, floorF) if s == 0 else 0))
                    mKld[(ga, s)] = p.mark()
                    p.op("sync", lambda e, g=g, s=s: e.dma_start(
                        out=vgb[:, s * 8:(s + 1) * 8, :],
                        in_=vscr.ap()[s * 1024:(s + 1) * 1024, g * 128:(g + 1) * 128].rearrange(
                            "(a pp) h -> pp a h", pp=128)), inc=16, wait_at=0)
                    mVld[(ga, s)] = p.mark()
                p.op("sync", lambda e, ga=ga: e.dma_start(out=kdiag, in_=kdscr.ap()[ga]),
                     inc=16, wait_at=mDiagDone.get(ga - 1, floorF))
                p.op("sync", lambda e, ga=ga: e.dma_start(
                    out=vdiag, in_=vdscr.ap()[ga].rearrange("(a pp) h -> pp a h", pp=128)),
                    inc=16, wait_at=0)
                mKVd = p.mark()

                for ih in range(4):
                    h = 4 * g + ih
                    iq = ga * 4 + ih
                    qt_ = qtsP[iq % 2]
                    p.op("sync", lambda e, h=h, qg=qg, qt_=qt_: e.dma_start(
                        out=qt_[:], in_=qscr.ap()[h * 128:(h + 1) * 128, qg * 512:(qg + 1) * 512]),
                        inc=16, wait_at=mSlast.get(iq - 2, floorF))
                    mQ = p.mark()

                    def emit_S(sx, iq=iq, ga=ga, qt_=qt_, mQ=mQ, mKVd=mKVd):
                        b0 = (sx % 2) * 2
                        wW = (mExp.get((iq, sx - 2)) if sx >= 2
                              else mExp.get((iq - 1, 34 + sx), floorF))
                        if sx < 32:
                            w = mmax(wW, mKld[(ga, sx // 4)], mQ if sx == 0 else 0)
                            p.chain("tensor", [
                                lambda e, sx=sx, k=k, b0=b0, qt_=qt_: e.matmul(
                                    psA[:, b0 + k, :], kgb[:, 2 * sx + k, :], qt_[:],
                                    start=True, stop=True)
                                for k in range(2)], wait_at=w)
                        else:
                            dd = sx - 32
                            w0 = 128 * dd
                            p.op("tensor", lambda e, dd=dd, w0=w0, b0=b0, qt_=qt_: e.matmul(
                                psA[:, b0, w0:512], kdiag[:, dd * 128:(dd + 1) * 128], qt_[:, w0:512],
                                start=True, stop=True, skip_group_check=True),
                                wait_at=mmax(wW, mKVd))
                        mSH[(iq, sx)] = p.mark()

                    def emit_tail(sx, iq=iq, ga=ga, qg=qg, mKVd=mKVd):
                        b0 = (sx % 2) * 2
                        acP = (slotA, slotB)[sx % 2]
                        wPt = (mPV.get((iq, sx - 2)) if sx >= 2
                               else mPV.get((iq - 1, 35), floorF))
                        wAc = (mAcc.get((iq, sx - 2)) if sx >= 2
                               else mAcc.get((iq - 1, 35), floorF))
                        if sx < 32:
                            hb = sx % 2
                            quad = sx // 2
                            p.op("scalar", lambda e, qg=qg, quad=quad, hb=hb, b0=b0: e.activation(
                                ptb4[:, hb * 1024:(hb + 1) * 1024],
                                psA[:, b0:b0 + 2, :].rearrange("p a b -> p (a b)"), AF.Exp,
                                bias=biasq[:, qg * 16 + quad:qg * 16 + quad + 1], scale=f32(INV_SQRT_HD)),
                                wait_at=mmax(mSH[(iq, sx)], wPt, wAc))
                            mExp[(iq, sx)] = p.mark()
                            wPV = mmax(mExp[(iq, sx)], mVld[(ga, sx // 4)],
                                       mEvTT.get(iq - 1, floorF) if sx == 0 else 0,
                                       mRecipF.get(iq - 1, floorF) if sx == 0 else 0)
                            p.chain("tensor", [
                                lambda e, sx=sx, k=k, hb=hb, st=(sx == 0 and k == 0): e.matmul(
                                    psA[:, 4, :], vgb[:, 2 * sx + k, :],
                                    ptb4[:, hb * 1024 + k * 512:hb * 1024 + (k + 1) * 512],
                                    start=st, stop=False, skip_group_check=True)
                                for k in range(2)], wait_at=wPV)
                            mPV[(iq, sx)] = p.mark()
                            if sx <= 1:
                                p.op("vector", lambda e, acP=acP, hb=hb: e.tensor_copy(
                                    acP[:], ptb4[:, hb * 1024:(hb + 1) * 1024]),
                                    wait_at=mmax(mExp[(iq, sx)], mSumM.get(iq - 1, floorF)))
                            else:
                                p.op("vector", lambda e, acP=acP, hb=hb: e.tensor_tensor(
                                    acP[:], acP[:], ptb4[:, hb * 1024:(hb + 1) * 1024], op=ALU.add),
                                    wait_at=mExp[(iq, sx)])
                            mAcc[(iq, sx)] = p.mark()
                        else:
                            dd = sx - 32
                            w0 = 128 * dd
                            p.op("vector", lambda e, w0=w0, b0=b0: e.tensor_tensor(
                                psA[:, b0, w0:w0 + 128], psA[:, b0, w0:w0 + 128], tri[:], op=ALU.add),
                                wait_at=mSH[(iq, sx)])
                            mTri = p.mark()
                            p.op("scalar", lambda e, w0=w0, dd=dd, b0=b0: e.activation(
                                ptb4[:, dd * 512 + w0:dd * 512 + 512], psA[:, b0, w0:512], AF.Exp,
                                bias=f32(-EXP_SHIFT), scale=f32(INV_SQRT_HD)),
                                wait_at=mmax(mTri, wPt, wAc))
                            mExp[(iq, sx)] = p.mark()
                            p.op("tensor", lambda e, dd=dd, w0=w0: e.matmul(
                                psA[:, 4, w0:512], vdiag[:, dd, :],
                                ptb4[:, dd * 512 + w0:dd * 512 + 512],
                                start=False, stop=(dd == 3), skip_group_check=True),
                                wait_at=mmax(mExp[(iq, sx)], mKVd))
                            mPV[(iq, sx)] = p.mark()
                            p.op("vector", lambda e, acP=acP, w0=w0, dd=dd: e.tensor_tensor(
                                acP[:, w0:512], acP[:, w0:512],
                                ptb4[:, dd * 512 + w0:dd * 512 + 512], op=ALU.add),
                                wait_at=mExp[(iq, sx)])
                            mAcc[(iq, sx)] = p.mark()

                    emit_S(0)
                    flushF()
                    for s in range(1, 37):
                        if s < 36:
                            emit_S(s)
                        emit_tail(s - 1)
                    mSlast[iq] = mSH[(iq, 35)]
                    if ih == 3:
                        mGkv[ga] = mPV[(iq, 31)]
                        mDiagDone[ga] = mPV[(iq, 35)]

                    def epiF(iq=iq, h=h, qg=qg):
                        p.chain("tensor", [
                            lambda e, acP=acP, hf=hf, st=(j == 0), sp=(j == 3): e.matmul(
                                psA[0:1, 5, :], onescf[:], acP[:, hf * 512:(hf + 1) * 512],
                                start=st, stop=sp)
                            for j, (acP, hf) in enumerate(
                                ((slotA, 0), (slotA, 1), (slotB, 0), (slotB, 1)))],
                            wait_at=mmax(mAcc[(iq, 35)], mRecipF.get(iq - 1, floorF)))
                        mSumM[iq] = p.mark()
                        p.op("vector", lambda e: e.reciprocal(rs[:], psA[0:1, 5, :]),
                             wait_at=mSumM[iq])
                        mRecipF[iq] = p.mark()
                        p.op("tensor", lambda e: e.matmul(psA[:, 6, :], onesf[:], rs[:], start=True, stop=True),
                             wait_at=mmax(mRecipF[iq], mRcbF.get(iq - 1, floorF)))
                        mBc = p.mark()
                        p.op("scalar", lambda e: e.copy(rcb[:], psA[:, 6, :]),
                             wait_at=mmax(mBc, mEvTT.get(iq - 1, floorF)))
                        mRcbF[iq] = p.mark()
                        p.op("vector", lambda e, h=h, qg=qg: e.tensor_tensor(
                            ctxb[:, h, qg * 512:(qg + 1) * 512], psA[:, 4, :], rcb[:], op=ALU.mult),
                            wait_at=mmax(mRcbF[iq], mPV[(iq, 35)]))
                        mEvTT[iq] = p.mark()
                    pendF.append(epiF)
        flushF()

        if debug:
            dma(p, "sync", dbg_ctx, ctxb[:, 0, 0:512])
        dma(p, "gpsimd", lab, la4_in)
        for i in range(8):
            p.op("vector", lambda e, i=i: e.tensor_tensor(gts[:], wtown1[:, i, :], alb1[:], op=ALU.mult), drain=True)
            p.op("vector", lambda e, i=i: e.tensor_tensor(sel8[:], wtown2[:, i, :], alb2[:], op=ALU.mult))
            p.op("vector", lambda e: e.tensor_tensor(gts[:], gts[:], sel8[:], op=ALU.add), drain=True)
            p.op("tensor", lambda e: e.transpose(psA[0:8, 3, 0:128], gts[:], ident[:]))
            p.op("scalar", lambda e, i=i: e.copy(wfm[:, i * 128:(i + 1) * 128], psA[0:8, 3, 0:128]))
        p.op("tensor", lambda e: e.matmul(psA[:, 0, :], ssel_sb[:], wfm[:, 0:512], start=True, stop=True))
        p.op("tensor", lambda e: e.matmul(psA[:, 1, :], ssel_sb[:], wfm[:, 512:1024], start=True, stop=True))
        p.op("scalar", lambda e: e.activation(wbcb[:, 0:512], psA[:, 0, :], AF.Copy, scale=f32(1.0 / 16.0)))
        p.op("scalar", lambda e: e.activation(wbcb[:, 512:1024], psA[:, 1, :], AF.Copy, scale=f32(1.0 / 16.0)))
        for half in range(2):
            hsl = slice(half * 512, (half + 1) * 512)
            p.chain("tensor", [
                lambda e, c=c, half=half, hsl=hsl, st=(c == 0), sp=(c == NCH - 1): e.matmul(
                    psA[:, half, :], lab[:, c, :], xnb[:, c, hsl], start=st, stop=sp)
                for c in range(NCH)])
        p.op("vector", lambda e: e.tensor_tensor(usb[:, 0:512], psA[:, 0, :], wbcb[:, 0:512], op=ALU.mult))
        p.op("vector", lambda e: e.tensor_tensor(usb[:, 512:1024], psA[:, 1, :], wbcb[:, 512:1024], op=ALU.mult))

        if debug:
            dma(p, "sync", dbg_wb, wbcb[:])
            dma(p, "sync", dbg_cnt, cgrow[:])
            dma(p, "sync", dbg_al[:, 0:8], alb1[:])
            dma(p, "sync", dbg_al[:, 8:16], alb2[:])
            dma(p, "sync", dbg_wt[:, 0, :], wtown1[:].rearrange("p a b -> p (a b)"))
            dma(p, "sync", dbg_wt[:, 1, :], wtown2[:].rearrange("p a b -> p (a b)"))
            dma(p, "sync", dbg_hf, hscr.ap())
            dma(p, "sync", dbg_us, usb[:])
        p.op("vector", lambda e: e.memset(slotC[:], 0.0))
        p.op("vector", lambda e: e.memset(slotD[:], 0.0))
        floorH = p.mark()
        chWm, chFm = {}, {}
        vA0m, vA1m, sB0m, sB1m = {}, {}, {}, {}
        addm, sqm, st1m, st2m, cdm, outm, cpm = {}, {}, {}, {}, {}, {}, {}

        def stats_emit(j):
            p.chain("tensor", [
                lambda e: e.matmul(psA[0:1, 4, :], onesb[:], x1bb[:, 0:512], start=True, stop=True),
                lambda e: e.matmul(psA[0:1, 5, :], onesb[:], x1bb[:, 512:1024], start=True, stop=True),
            ], wait_at=mmax(cpm[j], cdm.get(j - 1, floorH)))
            st1m[j] = p.mark()
            p.op("vector", lambda e: e.tensor_tensor(slotC[0:1, 0:512], slotC[0:1, 0:512], psA[0:1, 4, :], op=ALU.add),
                 wait_at=st1m[j])
            p.op("vector", lambda e: e.tensor_tensor(slotC[0:1, 512:1024], slotC[0:1, 512:1024], psA[0:1, 5, :], op=ALU.add),
                 wait_at=0)
            m_c = p.mark()
            p.chain("tensor", [
                lambda e: e.matmul(psA[0:1, 4, :], onesb[:], sqx[:, 0:512], start=True, stop=True),
                lambda e: e.matmul(psA[0:1, 5, :], onesb[:], sqx[:, 512:1024], start=True, stop=True),
            ], wait_at=mmax(sqm[j], m_c))
            st2m[j] = p.mark()
            p.op("vector", lambda e: e.tensor_tensor(slotD[0:1, 0:512], slotD[0:1, 0:512], psA[0:1, 4, :], op=ALU.add),
                 wait_at=st2m[j])
            p.op("vector", lambda e: e.tensor_tensor(slotD[0:1, 512:1024], slotD[0:1, 512:1024], psA[0:1, 5, :], op=ALU.add),
                 wait_at=0)
            cdm[j] = p.mark()

        for oc in range(32):
            wb = wtlH[oc % 2]
            wb2 = wtl2H[oc % 2]
            lb_ = lbtH[oc % 2]
            p.op("gpsimd", lambda e, wb=wb, oc=oc: e.dma_start(out=wb[:], in_=wo4_in[oc]),
                 inc=16, wait_at=chWm.get(oc - 2, floorH))
            p.op("gpsimd", lambda e, wb2=wb2, oc=oc: e.dma_start(out=wb2[:], in_=fw4_in[oc]),
                 inc=16, wait_at=chFm.get(oc - 2, floorH))
            p.op("gpsimd", lambda e, lb_=lb_, oc=oc: e.dma_start(out=lb_[:], in_=lb_in[:, oc * 128:(oc + 1) * 128]),
                 inc=16, wait_at=0)
            dmam = p.mark()
            p.chain("tensor", [
                lambda e, c=c, wb=wb, st=(c == 0), sp=(c == NCH - 1): e.matmul(
                    psA[:, 0, :], wb[:, c, :], ctxb[:, c, 0:512], start=st, stop=sp)
                for c in range(NCH)], wait_at=mmax(dmam, vA0m.get(oc - 1, floorH)))
            m_s0 = p.mark()
            p.chain("tensor", [
                lambda e, c=c, wb=wb, st=(c == 0), sp=(c == NCH - 1): e.matmul(
                    psA[:, 1, :], wb[:, c, :], ctxb[:, c, 512:1024], start=st, stop=sp)
                for c in range(NCH)], wait_at=vA1m.get(oc - 1, floorH))
            m_s1 = p.mark()
            chWm[oc] = m_s1
            if oc > 0:
                stats_emit(oc - 1)
            for half in range(2):
                hsl = slice(half * 512, (half + 1) * 512)
                mm2 = [lambda e, c=c, half=half, hsl=hsl, wb2=wb2: e.matmul(
                    psA[:, 2 + half, :], wb2[:, c, :], xnb[:, c, hsl], start=(c == 0), stop=False)
                    for c in range(NCH)]
                mm2.append(lambda e, half=half, hsl=hsl, lb_=lb_: e.matmul(
                    psA[:, 2 + half, :], lb_[:], usb[:, hsl], start=False, stop=True))
                p.chain("tensor", mm2,
                        wait_at=(sB0m if half == 0 else sB1m).get(oc - 1, floorH))
                if half == 0:
                    m_f0 = p.mark()
            m_f1 = p.mark()
            chFm[oc] = m_f1
            p.op("vector", lambda e, oc=oc: e.tensor_scalar(slotA[:, 0:512], psA[:, 0, :], boc[:, oc:oc + 1], None, op0=ALU.add),
                 wait_at=mmax(m_s0, cpm.get(oc - 1, floorH), outm.get(oc - 1, floorH), st1m.get(oc - 1, floorH)))
            vA0m[oc] = p.mark()
            p.op("vector", lambda e, oc=oc: e.tensor_scalar(slotA[:, 512:1024], psA[:, 1, :], boc[:, oc:oc + 1], None, op0=ALU.add),
                 wait_at=m_s1)
            vA1m[oc] = p.mark()
            p.op("scalar", lambda e: e.activation(slotB[:, 0:512], psA[:, 2, :], AF.Silu),
                 wait_at=mmax(m_f0, addm.get(oc - 1, floorH)))
            sB0m[oc] = p.mark()
            p.op("scalar", lambda e: e.activation(slotB[:, 512:1024], psA[:, 3, :], AF.Silu),
                 wait_at=m_f1)
            sB1m[oc] = p.mark()
            if debug and oc == 1:
                dma(p, "sync", dbg_sg[:, 0, :], slotA[:])
                dma(p, "sync", dbg_sg[:, 1, :], slotB[:])
            p.op("vector", lambda e: e.tensor_tensor(slotA[:], slotA[:], slotB[:], op=ALU.mult),
                 wait_at=sB1m[oc])
            m_vM = p.mark()
            p.op("sync", lambda e, oc=oc: e.dma_start(out=slotB[:], in_=xor_[:, oc, :]),
                 inc=16, wait_at=m_vM)
            m_x = p.mark()
            p.op("vector", lambda e: e.tensor_tensor(slotA[:], slotB[:], slotA[:], op=ALU.add),
                 wait_at=m_x)
            addm[oc] = p.mark()
            p.op("sync", lambda e, oc=oc: e.dma_start(out=x1r[:, oc, :], in_=slotA[:]),
                 inc=16, wait_at=addm[oc])
            outm[oc] = p.mark()
            p.op("vector", lambda e: e.tensor_copy(x1bb, slotA[:]),
                 wait_at=st1m.get(oc - 1, floorH))
            cpm[oc] = p.mark()
            p.op("scalar", lambda e: e.activation(sqx, x1bb, AF.Square),
                 wait_at=mmax(cpm[oc], st2m.get(oc - 1, floorH)))
            sqm[oc] = p.mark()
        stats_emit(31)
        if debug:
            dma(p, "sync", dbg_st[0:1, :], slotC[0:1, :])
            dma(p, "sync", dbg_st[1:2, :], slotD[0:1, :])
        p.op("vector", lambda e: e.tensor_scalar_mul(morow[0:1, :], slotC[0:1, :], 1.0 / D))
        p.op("vector", lambda e: e.tensor_scalar_mul(slotD[0:1, :], slotD[0:1, :], 1.0 / D))
        p.op("vector", lambda e: e.tensor_tensor(slotC[0:1, :], morow[0:1, :], morow[0:1, :], op=ALU.mult))
        p.op("vector", lambda e: e.tensor_tensor(slotD[0:1, :], slotD[0:1, :], slotC[0:1, :], op=ALU.subtract))
        p.op("vector", lambda e: e.tensor_scalar_add(slotD[0:1, :], slotD[0:1, :], LN_EPS))
        p.op("scalar", lambda e: e.activation(slotD[0:1, :], slotD[0:1, :], AF.Sqrt))
        p.op("vector", lambda e: e.reciprocal(slotC[0:1, :], slotD[0:1, :]))
        if debug:
            dma(p, "sync", dbg_st[2:3, :], morow[0:1, :])
            dma(p, "sync", dbg_st[3:4, :], slotC[0:1, :])
        for half in range(2):
            hsl = slice(half * 512, (half + 1) * 512)
            p.op("tensor", lambda e, hsl=hsl: e.matmul(psA[:, 6, :], onesf[:], morow[0:1, hsl], start=True, stop=True))
            p.op("scalar", lambda e, hsl=hsl: e.copy(slotA[:, hsl], psA[:, 6, :]))
            p.op("tensor", lambda e, hsl=hsl: e.matmul(psA[:, 6, :], onesf[:], slotC[0:1, hsl], start=True, stop=True))
            p.op("scalar", lambda e, hsl=hsl: e.copy(slotB[:, hsl], psA[:, 6, :]))
        if debug:
            dma(p, "sync", dbg_st[4:5, :], slotA[0:1, :])
            dma(p, "sync", dbg_st[5:6, :], slotB[0:1, :])
        floorH2 = p.mark()
        mrdN, mcpN = {}, {}
        for c in range(NCH):
            xb_ = (x1bb, sqx)[c % 2]
            p.op("gpsimd", lambda e, xb_=xb_, c=c: e.dma_start(out=xb_, in_=x1r[:, c, :]),
                 inc=16, wait_at=mcpN.get(c - 2, floorH2))
            mrdN[c] = p.mark()
            p.op("vector", lambda e, xb_=xb_: e.tensor_copy(slotC[:], xb_),
                 wait_at=(mmax(mrdN[c], floorH2) if c == 0 else mrdN[c]))
            mcpN[c] = p.mark()
            p.op("vector", lambda e: e.tensor_tensor(slotC[:], slotC[:], slotA[:], op=ALU.subtract),
                 wait_at=0)
            p.op("vector", lambda e, c=c: e.tensor_tensor(xnb[:, c, :], slotC[:], slotB[:], op=ALU.mult),
                 wait_at=0)

        if debug:
            dma(p, "sync", dbg_xn1, xnb[:, 0, :])
        floorI = p.mark()
        wtlI3 = (wtl, wtl2, ctb2)
        chmI, ev0I, ev1I, outI = {}, {}, {}, {}
        for wi, (w4, bc, scr) in enumerate(
                ((l1q4_in, bqc, q1scr), (l1k4_in, bkc, k1scr), (l1v4_in, bvc, v1scr))):
            for oc in range(32):
                u = wi * 32 + oc
                wb = wtlI3[u % 3]
                qt = qtileE[u % 2]
                p.op("gpsimd", lambda e, wb=wb, w4=w4, oc=oc: e.dma_start(out=wb[:], in_=w4[oc]),
                     inc=16, wait_at=chmI.get(u - 3, floorI))
                dmam = p.mark()
                p.chain("tensor", [
                    lambda e, c=c, wb=wb, st=(c == 0), sp=(c == NCH - 1): e.matmul(
                        psA[:, 0, :], wb[:, c, :], xnb[:, c, 0:512], start=st, stop=sp)
                    for c in range(NCH)], wait_at=mmax(dmam, ev0I.get(u - 1, floorI)))
                m0 = p.mark()
                p.chain("tensor", [
                    lambda e, c=c, wb=wb, st=(c == 0), sp=(c == NCH - 1): e.matmul(
                        psA[:, 1, :], wb[:, c, :], xnb[:, c, 512:1024], start=st, stop=sp)
                    for c in range(NCH)], wait_at=ev1I.get(u - 1, floorI))
                chmI[u] = p.mark()
                p.op("vector", lambda e, oc=oc, bc=bc, qt=qt: e.tensor_scalar(
                    qt[:, 0:512], psA[:, 0, :], bc[:, oc:oc + 1], None, op0=ALU.add),
                    wait_at=mmax(m0, outI.get(u - 2, floorI)))
                ev0I[u] = p.mark()
                p.op("vector", lambda e, oc=oc, bc=bc, qt=qt: e.tensor_scalar(
                    qt[:, 512:1024], psA[:, 1, :], bc[:, oc:oc + 1], None, op0=ALU.add),
                    wait_at=chmI[u])
                ev1I[u] = p.mark()
                p.op("sync", lambda e, qt=qt, oc=oc, scr=scr: e.dma_start(
                    out=scr.ap()[oc * 128:(oc + 1) * 128, :], in_=qt), inc=16,
                    wait_at=ev1I[u])
                outI[u] = p.mark()
        floorI2 = p.mark()
        mLastPE, mOutI2 = {}, {}
        for blk in range(8):
            bs = slice(blk * 128, (blk + 1) * 128)
            first = True
            for buf, scr in ((qb, q1scr), (kb, k1scr), (vb, v1scr)):
                for i4 in range(4):
                    p.op("sync", lambda e, buf=buf, scr=scr, i4=i4, bs=bs: e.dma_start(
                        out=buf[:, i4 * 8:(i4 + 1) * 8, :],
                        in_=scr.ap().rearrange("(hh pp) n -> pp hh n", pp=128)[:, i4 * 8:(i4 + 1) * 8, bs]),
                        inc=16, wait_at=(mLastPE.get(blk - 1, floorI2) if first else 0))
                    first = False
            mLd = p.mark()
            mExpG, mPVG, mVcpG, mEvG, mRcb2G = {}, {}, {}, {}, {}
            pendI = []
            for gi in range(8):
                hh = gi * 16
                p.chain("tensor", [
                    lambda e, j=j, hh=hh: e.matmul(
                        psA[0:32, 1, j * 32:(j + 1) * 32],
                        kb[:, :, hh + j], qb[:, :, hh + j], start=True, stop=True)
                    for j in range(16)], wait_at=(mmax(mLd, mExpG.get(gi - 1, floorI2))
                                                  if gi == 0 else mExpG[gi - 1]))
                mS = p.mark()
                for half8 in range(2):
                    p.chain("tensor", [
                        lambda e, j=j, hh=hh, half8=half8: e.transpose(
                            psVt[0:32, 0, (j % 8) * 128:(j % 8 + 1) * 128],
                            vb[:, :, hh + half8 * 8 + j], identb[:])
                        for j in range(8)],
                        wait_at=(mmax(mVcpG.get(gi - 1, floorI2), mLd if gi == 0 else 0)
                                 if half8 == 0 else mVcpG[(gi, 0)]))
                    mT = p.mark()
                    p.op("scalar", lambda e, half8=half8: e.copy(
                        vts1[0:32, half8 * 8:(half8 + 1) * 8, :],
                        psVt[0:32, 0, :].rearrange("p (j dd) -> p j dd", dd=128)),
                        wait_at=(mmax(mT, mPVG.get(gi - 1, floorI2)) if half8 == 0 else mT))
                    mVcpG[(gi, half8)] = p.mark()
                mVcpG[gi] = mVcpG[(gi, 1)]
                while pendI:
                    pendI.pop(0)()
                p.op("scalar", lambda e: e.activation(
                    pt1, psA[0:32, 1, :], AF.Exp, bias=f32(-EXP_SHIFT), scale=f32(INV_SQRT_HD)),
                    wait_at=mmax(mS, mPVG.get(gi - 1, floorI2)))
                mExpG[gi] = p.mark()
                p.op("tensor", lambda e: e.matmul(psA[0:1, 2, :], onesb[0:32, :], pt1, start=True, stop=True),
                     wait_at=mmax(mExpG[gi], mRcb2G.get(gi - 1, floorI2)))
                mSum = p.mark()
                p.op("vector", lambda e: e.reciprocal(rs[:], psA[0:1, 2, :]),
                     wait_at=mSum)
                mRc = p.mark()
                p.chain("tensor", [
                    lambda e, j=j: e.matmul(
                        psA[:, 3, j * 32:(j + 1) * 32], vts1[0:32, j, :], pt1[:, j * 32:(j + 1) * 32],
                        start=True, stop=True)
                    for j in range(16)], wait_at=mmax(mVcpG[gi], mExpG[gi], mEvG.get(gi - 1, floorI2)))
                mPVG[gi] = p.mark()

                def epiI(gi=gi, hh=hh, mRc=mRc, blk=blk):
                    p.op("tensor", lambda e: e.matmul(psA[:, 2, :], onesf[:], rs[:], start=True, stop=True),
                         wait_at=mRc)
                    mBc2 = p.mark()
                    p.op("scalar", lambda e: e.copy(rcb[:], psA[:, 2, :]),
                         wait_at=mmax(mBc2, mEvG.get(gi - 1, floorI2)))
                    mRcb2G[gi] = p.mark()
                    p.op("vector", lambda e, hh=hh: e.tensor_tensor(
                        ctb2[:, :, hh:hh + 16].rearrange("p h n -> p n h"),
                        psA[:, 3, :].rearrange("p (n h) -> p n h", h=32),
                        rcb[:].rearrange("p (n h) -> p n h", h=32),
                        op=ALU.mult),
                        wait_at=mmax(mPVG[gi], mRcb2G[gi],
                                     mOutI2.get(blk - 1, floorI2) if gi == 0 else 0))
                    mEvG[gi] = p.mark()
                pendI.append(epiI)
            while pendI:
                pendI.pop(0)()
            mLastPE[blk] = mPVG[7]
            for i4 in range(4):
                sl = slice(i4 * 8, (i4 + 1) * 8)
                p.op("sync", lambda e, sl=sl, bs=bs: e.dma_start(
                    out=ctxscr.ap().rearrange("(hh pp) n -> pp hh n", pp=128)[:, sl, bs],
                    in_=ctb2[:, sl, :]), inc=16, wait_at=(mEvG[7] if i4 == 0 else 0))
            mOutI2[blk] = p.mark()

        dma(p, "gpsimd", a1b, a1_in)
        dma(p, "gpsimd", b1tb, b1t_in)
        p.op("vector", lambda e: e.memset(cnacc[:], 0.0))
        floorJ = p.mark()
        w1nbP = (w1nb_f, w1nbB)
        effbP = (effb, effbB)
        effsP = (effs, effsB)
        mVaJ, mSqJ, mRdJ, mDmJ = {}, {}, {}, {}
        for oc in range(32):
            wn = w1nbP[oc % 2]
            p.op("gpsimd", lambda e, wn=wn, oc=oc: e.dma_start(
                out=wn, in_=w1n4_in[oc].rearrange("p c i -> p (c i)")),
                inc=16, wait_at=mVaJ.get((oc - 2) * 8 + 7, floorJ))
            mDmJ[oc] = p.mark()
            for ib in range(8):
                u = oc * 8 + ib
                bq = u % 2
                eb_ = effbP[u % 2]
                es_ = effsP[u % 2]
                p.op("tensor", lambda e, oc=oc, ib=ib, bq=bq: e.matmul(
                    psA[:, bq, :], b1tb[:, oc * 128:(oc + 1) * 128], a1b[:, ib * 512:(ib + 1) * 512],
                    start=True, stop=True), wait_at=mmax(mDmJ[oc], mVaJ.get(u - 2, floorJ)))
                mmj = p.mark()
                p.op("vector", lambda e, oc=oc, ib=ib, bq=bq, eb_=eb_, wn=wn: e.tensor_tensor(
                    eb_, psA[:, bq, :], wn[:, ib * 512:(ib + 1) * 512], op=ALU.add),
                    wait_at=mmax(mmj, mSqJ.get(u - 2, floorJ)))
                mVaJ[u] = p.mark()
                p.op("scalar", lambda e, eb_=eb_, es_=es_: e.activation(es_, eb_, AF.Square),
                     wait_at=mmax(mVaJ[u], mRdJ.get(u - 2, floorJ)))
                mSqJ[u] = p.mark()
                p.op("vector", lambda e, es_=es_: e.reduce_sum(rs1[:], es_, axis=AXX),
                     wait_at=mSqJ[u])
                mRdJ[u] = p.mark()
                p.op("vector", lambda e, oc=oc: e.tensor_tensor(cnacc[:, oc:oc + 1], cnacc[:, oc:oc + 1], rs1[:], op=ALU.add),
                     wait_at=0, drain=True)
        p.op("scalar", lambda e: e.activation(cnacc[:], cnacc[:], AF.Sqrt))
        p.op("vector", lambda e: e.reciprocal(rsc[:], cnacc[:]))
        p.op("vector", lambda e: e.tensor_tensor(rsc[:], rsc[:], magc[:], op=ALU.mult), drain=True)

        dma(p, "gpsimd", a1tb, a1t4_in)
        for half in range(2):
            hsl = slice(half * 512, (half + 1) * 512)
            p.chain("tensor", [
                lambda e, c=c, half=half, hsl=hsl, st=(c == 0), sp=(c == NCH - 1): e.matmul(
                    psA[0:32, 2 + half, :], a1tb[:, c, :], xnb[:, c, hsl], start=st, stop=sp)
                for c in range(NCH)])
        p.op("scalar", lambda e: e.copy(u1b[:, 0:512], psA[0:32, 2, :]))
        p.op("scalar", lambda e: e.copy(u1b[:, 512:1024], psA[0:32, 3, :]))
        floorK = p.mark()
        ctxrdP = (ctxrd, gw1b.rearrange("p a b -> p (a b)"))
        chmK, vA0K, vA1K, cpK, addK, out2K = {}, {}, {}, {}, {}, {}
        for oc in range(32):
            wb2 = wtl2H[oc % 2]
            cx_ = ctxrdP[oc % 2]
            p.op("gpsimd", lambda e, wb2=wb2, oc=oc: e.dma_start(out=wb2[:], in_=w1t4_in[oc]),
                 inc=16, wait_at=chmK.get(oc - 2, floorK))
            p.op("sync", lambda e, oc=oc, cx_=cx_: e.dma_start(
                out=cx_, in_=ctxscr.ap()[oc * 128:(oc + 1) * 128, :]), inc=16,
                wait_at=cpK.get(oc - 2, floorK))
            dmam = p.mark()
            for half in range(2):
                hsl = slice(half * 512, (half + 1) * 512)
                mm = [lambda e, c=c, half=half, hsl=hsl, wb2=wb2: e.matmul(
                    psA[:, half, :], wb2[:, c, :], xnb[:, c, hsl], start=(c == 0), stop=False)
                    for c in range(NCH)]
                mm.append(lambda e, half=half, hsl=hsl, oc=oc: e.matmul(
                    psA[:, half, :], b1tb[:, oc * 128:(oc + 1) * 128], u1b[:, hsl], start=False, stop=True))
                p.chain("tensor", mm, wait_at=(
                    mmax(dmam, vA0K.get(oc - 1, floorK)) if half == 0
                    else vA1K.get(oc - 1, floorK)))
                if half == 0:
                    m_k0 = p.mark()
            m_k1 = p.mark()
            chmK[oc] = m_k1
            p.op("vector", lambda e, oc=oc: e.tensor_scalar(slotA[:, 0:512], psA[:, 0, :], rsc[:, oc:oc + 1], None, op0=ALU.mult),
                 wait_at=mmax(m_k0, out2K.get(oc - 1, floorK)), drain=(oc == 0))
            vA0K[oc] = p.mark()
            p.op("vector", lambda e, oc=oc: e.tensor_scalar(slotA[:, 512:1024], psA[:, 1, :], rsc[:, oc:oc + 1], None, op0=ALU.mult),
                 wait_at=m_k1)
            vA1K[oc] = p.mark()
            p.op("scalar", lambda e: e.activation(slotB[:], slotA[:], AF.Silu),
                 wait_at=mmax(vA1K[oc], addK.get(oc - 1, floorK)))
            m_sil = p.mark()
            p.op("vector", lambda e, cx_=cx_: e.tensor_copy(slotA[:], cx_),
                 wait_at=m_sil)
            cpK[oc] = p.mark()
            p.op("vector", lambda e: e.tensor_tensor(slotA[:], slotA[:], slotB[:], op=ALU.mult),
                 wait_at=0)
            m_mu = p.mark()
            p.op("sync", lambda e, oc=oc: e.dma_start(out=slotB[:], in_=x1r[:, oc, :]),
                 inc=16, wait_at=m_mu)
            m_x1 = p.mark()
            p.op("vector", lambda e: e.tensor_tensor(slotA[:], slotB[:], slotA[:], op=ALU.add),
                 wait_at=m_x1)
            addK[oc] = p.mark()
            p.op("sync", lambda e, oc=oc: e.dma_start(out=x2r[:, oc, :], in_=slotA[:]),
                 inc=16, wait_at=addK[oc])
            out2K[oc] = p.mark()
        if debug:
            dma(p, "sync", dbg_mr, mrscr.ap()[:, 0:512])
            dma(p, "sync", dbg_k, kscr.ap()[0:128, 0:512])
            dma(p, "sync", dbg_v, vscr.ap()[0:256, 0:128])
            dma(p, "sync", dbg_h, hscr.ap()[:, 0:512])
            dma(p, "sync", dbg_q, qscr.ap()[0:128, :])
            dma(p, "sync", dbg_x1, x1scr.ap()[0:128, :])
            dma(p, "sync", dbg_c1, ctxscr.ap()[0:128, :])
            dma(p, "sync", dbg_q1, q1scr.ap())
            dma(p, "sync", dbg_x1b[:, 0, :], x1r[:, 10, :])
            dma(p, "sync", dbg_x1b[:, 1, :], x1r[:, 20, :])
            dma(p, "sync", dbg_x1b[:, 2, :], x1r[:, 31, :])
            dma(p, "sync", dbg_k1, k1scr.ap())
            dma(p, "sync", dbg_v1, v1scr.ap())
            dma(p, "sync", dbg_c1f, ctxscr.ap())
        p.finish(block)
    return nc


def kernel_fused(inputs, debug=False):
    c32 = np.ascontiguousarray
    xT = c32(np.asarray(inputs["L0_attn_x"], np.float32).reshape(T, D).T)
    Wq = np.asarray(inputs["L0_attn_Wq"], np.float32)
    Wkv = np.asarray(inputs["L0_attn_Wkv"], np.float32)
    WK = c32(Wkv.reshape(D, 8, 2, 128)[:, :, 0, :].reshape(D, 1024))
    WV = c32(Wkv.reshape(D, 8, 2, 128)[:, :, 1, :].reshape(D, 1024))

    def pcf(w):
        return c32(w.reshape(NCH, 128, w.shape[1]).transpose(1, 0, 2))
    wkk4 = pcf(WK)
    wvv4 = pcf(WV)
    gw1t4 = pcf(c32(np.asarray(inputs["L0_ffn_gw1"], np.float32).T))
    wq4 = _tile_w(Wq)
    wo4 = _tile_w(np.asarray(inputs["L0_attn_Wo"], np.float32))
    fw4 = _tile_w(c32(np.asarray(inputs["L0_ffn_weight"], np.float32).T))
    la4 = c32(np.asarray(inputs["L0_ffn_lora_A"], np.float32).transpose(1, 0, 2).reshape(D, 128)
              .reshape(NCH, 128, 128).transpose(1, 0, 2))
    lb = c32(np.asarray(inputs["L0_ffn_lora_B"], np.float32).reshape(128, D))
    ssel = np.zeros((8, 128), np.float32)
    for e_ in range(8):
        ssel[e_, e_ * 16:(e_ + 1) * 16] = 1.0
    l1q4 = _tile_w(c32(np.asarray(inputs["L1_attn_wq"], np.float32).T))
    l1k4 = _tile_w(c32(np.asarray(inputs["L1_attn_wk"], np.float32).T))
    l1v4 = _tile_w(c32(np.asarray(inputs["L1_attn_wv"], np.float32).T))
    W1 = np.asarray(inputs["L1_ffn_W"], np.float32)
    w1t4 = _tile_w(c32(W1.T))
    w1n4 = c32(W1.reshape(32, 128, NCH, 128))
    A1 = np.asarray(inputs["L1_ffn_lora_A"], np.float32)
    B1 = np.asarray(inputs["L1_ffn_lora_B"], np.float32)
    a1t4 = c32(A1.T.reshape(NCH, 128, 32).transpose(1, 0, 2))
    b1t = c32(B1.T)
    tri = np.where(np.arange(128)[:, None] > np.arange(128)[None, :],
                   np.float32(NEG), np.float32(0.0)).astype(np.float32)
    common = {
        "xfull": xT,
        "wkk4": wkk4, "wvv4": wvv4, "gw1t4": gw1t4,
        "gb1c": c32(np.asarray(inputs["L0_ffn_gb1"], np.float32).reshape(32, 1)),
        "gam": c32(np.asarray(inputs["L0_ffn_ln_gamma"], np.float32).reshape(1, 32)),
        "bet": c32(np.asarray(inputs["L0_ffn_ln_beta"], np.float32).reshape(1, 32)),
        "gw2t": c32(np.asarray(inputs["L0_ffn_gw2"], np.float32).T),
        "gb2": c32(np.asarray(inputs["L0_ffn_gb2"], np.float32).reshape(1, 8)),
        "wq4": wq4, "wo4": wo4, "fw4": fw4,
        "boc": _to_col(np.asarray(inputs["L0_attn_bo"], np.float32)),
        "la4": la4, "lb": lb, "ssel": ssel,
        "l1q4": l1q4, "l1k4": l1k4, "l1v4": l1v4,
        "bqc": _to_col(np.asarray(inputs["L1_attn_bq"], np.float32)),
        "bkc": _to_col(np.asarray(inputs["L1_attn_bk"], np.float32)),
        "bvc": _to_col(np.asarray(inputs["L1_attn_bv"], np.float32)),
        "w1t4": w1t4, "w1n4": w1n4, "a1t4": a1t4, "a1": A1, "b1t": b1t,
        "magc": _to_col(np.asarray(inputs["L1_ffn_magnitude"], np.float32)),
        "identf": _ident(), "identb": np.eye(128).astype(ml_dtypes.bfloat16),
        "tri": tri,
    }
    in_maps = []
    for c in range(NCORE):
        biasq = np.empty((128, 32), np.float32)
        for qg in range(2):
            band = 2 * c + qg
            for quad in range(16):
                biasq[:, qg * 16 + quad] = -EXP_SHIFT if quad < band else -1e9
        m = dict(common)
        m["xown"] = c32(xT[:, c * TL:(c + 1) * TL])
        m["biasq"] = biasq
        in_maps.append(m)
    nc = build_fused(debug=debug)
    res = run_bass_kernel_spmd(nc, in_maps, core_ids=list(range(NCORE)))
    if debug:
        return res
    x2 = np.concatenate([res.results[c]["x2"] for c in range(NCORE)], axis=1)
    return np.ascontiguousarray(x2.T).reshape(1, T, D).astype(np.float32)


# version 10
# speedup vs baseline: 1.6809x; 1.0068x over previous
import contextlib
import numpy as np
import ml_dtypes
import concourse.bass as bass
import concourse.mybir as mybir
from concourse.bass_utils import run_bass_kernel_spmd

F32 = mybir.dt.float32
BF16 = mybir.dt.bfloat16
I32 = mybir.dt.int32
AF = mybir.ActivationFunctionType
ALU = mybir.AluOpType
AX = mybir.AxisListType
f32 = np.float32

NCORE = 8
D = 4096
T = 8192
TL = T // NCORE
NCH = D // 128
EXP_SHIFT = 16.0
INV_SQRT_HD = 1.0 / float(np.sqrt(128.0))
LN_EPS = 1e-5
CAPACITY = 3072.0
NEG = -1e30

ENGINES = ("sync", "gpsimd", "scalar", "vector", "tensor")
ENG_TYPE = {"sync": mybir.EngineType.SP, "gpsimd": mybir.EngineType.Pool,
            "scalar": mybir.EngineType.Activation, "vector": mybir.EngineType.DVE,
            "tensor": mybir.EngineType.PE}


class Prog:

    def __init__(self, nc, es):
        self.nc = nc
        self.sems = {e: es.enter_context(nc.semaphore("sem_" + e)) for e in ENGINES}
        self.cnt = {e: 0 for e in ENGINES}
        self.seen = {e: {x: 0 for x in ENGINES} for e in ENGINES}
        self.items = {e: [] for e in ENGINES}
        self.force_drain = False

    def mark(self):
        return dict(self.cnt)

    def _waits(self, eng, wait_at):
        if wait_at is None:
            wait_at = self.cnt
        elif wait_at == 0:
            wait_at = {}
        seen = self.seen[eng]
        ws = []
        for x, v in wait_at.items():
            if x == eng or v <= 0 or v <= seen[x]:
                continue
            seen[x] = v
            ws.append((self.sems[x], v))
        return ws

    def op(self, eng, fn, inc=1, wait_at=None, drain=False):
        if eng == "vector" and self.force_drain:
            drain = True
        ws = self._waits(eng, wait_at)
        self.cnt[eng] += inc

        def emit(e, fn=fn, ws=ws, inc=inc, sem=self.sems[eng], drain=drain):
            for s, v in ws:
                e.wait_ge(s, v)
            if drain:
                e.drain()
            fn(e).then_inc(sem, inc)
        self.items[eng].append(emit)

    def switch(self, eng, body_fn, inc, wait_at=None):
        ws = self._waits(eng, wait_at)
        self.cnt[eng] += inc
        nc = self.nc

        def emit(e, body_fn=body_fn, ws=ws, sem=self.sems[eng], nc=nc, eng=eng):
            for s, v in ws:
                e.wait_ge(s, v)
            pid = nc.partition_id(engines=[ENG_TYPE[eng]])
            for c in nc.Switch(engines=[e], index=[pid], n=8):
                body_fn(e, c, sem)
        self.items[eng].append(emit)

    def chain(self, eng, fns, inc=1, wait_at=None):
        ws = self._waits(eng, wait_at)
        self.cnt[eng] += inc

        def emit(e, fns=fns, ws=ws, inc=inc, sem=self.sems[eng]):
            for s, v in ws:
                e.wait_ge(s, v)
            ins = None
            for fn in fns:
                ins = fn(e)
            ins.then_inc(sem, inc)
        self.items[eng].append(emit)

    def finish(self, block):
        finals = dict(self.cnt)

        for name in ENGINES:
            items = self.items[name]

            def body(e, items=items, name=name, finals=finals, sems=self.sems):
                for emit in items:
                    emit(e)
                for x in ENGINES:
                    if x != name and finals[x] > 0:
                        e.wait_ge(sems[x], finals[x])
            getattr(block, name)(body)


def mmax(*marks):
    out = {}
    for m in marks:
        if not m:
            continue
        for k, v in m.items():
            if v > out.get(k, 0):
                out[k] = v
    return out


def dma(p, eng, out, in_):
    p.op(eng, lambda e, out=out, in_=in_: e.dma_start(out=out, in_=in_), inc=16)


def _ident():
    return np.eye(128, dtype=np.float32)


def _to_col(v):
    return np.ascontiguousarray(v.reshape(-1, 128).T)


def _tile_w(w_in_out):
    din, dout = w_in_out.shape
    a = w_in_out.reshape(din // 128, 128, dout // 128, 128)
    return np.ascontiguousarray(a.transpose(2, 1, 0, 3))



def build_A():
    nc = bass.Bass("TRN2", target_bir_lowering=False)
    x_in = nc.dram_tensor("x", [D, TL], F32, kind="ExternalInput").ap()
    gw1t_in = nc.dram_tensor("gw1t", [D, 32], F32, kind="ExternalInput").ap()
    gb1c_in = nc.dram_tensor("gb1c", [32, 1], F32, kind="ExternalInput").ap()
    gam_in = nc.dram_tensor("gam", [1, 32], F32, kind="ExternalInput").ap()
    bet_in = nc.dram_tensor("bet", [1, 32], F32, kind="ExternalInput").ap()
    gw2t_in = nc.dram_tensor("gw2t", [32, 8], F32, kind="ExternalInput").ap()
    gb2_in = nc.dram_tensor("gb2", [1, 8], F32, kind="ExternalInput").ap()
    id_in = nc.dram_tensor("ident", [128, 128], F32, kind="ExternalInput").ap()
    xn_out = nc.dram_tensor("xn", [D, TL], BF16, kind="ExternalOutput").ap()
    cnt_out = nc.dram_tensor("cnt", [1, 16], F32, kind="ExternalOutput").ap()
    wg_out = nc.dram_tensor("wg", [2, TL, 8], F32, kind="ExternalOutput").ap()

    es = contextlib.ExitStack()
    with es:
        E = es.enter_context
        xb = E(nc.sbuf_tensor("xb", [128, NCH, TL], BF16))
        xnb = E(nc.sbuf_tensor("xnb", [128, NCH, TL], BF16))
        sq = E(nc.sbuf_tensor("sq", [128, TL], BF16))
        xf = E(nc.sbuf_tensor("xf", [128, TL], F32))
        gw1b = E(nc.sbuf_tensor("gw1b", [128, NCH, 32], BF16))
        gb1c = E(nc.sbuf_tensor("gb1c_s", [32, 1], F32))
        gamr = E(nc.sbuf_tensor("gamr", [1, 32], F32))
        betr = E(nc.sbuf_tensor("betr", [1, 32], F32))
        gw2t = E(nc.sbuf_tensor("gw2ts", [32, 8], F32))
        gb2r = E(nc.sbuf_tensor("gb2r", [1, 8], F32))
        ident = E(nc.sbuf_tensor("identsb", [128, 128], F32))
        onesf = E(nc.sbuf_tensor("onesf", [1, 128], F32))
        onesb = E(nc.sbuf_tensor("onesb", [128, 1], BF16))
        mrow = E(nc.sbuf_tensor("mrow", [1, TL], F32))
        qrow = E(nc.sbuf_tensor("qrow", [1, TL], F32))
        vrow = E(nc.sbuf_tensor("vrow", [1, TL], F32))
        rrow = E(nc.sbuf_tensor("rrow", [1, TL], F32))
        mbc = E(nc.sbuf_tensor("mbc", [128, TL], F32))
        rbc = E(nc.sbuf_tensor("rbc", [128, TL], F32))
        hfm = E(nc.sbuf_tensor("hfm", [32, TL], F32))
        gamb = E(nc.sbuf_tensor("gamb", [128, 32], F32))
        betb = E(nc.sbuf_tensor("betb", [128, 32], F32))
        gb2b = E(nc.sbuf_tensor("gb2b", [128, 8], F32))
        htm = E(nc.sbuf_tensor("htm", [128, 32], F32))
        hm = E(nc.sbuf_tensor("hm", [128, 1], F32))
        hv = E(nc.sbuf_tensor("hv", [128, 1], F32))
        hrs = E(nc.sbuf_tensor("hrs", [128, 1], F32))
        hsq = E(nc.sbuf_tensor("hsq", [128, 32], F32))
        hft = E(nc.sbuf_tensor("hft", [32, 128], F32))
        gts = E(nc.sbuf_tensor("gts", [128, 8], F32))
        iotf = E(nc.sbuf_tensor("iotf", [128, 8], F32))
        ioti = E(nc.sbuf_tensor("ioti", [128, 8], I32))
        v1 = E(nc.sbuf_tensor("v1", [128, 1], F32))
        nv1 = E(nc.sbuf_tensor("nv1", [128, 1], F32))
        v2 = E(nc.sbuf_tensor("v2", [128, 1], F32))
        e2 = E(nc.sbuf_tensor("e2", [128, 1], F32))
        g1 = E(nc.sbuf_tensor("g1", [128, 1], F32))
        g2c = E(nc.sbuf_tensor("g2c", [128, 1], F32))
        sel8 = E(nc.sbuf_tensor("sel8", [128, 8], F32))
        m1c = E(nc.sbuf_tensor("m1c", [128, 1], F32))
        oh1 = E(nc.sbuf_tensor("oh1", [128, 8], F32))
        oh2 = E(nc.sbuf_tensor("oh2", [128, 8], F32))
        g2t = E(nc.sbuf_tensor("g2t", [128, 8], F32))
        wgall1 = E(nc.sbuf_tensor("wgall1", [128, 8, 8], F32))
        wgall2 = E(nc.sbuf_tensor("wgall2", [128, 8, 8], F32))
        cac1 = E(nc.sbuf_tensor("cac1", [128, 8], F32))
        cac2 = E(nc.sbuf_tensor("cac2", [128, 8], F32))
        cacb = E(nc.sbuf_tensor("cacb", [128, 8], BF16))
        cnt = E(nc.sbuf_tensor("cntsb", [1, 16], F32))
        psA = E(nc.psum_tensor("psA", [128, 4, 512], F32))
        psB = E(nc.psum_tensor("psB", [128, 4, 512], F32))
        block = E(nc.Block())
        p = Prog(nc, es)

        xr = x_in.rearrange("(c p) t -> p c t", p=128)
        gw1r = gw1t_in.rearrange("(c p) f -> p c f", p=128)
        xnr = xn_out.rearrange("(c p) t -> p c t", p=128)

        for i in range(4):
            dma(p, "gpsimd", xb[:, i * 8:(i + 1) * 8, :], xr[:, i * 8:(i + 1) * 8, :])
        dma(p, "gpsimd", gw1b[:], gw1r)
        p.op("gpsimd", lambda e: e.iota(ioti[:], pattern=[[1, 8]], base=0, channel_multiplier=0))
        dma(p, "sync", gb1c[:], gb1c_in)
        dma(p, "sync", gamr[:], gam_in)
        dma(p, "sync", betr[:], bet_in)
        dma(p, "sync", gw2t[:], gw2t_in)
        dma(p, "sync", gb2r[:], gb2_in)
        dma(p, "sync", ident[:], id_in)
        p.op("vector", lambda e: e.memset(onesf[:], 1.0))
        p.op("vector", lambda e: e.memset(onesb[:], 1.0))
        p.op("vector", lambda e: e.tensor_copy(iotf[:], ioti[:]))
        p.op("vector", lambda e: e.memset(cac1[:], 0.0))
        p.op("vector", lambda e: e.memset(cac2[:], 0.0))

        for c in range(NCH):
            p.op("scalar", lambda e, c=c: e.activation(sq[:], xb[:, c, :], AF.Square))
            st, sp = (c == 0), (c == NCH - 1)
            p.chain("tensor", [
                lambda e, c=c, st=st, sp=sp: e.matmul(psA[0:1, 0, :], onesb[:], xb[:, c, 0:512], start=st, stop=sp),
                lambda e, c=c, st=st, sp=sp: e.matmul(psA[0:1, 1, :], onesb[:], xb[:, c, 512:1024], start=st, stop=sp),
                lambda e, st=st, sp=sp: e.matmul(psA[0:1, 2, :], onesb[:], sq[:, 0:512], start=st, stop=sp),
                lambda e, st=st, sp=sp: e.matmul(psA[0:1, 3, :], onesb[:], sq[:, 512:1024], start=st, stop=sp),
            ])
        p.op("scalar", lambda e: e.activation(mrow[0:1, 0:512], psA[0:1, 0, :], AF.Copy, scale=f32(1.0 / D)))
        p.op("scalar", lambda e: e.activation(mrow[0:1, 512:1024], psA[0:1, 1, :], AF.Copy, scale=f32(1.0 / D)))
        p.op("scalar", lambda e: e.activation(qrow[0:1, 0:512], psA[0:1, 2, :], AF.Copy, scale=f32(1.0 / D)))
        p.op("scalar", lambda e: e.activation(qrow[0:1, 512:1024], psA[0:1, 3, :], AF.Copy, scale=f32(1.0 / D)))
        p.op("vector", lambda e: e.tensor_tensor(vrow[0:1, :], mrow[0:1, :], mrow[0:1, :], op=ALU.mult))
        p.op("vector", lambda e: e.tensor_tensor(vrow[0:1, :], qrow[0:1, :], vrow[0:1, :], op=ALU.subtract))
        p.op("vector", lambda e: e.tensor_scalar_add(vrow[0:1, :], vrow[0:1, :], LN_EPS))
        p.op("scalar", lambda e: e.activation(vrow[0:1, :], vrow[0:1, :], AF.Sqrt))
        p.op("vector", lambda e: e.reciprocal(rrow[0:1, :], vrow[0:1, :]))
        p.chain("tensor", [
            lambda e: e.matmul(psB[:, 0, :], onesf[:], mrow[0:1, 0:512], start=True, stop=True),
            lambda e: e.matmul(psB[:, 1, :], onesf[:], mrow[0:1, 512:1024], start=True, stop=True),
            lambda e: e.matmul(psB[:, 2, :], onesf[:], rrow[0:1, 0:512], start=True, stop=True),
            lambda e: e.matmul(psB[:, 3, :], onesf[:], rrow[0:1, 512:1024], start=True, stop=True),
        ])
        p.op("scalar", lambda e: e.copy(mbc[:, 0:512], psB[:, 0, :]))
        p.op("scalar", lambda e: e.copy(mbc[:, 512:1024], psB[:, 1, :]))
        p.op("scalar", lambda e: e.copy(rbc[:, 0:512], psB[:, 2, :]))
        p.op("scalar", lambda e: e.copy(rbc[:, 512:1024], psB[:, 3, :]))
        for c in range(NCH):
            p.op("vector", lambda e, c=c: e.tensor_copy(xf[:], xb[:, c, :]))
            p.op("vector", lambda e: e.tensor_tensor(xf[:], xf[:], mbc[:], op=ALU.subtract))
            p.op("vector", lambda e, c=c: e.tensor_tensor(xnb[:, c, :], xf[:], rbc[:], op=ALU.mult))
        for i in range(4):
            dma(p, "sync", xnr[:, i * 8:(i + 1) * 8, :], xnb[:, i * 8:(i + 1) * 8, :])

        for c in range(NCH):
            st, sp = (c == 0), (c == NCH - 1)
            p.chain("tensor", [
                lambda e, c=c, st=st, sp=sp: e.matmul(psA[0:32, 0, :], gw1b[:, c, :], xnb[:, c, 0:512], start=st, stop=sp),
                lambda e, c=c, st=st, sp=sp: e.matmul(psA[0:32, 1, :], gw1b[:, c, :], xnb[:, c, 512:1024], start=st, stop=sp),
            ])
        p.op("vector", lambda e: e.tensor_scalar(hfm[:, 0:512], psA[0:32, 0, :], gb1c[:], None, op0=ALU.add))
        p.op("vector", lambda e: e.tensor_scalar(hfm[:, 512:1024], psA[0:32, 1, :], gb1c[:], None, op0=ALU.add))
        p.chain("tensor", [
            lambda e: e.matmul(psB[:, 0, 0:32], onesf[:], gamr[:], start=True, stop=True),
            lambda e: e.matmul(psB[:, 1, 0:32], onesf[:], betr[:], start=True, stop=True),
            lambda e: e.matmul(psB[:, 2, 0:8], onesf[:], gb2r[:], start=True, stop=True),
        ])
        p.op("scalar", lambda e: e.copy(gamb[:], psB[:, 0, 0:32]))
        p.op("scalar", lambda e: e.copy(betb[:], psB[:, 1, 0:32]))
        p.op("scalar", lambda e: e.copy(gb2b[:], psB[:, 2, 0:8]))

        for t in range(8):
            p.op("tensor", lambda e, t=t: e.transpose(psB[:, 3, 0:32], hfm[:, t * 128:(t + 1) * 128], ident[0:32, 0:32]))
            p.op("scalar", lambda e: e.copy(htm[:], psB[:, 3, 0:32]))
            p.op("vector", lambda e: e.reduce_sum(hm[:], htm[:], axis=AX.X))
            p.op("vector", lambda e: e.tensor_scalar_mul(hm[:], hm[:], 1.0 / 32.0), drain=True)
            p.op("vector", lambda e: e.tensor_scalar(htm[:], htm[:], hm[:], None, op0=ALU.subtract), drain=True)
            if debug and t == 0:
                dma(p, "sync", dbg_ln[:, 0:32], htm[:])
                p.op("sync", lambda e: e.dma_start(out=dbg_ln[:, 32:33], in_=hm[:], allow_slow_non_contiguous=True), inc=16)
            p.op("vector", lambda e: e.tensor_tensor(hsq[:], htm[:], htm[:], op=ALU.mult))
            p.op("vector", lambda e: e.reduce_sum(hv[:], hsq[:], axis=AX.X))
            p.op("vector", lambda e: e.tensor_scalar_mul(hv[:], hv[:], 1.0 / 32.0), drain=True)
            p.op("vector", lambda e: e.tensor_scalar_add(hv[:], hv[:], LN_EPS))
            p.op("scalar", lambda e: e.activation(hv[:], hv[:], AF.Sqrt))
            p.op("vector", lambda e: e.reciprocal(hrs[:], hv[:]))
            p.op("vector", lambda e: e.tensor_scalar(htm[:], htm[:], hrs[:], None, op0=ALU.mult), drain=True)
            if debug and t == 0:
                p.op("sync", lambda e: e.dma_start(out=dbg_ln[:, 33:34], in_=hv[:], allow_slow_non_contiguous=True), inc=16)
                p.op("sync", lambda e: e.dma_start(out=dbg_ln[:, 34:35], in_=hrs[:], allow_slow_non_contiguous=True), inc=16)
            p.op("vector", lambda e: e.tensor_tensor(htm[:], htm[:], gamb[:], op=ALU.mult))
            p.op("vector", lambda e: e.tensor_tensor(htm[:], htm[:], betb[:], op=ALU.add))
            p.op("vector", lambda e: e.tensor_relu(htm[:], htm[:]))
            p.op("tensor", lambda e: e.transpose(psB[0:32, 3, 0:128], htm[:], ident[:]))
            p.op("scalar", lambda e: e.copy(hft[:], psB[0:32, 3, 0:128]))
            p.op("tensor", lambda e: e.matmul(psB[:, 2, 64:72], hft[:], gw2t[:], start=True, stop=True))
            p.op("vector", lambda e: e.tensor_tensor(gts[:], psB[:, 2, 64:72], gb2b[:], op=ALU.add))
            p.op("vector", lambda e: e.reduce_max(v1[:], gts[:], axis=AX.X))
            p.op("vector", lambda e: e.tensor_scalar(oh1[:], gts[:], v1[:], None, op0=ALU.is_equal), drain=True)
            p.op("vector", lambda e: e.scalar_tensor_tensor(sel8[:], oh1[:], -256.0, iotf[:], op0=ALU.mult, op1=ALU.add))
            p.op("vector", lambda e: e.tensor_reduce(m1c[:], sel8[:], axis=AX.X, op=ALU.min))
            p.op("vector", lambda e: e.tensor_scalar(oh1[:], sel8[:], m1c[:], None, op0=ALU.is_equal), drain=True)
            p.op("vector", lambda e: e.scalar_tensor_tensor(g2t[:], oh1[:], NEG, gts[:], op0=ALU.mult, op1=ALU.add))
            p.op("vector", lambda e: e.reduce_max(v2[:], g2t[:], axis=AX.X))
            p.op("vector", lambda e: e.tensor_scalar(oh2[:], g2t[:], v2[:], None, op0=ALU.is_equal))
            p.op("vector", lambda e: e.scalar_tensor_tensor(sel8[:], oh2[:], -256.0, iotf[:], op0=ALU.mult, op1=ALU.add))
            p.op("vector", lambda e: e.tensor_reduce(m1c[:], sel8[:], axis=AX.X, op=ALU.min))
            p.op("vector", lambda e: e.tensor_scalar(oh2[:], sel8[:], m1c[:], None, op0=ALU.is_equal), drain=True)
            p.op("vector", lambda e: e.tensor_scalar_mul(nv1[:], v1[:], -1.0))
            p.op("scalar", lambda e: e.activation(e2[:], v2[:], AF.Exp, bias=nv1[:]))
            p.op("vector", lambda e: e.tensor_scalar_add(g1[:], e2[:], 1.0))
            p.op("vector", lambda e: e.reciprocal(g1[:], g1[:]))
            p.op("vector", lambda e: e.tensor_tensor(g2c[:], e2[:], g1[:], op=ALU.mult), drain=True)
            p.op("vector", lambda e: e.tensor_scalar(sel8[:], oh1[:], g1[:], None, op0=ALU.mult), drain=True)
            dma(p, "sync", wgscr.ap()[0, t * 128:(t + 1) * 128, :], sel8[:])
            p.op("vector", lambda e: e.tensor_scalar(sel8[:], oh2[:], g2c[:], None, op0=ALU.mult), drain=True)
            dma(p, "sync", wgscr.ap()[1, t * 128:(t + 1) * 128, :], sel8[:])
            p.op("vector", lambda e: e.tensor_tensor(cac1[:], cac1[:], oh1[:], op=ALU.add))
            p.op("vector", lambda e: e.tensor_tensor(cac2[:], cac2[:], oh2[:], op=ALU.add))
            if debug and t == 0:
                dma(p, "sync", dbg_hb, htb[:])
                dma(p, "sync", dbg_pc[:, 0:32], htm[:])
                dma(p, "sync", dbg_pc[:, 32:40], gts[:])
                dma(p, "sync", dbg_pc[:, 40:48], oh1[:])
                dma(p, "sync", dbg_pc[:, 48:56], oh2[:])
                dma(p, "sync", dbg_pc[:, 56:64], iotf[:])

        p.op("vector", lambda e: e.tensor_copy(cacb[:], cac1[:]))
        p.op("tensor", lambda e: e.matmul(psB[0:1, 0, 128:136], onesb[:], cacb[:], start=True, stop=True))
        p.op("scalar", lambda e: e.copy(cnt[0:1, 0:8], psB[0:1, 0, 128:136]))
        p.op("vector", lambda e: e.tensor_copy(cacb[:], cac2[:]))
        p.op("tensor", lambda e: e.matmul(psB[0:1, 0, 136:144], onesb[:], cacb[:], start=True, stop=True))
        p.op("scalar", lambda e: e.copy(cnt[0:1, 8:16], psB[0:1, 0, 136:144]))
        dma(p, "sync", cnt_out, cnt[:])
        dma(p, "sync", wg_out[0].rearrange("(t p) e -> p t e", p=128), wgall1[:])
        dma(p, "sync", wg_out[1].rearrange("(t p) e -> p t e", p=128), wgall2[:])
        p.finish(block)
    return nc



def build_attn(l1: bool):
    nc = bass.Bass("TRN2", target_bir_lowering=False)
    xn4_in = nc.dram_tensor("xn4", [32, 128, NCH, 256], BF16, kind="ExternalInput").ap()
    wq_in = nc.dram_tensor("wq4", [128, NCH, 512], F32, kind="ExternalInput").ap()
    if l1:
        wk_in = nc.dram_tensor("wk4", [128, NCH, 512], F32, kind="ExternalInput").ap()
        wv_in = nc.dram_tensor("wv4", [128, NCH, 512], F32, kind="ExternalInput").ap()
        bq_in = nc.dram_tensor("bqc", [128, 4], F32, kind="ExternalInput").ap()
        bk_in = nc.dram_tensor("bkc", [128, 4], F32, kind="ExternalInput").ap()
        bv_in = nc.dram_tensor("bvr", [1, 512], F32, kind="ExternalInput").ap()
    else:
        wkv_in = nc.dram_tensor("wkv4", [128, NCH, 256], F32, kind="ExternalInput").ap()
    tri_in = nc.dram_tensor("tri", [128, 128], F32, kind="ExternalInput").ap()
    ctx_out = nc.dram_tensor("ctx", [512, T], BF16, kind="ExternalOutput").ap()

    q_scr = nc.dram_tensor("q_scr", [512, T], BF16)

    es = contextlib.ExitStack()
    with es:
        E = es.enter_context
        xt = E(nc.sbuf_tensor("xt", [128, NCH, 256], BF16))
        wsh = E(nc.sbuf_tensor("wsh", [128, NCH, 512], BF16))
        if l1:
            bqc = E(nc.sbuf_tensor("bqc_s", [128, 4], F32))
            bkc = E(nc.sbuf_tensor("bkc_s", [128, 4], F32))
            bvr = E(nc.sbuf_tensor("bvr_s", [1, 512], F32))
            bvb = E(nc.sbuf_tensor("bvb", [128, 512], F32))
            kres = E(nc.sbuf_tensor("kres", [128, 4, T], BF16))
            vres = E(nc.sbuf_tensor("vres", [128, 64, 512], BF16))
        else:
            wkvb = E(nc.sbuf_tensor("wkvb", [128, NCH, 256], BF16))
            kres = E(nc.sbuf_tensor("kres", [128, 1, T], BF16))
            vres = E(nc.sbuf_tensor("vres", [128, 64, 128], BF16))
        tri = E(nc.sbuf_tensor("tri_s", [128, 128], F32))
        qt = E(nc.sbuf_tensor("qt", [128, 512], BF16))
        qsb = E(nc.sbuf_tensor("qsb", [128, 256], BF16))
        ptb = E(nc.sbuf_tensor("ptb", [128, 512], BF16))
        rs = E(nc.sbuf_tensor("rs", [1, 512], F32))
        rcb = E(nc.sbuf_tensor("rcb", [128, 512], F32))
        cts = E(nc.sbuf_tensor("cts", [128, 512], BF16))
        onesf = E(nc.sbuf_tensor("onesf", [1, 128], F32))
        onesb = E(nc.sbuf_tensor("onesb", [128, 1], BF16))
        psA = E(nc.psum_tensor("psA", [128, 4, 512], F32))
        psB = E(nc.psum_tensor("psB", [128, 4, 512], F32))
        block = E(nc.Block())
        p = Prog(nc, es)

        p.op("vector", lambda e: e.memset(onesf[:], 1.0))
        p.op("vector", lambda e: e.memset(onesb[:], 1.0))
        dma(p, "sync", tri[:], tri_in)
        if l1:
            dma(p, "sync", bqc[:], bq_in)
            dma(p, "sync", bkc[:], bk_in)
            dma(p, "sync", bvr[:], bv_in)
            p.op("tensor", lambda e: e.matmul(psB[:, 0, :], onesf[:], bvr[:], start=True, stop=True))
            p.op("scalar", lambda e: e.copy(bvb[:], psB[:, 0, :]))

        def load_w(src_ap):
            for i in range(4):
                sl = slice(i * 8, (i + 1) * 8)
                dma(p, "gpsimd", wsh[:, sl, :], src_ap[:, sl, :])

        if not l1:
            load_w(wq_in)
            for i in range(4):
                sl = slice(i * 8, (i + 1) * 8)
                dma(p, "gpsimd", wkvb[:, sl, :], wkv_in[:, sl, :])
            for tb in range(32):
                dma(p, "sync", xt[:], xn4_in[tb])
                for oc in range(4):
                    p.chain("tensor", [
                        lambda e, c=c, oc=oc, st=(c == 0), sp=(c == NCH - 1): e.matmul(
                            psA[:, 0, 0:256], wsh[:, c, oc * 128:(oc + 1) * 128], xt[:, c, :], start=st, stop=sp)
                        for c in range(NCH)])
                    p.op("scalar", lambda e: e.copy(qsb[:], psA[:, 0, 0:256]))
                    dma(p, "sync", q_scr.ap()[oc * 128:(oc + 1) * 128, tb * 256:(tb + 1) * 256], qsb[:])
                p.chain("tensor", [
                    lambda e, c=c, st=(c == 0), sp=(c == NCH - 1): e.matmul(
                        psA[:, 1, 0:256], wkvb[:, c, 0:128], xt[:, c, :], start=st, stop=sp)
                    for c in range(NCH)])
                p.op("scalar", lambda e, tb=tb: e.copy(kres[:, 0, tb * 256:(tb + 1) * 256], psA[:, 1, 0:256]))
                for vt in range(2):
                    ci = tb * 2 + vt
                    p.chain("tensor", [
                        lambda e, c=c, vt=vt, st=(c == 0), sp=(c == NCH - 1): e.matmul(
                            psA[:, 2, 0:128], xt[:, c, vt * 128:(vt + 1) * 128], wkvb[:, c, 128:256], start=st, stop=sp)
                        for c in range(NCH)])
                    p.op("scalar", lambda e, ci=ci: e.copy(vres[:, ci, :], psA[:, 2, 0:128]))
        else:
            load_w(wk_in)
            for tb in range(32):
                dma(p, "sync", xt[:], xn4_in[tb])
                for kc in range(4):
                    p.chain("tensor", [
                        lambda e, c=c, kc=kc, st=(c == 0), sp=(c == NCH - 1): e.matmul(
                            psA[:, 0, 0:256], wsh[:, c, kc * 128:(kc + 1) * 128], xt[:, c, :], start=st, stop=sp)
                        for c in range(NCH)])
                    p.op("vector", lambda e, kc=kc, tb=tb: e.tensor_scalar(
                        kres[:, kc, tb * 256:(tb + 1) * 256], psA[:, 0, 0:256], bkc[:, kc:kc + 1], None, op0=ALU.add))
            load_w(wv_in)
            for tb in range(32):
                dma(p, "sync", xt[:], xn4_in[tb])
                for vt in range(2):
                    ci = tb * 2 + vt
                    p.chain("tensor", [
                        lambda e, c=c, vt=vt, st=(c == 0), sp=(c == NCH - 1): e.matmul(
                            psA[:, 2, :], xt[:, c, vt * 128:(vt + 1) * 128], wsh[:, c, :], start=st, stop=sp)
                        for c in range(NCH)])
                    p.op("vector", lambda e, ci=ci: e.tensor_tensor(vres[:, ci, :], psA[:, 2, :], bvb[:], op=ALU.add))
            load_w(wq_in)
            for tb in range(32):
                dma(p, "sync", xt[:], xn4_in[tb])
                for oc in range(4):
                    p.chain("tensor", [
                        lambda e, c=c, oc=oc, st=(c == 0), sp=(c == NCH - 1): e.matmul(
                            psA[:, 0, 0:256], wsh[:, c, oc * 128:(oc + 1) * 128], xt[:, c, :], start=st, stop=sp)
                        for c in range(NCH)])
                    p.op("vector", lambda e, oc=oc: e.tensor_scalar(
                        qsb[:], psA[:, 0, 0:256], bqc[:, oc:oc + 1], None, op0=ALU.add))
                    dma(p, "sync", q_scr.ap()[oc * 128:(oc + 1) * 128, tb * 256:(tb + 1) * 256], qsb[:])

        for lh in range(4):
            for qg in range(16):
                dma(p, "sync", qt[:], q_scr.ap()[lh * 128:(lh + 1) * 128, qg * 512:(qg + 1) * 512])
                nch_k = 64 if l1 else 4 * (qg + 1)
                for c in range(nch_k):
                    d = c - 4 * qg
                    w0 = 128 * d if (not l1 and d >= 0) else 0
                    diag = (not l1) and d >= 0
                    klh = lh if l1 else 0
                    p.op("tensor", lambda e, c=c, klh=klh, w0=w0: e.matmul(
                        psA[:, 0, w0:512], kres[:, klh, c * 128:(c + 1) * 128], qt[:, w0:512],
                        start=True, stop=True))
                    if diag:
                        p.op("vector", lambda e, w0=w0: e.tensor_tensor(
                            psA[:, 0, w0:w0 + 128], psA[:, 0, w0:w0 + 128], tri[:], op=ALU.add))
                    p.op("scalar", lambda e, w0=w0: e.activation(
                        ptb[:, w0:512], psA[:, 0, w0:512], AF.Exp,
                        bias=f32(-EXP_SHIFT), scale=f32(INV_SQRT_HD)))
                    st = (c == 0)
                    sp = (c == nch_k - 1)
                    voff = lh * 128 if l1 else 0
                    p.chain("tensor", [
                        lambda e, c=c, w0=w0, st=st, sp=sp, voff=voff: e.matmul(
                            psB[:, 0, w0:512], vres[:, c, voff:voff + 128], ptb[:, w0:512], start=st, stop=sp),
                        lambda e, c=c, w0=w0, st=st, sp=sp: e.matmul(
                            psB[0:1, 2, w0:512], onesb[:], ptb[:, w0:512], start=st, stop=sp),
                    ])
                p.op("vector", lambda e: e.reciprocal(rs[:], psB[0:1, 2, :]))
                p.op("tensor", lambda e: e.matmul(psB[:, 1, :], onesf[:], rs[:], start=True, stop=True))
                p.op("scalar", lambda e: e.copy(rcb[:], psB[:, 1, :]))
                p.op("vector", lambda e: e.tensor_tensor(cts[:], psB[:, 0, :], rcb[:], op=ALU.mult))
                dma(p, "sync", ctx_out[lh * 128:(lh + 1) * 128, qg * 512:(qg + 1) * 512], cts[:])
        p.finish(block)
    return nc



def build_D():
    nc = bass.Bass("TRN2", target_bir_lowering=False)
    xn1_in = nc.dram_tensor("xn1", [D, TL], BF16, kind="ExternalInput").ap()
    wq4_in = nc.dram_tensor("wq4", [32, 128, NCH, 128], F32, kind="ExternalInput").ap()
    wk4_in = nc.dram_tensor("wk4", [32, 128, NCH, 128], F32, kind="ExternalInput").ap()
    wv4_in = nc.dram_tensor("wv4", [32, 128, NCH, 128], F32, kind="ExternalInput").ap()
    bq_in = nc.dram_tensor("bqc", [128, 32], F32, kind="ExternalInput").ap()
    bk_in = nc.dram_tensor("bkc", [128, 32], F32, kind="ExternalInput").ap()
    bv_in = nc.dram_tensor("bvc", [128, 32], F32, kind="ExternalInput").ap()
    idb_in = nc.dram_tensor("identb", [128, 128], BF16, kind="ExternalInput").ap()
    ctx_out = nc.dram_tensor("ctx", [D, TL], BF16, kind="ExternalOutput").ap()

    es = contextlib.ExitStack()
    with es:
        E = es.enter_context
        xnb = E(nc.sbuf_tensor("xnb", [128, NCH, TL], BF16))
        qstr = E(nc.sbuf_tensor("qstr", [128, 32, 512], BF16))
        kstr = E(nc.sbuf_tensor("kstr", [128, 32, 512], BF16))
        vstr = E(nc.sbuf_tensor("vstr", [128, 32, 512], BF16))
        ctb2 = E(nc.sbuf_tensor("ctb2", [128, 32, 256], BF16))
        wtl = E(nc.sbuf_tensor("wtl", [128, NCH, 128], BF16))
        bqc = E(nc.sbuf_tensor("bqc_s", [128, 32], F32))
        bkc = E(nc.sbuf_tensor("bkc_s", [128, 32], F32))
        bvc = E(nc.sbuf_tensor("bvc_s", [128, 32], F32))
        identb = E(nc.sbuf_tensor("identb_s", [128, 128], BF16))
        pt = E(nc.sbuf_tensor("pt", [32, 512], BF16))
        rs = E(nc.sbuf_tensor("rs", [1, 512], F32))
        rb = E(nc.sbuf_tensor("rb", [128, 512], F32))
        vts = E(nc.sbuf_tensor("vts", [32, 16, 128], BF16))
        onesf = E(nc.sbuf_tensor("onesf", [1, 128], F32))
        ones32 = E(nc.sbuf_tensor("ones32", [32, 1], BF16))
        psA = E(nc.psum_tensor("psA", [128, 4, 512], F32))
        psVt = E(nc.psum_tensor("psVt", [32, 2, 1024], BF16))
        block = E(nc.Block())
        p = Prog(nc, es)

        xn1r = xn1_in.rearrange("(c p) t -> p c t", p=128)
        ctxr = ctx_out.rearrange("(h p) n -> p h n", p=128)

        p.op("vector", lambda e: e.memset(onesf[:], 1.0))
        p.op("vector", lambda e: e.memset(ones32[:], 1.0))
        dma(p, "sync", bqc[:], bq_in)
        dma(p, "sync", bkc[:], bk_in)
        dma(p, "sync", bvc[:], bv_in)
        dma(p, "sync", identb[:], idb_in)
        for i in range(4):
            sl = slice(i * 8, (i + 1) * 8)
            dma(p, "sync", xnb[:, sl, :], xn1r[:, sl, :])

        for half in range(2):
            hs = slice(half * 512, (half + 1) * 512)
            for name, w4, bc, dst in (("q", wq4_in, bqc, qstr), ("k", wk4_in, bkc, kstr), ("v", wv4_in, bvc, vstr)):
                for oc in range(32):
                    dma(p, "gpsimd", wtl[:], w4[oc])
                    p.chain("tensor", [
                        lambda e, c=c, hs=hs, st=(c == 0), sp=(c == NCH - 1): e.matmul(
                            psA[:, 0, :], wtl[:, c, :], xnb[:, c, hs], start=st, stop=sp)
                        for c in range(NCH)])
                    p.op("vector", lambda e, oc=oc, bc=bc, dst=dst: e.tensor_scalar(
                        dst[:, oc, :], psA[:, 0, :], bc[:, oc:oc + 1], None, op0=ALU.add))
            for sub in range(2):
                for g in range(16):
                    hh = sub * 256 + g * 16
                    ng = g * 16
                    p.chain("tensor", [
                        lambda e, j=j, hh=hh: e.matmul(
                            psA[0:32, 1, j * 32:(j + 1) * 32], kstr[:, :, hh + j],
                            qstr[:, :, hh + j], start=True, stop=True)
                        for j in range(16)])
                    p.op("scalar", lambda e: e.activation(
                        pt[:], psA[0:32, 1, :], AF.Exp, bias=f32(-EXP_SHIFT), scale=f32(INV_SQRT_HD)))
                    p.op("tensor", lambda e: e.matmul(psA[0:1, 2, :], ones32[:], pt[:], start=True, stop=True))
                    p.op("vector", lambda e: e.reciprocal(rs[:], psA[0:1, 2, :]))
                    p.op("tensor", lambda e: e.matmul(psA[:, 2, :], onesf[:], rs[:], start=True, stop=True))
                    p.op("scalar", lambda e: e.copy(rb[:], psA[:, 2, :]))
                    p.chain("tensor", [
                        lambda e, j=j, hh=hh: e.transpose(
                            psVt[0:32, j // 8, (j % 8) * 128:(j % 8 + 1) * 128],
                            vstr[:, :, hh + j], identb[:])
                        for j in range(16)])
                    p.op("scalar", lambda e: e.copy(vts[:, 0:8, :], psVt[0:32, 0, :].rearrange("p (j d) -> p j d", d=128)))
                    p.op("scalar", lambda e: e.copy(vts[:, 8:16, :], psVt[0:32, 1, :].rearrange("p (j d) -> p j d", d=128)))
                    p.chain("tensor", [
                        lambda e, j=j: e.matmul(
                            psA[:, 3, j * 32:(j + 1) * 32], vts[:, j, :], pt[:, j * 32:(j + 1) * 32],
                            start=True, stop=True)
                        for j in range(16)])
                    p.op("vector", lambda e, ng=ng: e.tensor_tensor(
                        ctb2[:, :, ng:ng + 16].rearrange("p h n -> p n h"),
                        psA[:, 3, :].rearrange("p (n h) -> p n h", h=32), rb[:].rearrange("p (n h) -> p n h", h=32),
                        op=ALU.mult))
                for i in range(4):
                    sl = slice(i * 8, (i + 1) * 8)
                    dma(p, "sync", ctxr[:, sl, half * 512 + sub * 256:half * 512 + (sub + 1) * 256], ctb2[:, sl, :])
        p.finish(block)
    return nc



def build_C():
    nc = bass.Bass("TRN2", target_bir_lowering=False)
    x_in = nc.dram_tensor("x", [D, TL], F32, kind="ExternalInput").ap()
    xn_in = nc.dram_tensor("xn", [D, TL], BF16, kind="ExternalInput").ap()
    ctx_in = nc.dram_tensor("ctxt", [D, TL], BF16, kind="ExternalInput").ap()
    wo4_in = nc.dram_tensor("wo4", [32, 128, NCH, 128], F32, kind="ExternalInput").ap()
    fw4_in = nc.dram_tensor("fw4", [32, 128, NCH, 128], F32, kind="ExternalInput").ap()
    boc_in = nc.dram_tensor("boc", [128, 32], F32, kind="ExternalInput").ap()
    la4_in = nc.dram_tensor("la4", [128, NCH, 128], F32, kind="ExternalInput").ap()
    lb_in = nc.dram_tensor("lb", [128, D], F32, kind="ExternalInput").ap()
    cnts_in = nc.dram_tensor("cnts", [16, 8], F32, kind="ExternalInput").ap()
    sel2_in = nc.dram_tensor("sel2", [16, 2], F32, kind="ExternalInput").ap()
    wg_in = nc.dram_tensor("wg", [2, TL, 8], F32, kind="ExternalInput").ap()
    ssel_in = nc.dram_tensor("ssel", [8, 128], F32, kind="ExternalInput").ap()
    id_in = nc.dram_tensor("ident", [128, 128], F32, kind="ExternalInput").ap()
    x1_out = nc.dram_tensor("x1", [D, TL], F32, kind="ExternalOutput").ap()
    xn1_out = nc.dram_tensor("xn1", [D, TL], BF16, kind="ExternalOutput").ap()

    es = contextlib.ExitStack()
    with es:
        E = es.enter_context
        xnb = E(nc.sbuf_tensor("xnb", [128, NCH, TL], BF16))
        ctb = E(nc.sbuf_tensor("ctb", [128, NCH, TL], BF16))
        lab = E(nc.sbuf_tensor("lab", [128, NCH, 128], BF16))
        lbb = E(nc.sbuf_tensor("lbb", [128, D], BF16))
        wob = E(nc.sbuf_tensor("wob", [128, NCH, 128], BF16))
        fwb = E(nc.sbuf_tensor("fwb", [128, NCH, 128], BF16))
        boc = E(nc.sbuf_tensor("boc_s", [128, 32], F32))
        cnt16 = E(nc.sbuf_tensor("cnt16", [16, 8], F32))
        sel2 = E(nc.sbuf_tensor("sel2_s", [16, 2], F32))
        ssel = E(nc.sbuf_tensor("ssel_s", [8, 128], F32))
        ident = E(nc.sbuf_tensor("identsb", [128, 128], F32))
        cgrow = E(nc.sbuf_tensor("cgrow", [1, 16], F32))
        alw = E(nc.sbuf_tensor("alw", [1, 16], F32))
        alb1 = E(nc.sbuf_tensor("alb1", [128, 8], F32))
        alb2 = E(nc.sbuf_tensor("alb2", [128, 8], F32))
        wg1t = E(nc.sbuf_tensor("wg1t", [128, 8], F32))
        wg2t = E(nc.sbuf_tensor("wg2t", [128, 8], F32))
        wt = E(nc.sbuf_tensor("wt", [128, 8], F32))
        wfm = E(nc.sbuf_tensor("wfm", [8, TL], F32))
        wbc = E(nc.sbuf_tensor("wbc", [128, TL], F32))
        us = E(nc.sbuf_tensor("us", [128, TL], BF16))
        ssb = E(nc.sbuf_tensor("ssb", [128, TL], F32))
        gsb = E(nc.sbuf_tensor("gsb", [128, TL], F32))
        x1bb = E(nc.sbuf_tensor("x1bb", [128, TL], BF16))
        sqx = E(nc.sbuf_tensor("sqx", [128, TL], BF16))
        onesf = E(nc.sbuf_tensor("onesf", [1, 128], F32))
        onesb = E(nc.sbuf_tensor("onesb", [128, 1], BF16))
        mrow = E(nc.sbuf_tensor("mrow", [1, TL], F32))
        qrow = E(nc.sbuf_tensor("qrow", [1, TL], F32))
        vrow = E(nc.sbuf_tensor("vrow", [1, TL], F32))
        rrow = E(nc.sbuf_tensor("rrow", [1, TL], F32))
        psA = E(nc.psum_tensor("psA", [128, 4, 512], F32))
        psB = E(nc.psum_tensor("psB", [128, 4, 512], F32))
        block = E(nc.Block())
        p = Prog(nc, es)
        mbc = wbc
        rbc = gsb

        xr = x_in.rearrange("(c p) t -> p c t", p=128)
        xnr = xn_in.rearrange("(c p) t -> p c t", p=128)
        ctr = ctx_in.rearrange("(c p) t -> p c t", p=128)
        lar = la4_in
        x1r = x1_out.rearrange("(c p) t -> p c t", p=128)
        xn1r = xn1_out.rearrange("(c p) t -> p c t", p=128)

        p.op("vector", lambda e: e.memset(onesf[:], 1.0))
        p.op("vector", lambda e: e.memset(onesb[:], 1.0))
        for i in range(4):
            sl = slice(i * 8, (i + 1) * 8)
            dma(p, "sync", xnb[:, sl, :], xnr[:, sl, :])
            dma(p, "sync", ctb[:, sl, :], ctr[:, sl, :])
        dma(p, "gpsimd", lab[:], lar)
        dma(p, "gpsimd", lbb[:], lb_in)
        dma(p, "sync", boc[:], boc_in)
        dma(p, "sync", cnt16[:], cnts_in)
        dma(p, "sync", sel2[:], sel2_in)
        dma(p, "sync", ssel[:], ssel_in)
        dma(p, "sync", ident[:], id_in)

        p.op("tensor", lambda e: e.matmul(psA[0:1, 0, 0:8], sel2[:, 0:1], cnt16[:], start=True, stop=True))
        p.op("tensor", lambda e: e.matmul(psA[0:1, 1, 0:8], sel2[:, 1:2], cnt16[:], start=True, stop=True))
        p.op("scalar", lambda e: e.copy(cgrow[0:1, 0:8], psA[0:1, 0, 0:8]))
        p.op("scalar", lambda e: e.copy(cgrow[0:1, 8:16], psA[0:1, 1, 0:8]))
        p.op("vector", lambda e: e.tensor_scalar(alw[:], cgrow[:], CAPACITY, None, op0=ALU.is_le))
        p.op("tensor", lambda e: e.matmul(psA[:, 2, 0:8], onesf[:], alw[0:1, 0:8], start=True, stop=True))
        p.op("tensor", lambda e: e.matmul(psA[:, 3, 0:8], onesf[:], alw[0:1, 8:16], start=True, stop=True))
        p.op("scalar", lambda e: e.copy(alb1[:], psA[:, 2, 0:8]))
        p.op("scalar", lambda e: e.copy(alb2[:], psA[:, 3, 0:8]))

        for t in range(8):
            dma(p, "sync", wg1t[:], wg_in[0, t * 128:(t + 1) * 128, :])
            dma(p, "sync", wg2t[:], wg_in[1, t * 128:(t + 1) * 128, :])
            p.op("vector", lambda e: e.tensor_tensor(wt[:], wg1t[:], alb1[:], op=ALU.mult))
            p.op("vector", lambda e: e.tensor_tensor(wg2t[:], wg2t[:], alb2[:], op=ALU.mult))
            p.op("vector", lambda e: e.tensor_tensor(wt[:], wt[:], wg2t[:], op=ALU.add))
            p.op("tensor", lambda e: e.transpose(psB[0:8, 0, 0:128], wt[:], ident[:]))
            p.op("scalar", lambda e, t=t: e.copy(wfm[:, t * 128:(t + 1) * 128], psB[0:8, 0, 0:128]))
        p.op("tensor", lambda e: e.matmul(psB[:, 1, :], ssel[:], wfm[:, 0:512], start=True, stop=True))
        p.op("tensor", lambda e: e.matmul(psB[:, 2, :], ssel[:], wfm[:, 512:1024], start=True, stop=True))
        p.op("scalar", lambda e: e.activation(wbc[:, 0:512], psB[:, 1, :], AF.Copy, scale=f32(1.0 / 16.0)))
        p.op("scalar", lambda e: e.activation(wbc[:, 512:1024], psB[:, 2, :], AF.Copy, scale=f32(1.0 / 16.0)))

        for half in range(2):
            mms = [lambda e, c=c, half=half, st=(c == 0), sp=(c == NCH - 1):
                   e.matmul(psA[:, half, :], lab[:, c, :], xnb[:, c, half * 512:(half + 1) * 512], start=st, stop=sp)
                   for c in range(NCH)]
            p.chain("tensor", mms)
        p.op("vector", lambda e: e.tensor_tensor(us[:, 0:512], psA[:, 0, :], wbc[:, 0:512], op=ALU.mult))
        p.op("vector", lambda e: e.tensor_tensor(us[:, 512:1024], psA[:, 1, :], wbc[:, 512:1024], op=ALU.mult))

        for oc in range(32):
            dma(p, "gpsimd", wob[:], wo4_in[oc])
            dma(p, "gpsimd", fwb[:], fw4_in[oc])
            for half in range(2):
                hs = slice(half * 512, (half + 1) * 512)
                mms = [lambda e, c=c, half=half, hs=hs, st=(c == 0), sp=(c == NCH - 1):
                       e.matmul(psA[:, half, :], wob[:, c, :], ctb[:, c, hs], start=st, stop=sp)
                       for c in range(NCH)]
                p.chain("tensor", mms)
                mms2 = [lambda e, c=c, half=half, hs=hs:
                        e.matmul(psA[:, 2 + half, :], fwb[:, c, :], xnb[:, c, hs], start=(c == 0), stop=False)
                        for c in range(NCH)]
                mms2.append(lambda e, half=half, hs=hs, oc=oc: e.matmul(
                    psA[:, 2 + half, :], lbb[:, oc * 128:(oc + 1) * 128], us[:, hs], start=False, stop=True))
                p.chain("tensor", mms2)
            p.op("vector", lambda e, oc=oc: e.tensor_scalar(ssb[:, 0:512], psA[:, 0, :], boc[:, oc:oc + 1], None, op0=ALU.add))
            p.op("vector", lambda e, oc=oc: e.tensor_scalar(ssb[:, 512:1024], psA[:, 1, :], boc[:, oc:oc + 1], None, op0=ALU.add))
            p.op("scalar", lambda e: e.activation(gsb[:, 0:512], psA[:, 2, :], AF.Silu))
            p.op("scalar", lambda e: e.activation(gsb[:, 512:1024], psA[:, 3, :], AF.Silu))
            p.op("vector", lambda e: e.tensor_tensor(ssb[:], ssb[:], gsb[:], op=ALU.mult))
            dma(p, "sync", gsb[:], xr[:, oc, :])
            p.op("vector", lambda e: e.tensor_tensor(ssb[:], gsb[:], ssb[:], op=ALU.add))
            dma(p, "sync", x1r[:, oc, :], ssb[:])
            p.op("vector", lambda e: e.tensor_copy(x1bb[:], ssb[:]))
            p.op("scalar", lambda e: e.activation(sqx[:], x1bb[:], AF.Square))
            st, sp = (oc == 0), (oc == 31)
            p.chain("tensor", [
                lambda e, st=st, sp=sp: e.matmul(psB[0:1, 0, :], onesb[:], x1bb[:, 0:512], start=st, stop=sp),
                lambda e, st=st, sp=sp: e.matmul(psB[0:1, 1, :], onesb[:], x1bb[:, 512:1024], start=st, stop=sp),
                lambda e, st=st, sp=sp: e.matmul(psB[0:1, 2, :], onesb[:], sqx[:, 0:512], start=st, stop=sp),
                lambda e, st=st, sp=sp: e.matmul(psB[0:1, 3, :], onesb[:], sqx[:, 512:1024], start=st, stop=sp),
            ])

        p.op("scalar", lambda e: e.activation(mrow[0:1, 0:512], psB[0:1, 0, :], AF.Copy, scale=f32(1.0 / D)))
        p.op("scalar", lambda e: e.activation(mrow[0:1, 512:1024], psB[0:1, 1, :], AF.Copy, scale=f32(1.0 / D)))
        p.op("scalar", lambda e: e.activation(qrow[0:1, 0:512], psB[0:1, 2, :], AF.Copy, scale=f32(1.0 / D)))
        p.op("scalar", lambda e: e.activation(qrow[0:1, 512:1024], psB[0:1, 3, :], AF.Copy, scale=f32(1.0 / D)))
        p.op("vector", lambda e: e.tensor_tensor(vrow[0:1, :], mrow[0:1, :], mrow[0:1, :], op=ALU.mult))
        p.op("vector", lambda e: e.tensor_tensor(vrow[0:1, :], qrow[0:1, :], vrow[0:1, :], op=ALU.subtract))
        p.op("vector", lambda e: e.tensor_scalar_add(vrow[0:1, :], vrow[0:1, :], LN_EPS))
        p.op("scalar", lambda e: e.activation(vrow[0:1, :], vrow[0:1, :], AF.Sqrt))
        p.op("vector", lambda e: e.reciprocal(rrow[0:1, :], vrow[0:1, :]))
        p.chain("tensor", [
            lambda e: e.matmul(psB[:, 0, :], onesf[:], mrow[0:1, 0:512], start=True, stop=True),
            lambda e: e.matmul(psB[:, 1, :], onesf[:], mrow[0:1, 512:1024], start=True, stop=True),
            lambda e: e.matmul(psB[:, 2, :], onesf[:], rrow[0:1, 0:512], start=True, stop=True),
            lambda e: e.matmul(psB[:, 3, :], onesf[:], rrow[0:1, 512:1024], start=True, stop=True),
        ])
        p.op("scalar", lambda e: e.copy(mbc[:, 0:512], psB[:, 0, :]))
        p.op("scalar", lambda e: e.copy(mbc[:, 512:1024], psB[:, 1, :]))
        p.op("scalar", lambda e: e.copy(rbc[:, 0:512], psB[:, 2, :]))
        p.op("scalar", lambda e: e.copy(rbc[:, 512:1024], psB[:, 3, :]))
        for oc in range(32):
            dma(p, "gpsimd", x1bb[:], x1r[:, oc, :])
            p.op("vector", lambda e: e.tensor_copy(ssb[:], x1bb[:]))
            p.op("vector", lambda e: e.tensor_tensor(ssb[:], ssb[:], mbc[:], op=ALU.subtract))
            p.op("vector", lambda e: e.tensor_tensor(x1bb[:], ssb[:], rbc[:], op=ALU.mult))
            dma(p, "sync", xn1r[:, oc, :], x1bb[:])
        p.finish(block)
    return nc



def build_E():
    nc = bass.Bass("TRN2", target_bir_lowering=False)
    x1_in = nc.dram_tensor("x1", [D, TL], F32, kind="ExternalInput").ap()
    xn1_in = nc.dram_tensor("xn1", [D, TL], BF16, kind="ExternalInput").ap()
    ctx_in = nc.dram_tensor("ctxt", [D, TL], BF16, kind="ExternalInput").ap()
    w1t4_in = nc.dram_tensor("w1t4", [32, 128, NCH, 128], F32, kind="ExternalInput").ap()
    w1n4_in = nc.dram_tensor("w1n4", [32, 128, NCH, 128], F32, kind="ExternalInput").ap()
    a1t4_in = nc.dram_tensor("a1t4", [128, NCH, 32], F32, kind="ExternalInput").ap()
    a1_in = nc.dram_tensor("a1", [32, D], F32, kind="ExternalInput").ap()
    b1t_in = nc.dram_tensor("b1t", [32, D], F32, kind="ExternalInput").ap()
    mag_in = nc.dram_tensor("magc", [128, 32], F32, kind="ExternalInput").ap()
    x2_out = nc.dram_tensor("x2", [D, TL], F32, kind="ExternalOutput").ap()
    es = contextlib.ExitStack()
    with es:
        E = es.enter_context
        xn1b = E(nc.sbuf_tensor("xn1b", [128, NCH, TL], BF16))
        ct1 = E(nc.sbuf_tensor("ct1", [128, NCH, TL], BF16))
        a1tb = E(nc.sbuf_tensor("a1tb", [128, NCH, 32], BF16))
        a1b = E(nc.sbuf_tensor("a1b", [32, D], BF16))
        b1tb = E(nc.sbuf_tensor("b1tb", [32, D], BF16))
        w1nf = E(nc.sbuf_tensor("w1nf", [128, D], F32))
        fw1b = E(nc.sbuf_tensor("fw1b", [128, NCH, 128], BF16))
        magc = E(nc.sbuf_tensor("magc_s", [128, 32], F32))
        u1f = E(nc.sbuf_tensor("u1f", [32, TL], F32))
        u1b = E(nc.sbuf_tensor("u1b", [32, TL], BF16))
        effb = E(nc.sbuf_tensor("effb", [128, 512], BF16))
        effs = E(nc.sbuf_tensor("effs", [128, 512], BF16))
        rs1 = E(nc.sbuf_tensor("rs1", [128, 1], F32))
        cnacc = E(nc.sbuf_tensor("cnacc", [128, 32], F32))
        rsc = E(nc.sbuf_tensor("rsc", [128, 32], F32))
        resf = E(nc.sbuf_tensor("resf", [128, TL], F32))
        gsb = E(nc.sbuf_tensor("gsb", [128, TL], F32))
        cf = E(nc.sbuf_tensor("cf", [128, TL], F32))
        psA = E(nc.psum_tensor("psA", [128, 4, 512], F32))
        block = E(nc.Block())
        p = Prog(nc, es)

        xn1r = xn1_in.rearrange("(c p) t -> p c t", p=128)
        ctr = ctx_in.rearrange("(c p) t -> p c t", p=128)
        x1r = x1_in.rearrange("(c p) t -> p c t", p=128)
        x2r = x2_out.rearrange("(c p) t -> p c t", p=128)

        for i in range(4):
            sl = slice(i * 8, (i + 1) * 8)
            dma(p, "sync", xn1b[:, sl, :], xn1r[:, sl, :])
            dma(p, "sync", ct1[:, sl, :], ctr[:, sl, :])
        dma(p, "gpsimd", a1tb[:], a1t4_in)
        dma(p, "gpsimd", a1b[:], a1_in)
        dma(p, "gpsimd", b1tb[:], b1t_in)
        dma(p, "sync", magc[:], mag_in)
        p.op("vector", lambda e: e.memset(cnacc[:], 0.0))

        for half in range(2):
            mms = [lambda e, c=c, half=half, st=(c == 0), sp=(c == NCH - 1):
                   e.matmul(psA[0:32, half, :], a1tb[:, c, :], xn1b[:, c, half * 512:(half + 1) * 512], start=st, stop=sp)
                   for c in range(NCH)]
            p.chain("tensor", mms)
        p.op("scalar", lambda e: e.copy(u1f[:, 0:512], psA[0:32, 0, :]))
        p.op("scalar", lambda e: e.copy(u1f[:, 512:1024], psA[0:32, 1, :]))
        p.op("vector", lambda e: e.tensor_copy(u1b[:], u1f[:]))

        for oc in range(32):
            dma(p, "sync", w1nf[:], w1n4_in[oc].rearrange("p c i -> p (c i)"))
            for ib in range(8):
                isl = slice(ib * 512, (ib + 1) * 512)
                p.op("tensor", lambda e, oc=oc, isl=isl: e.matmul(
                    psA[:, 2, :], b1tb[:, oc * 128:(oc + 1) * 128], a1b[:, isl], start=True, stop=True))
                p.op("vector", lambda e, isl=isl: e.tensor_tensor(effb[:], psA[:, 2, :], w1nf[:, isl], op=ALU.add))
                p.op("scalar", lambda e: e.activation(effs[:], effb[:], AF.Square))
                p.op("vector", lambda e: e.reduce_sum(rs1[:], effs[:], axis=AX.X))
                p.op("vector", lambda e, oc=oc: e.tensor_tensor(cnacc[:, oc:oc + 1], cnacc[:, oc:oc + 1], rs1[:], op=ALU.add))
        p.op("scalar", lambda e: e.activation(cnacc[:], cnacc[:], AF.Sqrt))
        p.op("vector", lambda e: e.reciprocal(rsc[:], cnacc[:]))
        p.op("vector", lambda e: e.tensor_tensor(rsc[:], rsc[:], magc[:], op=ALU.mult), drain=True)

        for oc in range(32):
            dma(p, "gpsimd", fw1b[:], w1t4_in[oc])
            for half in range(2):
                hs = slice(half * 512, (half + 1) * 512)
                mms = [lambda e, c=c, half=half, hs=hs:
                       e.matmul(psA[:, half, :], fw1b[:, c, :], xn1b[:, c, hs], start=(c == 0), stop=False)
                       for c in range(NCH)]
                mms.append(lambda e, half=half, hs=hs, oc=oc: e.matmul(
                    psA[:, half, :], b1tb[:, oc * 128:(oc + 1) * 128], u1b[:, hs], start=False, stop=True))
                p.chain("tensor", mms)
            p.op("vector", lambda e, oc=oc: e.tensor_scalar(resf[:, 0:512], psA[:, 0, :], rsc[:, oc:oc + 1], None, op0=ALU.mult))
            p.op("vector", lambda e, oc=oc: e.tensor_scalar(resf[:, 512:1024], psA[:, 1, :], rsc[:, oc:oc + 1], None, op0=ALU.mult))
            p.op("scalar", lambda e: e.activation(gsb[:], resf[:], AF.Silu))
            p.op("vector", lambda e, oc=oc: e.tensor_copy(cf[:], ct1[:, oc, :]))
            p.op("vector", lambda e: e.tensor_tensor(cf[:], cf[:], gsb[:], op=ALU.mult))
            dma(p, "sync", gsb[:], x1r[:, oc, :])
            p.op("vector", lambda e: e.tensor_tensor(cf[:], gsb[:], cf[:], op=ALU.add))
            dma(p, "sync", x2r[:, oc, :], cf[:])
        p.finish(block)
    return nc



def _run(nc, in_maps):
    res = run_bass_kernel_spmd(nc, in_maps, core_ids=list(range(NCORE)))
    return res.results


USE_FUSED = True


def kernel(L0_attn_x, L0_attn_Wq, L0_attn_Wkv, L0_attn_Wo, L0_attn_bo,
           L0_ffn_weight, L0_ffn_lora_A, L0_ffn_lora_B, L0_ffn_gw1, L0_ffn_gb1,
           L0_ffn_ln_gamma, L0_ffn_ln_beta, L0_ffn_gw2, L0_ffn_gb2,
           L1_attn_wq, L1_attn_bq, L1_attn_wk, L1_attn_bk, L1_attn_wv,
           L1_attn_bv, L1_attn_wl, L1_attn_bl, L1_ffn_W, L1_ffn_lora_A,
           L1_ffn_lora_B, L1_ffn_magnitude):
    if USE_FUSED:
        return kernel_fused(dict(
            L0_attn_x=L0_attn_x, L0_attn_Wq=L0_attn_Wq, L0_attn_Wkv=L0_attn_Wkv,
            L0_attn_Wo=L0_attn_Wo, L0_attn_bo=L0_attn_bo, L0_ffn_weight=L0_ffn_weight,
            L0_ffn_lora_A=L0_ffn_lora_A, L0_ffn_lora_B=L0_ffn_lora_B,
            L0_ffn_gw1=L0_ffn_gw1, L0_ffn_gb1=L0_ffn_gb1,
            L0_ffn_ln_gamma=L0_ffn_ln_gamma, L0_ffn_ln_beta=L0_ffn_ln_beta,
            L0_ffn_gw2=L0_ffn_gw2, L0_ffn_gb2=L0_ffn_gb2,
            L1_attn_wq=L1_attn_wq, L1_attn_bq=L1_attn_bq, L1_attn_wk=L1_attn_wk,
            L1_attn_bk=L1_attn_bk, L1_attn_wv=L1_attn_wv, L1_attn_bv=L1_attn_bv,
            L1_ffn_W=L1_ffn_W, L1_ffn_lora_A=L1_ffn_lora_A,
            L1_ffn_lora_B=L1_ffn_lora_B, L1_ffn_magnitude=L1_ffn_magnitude))
    c32 = np.ascontiguousarray
    xT = c32(np.asarray(L0_attn_x, dtype=np.float32).reshape(T, D).T)
    ident = _ident()

    gw1t = c32(np.asarray(L0_ffn_gw1, np.float32).T)
    nc_a = build_A()
    in_a = []
    for c in range(NCORE):
        in_a.append({
            "x": c32(xT[:, c * TL:(c + 1) * TL]),
            "gw1t": gw1t,
            "gb1c": c32(np.asarray(L0_ffn_gb1, np.float32).reshape(32, 1)),
            "gam": c32(np.asarray(L0_ffn_ln_gamma, np.float32).reshape(1, 32)),
            "bet": c32(np.asarray(L0_ffn_ln_beta, np.float32).reshape(1, 32)),
            "gw2t": c32(np.asarray(L0_ffn_gw2, np.float32).T),
            "gb2": c32(np.asarray(L0_ffn_gb2, np.float32).reshape(1, 8)),
            "ident": ident,
        })
    res_a = _run(nc_a, in_a)
    xn_full = np.concatenate([res_a[c]["xn"] for c in range(NCORE)], axis=1)
    cnts = np.stack([res_a[c]["cnt"].reshape(2, 8) for c in range(NCORE)])
    wgs = [res_a[c]["wg"] for c in range(NCORE)]

    xn4 = c32(xn_full.reshape(NCH, 128, 32, 256).transpose(2, 1, 0, 3))

    def tile_pcf(w, width):
        return c32(w.reshape(NCH, 128, width).transpose(1, 0, 2))

    nc_b = build_attn(l1=False)
    tri = np.where(np.arange(128)[:, None] > np.arange(128)[None, :],
                   np.float32(NEG), np.float32(0.0)).astype(np.float32)
    Wq = np.asarray(L0_attn_Wq, np.float32)
    Wkv = np.asarray(L0_attn_Wkv, np.float32)
    in_b = []
    for c in range(NCORE):
        wq_sl = c32(Wq[:, 512 * c:512 * (c + 1)])
        wkv_sl = c32(Wkv[:, 256 * c:256 * (c + 1)])
        in_b.append({
            "xn4": xn4,
            "wq4": tile_pcf(wq_sl, 512),
            "wkv4": tile_pcf(wkv_sl, 256),
            "tri": tri,
        })
    res_b = _run(nc_b, in_b)
    ctx_full = np.concatenate([res_b[c]["ctx"] for c in range(NCORE)], axis=0)

    nc_c = build_C()
    wo4 = _tile_w(np.asarray(L0_attn_Wo, np.float32))
    fw4 = _tile_w(c32(np.asarray(L0_ffn_weight, np.float32).T))
    la_cat = c32(np.asarray(L0_ffn_lora_A, np.float32).transpose(1, 0, 2).reshape(D, 128))
    la4 = c32(la_cat.reshape(NCH, 128, 128).transpose(1, 0, 2))
    lb_cat = c32(np.asarray(L0_ffn_lora_B, np.float32).reshape(128, D))
    boc = _to_col(np.asarray(L0_attn_bo, np.float32))
    sel2 = np.zeros((16, 2), np.float32)
    sel2[0::2, 0] = 1.0
    sel2[1::2, 1] = 1.0
    ssel = np.zeros((8, 128), np.float32)
    for e_ in range(8):
        ssel[e_, e_ * 16:(e_ + 1) * 16] = 1.0
    cnt16 = c32(cnts.reshape(16, 8))
    in_c = []
    for c in range(NCORE):
        in_c.append({
            "x": c32(xT[:, c * TL:(c + 1) * TL]),
            "xn": c32(xn_full[:, c * TL:(c + 1) * TL]),
            "ctxt": c32(ctx_full[:, c * TL:(c + 1) * TL]),
            "wo4": wo4,
            "fw4": fw4,
            "boc": boc,
            "la4": la4,
            "lb": lb_cat,
            "cnts": cnt16,
            "sel2": sel2,
            "wg": wgs[c],
            "ssel": ssel,
            "ident": ident,
        })
    res_c = _run(nc_c, in_c)
    x1_sh = [res_c[c]["x1"] for c in range(NCORE)]
    xn1_full = np.concatenate([res_c[c]["xn1"] for c in range(NCORE)], axis=1)
    xn14 = c32(xn1_full.reshape(NCH, 128, 32, 256).transpose(2, 1, 0, 3))

    nc_d = build_D()
    wq1 = np.asarray(L1_attn_wq, np.float32)
    wk1 = np.asarray(L1_attn_wk, np.float32)
    wv1 = np.asarray(L1_attn_wv, np.float32)
    wq4 = _tile_w(c32(wq1.T))
    wk4 = _tile_w(c32(wk1.T))
    wv4 = _tile_w(c32(wv1.T))
    bqc = _to_col(np.asarray(L1_attn_bq, np.float32))
    bkc = _to_col(np.asarray(L1_attn_bk, np.float32))
    bvc = _to_col(np.asarray(L1_attn_bv, np.float32))
    identb = np.eye(128).astype(ml_dtypes.bfloat16)
    in_d = []
    for c in range(NCORE):
        in_d.append({
            "xn1": c32(xn1_full[:, c * TL:(c + 1) * TL]),
            "wq4": wq4, "wk4": wk4, "wv4": wv4,
            "bqc": bqc, "bkc": bkc, "bvc": bvc,
            "identb": identb,
        })
    res_d = _run(nc_d, in_d)
    ctx1_sh = [np.asarray(res_d[c]["ctx"]) for c in range(NCORE)]

    nc_e = build_E()
    W1 = np.asarray(L1_ffn_W, np.float32)
    w1t4 = _tile_w(c32(W1.T))
    w1n4 = c32(W1.reshape(32, 128, NCH, 128))
    A1 = np.asarray(L1_ffn_lora_A, np.float32)
    B1 = np.asarray(L1_ffn_lora_B, np.float32)
    a1t4 = c32(A1.T.reshape(NCH, 128, 32).transpose(1, 0, 2))
    b1t = c32(B1.T)
    magc = _to_col(np.asarray(L1_ffn_magnitude, np.float32))
    in_e = []
    for c in range(NCORE):
        in_e.append({
            "x1": x1_sh[c],
            "xn1": c32(xn1_full[:, c * TL:(c + 1) * TL]),
            "ctxt": ctx1_sh[c],
            "w1t4": w1t4,
            "w1n4": w1n4,
            "a1t4": a1t4,
            "a1": A1,
            "b1t": b1t,
            "magc": magc,
        })
    res_e = _run(nc_e, in_e)
    x2 = np.concatenate([res_e[c]["x2"] for c in range(NCORE)], axis=1)
    return np.ascontiguousarray(x2.T).reshape(1, T, D).astype(np.float32)




def build_fused(debug=False):
    nc = bass.Bass("TRN2", target_bir_lowering=False)
    AP_ = lambda name, shape, dt: nc.dram_tensor(name, shape, dt, kind="ExternalInput").ap()
    xfull_in = AP_("xfull", [D, T], F32)
    xown_in = AP_("xown", [D, TL], F32)
    wkk4_in = AP_("wkk4", [128, NCH, 1024], F32)
    wvv4_in = AP_("wvv4", [128, NCH, 1024], F32)
    gw1t4_in = AP_("gw1t4", [128, NCH, 32], F32)
    gb1c_in = AP_("gb1c", [32, 1], F32)
    gam_in = AP_("gam", [1, 32], F32)
    bet_in = AP_("bet", [1, 32], F32)
    gw2t_in = AP_("gw2t", [32, 8], F32)
    gb2_in = AP_("gb2", [1, 8], F32)
    wq4_in = AP_("wq4", [32, 128, NCH, 128], F32)
    wo4_in = AP_("wo4", [32, 128, NCH, 128], F32)
    fw4_in = AP_("fw4", [32, 128, NCH, 128], F32)
    boc_in = AP_("boc", [128, 32], F32)
    la4_in = AP_("la4", [128, NCH, 128], F32)
    lb_in = AP_("lb", [128, D], F32)
    ssel_in = AP_("ssel", [8, 128], F32)
    l1q4_in = AP_("l1q4", [32, 128, NCH, 128], F32)
    l1k4_in = AP_("l1k4", [32, 128, NCH, 128], F32)
    l1v4_in = AP_("l1v4", [32, 128, NCH, 128], F32)
    bqc_in = AP_("bqc", [128, 32], F32)
    bkc_in = AP_("bkc", [128, 32], F32)
    bvc_in = AP_("bvc", [128, 32], F32)
    w1t4_in = AP_("w1t4", [32, 128, NCH, 128], F32)
    w1n4_in = AP_("w1n4", [32, 128, NCH, 128], F32)
    a1t4_in = AP_("a1t4", [128, NCH, 32], F32)
    a1_in = AP_("a1", [32, D], F32)
    b1t_in = AP_("b1t", [32, D], F32)
    magc_in = AP_("magc", [128, 32], F32)
    id_in = AP_("identf", [128, 128], F32)
    idb_in = AP_("identb", [128, 128], BF16)
    tri_in = AP_("tri", [128, 128], F32)
    biasq_in = AP_("biasq", [128, 32], F32)
    x2_out = nc.dram_tensor("x2", [D, TL], F32, kind="ExternalOutput").ap()
    if debug:
        dbg_mr = nc.dram_tensor("dbg_mr", [2, 512], F32, kind="ExternalOutput").ap()
        dbg_k = nc.dram_tensor("dbg_k", [128, 512], BF16, kind="ExternalOutput").ap()
        dbg_v = nc.dram_tensor("dbg_v", [256, 128], BF16, kind="ExternalOutput").ap()
        dbg_h = nc.dram_tensor("dbg_h", [32, 512], BF16, kind="ExternalOutput").ap()
        dbg_q = nc.dram_tensor("dbg_q", [128, TL], BF16, kind="ExternalOutput").ap()
        dbg_x1 = nc.dram_tensor("dbg_x1", [128, TL], F32, kind="ExternalOutput").ap()
        dbg_c1 = nc.dram_tensor("dbg_c1", [128, TL], BF16, kind="ExternalOutput").ap()
        dbg_ctx = nc.dram_tensor("dbg_ctx", [128, 512], BF16, kind="ExternalOutput").ap()
        dbg_xn = nc.dram_tensor("dbg_xn", [128, TL], BF16, kind="ExternalOutput").ap()
        dbg_wb = nc.dram_tensor("dbg_wb", [128, TL], F32, kind="ExternalOutput").ap()
        dbg_xn1 = nc.dram_tensor("dbg_xn1", [128, TL], BF16, kind="ExternalOutput").ap()
        dbg_q1 = nc.dram_tensor("dbg_q1", [D, TL], BF16, kind="ExternalOutput").ap()
        dbg_k1 = nc.dram_tensor("dbg_k1", [D, TL], BF16, kind="ExternalOutput").ap()
        dbg_v1 = nc.dram_tensor("dbg_v1", [D, TL], BF16, kind="ExternalOutput").ap()
        dbg_c1f = nc.dram_tensor("dbg_c1f", [D, TL], BF16, kind="ExternalOutput").ap()
        dbg_st = nc.dram_tensor("dbg_st", [6, TL], F32, kind="ExternalOutput").ap()
        dbg_x1b = nc.dram_tensor("dbg_x1b", [128, 3, TL], F32, kind="ExternalOutput").ap()
        dbg_sg = nc.dram_tensor("dbg_sg", [128, 2, TL], F32, kind="ExternalOutput").ap()
        dbg_cnt = nc.dram_tensor("dbg_cnt", [1, 16], F32, kind="ExternalOutput").ap()
        dbg_al = nc.dram_tensor("dbg_al", [128, 16], F32, kind="ExternalOutput").ap()
        dbg_wt = nc.dram_tensor("dbg_wt", [128, 2, 64], F32, kind="ExternalOutput").ap()
        dbg_hf = nc.dram_tensor("dbg_hf", [32, T], BF16, kind="ExternalOutput").ap()
        dbg_us = nc.dram_tensor("dbg_us", [128, TL], BF16, kind="ExternalOutput").ap()
        dbg_pc = nc.dram_tensor("dbg_pc", [128, 64], F32, kind="ExternalOutput").ap()
        dbg_hb = nc.dram_tensor("dbg_hb", [32, 128], BF16, kind="ExternalOutput").ap()
        dbg_hr = nc.dram_tensor("dbg_hr", [128, 32], F32, kind="ExternalOutput").ap()
        dbg_ln = nc.dram_tensor("dbg_ln", [128, 40], F32, kind="ExternalOutput").ap()

    kscr = nc.dram_tensor("kscr", [1024, T], BF16)
    vscr = nc.dram_tensor("vscr", [T, 1024], BF16)
    qscr = nc.dram_tensor("qscr", [D, TL], BF16)
    x1scr = nc.dram_tensor("x1scr", [D, TL], F32)
    hscr = nc.dram_tensor("hscr", [32, T], BF16)
    mrscr = nc.dram_tensor("mrscr", [2, T], F32)
    wgscr = nc.dram_tensor("wgscr", [2, T, 8], F32)
    q1scr = nc.dram_tensor("q1scr", [D, TL], BF16)
    k1scr = nc.dram_tensor("k1scr", [D, TL], BF16)
    v1scr = nc.dram_tensor("v1scr", [D, TL], BF16)
    ctxscr = nc.dram_tensor("ctxscr", [D, TL], BF16)
    kdscr = nc.dram_tensor("kdscr", [16, 128, 512], BF16)
    vdscr = nc.dram_tensor("vdscr", [16, 512, 128], BF16)

    es = contextlib.ExitStack()
    with es:
        E = es.enter_context
        M = E(nc.sbuf_tensor("M", [128, 86016], BF16))

        def mv(off, *shape):
            n = 1
            for s in shape:
                n *= s
            ap = M[:, off:off + n]
            if len(shape) == 2:
                ap = ap.rearrange("p (a b) -> p a b", b=shape[1])
            elif len(shape) == 3:
                ap = ap.rearrange("p (a b c) -> p a b c", c=shape[2], b=shape[1])
            return ap
        W2 = 65536
        wkb = mv(0, 32, 1024)
        wvb = mv(0, 32, 1024)
        xnb = mv(0, 32, 1024)
        xtile = mv(32768, 32, 256)
        ctxb = mv(32768, 32, 1024)
        wtl = mv(W2, 32, 128)
        wtl2 = mv(W2 + 4096, 32, 128)
        wtlB = mv(W2 + 12288, 32, 128)
        wtl2B = mv(W2 + 16384, 32, 128)
        wtlE = (wtl, wtl2)
        wtlH = (wtl, wtlB)
        wtl2H = (wtl2, wtl2B)
        xtileB = mv(32768 + 8192, 32, 256)
        qtile = M[:, W2 + 8192:W2 + 9216]
        usb = M[:, W2 + 9216:W2 + 10240]
        x1bb = M[:, W2 + 10240:W2 + 11264]
        sqx = M[:, W2 + 11264:W2 + 12288]
        sqt = M[:, W2 + 12288:W2 + 12544]
        ktile = M[:, W2 + 12544:W2 + 12800]
        vtile = M[:, W2 + 12800:W2 + 13312]
        sqtB = M[:, W2:W2 + 256]
        ktileB = M[:, W2 + 256:W2 + 512]
        gtile = M[:, W2 + 512:W2 + 768]
        vtileB = M[:, W2 + 768:W2 + 1280]
        gw1b = mv(W2 + 13312, 32, 32)
        u1b_full = M[:, W2 + 14336:W2 + 15360]
        u1b = u1b_full[0:32, :]
        effb = M[:, W2 + 15360:W2 + 15872]
        effs = M[:, W2 + 15872:W2 + 16384]
        kgb = mv(W2, 64, 128)
        vgb = mv(W2 + 8192, 64, 128)
        ptb4 = M[:, W2 + 16384:W2 + 18432]
        kdiag = M[:, W2 + 18432:W2 + 18944]
        vdiag = mv(W2 + 18944, 4, 128)
        qb = mv(W2 + 4096, 32, 128)
        kb = mv(W2 + 8192, 32, 128)
        vb = mv(W2 + 12288, 32, 128)
        ctb2 = mv(W2 + 16384, 32, 128)
        vts1 = mv(W2 + 2048, 16, 128)
        pt1f = M[:, W2 + 0:W2 + 512]
        pt1 = pt1f[0:32, :]
        w1nb_f = M[:, W2:W2 + 4096]
        w1nbB = M[:, W2 + 16384:W2 + 20480]
        effbB = M[:, W2 + 12288:W2 + 12800]
        effsB = M[:, W2 + 12800:W2 + 13312]

        Ff = E(nc.sbuf_tensor("Ff", [128, 5120], F32))
        slotA = Ff[:, 0:1024]
        slotB = Ff[:, 1024:2048]
        slotC = Ff[:, 2048:3072]
        slotD = Ff[:, 3072:4096]
        colvb = slotD
        mbc = Ff[:, 4096:4352]
        rbc = Ff[:, 4352:4608]
        tmpc = Ff[:, 4608:4864]

        onesf = E(nc.sbuf_tensor("onesf", [1, 128], F32))
        onesb = E(nc.sbuf_tensor("onesb", [128, 1], BF16))
        ones11 = E(nc.sbuf_tensor("ones11", [1, 1], F32))
        colkc = E(nc.sbuf_tensor("colkc", [128, 8], F32))
        colg = E(nc.sbuf_tensor("colg", [32, 1], F32))
        mrow = E(nc.sbuf_tensor("mrow", [1, 256], F32))
        qrow = E(nc.sbuf_tensor("qrow", [1, 256], F32))
        vrow = E(nc.sbuf_tensor("vrow", [1, 256], F32))
        rrow = E(nc.sbuf_tensor("rrow", [1, 256], F32))
        mcol = E(nc.sbuf_tensor("mcol", [128, 2], F32))
        rcol = E(nc.sbuf_tensor("rcol", [128, 2], F32))
        morow = E(nc.sbuf_tensor("morow", [1, TL], F32))
        gb1c = E(nc.sbuf_tensor("gb1c_s", [32, 1], F32))
        gamr = E(nc.sbuf_tensor("gamr", [1, 32], F32))
        betr = E(nc.sbuf_tensor("betr", [1, 32], F32))
        gw2t = E(nc.sbuf_tensor("gw2ts", [32, 8], F32))
        gb2r = E(nc.sbuf_tensor("gb2r", [1, 8], F32))
        gamb = E(nc.sbuf_tensor("gamb", [128, 32], F32))
        betb = E(nc.sbuf_tensor("betb", [128, 32], F32))
        gb2b = E(nc.sbuf_tensor("gb2b", [128, 8], F32))
        htm = E(nc.sbuf_tensor("htm", [128, 32], F32))
        htb = E(nc.sbuf_tensor("htb", [32, 128], BF16))
        hm = E(nc.sbuf_tensor("hm", [128, 1], F32))
        hv = E(nc.sbuf_tensor("hv", [128, 1], F32))
        hrs = E(nc.sbuf_tensor("hrs", [128, 1], F32))
        hsq = E(nc.sbuf_tensor("hsq", [128, 32], F32))
        hft = E(nc.sbuf_tensor("hft", [32, 128], F32))
        gts = E(nc.sbuf_tensor("gts", [128, 8], F32))
        iotf = E(nc.sbuf_tensor("iotf", [128, 8], F32))
        ioti = E(nc.sbuf_tensor("ioti", [128, 8], I32))
        v1 = E(nc.sbuf_tensor("v1", [128, 1], F32))
        nv1 = E(nc.sbuf_tensor("nv1", [128, 1], F32))
        v2 = E(nc.sbuf_tensor("v2", [128, 1], F32))
        e2 = E(nc.sbuf_tensor("e2", [128, 1], F32))
        g1 = E(nc.sbuf_tensor("g1", [128, 1], F32))
        g2c = E(nc.sbuf_tensor("g2c", [128, 1], F32))
        sel8 = E(nc.sbuf_tensor("sel8", [128, 8], F32))
        m1c = E(nc.sbuf_tensor("m1c", [128, 1], F32))
        oh1 = E(nc.sbuf_tensor("oh1", [128, 8], F32))
        oh2 = E(nc.sbuf_tensor("oh2", [128, 8], F32))
        wtown1 = E(nc.sbuf_tensor("wtown1", [128, 8, 8], F32))
        wtown2 = E(nc.sbuf_tensor("wtown2", [128, 8, 8], F32))
        cac1 = E(nc.sbuf_tensor("cac1", [128, 8], F32))
        cac2 = E(nc.sbuf_tensor("cac2", [128, 8], F32))
        cacb = E(nc.sbuf_tensor("cacb", [128, 8], BF16))
        cgrow = E(nc.sbuf_tensor("cgrow", [1, 16], F32))
        alw = E(nc.sbuf_tensor("alw", [1, 16], F32))
        alb1 = E(nc.sbuf_tensor("alb1", [128, 8], F32))
        alb2 = E(nc.sbuf_tensor("alb2", [128, 8], F32))
        wfm = slotD[0:8, :]
        wbcb = slotC
        ssel_sb = E(nc.sbuf_tensor("ssel_sb", [8, 128], F32))
        boc = E(nc.sbuf_tensor("boc_s", [128, 32], F32))
        bqc = E(nc.sbuf_tensor("bqc_s", [128, 32], F32))
        bkc = E(nc.sbuf_tensor("bkc_s", [128, 32], F32))
        bvc = E(nc.sbuf_tensor("bvc_s", [128, 32], F32))
        magc = E(nc.sbuf_tensor("magc_s", [128, 32], F32))
        biasq = E(nc.sbuf_tensor("biasq_s", [128, 32], F32))
        tri = E(nc.sbuf_tensor("tri_s", [128, 128], F32))
        ident = E(nc.sbuf_tensor("ident_s", [128, 128], F32))
        identb = E(nc.sbuf_tensor("identb_s", [128, 128], BF16))
        qts = M[:, 84992:85504]
        qtsB = M[:, 85504:86016]
        lbt = E(nc.sbuf_tensor("lbt", [128, 128], BF16))
        lbtB = E(nc.sbuf_tensor("lbtB", [128, 128], BF16))
        lbtH = (lbt, lbtB)
        rcb = Ff[:, 4096:4608]
        rs = Ff[0:1, 4608:5120]
        lab = mv(W2, 32, 128)
        a1tb = mv(W2, 32, 32)
        a1b = M[:, W2 + 4096:W2 + 8192][0:32, :]
        b1tb = M[:, W2 + 8192:W2 + 12288][0:32, :]
        ctxrd = M[:, W2 + 12288:W2 + 13312]
        rs1 = E(nc.sbuf_tensor("rs1", [128, 1], F32))
        cnacc = E(nc.sbuf_tensor("cnacc", [128, 32], F32))
        rsc = E(nc.sbuf_tensor("rsc", [128, 32], F32))
        psA = E(nc.psum_tensor("psA", [128, 7, 512], F32))
        psVt = E(nc.psum_tensor("psVt", [128, 1, 1024], BF16))
        block = E(nc.Block())
        p = Prog(nc, es)

        xfr = xfull_in.rearrange("(c p) t -> p c t", p=128)
        xor_ = xown_in.rearrange("(c p) t -> p c t", p=128)
        x1r = x1scr.ap().rearrange("(c p) t -> p c t", p=128)
        x2r = x2_out.rearrange("(c p) t -> p c t", p=128)
        AXX = AX.X

        p.op("vector", lambda e: e.memset(onesf[:], 1.0))
        p.op("vector", lambda e: e.memset(onesb[:], 1.0))
        p.op("vector", lambda e: e.memset(ones11[:], 1.0))
        p.op("vector", lambda e: e.memset(cac1[:], 0.0))
        p.op("vector", lambda e: e.memset(cac2[:], 0.0))
        p.op("gpsimd", lambda e: e.iota(ioti[:], pattern=[[1, 8]], base=0, channel_multiplier=0))
        p.op("vector", lambda e: e.tensor_copy(iotf[:], ioti[:]))
        for dst, src_ in ((gb1c, gb1c_in), (gamr, gam_in), (betr, bet_in), (gw2t, gw2t_in),
                          (gb2r, gb2_in), (boc, boc_in), (bqc, bqc_in), (bkc, bkc_in),
                          (bvc, bvc_in), (magc, magc_in), (biasq, biasq_in), (tri, tri_in),
                          (ident, id_in), (identb, idb_in), (ssel_sb, ssel_in)):
            dma(p, "sync", dst[:], src_)
        dma(p, "gpsimd", gw1b, gw1t4_in)
        p.chain("tensor", [
            lambda e: e.matmul(psA[:, 0, 0:32], onesf[:], gamr[:], start=True, stop=True),
            lambda e: e.matmul(psA[:, 1, 0:32], onesf[:], betr[:], start=True, stop=True),
            lambda e: e.matmul(psA[:, 2, 0:8], onesf[:], gb2r[:], start=True, stop=True),
        ])
        p.op("scalar", lambda e: e.copy(gamb[:], psA[:, 0, 0:32]))
        p.op("scalar", lambda e: e.copy(betb[:], psA[:, 1, 0:32]))
        p.op("scalar", lambda e: e.copy(gb2b[:], psA[:, 2, 0:8]))

        for i in range(4):
            sl = slice(i * 8, (i + 1) * 8)
            dma(p, "gpsimd", wkb[:, sl, :], wkk4_in[:, sl, :])
        for oc in range(8):
            p.chain("tensor", [
                lambda e, c=c, oc=oc, st=(c == 0), sp=(c == NCH - 1): e.matmul(
                    psA[:, 3, 0:1], wkb[:, c, oc * 128:(oc + 1) * 128], onesb[:], start=st, stop=sp)
                for c in range(NCH)])
            p.op("scalar", lambda e, oc=oc: e.copy(colkc[:, oc:oc + 1], psA[:, 3, 0:1]))
        p.chain("tensor", [
            lambda e, c=c, st=(c == 0), sp=(c == NCH - 1): e.matmul(
                psA[0:32, 3, 0:1], gw1b[:, c, :], onesb[:], start=st, stop=sp)
            for c in range(NCH)])
        p.op("scalar", lambda e: e.copy(colg[:], psA[0:32, 3, 0:1]))

        endA = {}
        floorA = p.mark()
        sqtP = (sqt, sqtB)
        ktileP = (ktile, ktileB)
        mChA, mSubA, mKoutA, mGevA, mHoutA = {}, {}, {}, {}, {}
        for tb in range(32):
            ts_ = slice(tb * 256, (tb + 1) * 256)
            xt_ = (xtile, xtileB)[tb % 2]
            p.op("gpsimd", lambda e, xt_=xt_, ts_=ts_: e.dma_start(out=xt_[:], in_=xfr[:, :, ts_]),
                 inc=16, wait_at=endA.get(tb - 2, floorA))
            dmam = p.mark()
            for c in range(NCH):
                u = tb * NCH + c
                sq_ = sqtP[c % 2]
                p.op("scalar", lambda e, xt_=xt_, c=c, sq_=sq_: e.activation(sq_, xt_[:, c, :], AF.Square),
                     wait_at=mmax(dmam, mChA.get(u - 2, floorA)))
                msq = p.mark()
                st, sp = (c == 0), (c == NCH - 1)
                p.chain("tensor", [
                    lambda e, c=c, st=st, sp=sp, xt_=xt_: e.matmul(psA[0:1, 4, 0:256], onesb[:], xt_[:, c, :], start=st, stop=sp),
                    lambda e, st=st, sp=sp, sq_=sq_: e.matmul(psA[0:1, 5, 0:256], onesb[:], sq_, start=st, stop=sp),
                ], wait_at=msq)
                mChA[u] = p.mark()
            p.op("scalar", lambda e: e.activation(mrow[:], psA[0:1, 4, 0:256], AF.Copy, scale=f32(1.0 / D)))
            p.op("scalar", lambda e: e.activation(qrow[:], psA[0:1, 5, 0:256], AF.Copy, scale=f32(1.0 / D)))
            p.op("vector", lambda e: e.tensor_tensor(vrow[:], mrow[:], mrow[:], op=ALU.mult))
            p.op("vector", lambda e: e.tensor_tensor(vrow[:], qrow[:], vrow[:], op=ALU.subtract))
            p.op("vector", lambda e: e.tensor_scalar_add(vrow[:], vrow[:], LN_EPS))
            p.op("scalar", lambda e: e.activation(vrow[:], vrow[:], AF.Sqrt))
            p.op("vector", lambda e: e.reciprocal(rrow[:], vrow[:]))
            dma(p, "sync", mrscr.ap()[0:1, ts_], mrow[:])
            dma(p, "sync", mrscr.ap()[1:2, ts_], rrow[:])
            p.op("tensor", lambda e: e.matmul(psA[:, 6, 0:256], onesf[:], mrow[:], start=True, stop=True))
            p.op("scalar", lambda e: e.copy(mbc, psA[:, 6, 0:256]))
            p.op("tensor", lambda e: e.matmul(psA[:, 6, 0:256], onesf[:], rrow[:], start=True, stop=True))
            p.op("scalar", lambda e: e.copy(rbc, psA[:, 6, 0:256]))
            mRB = p.mark()
            for oc in range(8):
                g_ = tb * 8 + oc
                bq = oc % 2
                kt_ = ktileP[oc % 2]
                p.chain("tensor", [
                    lambda e, c=c, oc=oc, bq=bq, st=(c == 0), sp=(c == NCH - 1), xt_=xt_: e.matmul(
                        psA[:, bq, 0:256], wkb[:, c, oc * 128:(oc + 1) * 128], xt_[:, c, :], start=st, stop=sp)
                    for c in range(NCH)], wait_at=mSubA.get(g_ - 2, floorA))
                mchk = p.mark()
                p.op("vector", lambda e, oc=oc: e.tensor_scalar(tmpc, mbc, colkc[:, oc:oc + 1], None, op0=ALU.mult),
                     wait_at=(mRB if oc == 0 else 0))
                p.op("vector", lambda e, bq=bq: e.tensor_tensor(tmpc, psA[:, bq, 0:256], tmpc, op=ALU.subtract),
                     wait_at=mchk)
                mSubA[g_] = p.mark()
                p.op("vector", lambda e, kt_=kt_: e.tensor_tensor(kt_, tmpc, rbc, op=ALU.mult),
                     wait_at=mKoutA.get(g_ - 2, floorA))
                mmul = p.mark()
                p.op("sync", lambda e, oc=oc, ts_=ts_, kt_=kt_: e.dma_start(
                    out=kscr.ap()[oc * 128:(oc + 1) * 128, ts_], in_=kt_), inc=16, wait_at=mmul)
                mKoutA[g_] = p.mark()
            p.chain("tensor", [
                lambda e, c=c, st=(c == 0), sp=(c == NCH - 1), xt_=xt_: e.matmul(
                    psA[0:32, 2, 0:256], gw1b[:, c, :], xt_[:, c, :], start=st, stop=sp)
                for c in range(NCH)], wait_at=mGevA.get(tb - 1, floorA))
            mchg = p.mark()
            p.op("vector", lambda e: e.tensor_scalar(tmpc[0:32, :], mbc[0:32, :], colg[:], None, op0=ALU.mult),
                 wait_at=0)
            p.op("vector", lambda e: e.tensor_tensor(tmpc[0:32, :], psA[0:32, 2, 0:256], tmpc[0:32, :], op=ALU.subtract),
                 wait_at=mchg)
            mGevA[tb] = p.mark()
            p.op("vector", lambda e: e.tensor_tensor(tmpc[0:32, :], tmpc[0:32, :], rbc[0:32, :], op=ALU.mult),
                 wait_at=0)
            p.op("vector", lambda e: e.tensor_scalar(gtile[0:32, :], tmpc[0:32, :], gb1c[:], None, op0=ALU.add),
                 wait_at=mHoutA.get(tb - 1, floorA))
            mgt = p.mark()
            p.op("sync", lambda e, ts_=ts_: e.dma_start(out=hscr.ap()[:, ts_], in_=gtile[0:32, :]),
                 inc=16, wait_at=mgt)
            mHoutA[tb] = p.mark()
            endA[tb] = p.mark()

        for i in range(4):
            sl = slice(i * 8, (i + 1) * 8)
            dma(p, "gpsimd", wvb[:, sl, :], wvv4_in[:, sl, :])
        for half in range(2):
            p.chain("tensor", [
                lambda e, c=c, half=half, st=(c == 0), sp=(c == NCH - 1): e.matmul(
                    psA[0:1, 3, :], onesb[:], wvb[:, c, half * 512:(half + 1) * 512], start=st, stop=sp)
                for c in range(NCH)])
            p.op("scalar", lambda e: e.copy(rs[:], psA[0:1, 3, :]))
            p.op("tensor", lambda e: e.matmul(psA[:, 2, :], onesf[:], rs[:], start=True, stop=True))
            p.op("scalar", lambda e, half=half: e.copy(colvb[:, half * 512:(half + 1) * 512], psA[:, 2, :]))
        endB = {}
        floorB = p.mark()
        vtileP = (vtile, vtileB)
        slAB = (slotA[:, 0:512], slotA[:, 512:1024])
        mSubB, mVoutB, mColB = {}, {}, {}
        for tb in range(32):
            ts_ = slice(tb * 256, (tb + 1) * 256)
            xt_ = (xtile, xtileB)[tb % 2]
            p.op("gpsimd", lambda e, xt_=xt_, ts_=ts_: e.dma_start(out=xt_[:], in_=xfr[:, :, ts_]),
                 inc=16, wait_at=endB.get(tb - 2, floorB))
            dma(p, "sync", mrow[:], mrscr.ap()[0:1, ts_])
            dma(p, "sync", rrow[:], mrscr.ap()[1:2, ts_])
            dmam = p.mark()
            mcp = floorB
            for vt in range(2):
                p.chain("tensor", [
                    lambda e, vt=vt: e.matmul(psA[:, 4, 0:1], mrow[0:1, vt * 128:(vt + 1) * 128], ones11[:], start=True, stop=True),
                    lambda e, vt=vt: e.matmul(psA[:, 5, 0:1], rrow[0:1, vt * 128:(vt + 1) * 128], ones11[:], start=True, stop=True),
                ], wait_at=mmax(dmam, mColB.get(tb - 1, floorB)) if vt == 0 else mcp)
                mchc = p.mark()
                p.op("scalar", lambda e, vt=vt: e.copy(mcol[:, vt:vt + 1], psA[:, 4, 0:1]), wait_at=mchc)
                p.op("scalar", lambda e, vt=vt: e.copy(rcol[:, vt:vt + 1], psA[:, 5, 0:1]), wait_at=0)
                mcp = p.mark()
            mColB[tb] = p.mark()
            for vt in range(2):
                tk0 = tb * 256 + vt * 128
                for half in range(2):
                    u = tb * 4 + vt * 2 + half
                    bq = u % 2
                    vt_ = vtileP[u % 2]
                    sl_ = slAB[u % 2]
                    hsl = slice(half * 512, (half + 1) * 512)
                    p.chain("tensor", [
                        lambda e, c=c, vt=vt, hsl=hsl, bq=bq, st=(c == 0), sp=(c == NCH - 1), xt_=xt_: e.matmul(
                            psA[:, bq, :], xt_[:, c, vt * 128:(vt + 1) * 128], wvb[:, c, hsl], start=st, stop=sp)
                        for c in range(NCH)], wait_at=mSubB.get(u - 2, floorB))
                    mchv = p.mark()
                    p.op("vector", lambda e, hsl=hsl, vt=vt, sl_=sl_: e.tensor_scalar(
                        sl_, colvb[:, hsl], mcol[:, vt:vt + 1], None, op0=ALU.mult),
                        wait_at=(mColB[tb] if u % 4 == 0 else 0))
                    p.op("vector", lambda e, bq=bq, sl_=sl_: e.tensor_tensor(
                        sl_, psA[:, bq, :], sl_, op=ALU.subtract), wait_at=mchv)
                    mSubB[u] = p.mark()
                    p.op("vector", lambda e, vt=vt, vt_=vt_, sl_=sl_: e.tensor_scalar(
                        vt_, sl_, rcol[:, vt:vt + 1], None, op0=ALU.mult),
                        wait_at=mVoutB.get(u - 2, floorB))
                    mmul = p.mark()
                    p.op("sync", lambda e, tk0=tk0, hsl=hsl, vt_=vt_: e.dma_start(
                        out=vscr.ap()[tk0:tk0 + 128, hsl], in_=vt_), inc=16, wait_at=mmul)
                    mVoutB[u] = p.mark()
            endB[tb] = p.mark()

        p.force_drain = True
        for t in range(64):
            p.op("sync", lambda e, t=t: e.dma_start(out=htb[:], in_=hscr.ap()[:, t * 128:(t + 1) * 128]), inc=16)
            p.op("tensor", lambda e: e.transpose(psVt[:, 0, 0:32], htb[:], identb[0:32, 0:32]))
            p.op("scalar", lambda e: e.copy(htm[:], psVt[:, 0, 0:32]))
            if debug and t == 0:
                dma(p, "sync", dbg_hr, htm[:])
            p.op("vector", lambda e: e.reduce_sum(hm[:], htm[:], axis=AXX))
            p.op("vector", lambda e: e.tensor_scalar_mul(hm[:], hm[:], 1.0 / 32.0), drain=True)
            p.op("vector", lambda e: e.tensor_scalar(htm[:], htm[:], hm[:], None, op0=ALU.subtract), drain=True)
            if debug and t == 0:
                dma(p, "sync", dbg_ln[:, 0:32], htm[:])
                p.op("sync", lambda e: e.dma_start(out=dbg_ln[:, 32:33], in_=hm[:], allow_slow_non_contiguous=True), inc=16)
            p.op("vector", lambda e: e.tensor_tensor(hsq[:], htm[:], htm[:], op=ALU.mult))
            p.op("vector", lambda e: e.reduce_sum(hv[:], hsq[:], axis=AXX))
            p.op("vector", lambda e: e.tensor_scalar_mul(hv[:], hv[:], 1.0 / 32.0), drain=True)
            p.op("vector", lambda e: e.tensor_scalar_add(hv[:], hv[:], LN_EPS))
            p.op("scalar", lambda e: e.activation(hv[:], hv[:], AF.Sqrt))
            p.op("vector", lambda e: e.reciprocal(hrs[:], hv[:]))
            p.op("vector", lambda e: e.tensor_scalar(htm[:], htm[:], hrs[:], None, op0=ALU.mult), drain=True)
            if debug and t == 0:
                p.op("sync", lambda e: e.dma_start(out=dbg_ln[:, 33:34], in_=hv[:], allow_slow_non_contiguous=True), inc=16)
                p.op("sync", lambda e: e.dma_start(out=dbg_ln[:, 34:35], in_=hrs[:], allow_slow_non_contiguous=True), inc=16)
            p.op("vector", lambda e: e.tensor_tensor(htm[:], htm[:], gamb[:], op=ALU.mult))
            p.op("vector", lambda e: e.tensor_tensor(htm[:], htm[:], betb[:], op=ALU.add))
            p.op("vector", lambda e: e.tensor_relu(htm[:], htm[:]))
            p.op("tensor", lambda e: e.transpose(psA[0:32, 3, 0:128], htm[:], ident[:]))
            p.op("scalar", lambda e: e.copy(hft[:], psA[0:32, 3, 0:128]))
            p.op("tensor", lambda e: e.matmul(psA[:, 2, 64:72], hft[:], gw2t[:], start=True, stop=True))
            p.op("vector", lambda e: e.tensor_tensor(gts[:], psA[:, 2, 64:72], gb2b[:], op=ALU.add))
            p.op("vector", lambda e: e.reduce_max(v1[:], gts[:], axis=AXX))
            p.op("vector", lambda e: e.tensor_scalar(oh1[:], gts[:], v1[:], None, op0=ALU.is_equal), drain=True)
            p.op("vector", lambda e: e.scalar_tensor_tensor(sel8[:], oh1[:], -256.0, iotf[:], op0=ALU.mult, op1=ALU.add))
            p.op("vector", lambda e: e.tensor_reduce(m1c[:], sel8[:], axis=AXX, op=ALU.min))
            p.op("vector", lambda e: e.tensor_scalar(oh1[:], sel8[:], m1c[:], None, op0=ALU.is_equal), drain=True)
            p.op("vector", lambda e: e.scalar_tensor_tensor(sel8[:], oh1[:], NEG, gts[:], op0=ALU.mult, op1=ALU.add))
            p.op("vector", lambda e: e.reduce_max(v2[:], sel8[:], axis=AXX))
            p.op("vector", lambda e: e.tensor_scalar(oh2[:], sel8[:], v2[:], None, op0=ALU.is_equal), drain=True)
            p.op("vector", lambda e: e.scalar_tensor_tensor(sel8[:], oh2[:], -256.0, iotf[:], op0=ALU.mult, op1=ALU.add))
            p.op("vector", lambda e: e.tensor_reduce(m1c[:], sel8[:], axis=AXX, op=ALU.min))
            p.op("vector", lambda e: e.tensor_scalar(oh2[:], sel8[:], m1c[:], None, op0=ALU.is_equal), drain=True)
            p.op("vector", lambda e: e.tensor_scalar_mul(nv1[:], v1[:], -1.0))
            p.op("scalar", lambda e: e.activation(e2[:], v2[:], AF.Exp, bias=nv1[:]))
            p.op("vector", lambda e: e.tensor_scalar_add(g1[:], e2[:], 1.0))
            p.op("vector", lambda e: e.reciprocal(g1[:], g1[:]))
            p.op("vector", lambda e: e.tensor_tensor(g2c[:], e2[:], g1[:], op=ALU.mult), drain=True)
            p.op("vector", lambda e: e.tensor_scalar(sel8[:], oh1[:], g1[:], None, op0=ALU.mult), drain=True)
            dma(p, "sync", wgscr.ap()[0, t * 128:(t + 1) * 128, :], sel8[:])
            p.op("vector", lambda e: e.tensor_scalar(sel8[:], oh2[:], g2c[:], None, op0=ALU.mult), drain=True)
            dma(p, "sync", wgscr.ap()[1, t * 128:(t + 1) * 128, :], sel8[:])
            p.op("vector", lambda e: e.tensor_tensor(cac1[:], cac1[:], oh1[:], op=ALU.add))
            p.op("vector", lambda e: e.tensor_tensor(cac2[:], cac2[:], oh2[:], op=ALU.add))
            if debug and t == 0:
                dma(p, "sync", dbg_hb, htb[:])
                dma(p, "sync", dbg_pc[:, 0:32], htm[:])
                dma(p, "sync", dbg_pc[:, 32:40], gts[:])
                dma(p, "sync", dbg_pc[:, 40:48], oh1[:])
                dma(p, "sync", dbg_pc[:, 48:56], oh2[:])
                dma(p, "sync", dbg_pc[:, 56:64], iotf[:])
        p.op("vector", lambda e: e.tensor_copy(cacb[:], cac1[:]))
        p.op("tensor", lambda e: e.matmul(psA[0:1, 3, 128:136], onesb[:], cacb[:], start=True, stop=True))
        p.op("scalar", lambda e: e.copy(cgrow[0:1, 0:8], psA[0:1, 3, 128:136]))
        p.op("vector", lambda e: e.tensor_copy(cacb[:], cac2[:]))
        p.op("tensor", lambda e: e.matmul(psA[0:1, 3, 136:144], onesb[:], cacb[:], start=True, stop=True))
        p.op("scalar", lambda e: e.copy(cgrow[0:1, 8:16], psA[0:1, 3, 136:144]))
        p.op("vector", lambda e: e.tensor_scalar(alw[:], cgrow[:], CAPACITY, None, op0=ALU.is_le))
        p.op("tensor", lambda e: e.matmul(psA[:, 2, 0:8], onesf[:], alw[0:1, 0:8], start=True, stop=True))
        p.op("tensor", lambda e: e.matmul(psA[:, 3, 0:8], onesf[:], alw[0:1, 8:16], start=True, stop=True))
        p.op("scalar", lambda e: e.copy(alb1[:], psA[:, 2, 0:8]))
        p.op("scalar", lambda e: e.copy(alb2[:], psA[:, 3, 0:8]))
        p.force_drain = False

        def percore_body(e, c, tok):
            for qg in range(2):
                for g in range(8):
                    idx = qg * 8 + g
                    tq0 = 1024 * c + 512 * qg
                    e.dma_start(out=kdscr.ap()[idx], in_=kscr.ap()[g * 128:(g + 1) * 128, tq0:tq0 + 512]).then_inc(tok, 16)
                    e.dma_start(out=vdscr.ap()[idx], in_=vscr.ap()[tq0:tq0 + 512, g * 128:(g + 1) * 128]).then_inc(tok, 16)
            e.dma_start(out=wtown1[:], in_=wgscr.ap()[0, 1024 * c:1024 * (c + 1), :].rearrange(
                "(t pp) ee -> pp t ee", pp=128)).then_inc(tok, 16)
            e.dma_start(out=wtown2[:], in_=wgscr.ap()[1, 1024 * c:1024 * (c + 1), :].rearrange(
                "(t pp) ee -> pp t ee", pp=128)).then_inc(tok, 16)
        p.switch("gpsimd", percore_body, 544)

        for i in range(4):
            sl = slice(i * 8, (i + 1) * 8)
            dma(p, "gpsimd", xnb[:, sl, :], xor_[:, sl, :])
        for c in range(NCH):
            p.op("scalar", lambda e, c=c: e.activation(sqx, xnb[:, c, :], AF.Square))
            st, sp = (c == 0), (c == NCH - 1)
            p.chain("tensor", [
                lambda e, c=c, st=st, sp=sp: e.matmul(psA[0:1, 2, :], onesb[:], xnb[:, c, 0:512], start=st, stop=sp),
                lambda e, c=c, st=st, sp=sp: e.matmul(psA[0:1, 3, :], onesb[:], xnb[:, c, 512:1024], start=st, stop=sp),
                lambda e, st=st, sp=sp: e.matmul(psA[0:1, 4, :], onesb[:], sqx[:, 0:512], start=st, stop=sp),
                lambda e, st=st, sp=sp: e.matmul(psA[0:1, 5, :], onesb[:], sqx[:, 512:1024], start=st, stop=sp),
            ])
        p.op("scalar", lambda e: e.activation(morow[0:1, 0:512], psA[0:1, 2, :], AF.Copy, scale=f32(1.0 / D)))
        p.op("scalar", lambda e: e.activation(morow[0:1, 512:1024], psA[0:1, 3, :], AF.Copy, scale=f32(1.0 / D)))
        for half in range(2):
            hsl = slice(half * 512, (half + 1) * 512)
            p.op("tensor", lambda e, hsl=hsl: e.matmul(psA[:, 6, :], onesf[:], morow[0:1, hsl], start=True, stop=True))
            p.op("scalar", lambda e, hsl=hsl: e.copy(slotA[:, hsl], psA[:, 6, :]))
        p.op("scalar", lambda e: e.activation(morow[0:1, 0:512], psA[0:1, 4, :], AF.Copy, scale=f32(1.0 / D)))
        p.op("scalar", lambda e: e.activation(morow[0:1, 512:1024], psA[0:1, 5, :], AF.Copy, scale=f32(1.0 / D)))
        p.op("vector", lambda e: e.tensor_tensor(slotC[0:1, :], slotA[0:1, :], slotA[0:1, :], op=ALU.mult))
        p.op("vector", lambda e: e.tensor_tensor(morow[0:1, :], morow[0:1, :], slotC[0:1, :], op=ALU.subtract))
        p.op("vector", lambda e: e.tensor_scalar_add(morow[0:1, :], morow[0:1, :], LN_EPS))
        p.op("scalar", lambda e: e.activation(morow[0:1, :], morow[0:1, :], AF.Sqrt))
        p.op("vector", lambda e: e.reciprocal(slotC[0:1, :], morow[0:1, :]))
        for half in range(2):
            hsl = slice(half * 512, (half + 1) * 512)
            p.op("tensor", lambda e, hsl=hsl: e.matmul(psA[:, 6, :], onesf[:], slotC[0:1, hsl], start=True, stop=True))
            p.op("scalar", lambda e, hsl=hsl: e.copy(slotB[:, hsl], psA[:, 6, :]))
        for c in range(NCH):
            p.op("vector", lambda e, c=c: e.tensor_copy(slotC[:], xnb[:, c, :]))
            p.op("vector", lambda e: e.tensor_tensor(slotC[:], slotC[:], slotA[:], op=ALU.subtract))
            p.op("vector", lambda e, c=c: e.tensor_tensor(xnb[:, c, :], slotC[:], slotB[:], op=ALU.mult))

        if debug:
            dma(p, "sync", dbg_xn, xnb[:, 0, :])
        floorE = p.mark()
        qtileE = (qtile, usb)
        wtlE3 = (wtl, wtl2, wtlB)
        chmE, ev0E, ev1E, outE = {}, {}, {}, {}
        for oc in range(32):
            wb = wtlE3[oc % 3]
            qt = qtileE[oc % 2]
            p.op("gpsimd", lambda e, wb=wb, oc=oc: e.dma_start(out=wb[:], in_=wq4_in[oc]),
                 inc=16, wait_at=chmE.get(oc - 3, floorE))
            dmam = p.mark()
            p.chain("tensor", [
                lambda e, c=c, wb=wb, st=(c == 0), sp=(c == NCH - 1): e.matmul(
                    psA[:, 0, :], wb[:, c, :], xnb[:, c, 0:512], start=st, stop=sp)
                for c in range(NCH)], wait_at=mmax(dmam, ev0E.get(oc - 1, floorE)))
            m0 = p.mark()
            p.chain("tensor", [
                lambda e, c=c, wb=wb, st=(c == 0), sp=(c == NCH - 1): e.matmul(
                    psA[:, 1, :], wb[:, c, :], xnb[:, c, 512:1024], start=st, stop=sp)
                for c in range(NCH)], wait_at=ev1E.get(oc - 1, floorE))
            chmE[oc] = p.mark()
            p.op("scalar", lambda e, qt=qt: e.copy(qt[:, 0:512], psA[:, 0, :]),
                 wait_at=mmax(m0, outE.get(oc - 2, floorE)))
            ev0E[oc] = p.mark()
            p.op("scalar", lambda e, qt=qt: e.copy(qt[:, 512:1024], psA[:, 1, :]),
                 wait_at=chmE[oc])
            ev1E[oc] = p.mark()
            p.op("sync", lambda e, qt=qt, oc=oc: e.dma_start(
                out=qscr.ap()[oc * 128:(oc + 1) * 128, :], in_=qt), inc=16,
                wait_at=ev1E[oc])
            outE[oc] = p.mark()

        floorF = p.mark()
        qtsP = (qts, qtsB)
        mKld, mVld = {}, {}
        mGkv, mDiagDone = {}, {}
        mSlast, mEvTT, mRecipF, mRcbF = {}, {}, {}, {}
        mSH, mExp, mPV = {}, {}, {}
        pendF = []

        def flushF(p=p):
            while pendF:
                pendF.pop(0)()

        for qg in range(2):
            for g in range(8):
                ga = qg * 8 + g
                for s in range(8):
                    p.op("sync", lambda e, g=g, s=s: e.dma_start(
                        out=kgb[:, s * 8:(s + 1) * 8, :].rearrange("p a b -> p (a b)"),
                        in_=kscr.ap()[g * 128:(g + 1) * 128, s * 1024:(s + 1) * 1024]),
                        inc=16, wait_at=(mGkv.get(ga - 1, floorF) if s == 0 else 0))
                    mKld[(ga, s)] = p.mark()
                    p.op("sync", lambda e, g=g, s=s: e.dma_start(
                        out=vgb[:, s * 8:(s + 1) * 8, :],
                        in_=vscr.ap()[s * 1024:(s + 1) * 1024, g * 128:(g + 1) * 128].rearrange(
                            "(a pp) h -> pp a h", pp=128)), inc=16, wait_at=0)
                    mVld[(ga, s)] = p.mark()
                p.op("sync", lambda e, ga=ga: e.dma_start(out=kdiag, in_=kdscr.ap()[ga]),
                     inc=16, wait_at=mDiagDone.get(ga - 1, floorF))
                p.op("sync", lambda e, ga=ga: e.dma_start(
                    out=vdiag, in_=vdscr.ap()[ga].rearrange("(a pp) h -> pp a h", pp=128)),
                    inc=16, wait_at=0)
                mKVd = p.mark()

                for ih in range(4):
                    h = 4 * g + ih
                    iq = ga * 4 + ih
                    qt_ = qtsP[iq % 2]
                    p.op("sync", lambda e, h=h, qg=qg, qt_=qt_: e.dma_start(
                        out=qt_[:], in_=qscr.ap()[h * 128:(h + 1) * 128, qg * 512:(qg + 1) * 512]),
                        inc=16, wait_at=mSlast.get(iq - 2, floorF))
                    mQ = p.mark()

                    def emit_S(sx, iq=iq, ga=ga, qt_=qt_, mQ=mQ, mKVd=mKVd):
                        b0 = (sx % 2) * 2
                        wW = (mExp.get((iq, sx - 2)) if sx >= 2
                              else mExp.get((iq - 1, 34 + sx), floorF))
                        if sx < 32:
                            w = mmax(wW, mKld[(ga, sx // 4)], mQ if sx == 0 else 0)
                            p.chain("tensor", [
                                lambda e, sx=sx, k=k, b0=b0, qt_=qt_: e.matmul(
                                    psA[:, b0 + k, :], kgb[:, 2 * sx + k, :], qt_[:],
                                    start=True, stop=True)
                                for k in range(2)], wait_at=w)
                        else:
                            dd = sx - 32
                            w0 = 128 * dd
                            p.op("tensor", lambda e, dd=dd, w0=w0, b0=b0, qt_=qt_: e.matmul(
                                psA[:, b0, w0:512], kdiag[:, dd * 128:(dd + 1) * 128], qt_[:, w0:512],
                                start=True, stop=True, skip_group_check=True),
                                wait_at=mmax(wW, mKVd))
                        mSH[(iq, sx)] = p.mark()

                    def emit_tail(sx, iq=iq, ga=ga, qg=qg, mKVd=mKVd):
                        b0 = (sx % 2) * 2
                        wPt = (mPV.get((iq, sx - 2)) if sx >= 2
                               else mPV.get((iq - 1, 35), floorF))
                        if sx < 32:
                            hb = sx % 2
                            quad = sx // 2
                            p.op("scalar", lambda e, qg=qg, quad=quad, hb=hb, b0=b0: e.activation(
                                ptb4[:, hb * 1024:(hb + 1) * 1024],
                                psA[:, b0:b0 + 2, :].rearrange("p a b -> p (a b)"), AF.Exp,
                                bias=biasq[:, qg * 16 + quad:qg * 16 + quad + 1], scale=f32(INV_SQRT_HD)),
                                wait_at=mmax(mSH[(iq, sx)], wPt))
                            mExp[(iq, sx)] = p.mark()
                            wPV = mmax(mExp[(iq, sx)], mVld[(ga, sx // 4)],
                                       mEvTT.get(iq - 1, floorF) if sx == 0 else 0,
                                       mRecipF.get(iq - 1, floorF) if sx == 0 else 0)
                            p.chain("tensor", [
                                lambda e, sx=sx, k=k, hb=hb, st=(sx == 0 and k == 0): e.matmul(
                                    psA[:, 4, :], vgb[:, 2 * sx + k, :],
                                    ptb4[:, hb * 1024 + k * 512:hb * 1024 + (k + 1) * 512],
                                    start=st, stop=False, skip_group_check=True)
                                for k in range(2)] + [
                                lambda e, sx=sx, k=k, hb=hb, st=(sx == 0 and k == 0): e.matmul(
                                    psA[0:1, 5, :], onesb[:],
                                    ptb4[:, hb * 1024 + k * 512:hb * 1024 + (k + 1) * 512],
                                    start=st, stop=False, skip_group_check=True)
                                for k in range(2)], wait_at=wPV)
                            mPV[(iq, sx)] = p.mark()
                        else:
                            dd = sx - 32
                            w0 = 128 * dd
                            p.op("vector", lambda e, w0=w0, b0=b0: e.tensor_tensor(
                                psA[:, b0, w0:w0 + 128], psA[:, b0, w0:w0 + 128], tri[:], op=ALU.add),
                                wait_at=mSH[(iq, sx)])
                            mTri = p.mark()
                            p.op("scalar", lambda e, w0=w0, dd=dd, b0=b0: e.activation(
                                ptb4[:, dd * 512 + w0:dd * 512 + 512], psA[:, b0, w0:512], AF.Exp,
                                bias=f32(-EXP_SHIFT), scale=f32(INV_SQRT_HD)),
                                wait_at=mmax(mTri, wPt))
                            mExp[(iq, sx)] = p.mark()
                            p.chain("tensor", [
                                lambda e, dd=dd, w0=w0: e.matmul(
                                    psA[:, 4, w0:512], vdiag[:, dd, :],
                                    ptb4[:, dd * 512 + w0:dd * 512 + 512],
                                    start=False, stop=(dd == 3), skip_group_check=True),
                                lambda e, dd=dd, w0=w0: e.matmul(
                                    psA[0:1, 5, w0:512], onesb[:],
                                    ptb4[:, dd * 512 + w0:dd * 512 + 512],
                                    start=False, stop=(dd == 3), skip_group_check=True),
                            ], wait_at=mmax(mExp[(iq, sx)], mKVd))
                            mPV[(iq, sx)] = p.mark()

                    emit_S(0)
                    flushF()
                    for s in range(1, 37):
                        if s < 36:
                            emit_S(s)
                        emit_tail(s - 1)
                    mSlast[iq] = mSH[(iq, 35)]
                    if ih == 3:
                        mGkv[ga] = mPV[(iq, 31)]
                        mDiagDone[ga] = mPV[(iq, 35)]

                    def epiF(iq=iq, h=h, qg=qg):
                        p.op("vector", lambda e: e.reciprocal(rs[:], psA[0:1, 5, :]),
                             wait_at=mPV[(iq, 35)])
                        mRecipF[iq] = p.mark()
                        p.op("tensor", lambda e: e.matmul(psA[:, 6, :], onesf[:], rs[:], start=True, stop=True),
                             wait_at=mmax(mRecipF[iq], mRcbF.get(iq - 1, floorF)))
                        mBc = p.mark()
                        p.op("scalar", lambda e: e.copy(rcb[:], psA[:, 6, :]),
                             wait_at=mmax(mBc, mEvTT.get(iq - 1, floorF)))
                        mRcbF[iq] = p.mark()
                        p.op("vector", lambda e, h=h, qg=qg: e.tensor_tensor(
                            ctxb[:, h, qg * 512:(qg + 1) * 512], psA[:, 4, :], rcb[:], op=ALU.mult),
                            wait_at=mmax(mRcbF[iq], mPV[(iq, 35)]))
                        mEvTT[iq] = p.mark()
                    pendF.append(epiF)
        flushF()

        if debug:
            dma(p, "sync", dbg_ctx, ctxb[:, 0, 0:512])
        dma(p, "gpsimd", lab, la4_in)
        for i in range(8):
            p.op("vector", lambda e, i=i: e.tensor_tensor(gts[:], wtown1[:, i, :], alb1[:], op=ALU.mult), drain=True)
            p.op("vector", lambda e, i=i: e.tensor_tensor(sel8[:], wtown2[:, i, :], alb2[:], op=ALU.mult))
            p.op("vector", lambda e: e.tensor_tensor(gts[:], gts[:], sel8[:], op=ALU.add), drain=True)
            p.op("tensor", lambda e: e.transpose(psA[0:8, 3, 0:128], gts[:], ident[:]))
            p.op("scalar", lambda e, i=i: e.copy(wfm[:, i * 128:(i + 1) * 128], psA[0:8, 3, 0:128]))
        p.op("tensor", lambda e: e.matmul(psA[:, 0, :], ssel_sb[:], wfm[:, 0:512], start=True, stop=True))
        p.op("tensor", lambda e: e.matmul(psA[:, 1, :], ssel_sb[:], wfm[:, 512:1024], start=True, stop=True))
        p.op("scalar", lambda e: e.activation(wbcb[:, 0:512], psA[:, 0, :], AF.Copy, scale=f32(1.0 / 16.0)))
        p.op("scalar", lambda e: e.activation(wbcb[:, 512:1024], psA[:, 1, :], AF.Copy, scale=f32(1.0 / 16.0)))
        for half in range(2):
            hsl = slice(half * 512, (half + 1) * 512)
            p.chain("tensor", [
                lambda e, c=c, half=half, hsl=hsl, st=(c == 0), sp=(c == NCH - 1): e.matmul(
                    psA[:, half, :], lab[:, c, :], xnb[:, c, hsl], start=st, stop=sp)
                for c in range(NCH)])
        p.op("vector", lambda e: e.tensor_tensor(usb[:, 0:512], psA[:, 0, :], wbcb[:, 0:512], op=ALU.mult))
        p.op("vector", lambda e: e.tensor_tensor(usb[:, 512:1024], psA[:, 1, :], wbcb[:, 512:1024], op=ALU.mult))

        if debug:
            dma(p, "sync", dbg_wb, wbcb[:])
            dma(p, "sync", dbg_cnt, cgrow[:])
            dma(p, "sync", dbg_al[:, 0:8], alb1[:])
            dma(p, "sync", dbg_al[:, 8:16], alb2[:])
            dma(p, "sync", dbg_wt[:, 0, :], wtown1[:].rearrange("p a b -> p (a b)"))
            dma(p, "sync", dbg_wt[:, 1, :], wtown2[:].rearrange("p a b -> p (a b)"))
            dma(p, "sync", dbg_hf, hscr.ap())
            dma(p, "sync", dbg_us, usb[:])
        p.op("vector", lambda e: e.memset(slotC[:], 0.0))
        p.op("vector", lambda e: e.memset(slotD[:], 0.0))
        floorH = p.mark()
        chWm, chFm = {}, {}
        vA0m, vA1m, sB0m, sB1m = {}, {}, {}, {}
        addm, sqm, st1m, st2m, cdm, outm, cpm = {}, {}, {}, {}, {}, {}, {}

        def stats_emit(j):
            p.chain("tensor", [
                lambda e: e.matmul(psA[0:1, 4, :], onesb[:], x1bb[:, 0:512], start=True, stop=True),
                lambda e: e.matmul(psA[0:1, 5, :], onesb[:], x1bb[:, 512:1024], start=True, stop=True),
            ], wait_at=mmax(cpm[j], cdm.get(j - 1, floorH)))
            st1m[j] = p.mark()
            p.op("vector", lambda e: e.tensor_tensor(slotC[0:1, 0:512], slotC[0:1, 0:512], psA[0:1, 4, :], op=ALU.add),
                 wait_at=st1m[j])
            p.op("vector", lambda e: e.tensor_tensor(slotC[0:1, 512:1024], slotC[0:1, 512:1024], psA[0:1, 5, :], op=ALU.add),
                 wait_at=0)
            m_c = p.mark()
            p.chain("tensor", [
                lambda e: e.matmul(psA[0:1, 4, :], onesb[:], sqx[:, 0:512], start=True, stop=True),
                lambda e: e.matmul(psA[0:1, 5, :], onesb[:], sqx[:, 512:1024], start=True, stop=True),
            ], wait_at=mmax(sqm[j], m_c))
            st2m[j] = p.mark()
            p.op("vector", lambda e: e.tensor_tensor(slotD[0:1, 0:512], slotD[0:1, 0:512], psA[0:1, 4, :], op=ALU.add),
                 wait_at=st2m[j])
            p.op("vector", lambda e: e.tensor_tensor(slotD[0:1, 512:1024], slotD[0:1, 512:1024], psA[0:1, 5, :], op=ALU.add),
                 wait_at=0)
            cdm[j] = p.mark()

        for oc in range(32):
            wb = wtlH[oc % 2]
            wb2 = wtl2H[oc % 2]
            lb_ = lbtH[oc % 2]
            p.op("gpsimd", lambda e, wb=wb, oc=oc: e.dma_start(out=wb[:], in_=wo4_in[oc]),
                 inc=16, wait_at=chWm.get(oc - 2, floorH))
            p.op("gpsimd", lambda e, wb2=wb2, oc=oc: e.dma_start(out=wb2[:], in_=fw4_in[oc]),
                 inc=16, wait_at=chFm.get(oc - 2, floorH))
            p.op("gpsimd", lambda e, lb_=lb_, oc=oc: e.dma_start(out=lb_[:], in_=lb_in[:, oc * 128:(oc + 1) * 128]),
                 inc=16, wait_at=0)
            dmam = p.mark()
            p.chain("tensor", [
                lambda e, c=c, wb=wb, st=(c == 0), sp=(c == NCH - 1): e.matmul(
                    psA[:, 0, :], wb[:, c, :], ctxb[:, c, 0:512], start=st, stop=sp)
                for c in range(NCH)], wait_at=mmax(dmam, vA0m.get(oc - 1, floorH)))
            m_s0 = p.mark()
            p.chain("tensor", [
                lambda e, c=c, wb=wb, st=(c == 0), sp=(c == NCH - 1): e.matmul(
                    psA[:, 1, :], wb[:, c, :], ctxb[:, c, 512:1024], start=st, stop=sp)
                for c in range(NCH)], wait_at=vA1m.get(oc - 1, floorH))
            m_s1 = p.mark()
            chWm[oc] = m_s1
            if oc > 0:
                stats_emit(oc - 1)
            for half in range(2):
                hsl = slice(half * 512, (half + 1) * 512)
                mm2 = [lambda e, c=c, half=half, hsl=hsl, wb2=wb2: e.matmul(
                    psA[:, 2 + half, :], wb2[:, c, :], xnb[:, c, hsl], start=(c == 0), stop=False)
                    for c in range(NCH)]
                mm2.append(lambda e, half=half, hsl=hsl, lb_=lb_: e.matmul(
                    psA[:, 2 + half, :], lb_[:], usb[:, hsl], start=False, stop=True))
                p.chain("tensor", mm2,
                        wait_at=(sB0m if half == 0 else sB1m).get(oc - 1, floorH))
                if half == 0:
                    m_f0 = p.mark()
            m_f1 = p.mark()
            chFm[oc] = m_f1
            p.op("vector", lambda e, oc=oc: e.tensor_scalar(slotA[:, 0:512], psA[:, 0, :], boc[:, oc:oc + 1], None, op0=ALU.add),
                 wait_at=mmax(m_s0, cpm.get(oc - 1, floorH), outm.get(oc - 1, floorH), st1m.get(oc - 1, floorH)))
            vA0m[oc] = p.mark()
            p.op("vector", lambda e, oc=oc: e.tensor_scalar(slotA[:, 512:1024], psA[:, 1, :], boc[:, oc:oc + 1], None, op0=ALU.add),
                 wait_at=m_s1)
            vA1m[oc] = p.mark()
            p.op("scalar", lambda e: e.activation(slotB[:, 0:512], psA[:, 2, :], AF.Silu),
                 wait_at=mmax(m_f0, addm.get(oc - 1, floorH)))
            sB0m[oc] = p.mark()
            p.op("scalar", lambda e: e.activation(slotB[:, 512:1024], psA[:, 3, :], AF.Silu),
                 wait_at=m_f1)
            sB1m[oc] = p.mark()
            if debug and oc == 1:
                dma(p, "sync", dbg_sg[:, 0, :], slotA[:])
                dma(p, "sync", dbg_sg[:, 1, :], slotB[:])
            p.op("vector", lambda e: e.tensor_tensor(slotA[:], slotA[:], slotB[:], op=ALU.mult),
                 wait_at=sB1m[oc])
            m_vM = p.mark()
            p.op("sync", lambda e, oc=oc: e.dma_start(out=slotB[:], in_=xor_[:, oc, :]),
                 inc=16, wait_at=m_vM)
            m_x = p.mark()
            p.op("vector", lambda e: e.tensor_tensor(slotA[:], slotB[:], slotA[:], op=ALU.add),
                 wait_at=m_x)
            addm[oc] = p.mark()
            p.op("sync", lambda e, oc=oc: e.dma_start(out=x1r[:, oc, :], in_=slotA[:]),
                 inc=16, wait_at=addm[oc])
            outm[oc] = p.mark()
            p.op("vector", lambda e: e.tensor_copy(x1bb, slotA[:]),
                 wait_at=st1m.get(oc - 1, floorH))
            cpm[oc] = p.mark()
            p.op("scalar", lambda e: e.activation(sqx, x1bb, AF.Square),
                 wait_at=mmax(cpm[oc], st2m.get(oc - 1, floorH)))
            sqm[oc] = p.mark()
        stats_emit(31)
        if debug:
            dma(p, "sync", dbg_st[0:1, :], slotC[0:1, :])
            dma(p, "sync", dbg_st[1:2, :], slotD[0:1, :])
        p.op("vector", lambda e: e.tensor_scalar_mul(morow[0:1, :], slotC[0:1, :], 1.0 / D))
        p.op("vector", lambda e: e.tensor_scalar_mul(slotD[0:1, :], slotD[0:1, :], 1.0 / D))
        p.op("vector", lambda e: e.tensor_tensor(slotC[0:1, :], morow[0:1, :], morow[0:1, :], op=ALU.mult))
        p.op("vector", lambda e: e.tensor_tensor(slotD[0:1, :], slotD[0:1, :], slotC[0:1, :], op=ALU.subtract))
        p.op("vector", lambda e: e.tensor_scalar_add(slotD[0:1, :], slotD[0:1, :], LN_EPS))
        p.op("scalar", lambda e: e.activation(slotD[0:1, :], slotD[0:1, :], AF.Sqrt))
        p.op("vector", lambda e: e.reciprocal(slotC[0:1, :], slotD[0:1, :]))
        if debug:
            dma(p, "sync", dbg_st[2:3, :], morow[0:1, :])
            dma(p, "sync", dbg_st[3:4, :], slotC[0:1, :])
        for half in range(2):
            hsl = slice(half * 512, (half + 1) * 512)
            p.op("tensor", lambda e, hsl=hsl: e.matmul(psA[:, 6, :], onesf[:], morow[0:1, hsl], start=True, stop=True))
            p.op("scalar", lambda e, hsl=hsl: e.copy(slotA[:, hsl], psA[:, 6, :]))
            p.op("tensor", lambda e, hsl=hsl: e.matmul(psA[:, 6, :], onesf[:], slotC[0:1, hsl], start=True, stop=True))
            p.op("scalar", lambda e, hsl=hsl: e.copy(slotB[:, hsl], psA[:, 6, :]))
        if debug:
            dma(p, "sync", dbg_st[4:5, :], slotA[0:1, :])
            dma(p, "sync", dbg_st[5:6, :], slotB[0:1, :])
        floorH2 = p.mark()
        mrdN, mcpN = {}, {}
        for c in range(NCH):
            xb_ = (x1bb, sqx)[c % 2]
            p.op("gpsimd", lambda e, xb_=xb_, c=c: e.dma_start(out=xb_, in_=x1r[:, c, :]),
                 inc=16, wait_at=mcpN.get(c - 2, floorH2))
            mrdN[c] = p.mark()
            p.op("vector", lambda e, xb_=xb_: e.tensor_copy(slotC[:], xb_),
                 wait_at=(mmax(mrdN[c], floorH2) if c == 0 else mrdN[c]))
            mcpN[c] = p.mark()
            p.op("vector", lambda e: e.tensor_tensor(slotC[:], slotC[:], slotA[:], op=ALU.subtract),
                 wait_at=0)
            p.op("vector", lambda e, c=c: e.tensor_tensor(xnb[:, c, :], slotC[:], slotB[:], op=ALU.mult),
                 wait_at=0)

        if debug:
            dma(p, "sync", dbg_xn1, xnb[:, 0, :])
        floorI = p.mark()
        wtlI3 = (wtl, wtl2, ctb2)
        chmI, ev0I, ev1I, outI = {}, {}, {}, {}
        for wi, (w4, bc, scr) in enumerate(
                ((l1q4_in, bqc, q1scr), (l1k4_in, bkc, k1scr), (l1v4_in, bvc, v1scr))):
            for oc in range(32):
                u = wi * 32 + oc
                wb = wtlI3[u % 3]
                qt = qtileE[u % 2]
                p.op("gpsimd", lambda e, wb=wb, w4=w4, oc=oc: e.dma_start(out=wb[:], in_=w4[oc]),
                     inc=16, wait_at=chmI.get(u - 3, floorI))
                dmam = p.mark()
                p.chain("tensor", [
                    lambda e, c=c, wb=wb, st=(c == 0), sp=(c == NCH - 1): e.matmul(
                        psA[:, 0, :], wb[:, c, :], xnb[:, c, 0:512], start=st, stop=sp)
                    for c in range(NCH)], wait_at=mmax(dmam, ev0I.get(u - 1, floorI)))
                m0 = p.mark()
                p.chain("tensor", [
                    lambda e, c=c, wb=wb, st=(c == 0), sp=(c == NCH - 1): e.matmul(
                        psA[:, 1, :], wb[:, c, :], xnb[:, c, 512:1024], start=st, stop=sp)
                    for c in range(NCH)], wait_at=ev1I.get(u - 1, floorI))
                chmI[u] = p.mark()
                p.op("vector", lambda e, oc=oc, bc=bc, qt=qt: e.tensor_scalar(
                    qt[:, 0:512], psA[:, 0, :], bc[:, oc:oc + 1], None, op0=ALU.add),
                    wait_at=mmax(m0, outI.get(u - 2, floorI)))
                ev0I[u] = p.mark()
                p.op("vector", lambda e, oc=oc, bc=bc, qt=qt: e.tensor_scalar(
                    qt[:, 512:1024], psA[:, 1, :], bc[:, oc:oc + 1], None, op0=ALU.add),
                    wait_at=chmI[u])
                ev1I[u] = p.mark()
                p.op("sync", lambda e, qt=qt, oc=oc, scr=scr: e.dma_start(
                    out=scr.ap()[oc * 128:(oc + 1) * 128, :], in_=qt), inc=16,
                    wait_at=ev1I[u])
                outI[u] = p.mark()
        floorI2 = p.mark()
        mLastPE, mOutI2 = {}, {}
        for blk in range(8):
            bs = slice(blk * 128, (blk + 1) * 128)
            first = True
            for buf, scr in ((qb, q1scr), (kb, k1scr), (vb, v1scr)):
                for i4 in range(4):
                    p.op("sync", lambda e, buf=buf, scr=scr, i4=i4, bs=bs: e.dma_start(
                        out=buf[:, i4 * 8:(i4 + 1) * 8, :],
                        in_=scr.ap().rearrange("(hh pp) n -> pp hh n", pp=128)[:, i4 * 8:(i4 + 1) * 8, bs]),
                        inc=16, wait_at=(mLastPE.get(blk - 1, floorI2) if first else 0))
                    first = False
            mLd = p.mark()
            mExpG, mPVG, mVcpG, mEvG, mRcb2G = {}, {}, {}, {}, {}
            pendI = []
            for gi in range(8):
                hh = gi * 16
                p.chain("tensor", [
                    lambda e, j=j, hh=hh: e.matmul(
                        psA[0:32, 1, j * 32:(j + 1) * 32],
                        kb[:, :, hh + j], qb[:, :, hh + j], start=True, stop=True)
                    for j in range(16)], wait_at=(mmax(mLd, mExpG.get(gi - 1, floorI2))
                                                  if gi == 0 else mExpG[gi - 1]))
                mS = p.mark()
                for half8 in range(2):
                    p.chain("tensor", [
                        lambda e, j=j, hh=hh, half8=half8: e.transpose(
                            psVt[0:32, 0, (j % 8) * 128:(j % 8 + 1) * 128],
                            vb[:, :, hh + half8 * 8 + j], identb[:])
                        for j in range(8)],
                        wait_at=(mmax(mVcpG.get(gi - 1, floorI2), mLd if gi == 0 else 0)
                                 if half8 == 0 else mVcpG[(gi, 0)]))
                    mT = p.mark()
                    p.op("scalar", lambda e, half8=half8: e.copy(
                        vts1[0:32, half8 * 8:(half8 + 1) * 8, :],
                        psVt[0:32, 0, :].rearrange("p (j dd) -> p j dd", dd=128)),
                        wait_at=(mmax(mT, mPVG.get(gi - 1, floorI2)) if half8 == 0 else mT))
                    mVcpG[(gi, half8)] = p.mark()
                mVcpG[gi] = mVcpG[(gi, 1)]
                while pendI:
                    pendI.pop(0)()
                p.op("scalar", lambda e: e.activation(
                    pt1, psA[0:32, 1, :], AF.Exp, bias=f32(-EXP_SHIFT), scale=f32(INV_SQRT_HD)),
                    wait_at=mmax(mS, mPVG.get(gi - 1, floorI2)))
                mExpG[gi] = p.mark()
                p.op("tensor", lambda e: e.matmul(psA[0:1, 2, :], onesb[0:32, :], pt1, start=True, stop=True),
                     wait_at=mmax(mExpG[gi], mRcb2G.get(gi - 1, floorI2)))
                mSum = p.mark()
                p.op("vector", lambda e: e.reciprocal(rs[:], psA[0:1, 2, :]),
                     wait_at=mSum)
                mRc = p.mark()
                p.chain("tensor", [
                    lambda e, j=j: e.matmul(
                        psA[:, 3, j * 32:(j + 1) * 32], vts1[0:32, j, :], pt1[:, j * 32:(j + 1) * 32],
                        start=True, stop=True)
                    for j in range(16)], wait_at=mmax(mVcpG[gi], mExpG[gi], mEvG.get(gi - 1, floorI2)))
                mPVG[gi] = p.mark()

                def epiI(gi=gi, hh=hh, mRc=mRc, blk=blk):
                    p.op("tensor", lambda e: e.matmul(psA[:, 2, :], onesf[:], rs[:], start=True, stop=True),
                         wait_at=mRc)
                    mBc2 = p.mark()
                    p.op("scalar", lambda e: e.copy(rcb[:], psA[:, 2, :]),
                         wait_at=mmax(mBc2, mEvG.get(gi - 1, floorI2)))
                    mRcb2G[gi] = p.mark()
                    p.op("vector", lambda e, hh=hh: e.tensor_tensor(
                        ctb2[:, :, hh:hh + 16].rearrange("p h n -> p n h"),
                        psA[:, 3, :].rearrange("p (n h) -> p n h", h=32),
                        rcb[:].rearrange("p (n h) -> p n h", h=32),
                        op=ALU.mult),
                        wait_at=mmax(mPVG[gi], mRcb2G[gi],
                                     mOutI2.get(blk - 1, floorI2) if gi == 0 else 0))
                    mEvG[gi] = p.mark()
                pendI.append(epiI)
            while pendI:
                pendI.pop(0)()
            mLastPE[blk] = mPVG[7]
            for i4 in range(4):
                sl = slice(i4 * 8, (i4 + 1) * 8)
                p.op("sync", lambda e, sl=sl, bs=bs: e.dma_start(
                    out=ctxscr.ap().rearrange("(hh pp) n -> pp hh n", pp=128)[:, sl, bs],
                    in_=ctb2[:, sl, :]), inc=16, wait_at=(mEvG[7] if i4 == 0 else 0))
            mOutI2[blk] = p.mark()

        dma(p, "gpsimd", a1b, a1_in)
        dma(p, "gpsimd", b1tb, b1t_in)
        p.op("vector", lambda e: e.memset(cnacc[:], 0.0))
        floorJ = p.mark()
        w1nbP = (w1nb_f, w1nbB)
        effbP = (effb, effbB)
        effsP = (effs, effsB)
        mVaJ, mSqJ, mRdJ, mDmJ = {}, {}, {}, {}
        for oc in range(32):
            wn = w1nbP[oc % 2]
            p.op("gpsimd", lambda e, wn=wn, oc=oc: e.dma_start(
                out=wn, in_=w1n4_in[oc].rearrange("p c i -> p (c i)")),
                inc=16, wait_at=mVaJ.get((oc - 2) * 8 + 7, floorJ))
            mDmJ[oc] = p.mark()
            for ib in range(8):
                u = oc * 8 + ib
                bq = u % 2
                eb_ = effbP[u % 2]
                es_ = effsP[u % 2]
                p.op("tensor", lambda e, oc=oc, ib=ib, bq=bq: e.matmul(
                    psA[:, bq, :], b1tb[:, oc * 128:(oc + 1) * 128], a1b[:, ib * 512:(ib + 1) * 512],
                    start=True, stop=True), wait_at=mmax(mDmJ[oc], mVaJ.get(u - 2, floorJ)))
                mmj = p.mark()
                p.op("vector", lambda e, oc=oc, ib=ib, bq=bq, eb_=eb_, wn=wn: e.tensor_tensor(
                    eb_, psA[:, bq, :], wn[:, ib * 512:(ib + 1) * 512], op=ALU.add),
                    wait_at=mmax(mmj, mSqJ.get(u - 2, floorJ)))
                mVaJ[u] = p.mark()
                p.op("scalar", lambda e, eb_=eb_, es_=es_: e.activation(es_, eb_, AF.Square),
                     wait_at=mmax(mVaJ[u], mRdJ.get(u - 2, floorJ)))
                mSqJ[u] = p.mark()
                p.op("vector", lambda e, es_=es_: e.reduce_sum(rs1[:], es_, axis=AXX),
                     wait_at=mSqJ[u])
                mRdJ[u] = p.mark()
                p.op("vector", lambda e, oc=oc: e.tensor_tensor(cnacc[:, oc:oc + 1], cnacc[:, oc:oc + 1], rs1[:], op=ALU.add),
                     wait_at=0, drain=True)
        p.op("scalar", lambda e: e.activation(cnacc[:], cnacc[:], AF.Sqrt))
        p.op("vector", lambda e: e.reciprocal(rsc[:], cnacc[:]))
        p.op("vector", lambda e: e.tensor_tensor(rsc[:], rsc[:], magc[:], op=ALU.mult), drain=True)

        dma(p, "gpsimd", a1tb, a1t4_in)
        for half in range(2):
            hsl = slice(half * 512, (half + 1) * 512)
            p.chain("tensor", [
                lambda e, c=c, half=half, hsl=hsl, st=(c == 0), sp=(c == NCH - 1): e.matmul(
                    psA[0:32, 2 + half, :], a1tb[:, c, :], xnb[:, c, hsl], start=st, stop=sp)
                for c in range(NCH)])
        p.op("scalar", lambda e: e.copy(u1b[:, 0:512], psA[0:32, 2, :]))
        p.op("scalar", lambda e: e.copy(u1b[:, 512:1024], psA[0:32, 3, :]))
        floorK = p.mark()
        ctxrdP = (ctxrd, gw1b.rearrange("p a b -> p (a b)"))
        chmK, vA0K, vA1K, cpK, addK, out2K = {}, {}, {}, {}, {}, {}
        for oc in range(32):
            wb2 = wtl2H[oc % 2]
            cx_ = ctxrdP[oc % 2]
            p.op("gpsimd", lambda e, wb2=wb2, oc=oc: e.dma_start(out=wb2[:], in_=w1t4_in[oc]),
                 inc=16, wait_at=chmK.get(oc - 2, floorK))
            p.op("sync", lambda e, oc=oc, cx_=cx_: e.dma_start(
                out=cx_, in_=ctxscr.ap()[oc * 128:(oc + 1) * 128, :]), inc=16,
                wait_at=cpK.get(oc - 2, floorK))
            dmam = p.mark()
            for half in range(2):
                hsl = slice(half * 512, (half + 1) * 512)
                mm = [lambda e, c=c, half=half, hsl=hsl, wb2=wb2: e.matmul(
                    psA[:, half, :], wb2[:, c, :], xnb[:, c, hsl], start=(c == 0), stop=False)
                    for c in range(NCH)]
                mm.append(lambda e, half=half, hsl=hsl, oc=oc: e.matmul(
                    psA[:, half, :], b1tb[:, oc * 128:(oc + 1) * 128], u1b[:, hsl], start=False, stop=True))
                p.chain("tensor", mm, wait_at=(
                    mmax(dmam, vA0K.get(oc - 1, floorK)) if half == 0
                    else vA1K.get(oc - 1, floorK)))
                if half == 0:
                    m_k0 = p.mark()
            m_k1 = p.mark()
            chmK[oc] = m_k1
            p.op("vector", lambda e, oc=oc: e.tensor_scalar(slotA[:, 0:512], psA[:, 0, :], rsc[:, oc:oc + 1], None, op0=ALU.mult),
                 wait_at=mmax(m_k0, out2K.get(oc - 1, floorK)), drain=(oc == 0))
            vA0K[oc] = p.mark()
            p.op("vector", lambda e, oc=oc: e.tensor_scalar(slotA[:, 512:1024], psA[:, 1, :], rsc[:, oc:oc + 1], None, op0=ALU.mult),
                 wait_at=m_k1)
            vA1K[oc] = p.mark()
            p.op("scalar", lambda e: e.activation(slotB[:], slotA[:], AF.Silu),
                 wait_at=mmax(vA1K[oc], addK.get(oc - 1, floorK)))
            m_sil = p.mark()
            p.op("vector", lambda e, cx_=cx_: e.tensor_copy(slotA[:], cx_),
                 wait_at=m_sil)
            cpK[oc] = p.mark()
            p.op("vector", lambda e: e.tensor_tensor(slotA[:], slotA[:], slotB[:], op=ALU.mult),
                 wait_at=0)
            m_mu = p.mark()
            p.op("sync", lambda e, oc=oc: e.dma_start(out=slotB[:], in_=x1r[:, oc, :]),
                 inc=16, wait_at=m_mu)
            m_x1 = p.mark()
            p.op("vector", lambda e: e.tensor_tensor(slotA[:], slotB[:], slotA[:], op=ALU.add),
                 wait_at=m_x1)
            addK[oc] = p.mark()
            p.op("sync", lambda e, oc=oc: e.dma_start(out=x2r[:, oc, :], in_=slotA[:]),
                 inc=16, wait_at=addK[oc])
            out2K[oc] = p.mark()
        if debug:
            dma(p, "sync", dbg_mr, mrscr.ap()[:, 0:512])
            dma(p, "sync", dbg_k, kscr.ap()[0:128, 0:512])
            dma(p, "sync", dbg_v, vscr.ap()[0:256, 0:128])
            dma(p, "sync", dbg_h, hscr.ap()[:, 0:512])
            dma(p, "sync", dbg_q, qscr.ap()[0:128, :])
            dma(p, "sync", dbg_x1, x1scr.ap()[0:128, :])
            dma(p, "sync", dbg_c1, ctxscr.ap()[0:128, :])
            dma(p, "sync", dbg_q1, q1scr.ap())
            dma(p, "sync", dbg_x1b[:, 0, :], x1r[:, 10, :])
            dma(p, "sync", dbg_x1b[:, 1, :], x1r[:, 20, :])
            dma(p, "sync", dbg_x1b[:, 2, :], x1r[:, 31, :])
            dma(p, "sync", dbg_k1, k1scr.ap())
            dma(p, "sync", dbg_v1, v1scr.ap())
            dma(p, "sync", dbg_c1f, ctxscr.ap())
        p.finish(block)
    return nc


def kernel_fused(inputs, debug=False):
    c32 = np.ascontiguousarray
    xT = c32(np.asarray(inputs["L0_attn_x"], np.float32).reshape(T, D).T)
    Wq = np.asarray(inputs["L0_attn_Wq"], np.float32)
    Wkv = np.asarray(inputs["L0_attn_Wkv"], np.float32)
    WK = c32(Wkv.reshape(D, 8, 2, 128)[:, :, 0, :].reshape(D, 1024))
    WV = c32(Wkv.reshape(D, 8, 2, 128)[:, :, 1, :].reshape(D, 1024))

    def pcf(w):
        return c32(w.reshape(NCH, 128, w.shape[1]).transpose(1, 0, 2))
    wkk4 = pcf(WK)
    wvv4 = pcf(WV)
    gw1t4 = pcf(c32(np.asarray(inputs["L0_ffn_gw1"], np.float32).T))
    wq4 = _tile_w(Wq)
    wo4 = _tile_w(np.asarray(inputs["L0_attn_Wo"], np.float32))
    fw4 = _tile_w(c32(np.asarray(inputs["L0_ffn_weight"], np.float32).T))
    la4 = c32(np.asarray(inputs["L0_ffn_lora_A"], np.float32).transpose(1, 0, 2).reshape(D, 128)
              .reshape(NCH, 128, 128).transpose(1, 0, 2))
    lb = c32(np.asarray(inputs["L0_ffn_lora_B"], np.float32).reshape(128, D))
    ssel = np.zeros((8, 128), np.float32)
    for e_ in range(8):
        ssel[e_, e_ * 16:(e_ + 1) * 16] = 1.0
    l1q4 = _tile_w(c32(np.asarray(inputs["L1_attn_wq"], np.float32).T))
    l1k4 = _tile_w(c32(np.asarray(inputs["L1_attn_wk"], np.float32).T))
    l1v4 = _tile_w(c32(np.asarray(inputs["L1_attn_wv"], np.float32).T))
    W1 = np.asarray(inputs["L1_ffn_W"], np.float32)
    w1t4 = _tile_w(c32(W1.T))
    w1n4 = c32(W1.reshape(32, 128, NCH, 128))
    A1 = np.asarray(inputs["L1_ffn_lora_A"], np.float32)
    B1 = np.asarray(inputs["L1_ffn_lora_B"], np.float32)
    a1t4 = c32(A1.T.reshape(NCH, 128, 32).transpose(1, 0, 2))
    b1t = c32(B1.T)
    tri = np.where(np.arange(128)[:, None] > np.arange(128)[None, :],
                   np.float32(NEG), np.float32(0.0)).astype(np.float32)
    common = {
        "xfull": xT,
        "wkk4": wkk4, "wvv4": wvv4, "gw1t4": gw1t4,
        "gb1c": c32(np.asarray(inputs["L0_ffn_gb1"], np.float32).reshape(32, 1)),
        "gam": c32(np.asarray(inputs["L0_ffn_ln_gamma"], np.float32).reshape(1, 32)),
        "bet": c32(np.asarray(inputs["L0_ffn_ln_beta"], np.float32).reshape(1, 32)),
        "gw2t": c32(np.asarray(inputs["L0_ffn_gw2"], np.float32).T),
        "gb2": c32(np.asarray(inputs["L0_ffn_gb2"], np.float32).reshape(1, 8)),
        "wq4": wq4, "wo4": wo4, "fw4": fw4,
        "boc": _to_col(np.asarray(inputs["L0_attn_bo"], np.float32)),
        "la4": la4, "lb": lb, "ssel": ssel,
        "l1q4": l1q4, "l1k4": l1k4, "l1v4": l1v4,
        "bqc": _to_col(np.asarray(inputs["L1_attn_bq"], np.float32)),
        "bkc": _to_col(np.asarray(inputs["L1_attn_bk"], np.float32)),
        "bvc": _to_col(np.asarray(inputs["L1_attn_bv"], np.float32)),
        "w1t4": w1t4, "w1n4": w1n4, "a1t4": a1t4, "a1": A1, "b1t": b1t,
        "magc": _to_col(np.asarray(inputs["L1_ffn_magnitude"], np.float32)),
        "identf": _ident(), "identb": np.eye(128).astype(ml_dtypes.bfloat16),
        "tri": tri,
    }
    in_maps = []
    for c in range(NCORE):
        biasq = np.empty((128, 32), np.float32)
        for qg in range(2):
            band = 2 * c + qg
            for quad in range(16):
                biasq[:, qg * 16 + quad] = -EXP_SHIFT if quad < band else -1e9
        m = dict(common)
        m["xown"] = c32(xT[:, c * TL:(c + 1) * TL])
        m["biasq"] = biasq
        in_maps.append(m)
    nc = build_fused(debug=debug)
    res = run_bass_kernel_spmd(nc, in_maps, core_ids=list(range(NCORE)))
    if debug:
        return res
    x2 = np.concatenate([res.results[c]["x2"] for c in range(NCORE)], axis=1)
    return np.ascontiguousarray(x2.T).reshape(1, T, D).astype(np.float32)
